# version 15
# baseline (speedup 1.0000x reference)
import numpy as np
import ml_dtypes
import concourse.bass as bass
import concourse.mybir as mybir
from concourse.bass_utils import run_bass_kernel_spmd

F32 = mybir.dt.float32
BF16 = mybir.dt.bfloat16
F8 = mybir.dt.float8e5
OP = mybir.AluOpType
AF = mybir.ActivationFunctionType

L, D, E, NST, RK = 2048, 1024, 2048, 16, 64
TC = 512
NCH = L // TC
NG = E // 128
NDT = D // 128
NF = 4096
NL = 2
BETA = 0.6
NGRP = 2
PIPE = True

BF_NP = ml_dtypes.bfloat16


class Sched:
    ENG = ("sync", "vector", "scalar", "tensor", "gpsimd")

    def __init__(self):
        self.ops = []
        self.bw = {}
        self.br = {}
        self.last_on = {}
        self.last_dma = {}

    def op(self, engine, fn, reads=(), writes=(), dma=None):
        opid = len(self.ops)
        deps = set()
        for b in reads:
            w = self.bw.get(b)
            if w is not None:
                deps.add(w)
        for b in writes:
            deps.update(self.br.get(b, ()))
            w = self.bw.get(b)
            if w is not None:
                deps.add(w)
        deps.discard(opid)
        self.ops.append({"e": engine, "fn": fn, "deps": deps, "dma": dma})
        for b in reads:
            self.br.setdefault(b, []).append(opid)
        for b in writes:
            self.bw[b] = opid
            self.br[b] = []
        self.last_on[engine] = opid
        if dma:
            self.last_dma[dma] = opid
        return opid

    def barrier(self):
        lasts = list(self.last_on.values()) + list(self.last_dma.values())
        for e in self.ENG:
            opid = len(self.ops)
            self.ops.append({"e": e, "fn": None,
                             "deps": set(d for d in lasts if d != opid),
                             "dma": None})
            self.last_on[e] = opid

    def emit(self, nc, block):
        marked = set()
        for i, rec in enumerate(self.ops):
            if rec["dma"]:
                marked.add(i)
            me = rec["e"]
            for d in rec["deps"]:
                if self.ops[d]["e"] != me or self.ops[d]["dma"]:
                    marked.add(d)

        def stream(rec):
            return rec["dma"] if rec["dma"] else rec["e"]

        R = 8
        counters, incno = {}, {}
        dma_slot = {}
        for i, rec in enumerate(self.ops):
            if i in marked:
                s = stream(rec)
                counters[s] = counters.get(s, 0) + 1
                incno[i] = counters[s]
                if rec["dma"]:
                    o = counters[s] - 1
                    dma_slot[i] = (s, o % R, o // R + 1)
        sems = {}
        for s in ("vector", "scalar", "tensor", "gpsimd", "sync"):
            sems[s] = nc.semaphore(f"sem_{s}").__enter__()
        qsems = {}
        for q in ("dms", "dmg"):
            qsems[q] = [nc.semaphore(f"sem_{q}{j}").__enter__() for j in range(R)]
        per_eng = {e: [] for e in self.ENG}
        for i, rec in enumerate(self.ops):
            per_eng[rec["e"]].append(i)

        def emit_engine(eng_obj, ename):
            water = {}
            for i in per_eng[ename]:
                rec = self.ops[i]
                byprod = {}
                for d in rec["deps"]:
                    drec = self.ops[d]
                    if drec["e"] == ename and drec["dma"] is None:
                        continue
                    if drec["dma"]:
                        q, j, u = dma_slot[d]
                        key = (q, j)
                        byprod[key] = max(byprod.get(key, 0), u)
                    else:
                        byprod[drec["e"]] = max(byprod.get(drec["e"], 0), incno[d])
                for ds, v in sorted(byprod.items(), key=str):
                    if water.get(ds, 0) >= v:
                        continue
                    sem = qsems[ds[0]][ds[1]] if isinstance(ds, tuple) else sems[ds]
                    mult = 16 if isinstance(ds, tuple) else 1
                    eng_obj.wait_ge(sem, v * mult)
                    water[ds] = v
                if rec["dma"]:
                    q, j, u = dma_slot[i]
                    if u > 1 and water.get((q, j), 0) < u - 1:
                        eng_obj.wait_ge(qsems[q][j], (u - 1) * 16)
                        water[(q, j)] = u - 1
                    ins = rec["fn"](eng_obj)
                    ins.then_inc(qsems[q][j], 16)
                else:
                    ins = eng_obj.nop() if rec["fn"] is None else rec["fn"](eng_obj)
                    if i in incno:
                        ins.then_inc(sems[rec["e"]], 1)

        @block.sync
        def _(sync):
            emit_engine(sync, "sync")
            for q in ("dms", "dmg"):
                tot = counters.get(q, 0)
                for j in range(R):
                    uses = tot // R + (1 if j < tot % R else 0)
                    if uses:
                        sync.wait_ge(qsems[q][j], uses * 16)

        @block.vector
        def _(v):
            emit_engine(v, "vector")

        @block.scalar
        def _(s):
            emit_engine(s, "scalar")

        @block.tensor
        def _(t):
            emit_engine(t, "tensor")

        @block.gpsimd
        def _(g):
            emit_engine(g, "gpsimd")


def build_program():
    nc = bass.Bass("TRN2", target_bir_lowering=False)
    dram = {}

    def din(name, shape, dt=F32):
        dram[name] = nc.dram_tensor(name, list(shape), dt, kind="ExternalInput").ap()
        return dram[name]

    def dscr(name, shape, dt):
        dram[name] = nc.dram_tensor(name, list(shape), dt, kind="Internal").ap()
        return dram[name]

    din("x_t", (D, L))
    din("mask_t", (D, L), BF16)
    for l in range(NL):
        din(f"in_wT{l}", (D, 2 * E))
        din(f"owT{l}", (E, D))
        din(f"xpw_a{l}", (128, NG * 96))
        din(f"dtwT{l}", (RK, E))
        din(f"alog_a{l}", (128, NG * NST))
        din(f"dtb_a{l}", (128, NG))
        din(f"cb_a{l}", (128, NG))
        din(f"dp_a{l}", (128, NG))
        din(f"cw_a{l}", (128, NG * 4))
        din(f"nw_a{l}", (128, NDT))
    din("W1T", (D, NF))
    din("W2T", (NF, D))
    din("b1r", (1, NF))
    din("b2r", (1, D))
    din("onesr", (1, TC))
    din("ones_col", (128, 1))
    din("ones_row", (1, 128))
    din("sel32", (32, 32), BF16)
    din("consts", (128, 8))
    din("zeros48", (128, 48), BF16)
    out_t = nc.dram_tensor("out_t", [D, L], F32, kind="ExternalOutput").ap()
    DBG = False
    if DBG:
        dbg_xn = nc.dram_tensor("dbg_xn", [D, L], F32, kind="ExternalOutput").ap()
        dbg_a = [nc.dram_tensor(f"dbg_a{l}", [D, L], F32, kind="ExternalOutput").ap()
                 for l in range(NL)]

    for l in range(NL):
        dscr(f"s_inw{l}", (D, 2 * E), BF16)
        dscr(f"s_owT{l}", (E, D), BF16)
    dscr("s_w1", (D, NF), F8)
    dscr("s_w2", (NF, D), F8)
    dscr("s_a", (D, L), F32)
    dscr("s_xn", (D, L), F32)

    sch = Sched()
    from contextlib import ExitStack
    stP = ExitStack()
    stM = ExitStack()

    def sbP(name, shape, dt=F32):
        return stP.enter_context(nc.sbuf_tensor(name, list(shape), dt))

    def sbM(name, shape, dt=F32):
        return stM.enter_context(nc.sbuf_tensor(name, list(shape), dt))

    c_ones = sbP("c_ones", (128, 1))
    c_onesr = sbP("c_onesr", (1, 128))
    c_const = sbP("c_const", (128, 8))
    nc.const_aps.register(F32, -1.0, c_const[:, 0:1])
    nc.const_aps.register(F32, 1.0, c_const[:, 1:2])
    nc.const_aps.register(F32, -0.5, c_const[:, 2:3])
    nc.const_aps.register(F32, 1e-5, c_const[:, 3:4])
    nc.const_aps.register(F32, 1.0 / D, c_const[:, 5:6])
    r_dtw = sbP("r_dtw", (RK, NL * E), BF16)
    r_xpw = sbP("r_xpw", (128, NL * NG * 96), BF16)
    r_wneg = sbP("r_wneg", (128, NL * NG * NST))
    r_dtb = sbP("r_dtb", (128, NL * NG))
    r_cb = sbP("r_cb", (128, NL * NG))
    r_dp = sbP("r_dp", (128, NL * NG))
    r_cw = sbP("r_cw", (128, NL * NG * 4))
    r_nw = sbP("r_nw", (128, NL * NDT))
    r_b1q = sbP("r_b1q", (1, NF), F8)
    r_b2q = sbP("r_b2q", (1, D), F8)
    r_ones8 = sbP("r_ones8", (1, TC), F8)
    a_st = sbP("a_st", (128, NDT * TC))
    x2_st = sbP("x2_st", (128, NDT * TC))
    wS = [sbP(f"wS{i}", (128, 1024), BF16) for i in range(4)]
    row_m = sbP("row_m", (1, TC))[:]
    row_e2 = sbP("row_e2", (1, TC))[:]
    row_rstd = sbP("row_rstd", (1, TC))[:]
    row_rs2 = row_e2
    rstd_rep = sbP("rstd_rep", (128, TC))
    rs2_rep = sbP("rs2_rep", (128, TC))
    pg_f8 = [sbP(f"pg_f8{i}", (128, 1024), F8) for i in range(2)]

    c_sel = sbM("c_sel", (32, 32), BF16)
    xbp = sbM("xbp", (128, NG * (TC + 3)), BF16)
    zg_bf = sbM("zg_bf", (128, NG * TC), BF16)
    rny2 = sbM("rny2", (128, NG * TC), BF16)
    rn_sep = sbM("rn_sep", (128, NDT * TC), BF16)
    halo = sbM("halo", (128, NG * 3), BF16)
    dbc_sb = sbM("dbc_sb", (96, TC))
    bc_bf = sbM("bc_bf", (32, TC), BF16)
    drbf = sbM("drbf", (RK, TC), BF16)
    brep = sbM("brep", (128, NST * TC), BF16)
    crep = sbM("crep", (128, NST * TC), BF16)
    t_delta = [sbM(f"t_delta{i}", (128, TC)) for i in range(2)]
    t_sp = sbM("t_sp", (128, TC))
    t_u = sbM("t_u", (128, TC), BF16)
    t_dA = [sbM(f"t_dA{i}", (128, NGRP * TC)) for i in range(2)]
    t_mi = [sbM(f"t_mi{i}", (128, NGRP * TC), BF16) for i in range(2)]
    t_v = [sbM(f"t_v{i}", (128, NGRP * TC), BF16) for i in range(2)]
    t_h = [sbM(f"t_h{i}", (128, NGRP * TC), BF16) for i in range(2)]
    t_hC = sbM("t_hC", (128, TC), BF16)
    t_yA = sbM("t_yA", (128, TC), BF16)
    t_yB = sbM("t_yB", (128, TC), BF16)
    t_xc = [sbM(f"t_xc{i}", (128, TC)) for i in range(2)]
    st_v = sbM("st_v", (128, NG * NST))
    st_h = sbM("st_h", (128, NG * NST))

    pb = [stP.enter_context(nc.psum_tensor(f"pb{i}", [128, 512], F32)) for i in range(8)]

    beta_b = c_const[:, 4:5].broadcast_to((128, TC))
    rn_bf = rn_sep

    def dma(dst, src, q="dms", reads=(), writes=()):
        eng = "sync" if q == "dms" else "gpsimd"
        sch.op(eng, lambda e, d=dst, s=src: e.dma_start(out=d, in_=s),
               reads=reads, writes=writes, dma=q)

    dma(c_ones[:], dram["ones_col"], writes=["c_ones"])
    dma(c_onesr[:], dram["ones_row"], writes=["c_onesr"])
    dma(c_sel[:], dram["sel32"], writes=["c_sel"])
    dma(c_const[:], dram["consts"], writes=["c_const"])
    dma(x2_st[0:1, 0:NF], dram["b1r"], writes=["x2_st"])
    sch.op("vector", lambda e: e.tensor_copy(r_b1q[:], x2_st[0:1, 0:NF]),
           reads=["x2_st"], writes=["r_b1q"])
    dma(x2_st[0:1, 0:D], dram["b2r"], writes=["x2_st"])
    sch.op("vector", lambda e: e.tensor_copy(r_b2q[:], x2_st[0:1, 0:D]),
           reads=["x2_st"], writes=["r_b2q"])
    dma(x2_st[0:1, 0:TC], dram["onesr"], writes=["x2_st"])
    sch.op("vector", lambda e: e.tensor_copy(r_ones8[:], x2_st[0:1, 0:TC]),
           reads=["x2_st"], writes=["r_ones8"])
    for l in range(NL):
        dma(x2_st[0:RK, 0:E], dram[f"dtwT{l}"], writes=["x2_st"])
        sch.op("vector", lambda e, l=l: e.tensor_copy(
            r_dtw[:, l * E:(l + 1) * E], x2_st[0:RK, 0:E]),
            reads=["x2_st"], writes=["r_dtw"])
        dma(x2_st[:, 0:NG * 96], dram[f"xpw_a{l}"], writes=["x2_st"])
        sch.op("vector", lambda e, l=l: e.tensor_copy(
            r_xpw[:, l * NG * 96:(l + 1) * NG * 96], x2_st[:, 0:NG * 96]),
            reads=["x2_st"], writes=["r_xpw"])
        dma(x2_st[:, 0:NG * NST], dram[f"alog_a{l}"], writes=["x2_st"])
        sch.op("scalar", lambda e: e.activation(a_st[:, 0:NG * NST], x2_st[:, 0:NG * NST], AF.Exp),
               reads=["x2_st"], writes=["a_st"])
        sch.op("vector", lambda e, l=l: e.tensor_single_scalar(
            r_wneg[:, l * NG * NST:(l + 1) * NG * NST], a_st[:, 0:NG * NST], -1.0, OP.mult),
            reads=["a_st"], writes=["r_wneg"])
        for nm, buf in (("dtb_a", r_dtb), ("cb_a", r_cb), ("dp_a", r_dp)):
            dma(buf[:, l * NG:(l + 1) * NG], dram[f"{nm}{l}"], writes=[f"r_{nm}"])
        dma(r_cw[:, l * NG * 4:(l + 1) * NG * 4], dram[f"cw_a{l}"], writes=["r_cw"])
        dma(r_nw[:, l * NDT:(l + 1) * NDT], dram[f"nw_a{l}"], writes=["r_nw"])

    prep_i = [0]

    def prep_cast(src_ap, dst_ap, rows, cols, outdt, skey):
        for p in range(rows // 128):
            rs = p * 128
            for cs in range(0, cols, 512):
                cw = min(512, cols - cs)
                q4 = prep_i[0] % 4
                prep_i[0] += 1
                pin = t_dA[q4 // 2][:, (q4 % 2) * TC:(q4 % 2) * TC + cw]
                kin = f"tdA{q4 // 2}"
                dma(pin, src_ap[rs:rs + 128, cs:cs + cw], writes=[kin])
                if outdt == BF16:
                    pout, kout = wS[q4][:, 0:cw], f"wS{q4}"
                else:
                    pout, kout = pg_f8[q4 % 2][:, 0:cw], f"pg_f8{q4 % 2}"
                sch.op("vector", lambda e, pout=pout, pin=pin: e.tensor_copy(pout, pin),
                       reads=[kin], writes=[kout])
                dma(dst_ap[rs:rs + 128, cs:cs + cw], pout, q="dmg",
                    reads=[kout], writes=[skey])

    for l in range(NL):
        prep_cast(dram[f"in_wT{l}"], dram[f"s_inw{l}"], D, 2 * E, BF16, f"scr_in{l}")
        prep_cast(dram[f"owT{l}"], dram[f"s_owT{l}"], E, D, BF16, f"scr_ow{l}")
    prep_cast(dram["W1T"], dram["s_w1"], D, NF, F8, "scr_w1")
    prep_cast(dram["W2T"], dram["s_w2"], NF, D, F8, "scr_w2")

    def chunk_view(t, c):
        return dram[t][:, c * TC:(c + 1) * TC].rearrange("(g p) t -> p g t", p=128)

    def stg3(buf):
        return buf[:].rearrange("p (g t) -> p g t", t=TC)

    abuf = [a_st, x2_st]

    def ln_chunk(c):
        par = c % 2
        A = abuf[par]
        ak = f"ab{par}"
        dma(stg3(A), chunk_view("x_t", c), writes=[ak])
        for dt in range(NDT):
            i = dt % 2
            sch.op("scalar", lambda e, dt=dt, A=A, i=i: e.square(
                t_xc[i][:], A[:, dt * TC:(dt + 1) * TC]),
                reads=[ak], writes=[f"xc{i}"])
            sch.op("tensor", lambda e, dt=dt, i=i: e.matmul(
                pb[0][0:1, :], c_ones[:], A[:, dt * TC:(dt + 1) * TC],
                start=(dt == 0), stop=(dt == NDT - 1)),
                reads=[ak, "c_ones"], writes=["pb0"])
            sch.op("tensor", lambda e, dt=dt, i=i: e.matmul(
                pb[1][0:1, :], c_ones[:], t_xc[i][:],
                start=(dt == 0), stop=(dt == NDT - 1)),
                reads=[f"xc{i}", "c_ones"], writes=["pb1"])
        sch.op("vector", lambda e: e.tensor_single_scalar(row_m, pb[0][0:1, :], 1.0 / D, OP.mult),
               reads=["pb0"], writes=["row_m"])
        sch.op("vector", lambda e: e.tensor_single_scalar(row_e2, pb[1][0:1, :], 1.0 / D, OP.mult),
               reads=["pb1"], writes=["row_e2"])
        sch.op("vector", lambda e: e.tensor_mul(row_rstd, row_m, row_m),
               reads=["row_m"], writes=["row_rstd"])
        sch.op("vector", lambda e: e.tensor_sub(row_e2, row_e2, row_rstd),
               reads=["row_rstd", "row_e2"], writes=["row_e2"])
        sch.op("scalar", lambda e: e.activation(row_rstd, row_e2, AF.Ln, bias=1e-5),
               reads=["row_e2"], writes=["row_rstd"])
        sch.op("scalar", lambda e: e.activation(row_rstd, row_rstd, AF.Exp, scale=-0.5),
               reads=["row_rstd"], writes=["row_rstd"])
        sch.op("vector", lambda e: e.tensor_mul(row_rs2, row_m, row_rstd),
               reads=["row_m", "row_rstd", "row_e2"], writes=["row_rs2"])
        sch.op("tensor", lambda e: e.matmul(pb[2][:], c_onesr[:], row_rstd,
                                            start=True, stop=True),
               reads=["row_rstd", "c_onesr"], writes=["pb2"])
        sch.op("tensor", lambda e: e.matmul(pb[3][:], c_onesr[:], row_rs2,
                                            start=True, stop=True),
               reads=["row_rs2", "c_onesr"], writes=["pb3"])
        sch.op("vector", lambda e: e.tensor_copy(rstd_rep[:], pb[2][:]),
               reads=["pb2"], writes=["rstd_rep"])
        sch.op("vector", lambda e: e.tensor_copy(rs2_rep[:], pb[3][:]),
               reads=["pb3"], writes=["rs2_rep"])
        rb = rstd_rep[:].unsqueeze(1).broadcast_to((128, NDT, TC))
        s2 = rs2_rep[:].unsqueeze(1).broadcast_to((128, NDT, TC))
        av = A[:].rearrange("p (g t) -> p g t", g=NDT)
        sch.op("vector", lambda e, av=av, rb=rb: e.tensor_tensor(av, av, rb, OP.mult),
               reads=["rstd_rep", ak], writes=[ak])
        sch.op("vector", lambda e, av=av, s2=s2: e.tensor_tensor(av, av, s2, OP.subtract),
               reads=["rs2_rep", ak], writes=[ak])
        dma(chunk_view("s_a", c), stg3(A), q="dmg", reads=[ak], writes=[f"s_a{c}"])
        dma(chunk_view("s_xn", c), stg3(A), q="dmg", reads=[ak], writes=["s_xn"])
        if DBG:
            dma(dbg_xn[:, c * TC:(c + 1) * TC].rearrange("(g p) t -> p g t", p=128),
                stg3(A), q="dmg", reads=[ak], writes=["dbg"])

    for c in range(NCH):
        ln_chunk(c)

    def head_ops(l, c):
        par = c % 2
        A = abuf[par]
        ak = f"ab{par}"
        th = []

        def T(engine, fn, reads=(), writes=(), dma_q=None, ms=0):
            th.append((engine, fn, list(reads), list(writes), dma_q, ms))

        if c == 0:
            T("sync", lambda e: e.dma_start(out=halo[:], in_=dram["zeros48"]),
              [], ["halo"], "dms")
        T("sync", lambda e, A=A, c=c: e.dma_start(out=stg3(A), in_=chunk_view("s_a", c)),
          [f"s_a{c}"], [ak], "dms")
        for dt in range(NDT):
            i = dt % 2
            T("scalar", lambda e, dt=dt, A=A, i=i: e.square(
                t_xc[i][:], A[:, dt * TC:(dt + 1) * TC]), [ak], [f"xc{i}"])
            T("tensor", lambda e, dt=dt, i=i: e.matmul(
                pb[6][0:1, :], c_ones[:], t_xc[i][:],
                start=(dt == 0), stop=(dt == NDT - 1)),
                [f"xc{i}", "c_ones"], ["pb6"])
        T("scalar", lambda e: e.activation(row_rstd, pb[6][0:1, :], AF.Ln,
                                           scale=1.0 / D, bias=1e-5),
          ["pb6"], ["row_rstd"])
        T("scalar", lambda e: e.activation(row_rstd, row_rstd, AF.Exp, scale=-0.5),
          ["row_rstd"], ["row_rstd"])
        T("tensor", lambda e: e.matmul(pb[6][:], c_onesr[:], row_rstd,
                                       start=True, stop=True),
          ["row_rstd", "c_onesr"], ["pb6"])
        T("vector", lambda e: e.tensor_copy(rstd_rep[:], pb[6][:]),
          ["pb6"], ["rstd_rep"])
        for dt in range(NDT):
            T("vector", lambda e, dt=dt, l=l, A=A: e.scalar_tensor_tensor(
                rn_bf[:, dt * TC:(dt + 1) * TC], A[:, dt * TC:(dt + 1) * TC],
                r_nw[:, l * NDT + dt:l * NDT + dt + 1], rstd_rep[:], OP.mult, OP.mult),
                [ak, "rstd_rep", "r_nw"], ["rn_bf"])
        xbv3 = xbp[:].rearrange("p (g t) -> p g t", g=NG)
        T("vector", lambda e, xbv3=xbv3: e.tensor_copy(
            xbv3[:, :, 0:3], halo[:].rearrange("p (g t) -> p g t", g=NG)),
            ["halo"], ["xbp_pre"])
        for f0, fn_ in [(0, 6), (6, 6), (12, 4)]:
            for half in range(2):
                gate99 = (f0 == 12)
                def wsrc(d, half=half, f0=f0, fn_=fn_, l=l):
                    return dram[f"s_inw{l}"][d * 128:(d + 1) * 128,
                                             (half * 16 + f0) * 128:(half * 16 + f0 + fn_) * 128]
                for d in range(4):
                    wi = d % 4
                    T("sync", lambda e, d=d, wi=wi, wsrc=wsrc, fn_=fn_: e.dma_start(
                        out=wS[wi][:, 0:fn_ * 128], in_=wsrc(d)),
                        [f"scr_in{l}"], [f"wS{wi}"], "dms", ms=(99 if gate99 else 0))
                for d in range(NDT):
                    wi = d % 4
                    for fi in range(fn_):
                        T("tensor", lambda e, d=d, fi=fi, wi=wi: e.matmul(
                            pb[fi][:], wS[wi][:, fi * 128:(fi + 1) * 128],
                            rn_bf[:, d * TC:(d + 1) * TC],
                            start=(d == 0), stop=(d == NDT - 1)),
                            [f"wS{wi}", "rn_bf"], [f"pb{fi}"])
                    if d + 4 < NDT:
                        wi2 = (d + 4) % 4
                        T("sync", lambda e, d=d, wi2=wi2, wsrc=wsrc, fn_=fn_: e.dma_start(
                            out=wS[wi2][:, 0:fn_ * 128], in_=wsrc(d + 4)),
                            [f"scr_in{l}"], [f"wS{wi2}"], "dms")
                for fi in range(fn_):
                    g = f0 + fi
                    if half == 0:
                        T("scalar", lambda e, g=g, fi=fi: e.copy(
                            xbp[:, g * (TC + 3) + 3:(g + 1) * (TC + 3)], pb[fi][:]),
                            [f"pb{fi}"], [f"xbp{g}"], ms=g)
                    else:
                        T("scalar", lambda e, g=g, fi=fi: e.activation(
                            zg_bf[:, g * TC:(g + 1) * TC], pb[fi][:], AF.Silu),
                            [f"pb{fi}"], [f"zg{g}"], ms=g)
        T("vector", lambda e, xbv3=xbv3: e.tensor_copy(
            halo[:].rearrange("p (g t) -> p g t", g=NG), xbv3[:, :, TC:TC + 3]),
            [f"xbp{g}" for g in range(NG)] + ["xbp_pre"], ["halo"])
        for g in range(NG):
            xg = xbp[:, g * (TC + 3):(g + 1) * (TC + 3)]
            cwb = l * NG * 4 + g * 4
            T("vector", lambda e, xg=xg, cwb=cwb: e.tensor_scalar(
                t_xc[0][:], xg[:, 0:TC], r_cw[:, cwb:cwb + 1], None, OP.mult),
                [f"xbp{g}", "xbp_pre", "halo", "r_cw"], ["xc0"])
            for k in (1, 2, 3):
                si, so = (0, 1) if k % 2 == 1 else (1, 0)
                T("vector", lambda e, xg=xg, cwb=cwb, k=k, si=si, so=so:
                  e.scalar_tensor_tensor(
                      t_xc[so][:], xg[:, k:TC + k], r_cw[:, cwb + k:cwb + k + 1],
                      t_xc[si][:], OP.mult, OP.add),
                  [f"xc{si}", f"xbp{g}", "r_cw"], [f"xc{so}"])
            T("scalar", lambda e, xg=xg, g=g, l=l: e.activation(
                xg[:, 3:TC + 3], t_xc[1][:], AF.Silu,
                bias=r_cb[:, l * NG + g:l * NG + g + 1]),
                ["xc1", "r_cb"], [f"xbp{g}"], ms=g)
            T("tensor", lambda e, g=g, l=l: e.matmul(
                pb[6][0:96, :], r_xpw[:, (l * NG + g) * 96:(l * NG + g + 1) * 96],
                xbp[:, g * (TC + 3) + 3:(g + 1) * (TC + 3)],
                start=(g == 0), stop=(g == NG - 1)),
                [f"xbp{g}", "r_xpw"], ["pb6"], ms=g)
        T("vector", lambda e: e.tensor_copy(dbc_sb[:], pb[6][0:96, :]),
          ["pb6"], ["dbc_sb"])
        T("vector", lambda e: e.tensor_copy(drbf[:], dbc_sb[0:RK, :]),
          ["dbc_sb"], ["drbf"])
        T("gpsimd", lambda e: e.dma_start(out=bc_bf[:], in_=dbc_sb[RK:96, :]),
          ["dbc_sb"], ["bc_bf"], "dmg")
        for jj in range(32):
            nn, isC = jj // 2, jj % 2
            j = nn + 16 * isC
            bank = 6 + (jj % 2)
            T("tensor", lambda e, j=j, bank=bank: e.matmul(
                pb[bank][:], c_sel[:, j:j + 1].broadcast_to((32, 128)), bc_bf[:],
                start=True, stop=True),
                ["bc_bf", "c_sel"], [f"pb{bank}"])
            dstb, dkey = (brep, f"brep{nn}") if not isC else (crep, f"crep{nn}")
            T("scalar", lambda e, nn=nn, bank=bank, dstb=dstb: e.copy(
                dstb[:, nn * TC:(nn + 1) * TC], pb[bank][:]),
                [f"pb{bank}"], [dkey])
        return th

    def emit_thunks(th):
        for engine, fn, reads, writes, dma_q, ms in th:
            sch.op(engine, fn, reads=reads, writes=writes, dma=dma_q)

    def emit_gated(th, pos, g):
        while pos < len(th) and th[pos][5] <= g:
            engine, fn, reads, writes, dma_q, ms = th[pos]
            sch.op(engine, fn, reads=reads, writes=writes, dma=dma_q)
            pos += 1
        return pos

    seq = [(l, c) for l in range(NL) for c in range(NCH)]
    pend = head_ops(0, 0)
    emit_thunks(pend)

    for idx, (l, c) in enumerate(seq):
        par = c % 2
        A = abuf[par]
        ak = f"ab{par}"
        nxt = head_ops(*seq[idx + 1]) if idx + 1 < len(seq) else []
        if idx + 1 < len(seq) and seq[idx + 1][1] == 0:
            pass
        nxt_pos = 0
        for g in range(NG):
            sl = g % 2
            gl = l * NG + g
            sch.op("tensor", lambda e, g=g, l=l: e.matmul(
                pb[7][:], r_dtw[:, l * E + g * 128:l * E + (g + 1) * 128], drbf[:],
                start=True, stop=True),
                reads=["drbf", "r_dtw"], writes=["pb7"])
            sch.op("scalar", lambda e, gl=gl: e.activation(
                t_sp[:], pb[7][:], AF.Abs, bias=r_dtb[:, gl:gl + 1]),
                reads=["pb7", "r_dtb"], writes=["tsp"])
            sch.op("scalar", lambda e: e.activation(t_sp[:], t_sp[:], AF.Exp, scale=-1.0),
                   reads=["tsp"], writes=["tsp"])
            sch.op("scalar", lambda e: e.activation(t_sp[:], t_sp[:], AF.Ln, bias=1.0),
                   reads=["tsp"], writes=["tsp"])
            sch.op("scalar", lambda e, gl=gl, sl=sl: e.activation(
                t_delta[sl][:], pb[7][:], AF.Relu, bias=r_dtb[:, gl:gl + 1]),
                reads=["pb7", "r_dtb"], writes=[f"tdelta{sl}"])
            sch.op("vector", lambda e, sl=sl: e.tensor_add(
                t_delta[sl][:], t_delta[sl][:], t_sp[:]),
                reads=["tsp", f"tdelta{sl}"], writes=[f"tdelta{sl}"])
            sch.op("gpsimd", lambda e, g=g, sl=sl: e.tensor_tensor(
                t_u[:], t_delta[sl][:], xbp[:, g * (TC + 3) + 3:(g + 1) * (TC + 3)],
                OP.mult),
                reads=[f"tdelta{sl}", f"xbp{g}"], writes=["tu"])
            for ng in range(NST // NGRP):
                bsl = ng % 2
                for k in range(NGRP):
                    n = ng * NGRP + k
                    sch.op("scalar", lambda e, n=n, k=k, gl=gl, sl=sl, bsl=bsl: e.activation(
                        t_dA[bsl][:, k * TC:(k + 1) * TC], t_delta[sl][:], AF.Exp,
                        scale=r_wneg[:, gl * NST + n:gl * NST + n + 1]),
                        reads=[f"tdelta{sl}", "r_wneg"], writes=[f"tdA{bsl}"])
                    sch.op("gpsimd", lambda e, n=n, k=k, bsl=bsl: e.tensor_tensor(
                        t_mi[bsl][:, k * TC:(k + 1) * TC], t_u[:],
                        brep[:, n * TC:(n + 1) * TC], OP.mult),
                        reads=["tu", f"brep{n}"], writes=[f"tmi{bsl}"])
                for k in range(NGRP):
                    n = ng * NGRP + k
                    stc = g * NST + n
                    iv = 0.0 if c == 0 else st_v[:, stc:stc + 1]
                    ih = 0.0 if c == 0 else st_h[:, stc:stc + 1]
                    sch.op("vector", lambda e, k=k, bsl=bsl, iv=iv: e.tensor_tensor_scan(
                        t_v[bsl][:, k * TC:(k + 1) * TC], beta_b,
                        t_mi[bsl][:, k * TC:(k + 1) * TC], iv, OP.mult, OP.add),
                        reads=[f"tmi{bsl}", "st_v", "c_const"], writes=[f"tv{bsl}"])
                    sch.op("vector", lambda e, k=k, bsl=bsl, ih=ih: e.tensor_tensor_scan(
                        t_h[bsl][:, k * TC:(k + 1) * TC],
                        t_dA[bsl][:, k * TC:(k + 1) * TC],
                        t_v[bsl][:, k * TC:(k + 1) * TC], ih, OP.mult, OP.add),
                        reads=[f"tdA{bsl}", f"tv{bsl}", "st_h"], writes=[f"th{bsl}"])
                if c < NCH - 1:
                    st0 = g * NST + ng * NGRP
                    sch.op("vector", lambda e, bsl=bsl, st0=st0: e.tensor_copy(
                        st_v[:, st0:st0 + NGRP],
                        t_v[bsl][:, TC - 1:NGRP * TC:TC]),
                        reads=[f"tv{bsl}"], writes=["st_v"])
                    sch.op("vector", lambda e, bsl=bsl, st0=st0: e.tensor_copy(
                        st_h[:, st0:st0 + NGRP],
                        t_h[bsl][:, TC - 1:NGRP * TC:TC]),
                        reads=[f"th{bsl}"], writes=["st_h"])
                for k in range(NGRP):
                    n = ng * NGRP + k
                    ybuf, ykey = (t_yA, "tyA") if n % 2 == 0 else (t_yB, "tyB")
                    if n < 2:
                        sch.op("vector", lambda e, k=k, bsl=bsl, n=n, ybuf=ybuf: e.tensor_tensor(
                            ybuf[:], t_h[bsl][:, k * TC:(k + 1) * TC],
                            crep[:, n * TC:(n + 1) * TC], OP.mult),
                            reads=[f"th{bsl}", f"crep{n}"], writes=[ykey])
                    else:
                        sch.op("vector", lambda e, k=k, bsl=bsl, n=n: e.tensor_tensor(
                            t_hC[:], t_h[bsl][:, k * TC:(k + 1) * TC],
                            crep[:, n * TC:(n + 1) * TC], OP.mult),
                            reads=[f"th{bsl}", f"crep{n}"], writes=["thC"])
                        sch.op("vector", lambda e, ybuf=ybuf: e.tensor_add(
                            ybuf[:], ybuf[:], t_hC[:]),
                            reads=["thC", ykey], writes=[ykey])
            sch.op("vector", lambda e, g=g, gl=gl: e.scalar_tensor_tensor(
                t_xc[0][:], xbp[:, g * (TC + 3) + 3:(g + 1) * (TC + 3)],
                r_dp[:, gl:gl + 1], t_yA[:], OP.mult, OP.add),
                reads=[f"xbp{g}", "tyA", "r_dp"], writes=["xc0"])
            sch.op("vector", lambda e: e.tensor_add(t_xc[0][:], t_xc[0][:], t_yB[:]),
                   reads=["tyB", "xc0"], writes=["xc0"])
            sch.op("vector", lambda e, g=g: e.tensor_tensor(
                rny2[:, g * TC:(g + 1) * TC], t_xc[0][:],
                zg_bf[:, g * TC:(g + 1) * TC], OP.mult),
                reads=["xc0", f"zg{g}"], writes=[f"y2{g}"])
            if nxt and PIPE:
                nxt_pos = emit_gated(nxt, nxt_pos, g)
        for g in range(NG):
            wbuf, key = wS[g % 4], f"wS{g % 4}"
            dma(wbuf[:, 0:1024], dram[f"s_owT{l}"][g * 128:(g + 1) * 128, :],
                reads=[f"scr_ow{l}"], writes=[key])
            for d in range(NDT):
                sch.op("tensor", lambda e, g=g, d=d, wbuf=wbuf: e.matmul(
                    pb[d][:], wbuf[:, d * 128:(d + 1) * 128],
                    rny2[:, g * TC:(g + 1) * TC],
                    start=(g == 0), stop=(g == NG - 1)),
                    reads=[key, f"y2{g}"], writes=[f"pb{d}"])
        for d in range(NDT):
            sch.op("vector", lambda e, d=d, A=A: e.tensor_add(
                A[:, d * TC:(d + 1) * TC], A[:, d * TC:(d + 1) * TC], pb[d][:]),
                reads=[f"pb{d}"], writes=[ak])
        dma(chunk_view("s_a", c), stg3(A), q="dmg", reads=[ak], writes=[f"s_a{c}"])
        if DBG:
            dma(dbg_a[l][:, c * TC:(c + 1) * TC].rearrange("(g p) t -> p g t", p=128),
                stg3(A), q="dmg", reads=[ak], writes=["dbg"])
        while nxt_pos < len(nxt):
            e_, f_, r_, w_, d_, m_ = nxt[nxt_pos]
            sch.op(e_, f_, reads=r_, writes=w_, dma=d_)
            nxt_pos += 1

    sch.barrier()
    stM.close()
    stF = ExitStack()

    def sbF(name, shape, dt=F32):
        return stF.enter_context(nc.sbuf_tensor(name, list(shape), dt))

    f_w1 = sbF("f_w1", (128, 8 * 4096), F8)
    f_w2 = sbF("f_w2", (128, 32 * 1024), F8)
    f_h8 = sbF("f_h8", (128, 32 * TC), F8)
    f_hb = sbF("f_hb", (128, TC), BF16)
    f_fb = sbF("f_fb", (128, NDT * TC), BF16)
    f_msk = sbF("f_msk", (128, NDT * TC), BF16)
    f_q8 = sbF("f_q8", (128, NDT * TC), F8)
    f_o = sbF("f_o", (128, NDT * TC))

    for d in range(NDT):
        dma(f_w1[:, d * 4096:(d + 1) * 4096], dram["s_w1"][d * 128:(d + 1) * 128, :],
            reads=["scr_w1"], writes=["f_w1"])
    for f in range(32):
        dma(f_w2[:, f * 1024:(f + 1) * 1024], dram["s_w2"][f * 128:(f + 1) * 128, :],
            reads=["scr_w2"], writes=["f_w2"])
    for c in range(NCH):
        dma(stg3(f_msk), chunk_view("mask_t", c), writes=["f_msk"])
        dma(stg3(a_st), chunk_view("s_a", c), reads=[f"s_a{c}"], writes=["ab0"])
        dma(stg3(x2_st), chunk_view("s_xn", c), reads=["s_xn"], writes=["ab1"])
        sch.op("vector", lambda e: e.tensor_copy(f_q8[:], x2_st[:]),
               reads=["ab1"], writes=["f_q8"])
        dma(stg3(x2_st), chunk_view("x_t", c), writes=["ab1"])
        for fg in range(8):
            for f4i in range(4):
                f = fg * 4 + f4i
                bk = 4 + f4i
                for d in range(NDT):
                    sch.op("tensor", lambda e, f=f, d=d, bk=bk: e.matmul(
                        pb[bk][:], f_w1[:, d * 4096 + f * 128:d * 4096 + (f + 1) * 128],
                        f_q8[:, d * TC:(d + 1) * TC],
                        start=(d == 0), stop=False),
                        reads=["f_w1", "f_q8"], writes=[f"pb{bk}"])
                sch.op("tensor", lambda e, f=f, bk=bk: e.matmul(
                    pb[bk][:], r_b1q[0:1, f * 128:(f + 1) * 128], r_ones8[:],
                    start=False, stop=True),
                    reads=["r_b1q", "r_ones8"], writes=[f"pb{bk}"])
            for f4i in range(4):
                f = fg * 4 + f4i
                bk = 4 + f4i
                sch.op("scalar", lambda e, bk=bk: e.copy(f_hb[:], pb[bk][:]),
                       reads=[f"pb{bk}"], writes=["f_hb"])
                sch.op("vector", lambda e: e.tensor_relu(f_hb[:], f_hb[:]),
                       reads=["f_hb"], writes=["f_hb"])
                sch.op("vector", lambda e, f=f: e.tensor_copy(
                    f_h8[:, f * TC:(f + 1) * TC], f_hb[:]),
                    reads=["f_hb"], writes=["f_h8"])
        for d in range(NDT):
            bk = d % 4
            for f in range(32):
                sch.op("tensor", lambda e, d=d, f=f, bk=bk: e.matmul(
                    pb[bk][:], f_w2[:, f * 1024 + d * 128:f * 1024 + (d + 1) * 128],
                    f_h8[:, f * TC:(f + 1) * TC],
                    start=(f == 0), stop=False),
                    reads=["f_w2", "f_h8"], writes=[f"pb{bk}"])
            sch.op("tensor", lambda e, d=d, bk=bk: e.matmul(
                pb[bk][:], r_b2q[0:1, d * 128:(d + 1) * 128], r_ones8[:],
                start=False, stop=True),
                reads=["r_b2q", "r_ones8"], writes=[f"pb{bk}"])
            sch.op("scalar", lambda e, d=d, bk=bk: e.copy(
                f_fb[:, d * TC:(d + 1) * TC], pb[bk][:]),
                reads=[f"pb{bk}"], writes=["f_fb"])
        sch.op("vector", lambda e: e.tensor_mul(f_o[:], f_fb[:], f_msk[:]),
               reads=["f_fb", "f_msk"], writes=["f_o"])
        sch.op("vector", lambda e: e.tensor_add(f_o[:], f_o[:], a_st[:]),
               reads=["ab0", "f_o"], writes=["f_o"])
        sch.op("vector", lambda e: e.tensor_add(f_o[:], f_o[:], x2_st[:]),
               reads=["ab1", "f_o"], writes=["f_o"])
        dma(out_t[:, c * TC:(c + 1) * TC].rearrange("(g p) t -> p g t", p=128),
            stg3(f_o), reads=["f_o"], writes=["out"])

    with nc.Block() as block:
        sch.emit(nc, block)
    stF.close()
    stP.close()
    return nc


_PROG = None


def _get_prog():
    global _PROG
    if _PROG is None:
        _PROG = build_program()
    return _PROG


def kernel(L0_attn_x, L0_attn_norm_w, L0_attn_in_proj_w, L0_attn_conv_w,
           L0_attn_conv_b, L0_attn_x_proj_w, L0_attn_dt_w, L0_attn_dt_b,
           L0_attn_A_log, L0_attn_Dp, L0_attn_out_w, L0_ffn_W1, L0_ffn_b1,
           L0_ffn_W2, L0_ffn_b2):
    import jax
    x = np.ascontiguousarray(np.asarray(L0_attn_x, np.float32))
    B = x.shape[0]
    with jax.default_device(jax.devices("cpu")[0]):
        jax.config.update("jax_threefry_partitionable", True)
        keep = np.asarray(jax.random.bernoulli(
            jax.random.key(42, impl="threefry2x32"), 0.8, (8, 2048, 1024)))
    mask = (keep.astype(np.float32) * np.float32(1.25)).astype(BF_NP)

    def arr(a):
        return np.ascontiguousarray(np.asarray(a, np.float32))

    common = {}
    for l in range(NL):
        common[f"in_wT{l}"] = arr(np.asarray(L0_attn_in_proj_w)[l].T)
        common[f"owT{l}"] = arr(np.asarray(L0_attn_out_w)[l].T)
        xpw = arr(np.asarray(L0_attn_x_proj_w)[l].T)
        common[f"xpw_a{l}"] = np.ascontiguousarray(
            xpw.reshape(NG, 128, 96).transpose(1, 0, 2).reshape(128, NG * 96))
        common[f"dtwT{l}"] = arr(np.asarray(L0_attn_dt_w)[l].T)
        al = arr(np.asarray(L0_attn_A_log)[l])
        common[f"alog_a{l}"] = np.ascontiguousarray(
            al.reshape(NG, 128, NST).transpose(1, 0, 2).reshape(128, NG * NST))
        for nm, src in (("dtb_a", L0_attn_dt_b), ("cb_a", L0_attn_conv_b),
                        ("dp_a", L0_attn_Dp)):
            v = arr(np.asarray(src)[l])
            common[f"{nm}{l}"] = np.ascontiguousarray(v.reshape(NG, 128).T)
        cw = arr(np.asarray(L0_attn_conv_w)[l])
        common[f"cw_a{l}"] = np.ascontiguousarray(
            cw.reshape(NG, 128, 4).transpose(1, 0, 2).reshape(128, NG * 4))
        nw = arr(np.asarray(L0_attn_norm_w)[l])
        common[f"nw_a{l}"] = np.ascontiguousarray(nw.reshape(NDT, 128).T)
    common["W1T"] = arr(np.asarray(L0_ffn_W1).T)
    common["W2T"] = arr(np.asarray(L0_ffn_W2).T)
    common["b1r"] = arr(L0_ffn_b1).reshape(1, NF)
    common["b2r"] = arr(L0_ffn_b2).reshape(1, D)
    common["onesr"] = np.ones((1, TC), np.float32)
    common["ones_col"] = np.ones((128, 1), np.float32)
    common["ones_row"] = np.ones((1, 128), np.float32)
    common["sel32"] = np.eye(32, dtype=np.float32).astype(BF_NP)
    consts = np.zeros((128, 8), np.float32)
    consts[:, 0] = -1.0
    consts[:, 1] = 1.0
    consts[:, 2] = -0.5
    consts[:, 3] = 1e-5
    consts[:, 4] = BETA
    consts[:, 5] = 1.0 / D
    common["consts"] = consts
    common["zeros48"] = np.zeros((128, 48), BF_NP)

    in_maps = []
    for b in range(B):
        m = dict(common)
        m["x_t"] = np.ascontiguousarray(x[b].T)
        m["mask_t"] = np.ascontiguousarray(mask[b].T)
        in_maps.append(m)

    nc = _get_prog()
    res = run_bass_kernel_spmd(nc, in_maps, core_ids=list(range(B)))
    out = np.stack([np.asarray(r["out_t"]).T for r in res.results], axis=0)
    return out.astype(np.float32)


# version 16
# speedup vs baseline: 1.0182x; 1.0182x over previous
import numpy as np
import ml_dtypes
import concourse.bass as bass
import concourse.mybir as mybir
from concourse.bass_utils import run_bass_kernel_spmd

F32 = mybir.dt.float32
BF16 = mybir.dt.bfloat16
F8 = mybir.dt.float8e5
OP = mybir.AluOpType
AF = mybir.ActivationFunctionType

L, D, E, NST, RK = 2048, 1024, 2048, 16, 64
TC = 512
NCH = L // TC
NG = E // 128
NDT = D // 128
NF = 4096
NL = 2
BETA = 0.6
NGRP = 2
PIPE = True

BF_NP = ml_dtypes.bfloat16


class Sched:
    ENG = ("sync", "vector", "scalar", "tensor", "gpsimd")

    def __init__(self):
        self.ops = []
        self.bw = {}
        self.br = {}
        self.last_on = {}
        self.last_dma = {}

    def op(self, engine, fn, reads=(), writes=(), dma=None):
        opid = len(self.ops)
        deps = set()
        for b in reads:
            w = self.bw.get(b)
            if w is not None:
                deps.add(w)
        for b in writes:
            deps.update(self.br.get(b, ()))
            w = self.bw.get(b)
            if w is not None:
                deps.add(w)
        deps.discard(opid)
        self.ops.append({"e": engine, "fn": fn, "deps": deps, "dma": dma})
        for b in reads:
            self.br.setdefault(b, []).append(opid)
        for b in writes:
            self.bw[b] = opid
            self.br[b] = []
        self.last_on[engine] = opid
        if dma:
            self.last_dma[dma] = opid
        return opid

    def barrier(self):
        lasts = list(self.last_on.values()) + list(self.last_dma.values())
        for e in self.ENG:
            opid = len(self.ops)
            self.ops.append({"e": e, "fn": None,
                             "deps": set(d for d in lasts if d != opid),
                             "dma": None})
            self.last_on[e] = opid

    def emit(self, nc, block):
        marked = set()
        for i, rec in enumerate(self.ops):
            if rec["dma"]:
                marked.add(i)
            me = rec["e"]
            for d in rec["deps"]:
                if self.ops[d]["e"] != me or self.ops[d]["dma"]:
                    marked.add(d)

        def stream(rec):
            return rec["dma"] if rec["dma"] else rec["e"]

        R = 8
        counters, incno = {}, {}
        dma_slot = {}
        for i, rec in enumerate(self.ops):
            if i in marked:
                s = stream(rec)
                counters[s] = counters.get(s, 0) + 1
                incno[i] = counters[s]
                if rec["dma"]:
                    o = counters[s] - 1
                    dma_slot[i] = (s, o % R, o // R + 1)
        sems = {}
        for s in ("vector", "scalar", "tensor", "gpsimd", "sync"):
            sems[s] = nc.semaphore(f"sem_{s}").__enter__()
        qsems = {}
        for q in ("dms", "dmg"):
            qsems[q] = [nc.semaphore(f"sem_{q}{j}").__enter__() for j in range(R)]
        per_eng = {e: [] for e in self.ENG}
        for i, rec in enumerate(self.ops):
            per_eng[rec["e"]].append(i)

        def emit_engine(eng_obj, ename):
            water = {}
            for i in per_eng[ename]:
                rec = self.ops[i]
                byprod = {}
                for d in rec["deps"]:
                    drec = self.ops[d]
                    if drec["e"] == ename and drec["dma"] is None:
                        continue
                    if drec["dma"]:
                        q, j, u = dma_slot[d]
                        key = (q, j)
                        byprod[key] = max(byprod.get(key, 0), u)
                    else:
                        byprod[drec["e"]] = max(byprod.get(drec["e"], 0), incno[d])
                for ds, v in sorted(byprod.items(), key=str):
                    if water.get(ds, 0) >= v:
                        continue
                    sem = qsems[ds[0]][ds[1]] if isinstance(ds, tuple) else sems[ds]
                    mult = 16 if isinstance(ds, tuple) else 1
                    eng_obj.wait_ge(sem, v * mult)
                    water[ds] = v
                if rec["dma"]:
                    q, j, u = dma_slot[i]
                    if u > 1 and water.get((q, j), 0) < u - 1:
                        eng_obj.wait_ge(qsems[q][j], (u - 1) * 16)
                        water[(q, j)] = u - 1
                    ins = rec["fn"](eng_obj)
                    ins.then_inc(qsems[q][j], 16)
                else:
                    ins = eng_obj.nop() if rec["fn"] is None else rec["fn"](eng_obj)
                    if i in incno:
                        ins.then_inc(sems[rec["e"]], 1)

        @block.sync
        def _(sync):
            emit_engine(sync, "sync")
            for q in ("dms", "dmg"):
                tot = counters.get(q, 0)
                for j in range(R):
                    uses = tot // R + (1 if j < tot % R else 0)
                    if uses:
                        sync.wait_ge(qsems[q][j], uses * 16)

        @block.vector
        def _(v):
            emit_engine(v, "vector")

        @block.scalar
        def _(s):
            emit_engine(s, "scalar")

        @block.tensor
        def _(t):
            emit_engine(t, "tensor")

        @block.gpsimd
        def _(g):
            emit_engine(g, "gpsimd")


def build_program():
    nc = bass.Bass("TRN2", target_bir_lowering=False)
    dram = {}

    def din(name, shape, dt=F32):
        dram[name] = nc.dram_tensor(name, list(shape), dt, kind="ExternalInput").ap()
        return dram[name]

    def dscr(name, shape, dt):
        dram[name] = nc.dram_tensor(name, list(shape), dt, kind="Internal").ap()
        return dram[name]

    din("x_t", (D, L))
    din("mask_t", (D, L), BF16)
    for l in range(NL):
        din(f"in_wT{l}", (D, 2 * E))
        din(f"owT{l}", (E, D))
        din(f"xpw_a{l}", (128, NG * 96))
        din(f"dtwT{l}", (RK, E))
        din(f"alog_a{l}", (128, NG * NST))
        din(f"dtb_a{l}", (128, NG))
        din(f"cb_a{l}", (128, NG))
        din(f"dp_a{l}", (128, NG))
        din(f"cw_a{l}", (128, NG * 4))
        din(f"nw_a{l}", (128, NDT))
    din("W1T", (D, NF))
    din("W2T", (NF, D))
    din("b1r", (1, NF))
    din("b2r", (1, D))
    din("onesr", (1, TC))
    din("ones_col", (128, 1))
    din("ones_row", (1, 128))
    din("sel32", (32, 32), BF16)
    din("consts", (128, 8))
    din("zeros48", (128, 48), BF16)
    out_t = nc.dram_tensor("out_t", [D, L], F32, kind="ExternalOutput").ap()
    DBG = False
    if DBG:
        dbg_xn = nc.dram_tensor("dbg_xn", [D, L], F32, kind="ExternalOutput").ap()
        dbg_a = [nc.dram_tensor(f"dbg_a{l}", [D, L], F32, kind="ExternalOutput").ap()
                 for l in range(NL)]

    for l in range(NL):
        dscr(f"s_inw{l}", (D, 2 * E), BF16)
        dscr(f"s_owT{l}", (E, D), BF16)
    dscr("s_w1", (D, NF), F8)
    dscr("s_w2", (NF, D), F8)
    dscr("s_a", (D, L), F32)
    dscr("s_xn", (D, L), F32)

    sch = Sched()
    from contextlib import ExitStack
    stP = ExitStack()
    stM = ExitStack()

    def sbP(name, shape, dt=F32):
        return stP.enter_context(nc.sbuf_tensor(name, list(shape), dt))

    def sbM(name, shape, dt=F32):
        return stM.enter_context(nc.sbuf_tensor(name, list(shape), dt))

    c_ones = sbP("c_ones", (128, 1))
    c_onesr = sbP("c_onesr", (1, 128))
    c_const = sbP("c_const", (128, 8))
    nc.const_aps.register(F32, -1.0, c_const[:, 0:1])
    nc.const_aps.register(F32, 1.0, c_const[:, 1:2])
    nc.const_aps.register(F32, -0.5, c_const[:, 2:3])
    nc.const_aps.register(F32, 1e-5, c_const[:, 3:4])
    nc.const_aps.register(F32, 1.0 / D, c_const[:, 5:6])
    r_dtw = sbP("r_dtw", (RK, NL * E), BF16)
    r_xpw = sbP("r_xpw", (128, NL * NG * 96), BF16)
    r_wneg = sbP("r_wneg", (128, NL * NG * NST))
    r_dtb = sbP("r_dtb", (128, NL * NG))
    r_cb = sbP("r_cb", (128, NL * NG))
    r_dp = sbP("r_dp", (128, NL * NG))
    r_cw = sbP("r_cw", (128, NL * NG * 4))
    r_nw = sbP("r_nw", (128, NL * NDT))
    r_b1q = sbP("r_b1q", (1, NF), F8)
    r_b2q = sbP("r_b2q", (1, D), F8)
    r_ones8 = sbP("r_ones8", (1, TC), F8)
    a_st = sbP("a_st", (128, NDT * TC))
    x2_st = sbP("x2_st", (128, NDT * TC))
    wS = [sbP(f"wS{i}", (128, 1024), BF16) for i in range(4)]
    row_m = sbP("row_m", (1, TC))[:]
    row_e2 = sbP("row_e2", (1, TC))[:]
    row_rstd = sbP("row_rstd", (1, TC))[:]
    row_rs2 = row_e2
    rstd_rep = sbP("rstd_rep", (128, TC))
    rs2_rep = sbP("rs2_rep", (128, TC))
    pg_f8 = [sbP(f"pg_f8{i}", (128, 1024), F8) for i in range(2)]

    c_sel = sbM("c_sel", (32, 32), BF16)
    xbp = sbM("xbp", (128, NG * (TC + 3)), BF16)
    zg_bf = sbM("zg_bf", (128, NG * TC), BF16)
    rny2 = sbM("rny2", (128, NG * TC), BF16)
    rn_sep = sbM("rn_sep", (128, NDT * TC), BF16)
    halo = sbM("halo", (128, NG * 3), BF16)
    dbc_sb = sbM("dbc_sb", (96, TC))
    bc_bf = sbM("bc_bf", (32, TC), BF16)
    drbf = sbM("drbf", (RK, TC), BF16)
    brep = sbM("brep", (128, NST * TC), BF16)
    crep = sbM("crep", (128, NST * TC), BF16)
    t_delta = [sbM(f"t_delta{i}", (128, TC)) for i in range(2)]
    t_sp = sbM("t_sp", (128, TC))
    t_u = sbM("t_u", (128, TC), BF16)
    t_dA = [sbM(f"t_dA{i}", (128, NGRP * TC)) for i in range(2)]
    t_mi = [sbM(f"t_mi{i}", (128, NGRP * TC), BF16) for i in range(2)]
    t_v = [sbM(f"t_v{i}", (128, NGRP * TC), BF16) for i in range(2)]
    t_h = [sbM(f"t_h{i}", (128, NGRP * TC), BF16) for i in range(2)]
    t_hC = sbM("t_hC", (128, TC), BF16)
    t_yA = sbM("t_yA", (128, TC), BF16)
    t_yB = sbM("t_yB", (128, TC), BF16)
    t_xc = [sbM(f"t_xc{i}", (128, TC)) for i in range(2)]
    st_v = sbM("st_v", (128, NG * NST))
    st_h = sbM("st_h", (128, NG * NST))

    pb = [stP.enter_context(nc.psum_tensor(f"pb{i}", [128, 512], F32)) for i in range(8)]

    beta_b = c_const[:, 4:5].broadcast_to((128, TC))
    rn_bf = rn_sep

    def dma(dst, src, q="dms", reads=(), writes=()):
        eng = "sync" if q == "dms" else "gpsimd"
        sch.op(eng, lambda e, d=dst, s=src: e.dma_start(out=d, in_=s),
               reads=reads, writes=writes, dma=q)

    dma(c_ones[:], dram["ones_col"], writes=["c_ones"])
    dma(c_onesr[:], dram["ones_row"], writes=["c_onesr"])
    dma(c_sel[:], dram["sel32"], writes=["c_sel"])
    dma(c_const[:], dram["consts"], writes=["c_const"])
    dma(x2_st[0:1, 0:NF], dram["b1r"], writes=["x2_st"])
    sch.op("vector", lambda e: e.tensor_copy(r_b1q[:], x2_st[0:1, 0:NF]),
           reads=["x2_st"], writes=["r_b1q"])
    dma(x2_st[0:1, 0:D], dram["b2r"], writes=["x2_st"])
    sch.op("vector", lambda e: e.tensor_copy(r_b2q[:], x2_st[0:1, 0:D]),
           reads=["x2_st"], writes=["r_b2q"])
    dma(x2_st[0:1, 0:TC], dram["onesr"], writes=["x2_st"])
    sch.op("vector", lambda e: e.tensor_copy(r_ones8[:], x2_st[0:1, 0:TC]),
           reads=["x2_st"], writes=["r_ones8"])
    for l in range(NL):
        dma(x2_st[0:RK, 0:E], dram[f"dtwT{l}"], writes=["x2_st"])
        sch.op("vector", lambda e, l=l: e.tensor_copy(
            r_dtw[:, l * E:(l + 1) * E], x2_st[0:RK, 0:E]),
            reads=["x2_st"], writes=["r_dtw"])
        dma(x2_st[:, 0:NG * 96], dram[f"xpw_a{l}"], writes=["x2_st"])
        sch.op("vector", lambda e, l=l: e.tensor_copy(
            r_xpw[:, l * NG * 96:(l + 1) * NG * 96], x2_st[:, 0:NG * 96]),
            reads=["x2_st"], writes=["r_xpw"])
        dma(x2_st[:, 0:NG * NST], dram[f"alog_a{l}"], writes=["x2_st"])
        sch.op("scalar", lambda e: e.activation(a_st[:, 0:NG * NST], x2_st[:, 0:NG * NST], AF.Exp),
               reads=["x2_st"], writes=["a_st"])
        sch.op("vector", lambda e, l=l: e.tensor_single_scalar(
            r_wneg[:, l * NG * NST:(l + 1) * NG * NST], a_st[:, 0:NG * NST], -1.0, OP.mult),
            reads=["a_st"], writes=["r_wneg"])
        for nm, buf in (("dtb_a", r_dtb), ("cb_a", r_cb), ("dp_a", r_dp)):
            dma(buf[:, l * NG:(l + 1) * NG], dram[f"{nm}{l}"], writes=[f"r_{nm}"])
        dma(r_cw[:, l * NG * 4:(l + 1) * NG * 4], dram[f"cw_a{l}"], writes=["r_cw"])
        dma(r_nw[:, l * NDT:(l + 1) * NDT], dram[f"nw_a{l}"], writes=["r_nw"])

    prep_i = [0]

    def prep_cast(src_ap, dst_ap, rows, cols, outdt, skey):
        for p in range(rows // 128):
            rs = p * 128
            for cs in range(0, cols, 1024):
                cw = min(1024, cols - cs)
                i = prep_i[0] % 2
                j = prep_i[0] % 4
                prep_i[0] += 1
                pin = x2_st[:, i * 2048:i * 2048 + cw]
                kin = f"x2h{i}"
                dma(pin, src_ap[rs:rs + 128, cs:cs + cw], writes=[kin])
                if outdt == BF16:
                    pout, kout = wS[j][:, 0:cw], f"wS{j}"
                else:
                    pout, kout = pg_f8[i][:, 0:cw], f"pg_f8{i}"
                sch.op("vector", lambda e, pout=pout, pin=pin: e.tensor_copy(pout, pin),
                       reads=[kin], writes=[kout])
                dma(dst_ap[rs:rs + 128, cs:cs + cw], pout, q="dmg",
                    reads=[kout], writes=[skey])

    for l in range(NL):
        prep_cast(dram[f"in_wT{l}"], dram[f"s_inw{l}"], D, 2 * E, BF16, f"scr_in{l}")
        prep_cast(dram[f"owT{l}"], dram[f"s_owT{l}"], E, D, BF16, f"scr_ow{l}")
    prep_cast(dram["W1T"], dram["s_w1"], D, NF, F8, "scr_w1")
    prep_cast(dram["W2T"], dram["s_w2"], NF, D, F8, "scr_w2")

    def chunk_view(t, c):
        return dram[t][:, c * TC:(c + 1) * TC].rearrange("(g p) t -> p g t", p=128)

    def stg3(buf):
        return buf[:].rearrange("p (g t) -> p g t", t=TC)

    abuf = [a_st, x2_st]

    def ln_chunk(c):
        par = 0
        A = abuf[par]
        ak = f"ab{par}"
        dma(stg3(A), chunk_view("x_t", c), writes=[ak])
        for dt in range(NDT):
            i = dt % 2
            sch.op("scalar", lambda e, dt=dt, A=A, i=i: e.square(
                t_xc[i][:], A[:, dt * TC:(dt + 1) * TC]),
                reads=[ak], writes=[f"xc{i}"])
            sch.op("tensor", lambda e, dt=dt, i=i: e.matmul(
                pb[0][0:1, :], c_ones[:], A[:, dt * TC:(dt + 1) * TC],
                start=(dt == 0), stop=(dt == NDT - 1)),
                reads=[ak, "c_ones"], writes=["pb0"])
            sch.op("tensor", lambda e, dt=dt, i=i: e.matmul(
                pb[1][0:1, :], c_ones[:], t_xc[i][:],
                start=(dt == 0), stop=(dt == NDT - 1)),
                reads=[f"xc{i}", "c_ones"], writes=["pb1"])
        sch.op("vector", lambda e: e.tensor_single_scalar(row_m, pb[0][0:1, :], 1.0 / D, OP.mult),
               reads=["pb0"], writes=["row_m"])
        sch.op("vector", lambda e: e.tensor_single_scalar(row_e2, pb[1][0:1, :], 1.0 / D, OP.mult),
               reads=["pb1"], writes=["row_e2"])
        sch.op("vector", lambda e: e.tensor_mul(row_rstd, row_m, row_m),
               reads=["row_m"], writes=["row_rstd"])
        sch.op("vector", lambda e: e.tensor_sub(row_e2, row_e2, row_rstd),
               reads=["row_rstd", "row_e2"], writes=["row_e2"])
        sch.op("scalar", lambda e: e.activation(row_rstd, row_e2, AF.Ln, bias=1e-5),
               reads=["row_e2"], writes=["row_rstd"])
        sch.op("scalar", lambda e: e.activation(row_rstd, row_rstd, AF.Exp, scale=-0.5),
               reads=["row_rstd"], writes=["row_rstd"])
        sch.op("vector", lambda e: e.tensor_mul(row_rs2, row_m, row_rstd),
               reads=["row_m", "row_rstd", "row_e2"], writes=["row_rs2"])
        sch.op("tensor", lambda e: e.matmul(pb[2][:], c_onesr[:], row_rstd,
                                            start=True, stop=True),
               reads=["row_rstd", "c_onesr"], writes=["pb2"])
        sch.op("tensor", lambda e: e.matmul(pb[3][:], c_onesr[:], row_rs2,
                                            start=True, stop=True),
               reads=["row_rs2", "c_onesr"], writes=["pb3"])
        sch.op("vector", lambda e: e.tensor_copy(rstd_rep[:], pb[2][:]),
               reads=["pb2"], writes=["rstd_rep"])
        sch.op("vector", lambda e: e.tensor_copy(rs2_rep[:], pb[3][:]),
               reads=["pb3"], writes=["rs2_rep"])
        rb = rstd_rep[:].unsqueeze(1).broadcast_to((128, NDT, TC))
        s2 = rs2_rep[:].unsqueeze(1).broadcast_to((128, NDT, TC))
        av = A[:].rearrange("p (g t) -> p g t", g=NDT)
        sch.op("vector", lambda e, av=av, rb=rb: e.tensor_tensor(av, av, rb, OP.mult),
               reads=["rstd_rep", ak], writes=[ak])
        sch.op("vector", lambda e, av=av, s2=s2: e.tensor_tensor(av, av, s2, OP.subtract),
               reads=["rs2_rep", ak], writes=[ak])
        dma(chunk_view("s_a", c), stg3(A), q="dmg", reads=[ak], writes=[f"s_a{c}"])
        dma(chunk_view("s_xn", c), stg3(A), q="dmg", reads=[ak], writes=["s_xn"])
        if DBG:
            dma(dbg_xn[:, c * TC:(c + 1) * TC].rearrange("(g p) t -> p g t", p=128),
                stg3(A), q="dmg", reads=[ak], writes=["dbg"])

    for c in range(NCH):
        ln_chunk(c)

    def head_ops(l, c):
        par = c % 2
        A = abuf[par]
        ak = f"ab{par}"
        th = []

        def T(engine, fn, reads=(), writes=(), dma_q=None, ms=0):
            th.append((engine, fn, list(reads), list(writes), dma_q, ms))

        if c == 0:
            T("sync", lambda e: e.dma_start(out=halo[:], in_=dram["zeros48"]),
              [], ["halo"], "dms")
        T("sync", lambda e, A=A, c=c: e.dma_start(out=stg3(A), in_=chunk_view("s_a", c)),
          [f"s_a{c}", "x2h0", "x2h1"] if par == 1 else [f"s_a{c}"], [ak], "dms")
        for dt in range(NDT):
            i = dt % 2
            T("scalar", lambda e, dt=dt, A=A, i=i: e.square(
                t_xc[i][:], A[:, dt * TC:(dt + 1) * TC]), [ak], [f"xc{i}"])
            T("tensor", lambda e, dt=dt, i=i: e.matmul(
                pb[6][0:1, :], c_ones[:], t_xc[i][:],
                start=(dt == 0), stop=(dt == NDT - 1)),
                [f"xc{i}", "c_ones"], ["pb6"])
        T("scalar", lambda e: e.activation(row_rstd, pb[6][0:1, :], AF.Ln,
                                           scale=1.0 / D, bias=1e-5),
          ["pb6"], ["row_rstd"])
        T("scalar", lambda e: e.activation(row_rstd, row_rstd, AF.Exp, scale=-0.5),
          ["row_rstd"], ["row_rstd"])
        T("tensor", lambda e: e.matmul(pb[6][:], c_onesr[:], row_rstd,
                                       start=True, stop=True),
          ["row_rstd", "c_onesr"], ["pb6"])
        T("vector", lambda e: e.tensor_copy(rstd_rep[:], pb[6][:]),
          ["pb6"], ["rstd_rep"])
        for dt in range(NDT):
            T("vector", lambda e, dt=dt, l=l, A=A: e.scalar_tensor_tensor(
                rn_bf[:, dt * TC:(dt + 1) * TC], A[:, dt * TC:(dt + 1) * TC],
                r_nw[:, l * NDT + dt:l * NDT + dt + 1], rstd_rep[:], OP.mult, OP.mult),
                [ak, "rstd_rep", "r_nw"], ["rn_bf"])
        xbv3 = xbp[:].rearrange("p (g t) -> p g t", g=NG)
        T("vector", lambda e, xbv3=xbv3: e.tensor_copy(
            xbv3[:, :, 0:3], halo[:].rearrange("p (g t) -> p g t", g=NG)),
            ["halo"], ["xbp_pre"])
        for f0, fn_ in [(0, 6), (6, 6), (12, 4)]:
            for half in range(2):
                gate99 = (f0 == 12)
                def wsrc(d, half=half, f0=f0, fn_=fn_, l=l):
                    return dram[f"s_inw{l}"][d * 128:(d + 1) * 128,
                                             (half * 16 + f0) * 128:(half * 16 + f0 + fn_) * 128]
                for d in range(4):
                    wi = d % 4
                    T("sync", lambda e, d=d, wi=wi, wsrc=wsrc, fn_=fn_: e.dma_start(
                        out=wS[wi][:, 0:fn_ * 128], in_=wsrc(d)),
                        [f"scr_in{l}"], [f"wS{wi}"], "dms", ms=(99 if gate99 else 0))
                for d in range(NDT):
                    wi = d % 4
                    for fi in range(fn_):
                        T("tensor", lambda e, d=d, fi=fi, wi=wi: e.matmul(
                            pb[fi][:], wS[wi][:, fi * 128:(fi + 1) * 128],
                            rn_bf[:, d * TC:(d + 1) * TC],
                            start=(d == 0), stop=(d == NDT - 1)),
                            [f"wS{wi}", "rn_bf"], [f"pb{fi}"])
                    if d + 4 < NDT:
                        wi2 = (d + 4) % 4
                        T("sync", lambda e, d=d, wi2=wi2, wsrc=wsrc, fn_=fn_: e.dma_start(
                            out=wS[wi2][:, 0:fn_ * 128], in_=wsrc(d + 4)),
                            [f"scr_in{l}"], [f"wS{wi2}"], "dms")
                for fi in range(fn_):
                    g = f0 + fi
                    if half == 0:
                        T("scalar", lambda e, g=g, fi=fi: e.copy(
                            xbp[:, g * (TC + 3) + 3:(g + 1) * (TC + 3)], pb[fi][:]),
                            [f"pb{fi}"], [f"xbp{g}"], ms=g)
                    else:
                        T("scalar", lambda e, g=g, fi=fi: e.activation(
                            zg_bf[:, g * TC:(g + 1) * TC], pb[fi][:], AF.Silu),
                            [f"pb{fi}"], [f"zg{g}"], ms=g)
        T("vector", lambda e, xbv3=xbv3: e.tensor_copy(
            halo[:].rearrange("p (g t) -> p g t", g=NG), xbv3[:, :, TC:TC + 3]),
            [f"xbp{g}" for g in range(NG)] + ["xbp_pre"], ["halo"])
        for g in range(NG):
            xg = xbp[:, g * (TC + 3):(g + 1) * (TC + 3)]
            cwb = l * NG * 4 + g * 4
            T("vector", lambda e, xg=xg, cwb=cwb: e.tensor_scalar(
                t_xc[0][:], xg[:, 0:TC], r_cw[:, cwb:cwb + 1], None, OP.mult),
                [f"xbp{g}", "xbp_pre", "halo", "r_cw"], ["xc0"])
            for k in (1, 2, 3):
                si, so = (0, 1) if k % 2 == 1 else (1, 0)
                T("vector", lambda e, xg=xg, cwb=cwb, k=k, si=si, so=so:
                  e.scalar_tensor_tensor(
                      t_xc[so][:], xg[:, k:TC + k], r_cw[:, cwb + k:cwb + k + 1],
                      t_xc[si][:], OP.mult, OP.add),
                  [f"xc{si}", f"xbp{g}", "r_cw"], [f"xc{so}"])
            T("scalar", lambda e, xg=xg, g=g, l=l: e.activation(
                xg[:, 3:TC + 3], t_xc[1][:], AF.Silu,
                bias=r_cb[:, l * NG + g:l * NG + g + 1]),
                ["xc1", "r_cb"], [f"xbp{g}"], ms=g)
            T("tensor", lambda e, g=g, l=l: e.matmul(
                pb[6][0:96, :], r_xpw[:, (l * NG + g) * 96:(l * NG + g + 1) * 96],
                xbp[:, g * (TC + 3) + 3:(g + 1) * (TC + 3)],
                start=(g == 0), stop=(g == NG - 1)),
                [f"xbp{g}", "r_xpw"], ["pb6"], ms=g)
        T("vector", lambda e: e.tensor_copy(dbc_sb[:], pb[6][0:96, :]),
          ["pb6"], ["dbc_sb"])
        T("vector", lambda e: e.tensor_copy(drbf[:], dbc_sb[0:RK, :]),
          ["dbc_sb"], ["drbf"])
        T("gpsimd", lambda e: e.dma_start(out=bc_bf[:], in_=dbc_sb[RK:96, :]),
          ["dbc_sb"], ["bc_bf"], "dmg")
        for jj in range(32):
            nn, isC = jj // 2, jj % 2
            j = nn + 16 * isC
            bank = 6 + (jj % 2)
            T("tensor", lambda e, j=j, bank=bank: e.matmul(
                pb[bank][:], c_sel[:, j:j + 1].broadcast_to((32, 128)), bc_bf[:],
                start=True, stop=True),
                ["bc_bf", "c_sel"], [f"pb{bank}"])
            dstb, dkey = (brep, f"brep{nn}") if not isC else (crep, f"crep{nn}")
            T("scalar", lambda e, nn=nn, bank=bank, dstb=dstb: e.copy(
                dstb[:, nn * TC:(nn + 1) * TC], pb[bank][:]),
                [f"pb{bank}"], [dkey])
        return th

    def emit_thunks(th):
        for engine, fn, reads, writes, dma_q, ms in th:
            sch.op(engine, fn, reads=reads, writes=writes, dma=dma_q)

    def emit_gated(th, pos, g):
        while pos < len(th) and th[pos][5] <= g:
            engine, fn, reads, writes, dma_q, ms = th[pos]
            sch.op(engine, fn, reads=reads, writes=writes, dma=dma_q)
            pos += 1
        return pos

    seq = [(l, c) for l in range(NL) for c in range(NCH)]
    pend = head_ops(0, 0)
    emit_thunks(pend)

    for idx, (l, c) in enumerate(seq):
        par = c % 2
        A = abuf[par]
        ak = f"ab{par}"
        nxt = head_ops(*seq[idx + 1]) if idx + 1 < len(seq) else []
        if idx + 1 < len(seq) and seq[idx + 1][1] == 0:
            pass
        nxt_pos = 0
        for g in range(NG):
            sl = g % 2
            gl = l * NG + g
            sch.op("tensor", lambda e, g=g, l=l: e.matmul(
                pb[7][:], r_dtw[:, l * E + g * 128:l * E + (g + 1) * 128], drbf[:],
                start=True, stop=True),
                reads=["drbf", "r_dtw"], writes=["pb7"])
            sch.op("scalar", lambda e, gl=gl: e.activation(
                t_sp[:], pb[7][:], AF.Abs, bias=r_dtb[:, gl:gl + 1]),
                reads=["pb7", "r_dtb"], writes=["tsp"])
            sch.op("scalar", lambda e: e.activation(t_sp[:], t_sp[:], AF.Exp, scale=-1.0),
                   reads=["tsp"], writes=["tsp"])
            sch.op("scalar", lambda e: e.activation(t_sp[:], t_sp[:], AF.Ln, bias=1.0),
                   reads=["tsp"], writes=["tsp"])
            sch.op("scalar", lambda e, gl=gl, sl=sl: e.activation(
                t_delta[sl][:], pb[7][:], AF.Relu, bias=r_dtb[:, gl:gl + 1]),
                reads=["pb7", "r_dtb"], writes=[f"tdelta{sl}"])
            sch.op("vector", lambda e, sl=sl: e.tensor_add(
                t_delta[sl][:], t_delta[sl][:], t_sp[:]),
                reads=["tsp", f"tdelta{sl}"], writes=[f"tdelta{sl}"])
            sch.op("gpsimd", lambda e, g=g, sl=sl: e.tensor_tensor(
                t_u[:], t_delta[sl][:], xbp[:, g * (TC + 3) + 3:(g + 1) * (TC + 3)],
                OP.mult),
                reads=[f"tdelta{sl}", f"xbp{g}"], writes=["tu"])
            for ng in range(NST // NGRP):
                bsl = ng % 2
                for k in range(NGRP):
                    n = ng * NGRP + k
                    sch.op("scalar", lambda e, n=n, k=k, gl=gl, sl=sl, bsl=bsl: e.activation(
                        t_dA[bsl][:, k * TC:(k + 1) * TC], t_delta[sl][:], AF.Exp,
                        scale=r_wneg[:, gl * NST + n:gl * NST + n + 1]),
                        reads=[f"tdelta{sl}", "r_wneg"], writes=[f"tdA{bsl}"])
                    sch.op("gpsimd", lambda e, n=n, k=k, bsl=bsl: e.tensor_tensor(
                        t_mi[bsl][:, k * TC:(k + 1) * TC], t_u[:],
                        brep[:, n * TC:(n + 1) * TC], OP.mult),
                        reads=["tu", f"brep{n}"], writes=[f"tmi{bsl}"])
                for k in range(NGRP):
                    n = ng * NGRP + k
                    stc = g * NST + n
                    iv = 0.0 if c == 0 else st_v[:, stc:stc + 1]
                    ih = 0.0 if c == 0 else st_h[:, stc:stc + 1]
                    sch.op("vector", lambda e, k=k, bsl=bsl, iv=iv: e.tensor_tensor_scan(
                        t_v[bsl][:, k * TC:(k + 1) * TC], beta_b,
                        t_mi[bsl][:, k * TC:(k + 1) * TC], iv, OP.mult, OP.add),
                        reads=[f"tmi{bsl}", "st_v", "c_const"], writes=[f"tv{bsl}"])
                    sch.op("vector", lambda e, k=k, bsl=bsl, ih=ih: e.tensor_tensor_scan(
                        t_h[bsl][:, k * TC:(k + 1) * TC],
                        t_dA[bsl][:, k * TC:(k + 1) * TC],
                        t_v[bsl][:, k * TC:(k + 1) * TC], ih, OP.mult, OP.add),
                        reads=[f"tdA{bsl}", f"tv{bsl}", "st_h"], writes=[f"th{bsl}"])
                if c < NCH - 1:
                    st0 = g * NST + ng * NGRP
                    sch.op("vector", lambda e, bsl=bsl, st0=st0: e.tensor_copy(
                        st_v[:, st0:st0 + NGRP],
                        t_v[bsl][:, TC - 1:NGRP * TC:TC]),
                        reads=[f"tv{bsl}"], writes=["st_v"])
                    sch.op("vector", lambda e, bsl=bsl, st0=st0: e.tensor_copy(
                        st_h[:, st0:st0 + NGRP],
                        t_h[bsl][:, TC - 1:NGRP * TC:TC]),
                        reads=[f"th{bsl}"], writes=["st_h"])
                for k in range(NGRP):
                    n = ng * NGRP + k
                    ybuf, ykey = (t_yA, "tyA") if n % 2 == 0 else (t_yB, "tyB")
                    if n < 2:
                        sch.op("vector", lambda e, k=k, bsl=bsl, n=n, ybuf=ybuf: e.tensor_tensor(
                            ybuf[:], t_h[bsl][:, k * TC:(k + 1) * TC],
                            crep[:, n * TC:(n + 1) * TC], OP.mult),
                            reads=[f"th{bsl}", f"crep{n}"], writes=[ykey])
                    else:
                        sch.op("vector", lambda e, k=k, bsl=bsl, n=n: e.tensor_tensor(
                            t_hC[:], t_h[bsl][:, k * TC:(k + 1) * TC],
                            crep[:, n * TC:(n + 1) * TC], OP.mult),
                            reads=[f"th{bsl}", f"crep{n}"], writes=["thC"])
                        sch.op("vector", lambda e, ybuf=ybuf: e.tensor_add(
                            ybuf[:], ybuf[:], t_hC[:]),
                            reads=["thC", ykey], writes=[ykey])
            sch.op("vector", lambda e, g=g, gl=gl: e.scalar_tensor_tensor(
                t_xc[0][:], xbp[:, g * (TC + 3) + 3:(g + 1) * (TC + 3)],
                r_dp[:, gl:gl + 1], t_yA[:], OP.mult, OP.add),
                reads=[f"xbp{g}", "tyA", "r_dp"], writes=["xc0"])
            sch.op("vector", lambda e: e.tensor_add(t_xc[0][:], t_xc[0][:], t_yB[:]),
                   reads=["tyB", "xc0"], writes=["xc0"])
            sch.op("vector", lambda e, g=g: e.tensor_tensor(
                rny2[:, g * TC:(g + 1) * TC], t_xc[0][:],
                zg_bf[:, g * TC:(g + 1) * TC], OP.mult),
                reads=["xc0", f"zg{g}"], writes=[f"y2{g}"])
            if nxt and PIPE:
                nxt_pos = emit_gated(nxt, nxt_pos, g)
        for g in range(NG):
            wbuf, key = wS[g % 4], f"wS{g % 4}"
            dma(wbuf[:, 0:1024], dram[f"s_owT{l}"][g * 128:(g + 1) * 128, :],
                reads=[f"scr_ow{l}"], writes=[key])
            for d in range(NDT):
                sch.op("tensor", lambda e, g=g, d=d, wbuf=wbuf: e.matmul(
                    pb[d][:], wbuf[:, d * 128:(d + 1) * 128],
                    rny2[:, g * TC:(g + 1) * TC],
                    start=(g == 0), stop=(g == NG - 1)),
                    reads=[key, f"y2{g}"], writes=[f"pb{d}"])
        for d in range(NDT):
            sch.op("vector", lambda e, d=d, A=A: e.tensor_add(
                A[:, d * TC:(d + 1) * TC], A[:, d * TC:(d + 1) * TC], pb[d][:]),
                reads=[f"pb{d}"], writes=[ak])
        dma(chunk_view("s_a", c), stg3(A), q="dmg", reads=[ak], writes=[f"s_a{c}"])
        if DBG:
            dma(dbg_a[l][:, c * TC:(c + 1) * TC].rearrange("(g p) t -> p g t", p=128),
                stg3(A), q="dmg", reads=[ak], writes=["dbg"])
        while nxt_pos < len(nxt):
            e_, f_, r_, w_, d_, m_ = nxt[nxt_pos]
            sch.op(e_, f_, reads=r_, writes=w_, dma=d_)
            nxt_pos += 1

    sch.barrier()
    stM.close()
    stF = ExitStack()

    def sbF(name, shape, dt=F32):
        return stF.enter_context(nc.sbuf_tensor(name, list(shape), dt))

    f_w1 = sbF("f_w1", (128, 8 * 4096), F8)
    f_w2 = sbF("f_w2", (128, 32 * 1024), F8)
    f_h8 = sbF("f_h8", (128, 32 * TC), F8)
    f_hb = sbF("f_hb", (128, TC), BF16)
    f_fb = sbF("f_fb", (128, NDT * TC), BF16)
    f_msk = sbF("f_msk", (128, NDT * TC), BF16)
    f_q8 = sbF("f_q8", (128, NDT * TC), F8)
    f_o = sbF("f_o", (128, NDT * TC))

    for d in range(NDT):
        dma(f_w1[:, d * 4096:(d + 1) * 4096], dram["s_w1"][d * 128:(d + 1) * 128, :],
            reads=["scr_w1"], writes=["f_w1"])
    for f in range(32):
        dma(f_w2[:, f * 1024:(f + 1) * 1024], dram["s_w2"][f * 128:(f + 1) * 128, :],
            reads=["scr_w2"], writes=["f_w2"])
    for c in range(NCH):
        dma(stg3(f_msk), chunk_view("mask_t", c), writes=["f_msk"])
        dma(stg3(a_st), chunk_view("s_a", c), reads=[f"s_a{c}"], writes=["ab0"])
        dma(stg3(x2_st), chunk_view("s_xn", c), reads=["s_xn", "x2h0", "x2h1"], writes=["ab1"])
        sch.op("vector", lambda e: e.tensor_copy(f_q8[:], x2_st[:]),
               reads=["ab1"], writes=["f_q8"])
        dma(stg3(x2_st), chunk_view("x_t", c), writes=["ab1"])
        for fg in range(8):
            for f4i in range(4):
                f = fg * 4 + f4i
                bk = 4 + f4i
                for d in range(NDT):
                    sch.op("tensor", lambda e, f=f, d=d, bk=bk: e.matmul(
                        pb[bk][:], f_w1[:, d * 4096 + f * 128:d * 4096 + (f + 1) * 128],
                        f_q8[:, d * TC:(d + 1) * TC],
                        start=(d == 0), stop=False),
                        reads=["f_w1", "f_q8"], writes=[f"pb{bk}"])
                sch.op("tensor", lambda e, f=f, bk=bk: e.matmul(
                    pb[bk][:], r_b1q[0:1, f * 128:(f + 1) * 128], r_ones8[:],
                    start=False, stop=True),
                    reads=["r_b1q", "r_ones8"], writes=[f"pb{bk}"])
            for f4i in range(4):
                f = fg * 4 + f4i
                bk = 4 + f4i
                sch.op("scalar", lambda e, bk=bk: e.copy(f_hb[:], pb[bk][:]),
                       reads=[f"pb{bk}"], writes=["f_hb"])
                sch.op("vector", lambda e: e.tensor_relu(f_hb[:], f_hb[:]),
                       reads=["f_hb"], writes=["f_hb"])
                sch.op("vector", lambda e, f=f: e.tensor_copy(
                    f_h8[:, f * TC:(f + 1) * TC], f_hb[:]),
                    reads=["f_hb"], writes=["f_h8"])
        for d in range(NDT):
            bk = d % 4
            for f in range(32):
                sch.op("tensor", lambda e, d=d, f=f, bk=bk: e.matmul(
                    pb[bk][:], f_w2[:, f * 1024 + d * 128:f * 1024 + (d + 1) * 128],
                    f_h8[:, f * TC:(f + 1) * TC],
                    start=(f == 0), stop=False),
                    reads=["f_w2", "f_h8"], writes=[f"pb{bk}"])
            sch.op("tensor", lambda e, d=d, bk=bk: e.matmul(
                pb[bk][:], r_b2q[0:1, d * 128:(d + 1) * 128], r_ones8[:],
                start=False, stop=True),
                reads=["r_b2q", "r_ones8"], writes=[f"pb{bk}"])
            sch.op("scalar", lambda e, d=d, bk=bk: e.copy(
                f_fb[:, d * TC:(d + 1) * TC], pb[bk][:]),
                reads=[f"pb{bk}"], writes=["f_fb"])
        sch.op("vector", lambda e: e.tensor_mul(f_o[:], f_fb[:], f_msk[:]),
               reads=["f_fb", "f_msk"], writes=["f_o"])
        sch.op("vector", lambda e: e.tensor_add(f_o[:], f_o[:], a_st[:]),
               reads=["ab0", "f_o"], writes=["f_o"])
        sch.op("vector", lambda e: e.tensor_add(f_o[:], f_o[:], x2_st[:]),
               reads=["ab1", "f_o"], writes=["f_o"])
        dma(out_t[:, c * TC:(c + 1) * TC].rearrange("(g p) t -> p g t", p=128),
            stg3(f_o), reads=["f_o"], writes=["out"])

    with nc.Block() as block:
        sch.emit(nc, block)
    stF.close()
    stP.close()
    return nc


_PROG = None


def _get_prog():
    global _PROG
    if _PROG is None:
        _PROG = build_program()
    return _PROG


def kernel(L0_attn_x, L0_attn_norm_w, L0_attn_in_proj_w, L0_attn_conv_w,
           L0_attn_conv_b, L0_attn_x_proj_w, L0_attn_dt_w, L0_attn_dt_b,
           L0_attn_A_log, L0_attn_Dp, L0_attn_out_w, L0_ffn_W1, L0_ffn_b1,
           L0_ffn_W2, L0_ffn_b2):
    import jax
    x = np.ascontiguousarray(np.asarray(L0_attn_x, np.float32))
    B = x.shape[0]
    with jax.default_device(jax.devices("cpu")[0]):
        jax.config.update("jax_threefry_partitionable", True)
        keep = np.asarray(jax.random.bernoulli(
            jax.random.key(42, impl="threefry2x32"), 0.8, (8, 2048, 1024)))
    mask = (keep.astype(np.float32) * np.float32(1.25)).astype(BF_NP)

    def arr(a):
        return np.ascontiguousarray(np.asarray(a, np.float32))

    common = {}
    for l in range(NL):
        common[f"in_wT{l}"] = arr(np.asarray(L0_attn_in_proj_w)[l].T)
        common[f"owT{l}"] = arr(np.asarray(L0_attn_out_w)[l].T)
        xpw = arr(np.asarray(L0_attn_x_proj_w)[l].T)
        common[f"xpw_a{l}"] = np.ascontiguousarray(
            xpw.reshape(NG, 128, 96).transpose(1, 0, 2).reshape(128, NG * 96))
        common[f"dtwT{l}"] = arr(np.asarray(L0_attn_dt_w)[l].T)
        al = arr(np.asarray(L0_attn_A_log)[l])
        common[f"alog_a{l}"] = np.ascontiguousarray(
            al.reshape(NG, 128, NST).transpose(1, 0, 2).reshape(128, NG * NST))
        for nm, src in (("dtb_a", L0_attn_dt_b), ("cb_a", L0_attn_conv_b),
                        ("dp_a", L0_attn_Dp)):
            v = arr(np.asarray(src)[l])
            common[f"{nm}{l}"] = np.ascontiguousarray(v.reshape(NG, 128).T)
        cw = arr(np.asarray(L0_attn_conv_w)[l])
        common[f"cw_a{l}"] = np.ascontiguousarray(
            cw.reshape(NG, 128, 4).transpose(1, 0, 2).reshape(128, NG * 4))
        nw = arr(np.asarray(L0_attn_norm_w)[l])
        common[f"nw_a{l}"] = np.ascontiguousarray(nw.reshape(NDT, 128).T)
    common["W1T"] = arr(np.asarray(L0_ffn_W1).T)
    common["W2T"] = arr(np.asarray(L0_ffn_W2).T)
    common["b1r"] = arr(L0_ffn_b1).reshape(1, NF)
    common["b2r"] = arr(L0_ffn_b2).reshape(1, D)
    common["onesr"] = np.ones((1, TC), np.float32)
    common["ones_col"] = np.ones((128, 1), np.float32)
    common["ones_row"] = np.ones((1, 128), np.float32)
    common["sel32"] = np.eye(32, dtype=np.float32).astype(BF_NP)
    consts = np.zeros((128, 8), np.float32)
    consts[:, 0] = -1.0
    consts[:, 1] = 1.0
    consts[:, 2] = -0.5
    consts[:, 3] = 1e-5
    consts[:, 4] = BETA
    consts[:, 5] = 1.0 / D
    common["consts"] = consts
    common["zeros48"] = np.zeros((128, 48), BF_NP)

    in_maps = []
    for b in range(B):
        m = dict(common)
        m["x_t"] = np.ascontiguousarray(x[b].T)
        m["mask_t"] = np.ascontiguousarray(mask[b].T)
        in_maps.append(m)

    nc = _get_prog()
    res = run_bass_kernel_spmd(nc, in_maps, core_ids=list(range(B)))
    out = np.stack([np.asarray(r["out_t"]).T for r in res.results], axis=0)
    return out.astype(np.float32)


# version 19
# speedup vs baseline: 1.0346x; 1.0161x over previous
import numpy as np
import ml_dtypes
import concourse.bass as bass
import concourse.mybir as mybir
from concourse.bass_utils import run_bass_kernel_spmd

F32 = mybir.dt.float32
BF16 = mybir.dt.bfloat16
F8 = mybir.dt.float8e5
OP = mybir.AluOpType
AF = mybir.ActivationFunctionType

L, D, E, NST, RK = 2048, 1024, 2048, 16, 64
TC = 512
NCH = L // TC
NG = E // 128
NDT = D // 128
NF = 4096
NL = 2
BETA = 0.6
NGRP = 2
PIPE = True

BF_NP = ml_dtypes.bfloat16


class Sched:
    ENG = ("sync", "vector", "scalar", "tensor", "gpsimd")

    def __init__(self):
        self.ops = []
        self.bw = {}
        self.br = {}
        self.last_on = {}
        self.last_dma = {}

    def op(self, engine, fn, reads=(), writes=(), dma=None):
        opid = len(self.ops)
        deps = set()
        for b in reads:
            w = self.bw.get(b)
            if w is not None:
                deps.add(w)
        for b in writes:
            deps.update(self.br.get(b, ()))
            w = self.bw.get(b)
            if w is not None:
                deps.add(w)
        deps.discard(opid)
        self.ops.append({"e": engine, "fn": fn, "deps": deps, "dma": dma})
        for b in reads:
            self.br.setdefault(b, []).append(opid)
        for b in writes:
            self.bw[b] = opid
            self.br[b] = []
        self.last_on[engine] = opid
        if dma:
            self.last_dma[dma] = opid
        return opid

    def barrier(self):
        lasts = list(self.last_on.values()) + list(self.last_dma.values())
        for e in self.ENG:
            opid = len(self.ops)
            self.ops.append({"e": e, "fn": None,
                             "deps": set(d for d in lasts if d != opid),
                             "dma": None})
            self.last_on[e] = opid

    def emit(self, nc, block):
        marked = set()
        for i, rec in enumerate(self.ops):
            if rec["dma"]:
                marked.add(i)
            me = rec["e"]
            for d in rec["deps"]:
                if self.ops[d]["e"] != me or self.ops[d]["dma"]:
                    marked.add(d)

        def stream(rec):
            return rec["dma"] if rec["dma"] else rec["e"]

        R = 8
        counters, incno = {}, {}
        dma_slot = {}
        for i, rec in enumerate(self.ops):
            if i in marked:
                s = stream(rec)
                counters[s] = counters.get(s, 0) + 1
                incno[i] = counters[s]
                if rec["dma"]:
                    o = counters[s] - 1
                    dma_slot[i] = (s, o % R, o // R + 1)
        sems = {}
        for s in ("vector", "scalar", "tensor", "gpsimd", "sync"):
            sems[s] = nc.semaphore(f"sem_{s}").__enter__()
        qsems = {}
        for q in ("dms", "dmg"):
            qsems[q] = [nc.semaphore(f"sem_{q}{j}").__enter__() for j in range(R)]
        per_eng = {e: [] for e in self.ENG}
        for i, rec in enumerate(self.ops):
            per_eng[rec["e"]].append(i)

        def emit_engine(eng_obj, ename):
            water = {}
            for i in per_eng[ename]:
                rec = self.ops[i]
                byprod = {}
                for d in rec["deps"]:
                    drec = self.ops[d]
                    if drec["e"] == ename and drec["dma"] is None:
                        continue
                    if drec["dma"]:
                        q, j, u = dma_slot[d]
                        key = (q, j)
                        byprod[key] = max(byprod.get(key, 0), u)
                    else:
                        byprod[drec["e"]] = max(byprod.get(drec["e"], 0), incno[d])
                for ds, v in sorted(byprod.items(), key=str):
                    if water.get(ds, 0) >= v:
                        continue
                    sem = qsems[ds[0]][ds[1]] if isinstance(ds, tuple) else sems[ds]
                    mult = 16 if isinstance(ds, tuple) else 1
                    eng_obj.wait_ge(sem, v * mult)
                    water[ds] = v
                if rec["dma"]:
                    q, j, u = dma_slot[i]
                    if u > 1 and water.get((q, j), 0) < u - 1:
                        eng_obj.wait_ge(qsems[q][j], (u - 1) * 16)
                        water[(q, j)] = u - 1
                    ins = rec["fn"](eng_obj)
                    ins.then_inc(qsems[q][j], 16)
                else:
                    ins = eng_obj.nop() if rec["fn"] is None else rec["fn"](eng_obj)
                    if i in incno:
                        ins.then_inc(sems[rec["e"]], 1)

        @block.sync
        def _(sync):
            emit_engine(sync, "sync")
            for q in ("dms", "dmg"):
                tot = counters.get(q, 0)
                for j in range(R):
                    uses = tot // R + (1 if j < tot % R else 0)
                    if uses:
                        sync.wait_ge(qsems[q][j], uses * 16)

        @block.vector
        def _(v):
            emit_engine(v, "vector")

        @block.scalar
        def _(s):
            emit_engine(s, "scalar")

        @block.tensor
        def _(t):
            emit_engine(t, "tensor")

        @block.gpsimd
        def _(g):
            emit_engine(g, "gpsimd")


def build_program():
    nc = bass.Bass("TRN2", target_bir_lowering=False)
    dram = {}

    def din(name, shape, dt=F32):
        dram[name] = nc.dram_tensor(name, list(shape), dt, kind="ExternalInput").ap()
        return dram[name]

    def dscr(name, shape, dt):
        dram[name] = nc.dram_tensor(name, list(shape), dt, kind="Internal").ap()
        return dram[name]

    din("x_t", (D, L))
    din("mask_t", (D, L), BF16)
    for l in range(NL):
        din(f"in_wT{l}", (D, 2 * E))
        din(f"owT{l}", (E, D))
        din(f"xpw_a{l}", (128, NG * 96))
        din(f"dtwT{l}", (RK, E))
        din(f"alog_a{l}", (128, NG * NST))
        din(f"dtb_a{l}", (128, NG))
        din(f"cb_a{l}", (128, NG))
        din(f"dp_a{l}", (128, NG))
        din(f"cw_a{l}", (128, NG * 4))
        din(f"nw_a{l}", (128, NDT))
    din("W1T", (D, NF))
    din("W2T", (NF, D))
    din("b1r", (1, NF))
    din("b2r", (1, D))
    din("onesr", (1, TC))
    din("ones_col", (128, 1))
    din("ones_row", (1, 128))
    din("sel32", (32, 32), BF16)
    din("consts", (128, 8))
    din("zeros48", (128, 48), BF16)
    out_t = nc.dram_tensor("out_t", [D, L], F32, kind="ExternalOutput").ap()
    DBG = False
    if DBG:
        dbg_xn = nc.dram_tensor("dbg_xn", [D, L], F32, kind="ExternalOutput").ap()
        dbg_a = [nc.dram_tensor(f"dbg_a{l}", [D, L], F32, kind="ExternalOutput").ap()
                 for l in range(NL)]

    for l in range(NL):
        dscr(f"s_inw{l}", (D, 2 * E), BF16)
        dscr(f"s_owT{l}", (E, D), BF16)
    dscr("s_w1", (D, NF), F8)
    dscr("s_w2", (NF, D), F8)
    dscr("s_a", (D, L), F32)
    dscr("s_xn", (D, L), F32)

    sch = Sched()
    from contextlib import ExitStack
    stP = ExitStack()
    stM = ExitStack()

    def sbP(name, shape, dt=F32):
        return stP.enter_context(nc.sbuf_tensor(name, list(shape), dt))

    def sbM(name, shape, dt=F32):
        return stM.enter_context(nc.sbuf_tensor(name, list(shape), dt))

    c_ones = sbP("c_ones", (128, 1))
    c_onesr = sbP("c_onesr", (1, 128))
    c_const = sbP("c_const", (128, 8))
    nc.const_aps.register(F32, -1.0, c_const[:, 0:1])
    nc.const_aps.register(F32, 1.0, c_const[:, 1:2])
    nc.const_aps.register(F32, -0.5, c_const[:, 2:3])
    nc.const_aps.register(F32, 1e-5, c_const[:, 3:4])
    nc.const_aps.register(F32, 1.0 / D, c_const[:, 5:6])
    r_dtw = sbP("r_dtw", (RK, NL * E), BF16)
    r_xpw = sbP("r_xpw", (128, NL * NG * 96), BF16)
    r_wneg = sbP("r_wneg", (128, NL * NG * NST))
    r_dtb = sbP("r_dtb", (128, NL * NG))
    r_cb = sbP("r_cb", (128, NL * NG))
    r_dp = sbP("r_dp", (128, NL * NG))
    r_cw = sbP("r_cw", (128, NL * NG * 4))
    r_nw = sbP("r_nw", (128, NL * NDT))
    r_b1q = sbP("r_b1q", (1, NF), F8)
    r_b2q = sbP("r_b2q", (1, D), F8)
    r_ones8 = sbP("r_ones8", (1, TC), F8)
    a_st = sbP("a_st", (128, NDT * TC))
    x2_st = sbP("x2_st", (128, NDT * TC))
    wS = [sbP(f"wS{i}", (128, 1024), BF16) for i in range(4)]
    row_m = sbP("row_m", (1, TC))[:]
    row_e2 = sbP("row_e2", (1, TC))[:]
    row_rstd = sbP("row_rstd", (1, TC))[:]
    row_rs2 = row_e2
    rstd_rep = sbP("rstd_rep", (128, TC))
    rs2_rep = sbP("rs2_rep", (128, TC))
    pg_f8 = [sbP(f"pg_f8{i}", (128, 1024), F8) for i in range(2)]

    c_sel = sbM("c_sel", (32, 32), BF16)
    xbp = sbM("xbp", (128, NG * (TC + 3)), BF16)
    zg_bf = sbM("zg_bf", (128, NG * TC), BF16)
    rny2 = sbM("rny2", (128, NG * TC), BF16)
    rn_sep = sbM("rn_sep", (128, NDT * TC), BF16)
    halo = sbM("halo", (128, NG * 3), BF16)
    dbc_sb = sbM("dbc_sb", (96, TC))
    bc_bf = sbM("bc_bf", (32, TC), BF16)
    drbf = sbM("drbf", (RK, TC), BF16)
    brep = sbM("brep", (128, NST * TC), BF16)
    crep = sbM("crep", (128, NST * TC), BF16)
    t_delta = [sbM(f"t_delta{i}", (128, TC)) for i in range(2)]
    t_sp = sbM("t_sp", (128, TC))
    t_u = sbM("t_u", (128, TC), BF16)
    t_dA = [sbM(f"t_dA{i}", (128, NGRP * TC)) for i in range(2)]
    t_mi = [sbM(f"t_mi{i}", (128, NGRP * TC), BF16) for i in range(2)]
    t_v = [sbM(f"t_v{i}", (128, NGRP * TC), BF16) for i in range(2)]
    t_h = [sbM(f"t_h{i}", (128, NGRP * TC), BF16) for i in range(2)]
    t_yAB = sbM("t_yAB", (128, 2 * TC), BF16)
    t_hC2 = sbM("t_hC2", (128, 2 * TC), BF16)
    t_xc = [sbM(f"t_xc{i}", (128, TC)) for i in range(2)]
    st_v = sbM("st_v", (128, NG * NST))
    st_h = sbM("st_h", (128, NG * NST))

    pb = [stP.enter_context(nc.psum_tensor(f"pb{i}", [128, 512], F32)) for i in range(8)]

    beta_b = c_const[:, 4:5].broadcast_to((128, TC))
    rn_bf = rn_sep

    def dma(dst, src, q="dms", reads=(), writes=()):
        eng = "sync" if q == "dms" else "gpsimd"
        sch.op(eng, lambda e, d=dst, s=src: e.dma_start(out=d, in_=s),
               reads=reads, writes=writes, dma=q)

    dma(c_ones[:], dram["ones_col"], writes=["c_ones"])
    dma(c_onesr[:], dram["ones_row"], writes=["c_onesr"])
    dma(c_sel[:], dram["sel32"], writes=["c_sel"])
    dma(c_const[:], dram["consts"], writes=["c_const"])
    dma(x2_st[0:1, 0:NF], dram["b1r"], writes=["x2_st"])
    sch.op("vector", lambda e: e.tensor_copy(r_b1q[:], x2_st[0:1, 0:NF]),
           reads=["x2_st"], writes=["r_b1q"])
    dma(x2_st[0:1, 0:D], dram["b2r"], writes=["x2_st"])
    sch.op("vector", lambda e: e.tensor_copy(r_b2q[:], x2_st[0:1, 0:D]),
           reads=["x2_st"], writes=["r_b2q"])
    dma(x2_st[0:1, 0:TC], dram["onesr"], writes=["x2_st"])
    sch.op("vector", lambda e: e.tensor_copy(r_ones8[:], x2_st[0:1, 0:TC]),
           reads=["x2_st"], writes=["r_ones8"])
    for l in range(NL):
        dma(x2_st[0:RK, 0:E], dram[f"dtwT{l}"], writes=["x2_st"])
        sch.op("vector", lambda e, l=l: e.tensor_copy(
            r_dtw[:, l * E:(l + 1) * E], x2_st[0:RK, 0:E]),
            reads=["x2_st"], writes=["r_dtw"])
        dma(x2_st[:, 0:NG * 96], dram[f"xpw_a{l}"], writes=["x2_st"])
        sch.op("vector", lambda e, l=l: e.tensor_copy(
            r_xpw[:, l * NG * 96:(l + 1) * NG * 96], x2_st[:, 0:NG * 96]),
            reads=["x2_st"], writes=["r_xpw"])
        dma(x2_st[:, 0:NG * NST], dram[f"alog_a{l}"], writes=["x2_st"])
        sch.op("scalar", lambda e: e.activation(a_st[:, 0:NG * NST], x2_st[:, 0:NG * NST], AF.Exp),
               reads=["x2_st"], writes=["a_st"])
        sch.op("vector", lambda e, l=l: e.tensor_single_scalar(
            r_wneg[:, l * NG * NST:(l + 1) * NG * NST], a_st[:, 0:NG * NST], -1.0, OP.mult),
            reads=["a_st"], writes=["r_wneg"])
        for nm, buf in (("dtb_a", r_dtb), ("cb_a", r_cb), ("dp_a", r_dp)):
            dma(buf[:, l * NG:(l + 1) * NG], dram[f"{nm}{l}"], writes=[f"r_{nm}"])
        dma(r_cw[:, l * NG * 4:(l + 1) * NG * 4], dram[f"cw_a{l}"], writes=["r_cw"])
        dma(r_nw[:, l * NDT:(l + 1) * NDT], dram[f"nw_a{l}"], writes=["r_nw"])

    prep_i = [0]

    def prep_cast(src_ap, dst_ap, rows, cols, outdt, skey):
        for p in range(rows // 128):
            rs = p * 128
            for cs in range(0, cols, 1024):
                cw = min(1024, cols - cs)
                i = prep_i[0] % 2
                j = prep_i[0] % 4
                prep_i[0] += 1
                pin = x2_st[:, i * 2048:i * 2048 + cw]
                kin = f"x2h{i}"
                dma(pin, src_ap[rs:rs + 128, cs:cs + cw], writes=[kin])
                if outdt == BF16:
                    pout, kout = wS[j][:, 0:cw], f"wS{j}"
                else:
                    pout, kout = pg_f8[i][:, 0:cw], f"pg_f8{i}"
                sch.op("vector", lambda e, pout=pout, pin=pin: e.tensor_copy(pout, pin),
                       reads=[kin], writes=[kout])
                dma(dst_ap[rs:rs + 128, cs:cs + cw], pout, q="dmg",
                    reads=[kout], writes=[skey])

    for l in range(NL):
        prep_cast(dram[f"in_wT{l}"], dram[f"s_inw{l}"], D, 2 * E, BF16, f"scr_in{l}")
        prep_cast(dram[f"owT{l}"], dram[f"s_owT{l}"], E, D, BF16, f"scr_ow{l}")
    prep_cast(dram["W1T"], dram["s_w1"], D, NF, F8, "scr_w1")
    prep_cast(dram["W2T"], dram["s_w2"], NF, D, F8, "scr_w2")

    def chunk_view(t, c):
        return dram[t][:, c * TC:(c + 1) * TC].rearrange("(g p) t -> p g t", p=128)

    def stg3(buf):
        return buf[:].rearrange("p (g t) -> p g t", t=TC)

    abuf = [a_st, x2_st]

    def ln_chunk(c):
        par = 0
        A = abuf[par]
        ak = f"ab{par}"
        dma(stg3(A), chunk_view("x_t", c), writes=[ak])
        for dt in range(NDT):
            i = dt % 2
            sch.op("scalar", lambda e, dt=dt, A=A, i=i: e.square(
                t_xc[i][:], A[:, dt * TC:(dt + 1) * TC]),
                reads=[ak], writes=[f"xc{i}"])
            sch.op("tensor", lambda e, dt=dt, i=i: e.matmul(
                pb[0][0:1, :], c_ones[:], A[:, dt * TC:(dt + 1) * TC],
                start=(dt == 0), stop=(dt == NDT - 1)),
                reads=[ak, "c_ones"], writes=["pb0"])
            sch.op("tensor", lambda e, dt=dt, i=i: e.matmul(
                pb[1][0:1, :], c_ones[:], t_xc[i][:],
                start=(dt == 0), stop=(dt == NDT - 1)),
                reads=[f"xc{i}", "c_ones"], writes=["pb1"])
        sch.op("vector", lambda e: e.tensor_single_scalar(row_m, pb[0][0:1, :], 1.0 / D, OP.mult),
               reads=["pb0"], writes=["row_m"])
        sch.op("vector", lambda e: e.tensor_single_scalar(row_e2, pb[1][0:1, :], 1.0 / D, OP.mult),
               reads=["pb1"], writes=["row_e2"])
        sch.op("vector", lambda e: e.tensor_mul(row_rstd, row_m, row_m),
               reads=["row_m"], writes=["row_rstd"])
        sch.op("vector", lambda e: e.tensor_sub(row_e2, row_e2, row_rstd),
               reads=["row_rstd", "row_e2"], writes=["row_e2"])
        sch.op("scalar", lambda e: e.activation(row_rstd, row_e2, AF.Ln, bias=1e-5),
               reads=["row_e2"], writes=["row_rstd"])
        sch.op("scalar", lambda e: e.activation(row_rstd, row_rstd, AF.Exp, scale=-0.5),
               reads=["row_rstd"], writes=["row_rstd"])
        sch.op("vector", lambda e: e.tensor_mul(row_rs2, row_m, row_rstd),
               reads=["row_m", "row_rstd", "row_e2"], writes=["row_rs2"])
        sch.op("tensor", lambda e: e.matmul(pb[2][:], c_onesr[:], row_rstd,
                                            start=True, stop=True),
               reads=["row_rstd", "c_onesr"], writes=["pb2"])
        sch.op("tensor", lambda e: e.matmul(pb[3][:], c_onesr[:], row_rs2,
                                            start=True, stop=True),
               reads=["row_rs2", "c_onesr"], writes=["pb3"])
        sch.op("vector", lambda e: e.tensor_copy(rstd_rep[:], pb[2][:]),
               reads=["pb2"], writes=["rstd_rep"])
        sch.op("vector", lambda e: e.tensor_copy(rs2_rep[:], pb[3][:]),
               reads=["pb3"], writes=["rs2_rep"])
        rb = rstd_rep[:].unsqueeze(1).broadcast_to((128, NDT, TC))
        s2 = rs2_rep[:].unsqueeze(1).broadcast_to((128, NDT, TC))
        av = A[:].rearrange("p (g t) -> p g t", g=NDT)
        sch.op("vector", lambda e, av=av, rb=rb: e.tensor_tensor(av, av, rb, OP.mult),
               reads=["rstd_rep", ak], writes=[ak])
        sch.op("vector", lambda e, av=av, s2=s2: e.tensor_tensor(av, av, s2, OP.subtract),
               reads=["rs2_rep", ak], writes=[ak])
        dma(chunk_view("s_a", c), stg3(A), q="dmg", reads=[ak], writes=[f"s_a{c}"])
        dma(chunk_view("s_xn", c), stg3(A), q="dmg", reads=[ak], writes=["s_xn"])
        if DBG:
            dma(dbg_xn[:, c * TC:(c + 1) * TC].rearrange("(g p) t -> p g t", p=128),
                stg3(A), q="dmg", reads=[ak], writes=["dbg"])

    for c in range(NCH):
        ln_chunk(c)

    def head_ops(l, c):
        par = c % 2
        A = abuf[par]
        ak = f"ab{par}"
        th = []

        def T(engine, fn, reads=(), writes=(), dma_q=None, ms=0):
            th.append((engine, fn, list(reads), list(writes), dma_q, ms))

        if c == 0:
            T("sync", lambda e: e.dma_start(out=halo[:], in_=dram["zeros48"]),
              [], ["halo"], "dms")
        T("sync", lambda e, A=A, c=c: e.dma_start(out=stg3(A), in_=chunk_view("s_a", c)),
          [f"s_a{c}", "x2h0", "x2h1"] if par == 1 else [f"s_a{c}"], [ak], "dms")
        for dt in range(NDT):
            i = dt % 2
            T("scalar", lambda e, dt=dt, A=A, i=i: e.square(
                t_xc[i][:], A[:, dt * TC:(dt + 1) * TC]), [ak], [f"xc{i}"])
            T("tensor", lambda e, dt=dt, i=i: e.matmul(
                pb[6][0:1, :], c_ones[:], t_xc[i][:],
                start=(dt == 0), stop=(dt == NDT - 1)),
                [f"xc{i}", "c_ones"], ["pb6"])
        T("scalar", lambda e: e.activation(row_rstd, pb[6][0:1, :], AF.Ln,
                                           scale=1.0 / D, bias=1e-5),
          ["pb6"], ["row_rstd"])
        T("scalar", lambda e: e.activation(row_rstd, row_rstd, AF.Exp, scale=-0.5),
          ["row_rstd"], ["row_rstd"])
        T("tensor", lambda e: e.matmul(pb[6][:], c_onesr[:], row_rstd,
                                       start=True, stop=True),
          ["row_rstd", "c_onesr"], ["pb6"])
        T("vector", lambda e: e.tensor_copy(rstd_rep[:], pb[6][:]),
          ["pb6"], ["rstd_rep"])
        for dt in range(NDT):
            T("vector", lambda e, dt=dt, l=l, A=A: e.scalar_tensor_tensor(
                rn_bf[:, dt * TC:(dt + 1) * TC], A[:, dt * TC:(dt + 1) * TC],
                r_nw[:, l * NDT + dt:l * NDT + dt + 1], rstd_rep[:], OP.mult, OP.mult),
                [ak, "rstd_rep", "r_nw"], ["rn_bf"])
        xbv3 = xbp[:].rearrange("p (g t) -> p g t", g=NG)
        T("vector", lambda e, xbv3=xbv3: e.tensor_copy(
            xbv3[:, :, 0:3], halo[:].rearrange("p (g t) -> p g t", g=NG)),
            ["halo"], ["xbp_pre"])
        for f0, fn_ in [(0, 6), (6, 6), (12, 4)]:
            for half in range(2):
                gate99 = (f0 == 12)
                def wsrc(d, half=half, f0=f0, fn_=fn_, l=l):
                    return dram[f"s_inw{l}"][d * 128:(d + 1) * 128,
                                             (half * 16 + f0) * 128:(half * 16 + f0 + fn_) * 128]
                for d in range(4):
                    wi = d % 4
                    T("sync", lambda e, d=d, wi=wi, wsrc=wsrc, fn_=fn_: e.dma_start(
                        out=wS[wi][:, 0:fn_ * 128], in_=wsrc(d)),
                        [f"scr_in{l}"], [f"wS{wi}"], "dms", ms=(99 if gate99 else 0))
                for d in range(NDT):
                    wi = d % 4
                    for fi in range(fn_):
                        T("tensor", lambda e, d=d, fi=fi, wi=wi: e.matmul(
                            pb[fi][:], wS[wi][:, fi * 128:(fi + 1) * 128],
                            rn_bf[:, d * TC:(d + 1) * TC],
                            start=(d == 0), stop=(d == NDT - 1)),
                            [f"wS{wi}", "rn_bf"], [f"pb{fi}"])
                    if d + 4 < NDT:
                        wi2 = (d + 4) % 4
                        T("sync", lambda e, d=d, wi2=wi2, wsrc=wsrc, fn_=fn_: e.dma_start(
                            out=wS[wi2][:, 0:fn_ * 128], in_=wsrc(d + 4)),
                            [f"scr_in{l}"], [f"wS{wi2}"], "dms")
                for fi in range(fn_):
                    g = f0 + fi
                    if half == 0:
                        T("scalar", lambda e, g=g, fi=fi: e.copy(
                            xbp[:, g * (TC + 3) + 3:(g + 1) * (TC + 3)], pb[fi][:]),
                            [f"pb{fi}"], [f"xbp{g}"], ms=g)
                    else:
                        T("scalar", lambda e, g=g, fi=fi: e.activation(
                            zg_bf[:, g * TC:(g + 1) * TC], pb[fi][:], AF.Silu),
                            [f"pb{fi}"], [f"zg{g}"], ms=g)
        T("vector", lambda e, xbv3=xbv3: e.tensor_copy(
            halo[:].rearrange("p (g t) -> p g t", g=NG), xbv3[:, :, TC:TC + 3]),
            [f"xbp{g}" for g in range(NG)] + ["xbp_pre"], ["halo"])
        for g in range(NG):
            xg = xbp[:, g * (TC + 3):(g + 1) * (TC + 3)]
            cwb = l * NG * 4 + g * 4
            T("vector", lambda e, xg=xg, cwb=cwb: e.tensor_scalar(
                t_xc[0][:], xg[:, 0:TC], r_cw[:, cwb:cwb + 1], None, OP.mult),
                [f"xbp{g}", "xbp_pre", "halo", "r_cw"], ["xc0"])
            for k in (1, 2, 3):
                si, so = (0, 1) if k % 2 == 1 else (1, 0)
                T("vector", lambda e, xg=xg, cwb=cwb, k=k, si=si, so=so:
                  e.scalar_tensor_tensor(
                      t_xc[so][:], xg[:, k:TC + k], r_cw[:, cwb + k:cwb + k + 1],
                      t_xc[si][:], OP.mult, OP.add),
                  [f"xc{si}", f"xbp{g}", "r_cw"], [f"xc{so}"])
            T("scalar", lambda e, xg=xg, g=g, l=l: e.activation(
                xg[:, 3:TC + 3], t_xc[1][:], AF.Silu,
                bias=r_cb[:, l * NG + g:l * NG + g + 1]),
                ["xc1", "r_cb"], [f"xbp{g}"], ms=g)
            T("tensor", lambda e, g=g, l=l: e.matmul(
                pb[6][0:96, :], r_xpw[:, (l * NG + g) * 96:(l * NG + g + 1) * 96],
                xbp[:, g * (TC + 3) + 3:(g + 1) * (TC + 3)],
                start=(g == 0), stop=(g == NG - 1)),
                [f"xbp{g}", "r_xpw"], ["pb6"], ms=g)
        T("vector", lambda e: e.tensor_copy(dbc_sb[:], pb[6][0:96, :]),
          ["pb6"], ["dbc_sb"])
        T("vector", lambda e: e.tensor_copy(drbf[:], dbc_sb[0:RK, :]),
          ["dbc_sb"], ["drbf"])
        T("gpsimd", lambda e: e.dma_start(out=bc_bf[:], in_=dbc_sb[RK:96, :]),
          ["dbc_sb"], ["bc_bf"], "dmg")
        for jj in range(32):
            nn, isC = jj // 2, jj % 2
            j = nn + 16 * isC
            bank = 6 + (jj % 2)
            T("tensor", lambda e, j=j, bank=bank: e.matmul(
                pb[bank][:], c_sel[:, j:j + 1].broadcast_to((32, 128)), bc_bf[:],
                start=True, stop=True),
                ["bc_bf", "c_sel"], [f"pb{bank}"])
            dstb, dkey = (brep, f"brep{nn}") if not isC else (crep, f"crep{nn}")
            T("scalar", lambda e, nn=nn, bank=bank, dstb=dstb: e.copy(
                dstb[:, nn * TC:(nn + 1) * TC], pb[bank][:]),
                [f"pb{bank}"], [dkey])
        return th

    def emit_thunks(th):
        for engine, fn, reads, writes, dma_q, ms in th:
            sch.op(engine, fn, reads=reads, writes=writes, dma=dma_q)

    def emit_gated(th, pos, g):
        while pos < len(th) and th[pos][5] <= g:
            engine, fn, reads, writes, dma_q, ms = th[pos]
            sch.op(engine, fn, reads=reads, writes=writes, dma=dma_q)
            pos += 1
        return pos

    seq = [(l, c) for l in range(NL) for c in range(NCH)]
    pend = head_ops(0, 0)
    emit_thunks(pend)

    for idx, (l, c) in enumerate(seq):
        par = c % 2
        A = abuf[par]
        ak = f"ab{par}"
        nxt = head_ops(*seq[idx + 1]) if idx + 1 < len(seq) else []
        if idx + 1 < len(seq) and seq[idx + 1][1] == 0:
            pass
        nxt_pos = 0
        for g in range(NG):
            sl = g % 2
            gl = l * NG + g
            sch.op("tensor", lambda e, g=g, l=l: e.matmul(
                pb[7][:], r_dtw[:, l * E + g * 128:l * E + (g + 1) * 128], drbf[:],
                start=True, stop=True),
                reads=["drbf", "r_dtw"], writes=["pb7"])
            sch.op("scalar", lambda e, gl=gl: e.activation(
                t_sp[:], pb[7][:], AF.Abs, bias=r_dtb[:, gl:gl + 1]),
                reads=["pb7", "r_dtb"], writes=["tsp"])
            sch.op("scalar", lambda e: e.activation(t_sp[:], t_sp[:], AF.Exp, scale=-1.0),
                   reads=["tsp"], writes=["tsp"])
            sch.op("scalar", lambda e: e.activation(t_sp[:], t_sp[:], AF.Ln, bias=1.0),
                   reads=["tsp"], writes=["tsp"])
            sch.op("scalar", lambda e, gl=gl, sl=sl: e.activation(
                t_delta[sl][:], pb[7][:], AF.Relu, bias=r_dtb[:, gl:gl + 1]),
                reads=["pb7", "r_dtb"], writes=[f"tdelta{sl}"])
            sch.op("vector", lambda e, sl=sl: e.tensor_add(
                t_delta[sl][:], t_delta[sl][:], t_sp[:]),
                reads=["tsp", f"tdelta{sl}"], writes=[f"tdelta{sl}"])
            sch.op("gpsimd", lambda e, g=g, sl=sl: e.tensor_tensor(
                t_u[:], t_delta[sl][:], xbp[:, g * (TC + 3) + 3:(g + 1) * (TC + 3)],
                OP.mult),
                reads=[f"tdelta{sl}", f"xbp{g}"], writes=["tu"])
            for ng in range(NST // NGRP):
                bsl = ng % 2
                for k in range(NGRP):
                    n = ng * NGRP + k
                    sch.op("scalar", lambda e, n=n, k=k, gl=gl, sl=sl, bsl=bsl: e.activation(
                        t_dA[bsl][:, k * TC:(k + 1) * TC], t_delta[sl][:], AF.Exp,
                        scale=r_wneg[:, gl * NST + n:gl * NST + n + 1]),
                        reads=[f"tdelta{sl}", "r_wneg"], writes=[f"tdA{bsl}"])
                    sch.op("gpsimd", lambda e, n=n, k=k, bsl=bsl: e.tensor_tensor(
                        t_mi[bsl][:, k * TC:(k + 1) * TC], t_u[:],
                        brep[:, n * TC:(n + 1) * TC], OP.mult),
                        reads=["tu", f"brep{n}"], writes=[f"tmi{bsl}"])
                for k in range(NGRP):
                    n = ng * NGRP + k
                    stc = g * NST + n
                    iv = 0.0 if c == 0 else st_v[:, stc:stc + 1]
                    ih = 0.0 if c == 0 else st_h[:, stc:stc + 1]
                    sch.op("vector", lambda e, k=k, bsl=bsl, iv=iv: e.tensor_tensor_scan(
                        t_v[bsl][:, k * TC:(k + 1) * TC], beta_b,
                        t_mi[bsl][:, k * TC:(k + 1) * TC], iv, OP.mult, OP.add),
                        reads=[f"tmi{bsl}", "st_v", "c_const"], writes=[f"tv{bsl}"])
                    sch.op("vector", lambda e, k=k, bsl=bsl, ih=ih: e.tensor_tensor_scan(
                        t_h[bsl][:, k * TC:(k + 1) * TC],
                        t_dA[bsl][:, k * TC:(k + 1) * TC],
                        t_v[bsl][:, k * TC:(k + 1) * TC], ih, OP.mult, OP.add),
                        reads=[f"tdA{bsl}", f"tv{bsl}", "st_h"], writes=[f"th{bsl}"])
                if c < NCH - 1:
                    st0 = g * NST + ng * NGRP
                    sch.op("vector", lambda e, bsl=bsl, st0=st0: e.tensor_copy(
                        st_v[:, st0:st0 + NGRP],
                        t_v[bsl][:, TC - 1:NGRP * TC:TC]),
                        reads=[f"tv{bsl}"], writes=["st_v"])
                    sch.op("vector", lambda e, bsl=bsl, st0=st0: e.tensor_copy(
                        st_h[:, st0:st0 + NGRP],
                        t_h[bsl][:, TC - 1:NGRP * TC:TC]),
                        reads=[f"th{bsl}"], writes=["st_h"])
                n0 = ng * NGRP
                if ng == 0:
                    sch.op("vector", lambda e, bsl=bsl, n0=n0: e.tensor_tensor(
                        t_yAB[:], t_h[bsl][:],
                        crep[:, n0 * TC:(n0 + 2) * TC], OP.mult),
                        reads=[f"th{bsl}", f"crep{n0}", f"crep{n0 + 1}"],
                        writes=["tyA", "tyB"])
                else:
                    sch.op("vector", lambda e, bsl=bsl, n0=n0: e.tensor_tensor(
                        t_hC2[:], t_h[bsl][:],
                        crep[:, n0 * TC:(n0 + 2) * TC], OP.mult),
                        reads=[f"th{bsl}", f"crep{n0}", f"crep{n0 + 1}"],
                        writes=["thC"])
                    sch.op("vector", lambda e: e.tensor_add(
                        t_yAB[:], t_yAB[:], t_hC2[:]),
                        reads=["thC", "tyA", "tyB"], writes=["tyA", "tyB"])
            sch.op("vector", lambda e, g=g, gl=gl: e.scalar_tensor_tensor(
                t_xc[0][:], xbp[:, g * (TC + 3) + 3:(g + 1) * (TC + 3)],
                r_dp[:, gl:gl + 1], t_yAB[:, 0:TC], OP.mult, OP.add),
                reads=[f"xbp{g}", "tyA", "r_dp"], writes=["xc0"])
            sch.op("vector", lambda e: e.tensor_add(t_xc[0][:], t_xc[0][:], t_yAB[:, TC:2 * TC]),
                   reads=["tyB", "xc0"], writes=["xc0"])
            sch.op("vector", lambda e, g=g: e.tensor_tensor(
                rny2[:, g * TC:(g + 1) * TC], t_xc[0][:],
                zg_bf[:, g * TC:(g + 1) * TC], OP.mult),
                reads=["xc0", f"zg{g}"], writes=[f"y2{g}"])
            if nxt and PIPE:
                nxt_pos = emit_gated(nxt, nxt_pos, g)
        for g in range(NG):
            wbuf, key = wS[g % 4], f"wS{g % 4}"
            dma(wbuf[:, 0:1024], dram[f"s_owT{l}"][g * 128:(g + 1) * 128, :],
                reads=[f"scr_ow{l}"], writes=[key])
            for d in range(NDT):
                sch.op("tensor", lambda e, g=g, d=d, wbuf=wbuf: e.matmul(
                    pb[d][:], wbuf[:, d * 128:(d + 1) * 128],
                    rny2[:, g * TC:(g + 1) * TC],
                    start=(g == 0), stop=(g == NG - 1)),
                    reads=[key, f"y2{g}"], writes=[f"pb{d}"])
        for d in range(NDT):
            sch.op("vector", lambda e, d=d, A=A: e.tensor_add(
                A[:, d * TC:(d + 1) * TC], A[:, d * TC:(d + 1) * TC], pb[d][:]),
                reads=[f"pb{d}"], writes=[ak])
        dma(chunk_view("s_a", c), stg3(A), q="dmg", reads=[ak], writes=[f"s_a{c}"])
        if DBG:
            dma(dbg_a[l][:, c * TC:(c + 1) * TC].rearrange("(g p) t -> p g t", p=128),
                stg3(A), q="dmg", reads=[ak], writes=["dbg"])
        while nxt_pos < len(nxt):
            e_, f_, r_, w_, d_, m_ = nxt[nxt_pos]
            sch.op(e_, f_, reads=r_, writes=w_, dma=d_)
            nxt_pos += 1

    sch.barrier()
    stM.close()
    stF = ExitStack()

    def sbF(name, shape, dt=F32):
        return stF.enter_context(nc.sbuf_tensor(name, list(shape), dt))

    f_w1 = sbF("f_w1", (128, 8 * 4096), F8)
    f_w2 = sbF("f_w2", (128, 32 * 1024), F8)
    f_h8_ = [sbF(f"f_h8_{i}", (128, 32 * TC), F8) for i in range(2)]
    f_hb = sbF("f_hb", (128, TC), BF16)
    f_fb = sbF("f_fb", (128, NDT * TC), BF16)
    f_msk = sbF("f_msk", (128, NDT * TC), BF16)
    f_q8_ = [sbF(f"f_q8_{i}", (128, NDT * TC), F8) for i in range(2)]

    for d in range(NDT):
        dma(f_w1[:, d * 4096:(d + 1) * 4096], dram["s_w1"][d * 128:(d + 1) * 128, :],
            reads=["scr_w1"], writes=["f_w1"])
    for f in range(32):
        dma(f_w2[:, f * 1024:(f + 1) * 1024], dram["s_w2"][f * 128:(f + 1) * 128, :],
            reads=["scr_w2"], writes=["f_w2"])
    for c in range(NCH):
        fpar = c % 2
        f_q8 = f_q8_[fpar]
        f_h8 = f_h8_[fpar]
        kq8 = f"f_q8{fpar}"
        kh8 = f"f_h8{fpar}"
        dma(stg3(f_msk), chunk_view("mask_t", c), writes=["f_msk"])
        dma(stg3(a_st), chunk_view("s_a", c), reads=[f"s_a{c}"], writes=["ab0"])
        dma(stg3(x2_st), chunk_view("s_xn", c), reads=["s_xn", "x2h0", "x2h1"], writes=["ab1"])
        sch.op("vector", lambda e, f_q8=f_q8: e.tensor_copy(f_q8[:], x2_st[:]),
               reads=["ab1"], writes=[kq8])
        dma(stg3(x2_st), chunk_view("x_t", c), writes=["ab1"])
        for fg in range(8):
            for f4i in range(4):
                f = fg * 4 + f4i
                bk = 4 + f4i
                for d in range(NDT):
                    sch.op("tensor", lambda e, f=f, d=d, bk=bk, f_q8=f_q8: e.matmul(
                        pb[bk][:], f_w1[:, d * 4096 + f * 128:d * 4096 + (f + 1) * 128],
                        f_q8[:, d * TC:(d + 1) * TC],
                        start=(d == 0), stop=False),
                        reads=["f_w1", kq8], writes=[f"pb{bk}"])
                sch.op("tensor", lambda e, f=f, bk=bk: e.matmul(
                    pb[bk][:], r_b1q[0:1, f * 128:(f + 1) * 128], r_ones8[:],
                    start=False, stop=True),
                    reads=["r_b1q", "r_ones8"], writes=[f"pb{bk}"])
            for f4i in range(4):
                f = fg * 4 + f4i
                bk = 4 + f4i
                sch.op("scalar", lambda e, bk=bk: e.copy(f_hb[:], pb[bk][:]),
                       reads=[f"pb{bk}"], writes=["f_hb"])
                sch.op("vector", lambda e: e.tensor_relu(f_hb[:], f_hb[:]),
                       reads=["f_hb"], writes=["f_hb"])
                sch.op("vector", lambda e, f=f, f_h8=f_h8: e.tensor_copy(
                    f_h8[:, f * TC:(f + 1) * TC], f_hb[:]),
                    reads=["f_hb"], writes=[kh8])
        for d in range(NDT):
            bk = d % 4
            for f in range(32):
                sch.op("tensor", lambda e, d=d, f=f, bk=bk, f_h8=f_h8: e.matmul(
                    pb[bk][:], f_w2[:, f * 1024 + d * 128:f * 1024 + (d + 1) * 128],
                    f_h8[:, f * TC:(f + 1) * TC],
                    start=(f == 0), stop=False),
                    reads=["f_w2", kh8], writes=[f"pb{bk}"])
            sch.op("tensor", lambda e, d=d, bk=bk: e.matmul(
                pb[bk][:], r_b2q[0:1, d * 128:(d + 1) * 128], r_ones8[:],
                start=False, stop=True),
                reads=["r_b2q", "r_ones8"], writes=[f"pb{bk}"])
            sch.op("scalar", lambda e, d=d, bk=bk: e.copy(
                f_fb[:, d * TC:(d + 1) * TC], pb[bk][:]),
                reads=[f"pb{bk}"], writes=["f_fb"])
        sch.op("vector", lambda e: e.tensor_add(a_st[:], a_st[:], x2_st[:]),
               reads=["ab1", "ab0"], writes=["ab0"])
        sch.op("vector", lambda e: e.tensor_mul(x2_st[:], f_fb[:], f_msk[:]),
               reads=["f_fb", "f_msk", "ab1"], writes=["ab1"])
        sch.op("vector", lambda e: e.tensor_add(a_st[:], a_st[:], x2_st[:]),
               reads=["ab1", "ab0"], writes=["ab0"])
        dma(out_t[:, c * TC:(c + 1) * TC].rearrange("(g p) t -> p g t", p=128),
            stg3(a_st), reads=["ab0"], writes=["out"])

    with nc.Block() as block:
        sch.emit(nc, block)
    stF.close()
    stP.close()
    return nc


_PROG = None


def _get_prog():
    global _PROG
    if _PROG is None:
        _PROG = build_program()
    return _PROG


def kernel(L0_attn_x, L0_attn_norm_w, L0_attn_in_proj_w, L0_attn_conv_w,
           L0_attn_conv_b, L0_attn_x_proj_w, L0_attn_dt_w, L0_attn_dt_b,
           L0_attn_A_log, L0_attn_Dp, L0_attn_out_w, L0_ffn_W1, L0_ffn_b1,
           L0_ffn_W2, L0_ffn_b2):
    import jax
    x = np.ascontiguousarray(np.asarray(L0_attn_x, np.float32))
    B = x.shape[0]
    with jax.default_device(jax.devices("cpu")[0]):
        jax.config.update("jax_threefry_partitionable", True)
        keep = np.asarray(jax.random.bernoulli(
            jax.random.key(42, impl="threefry2x32"), 0.8, (8, 2048, 1024)))
    mask = (keep.astype(np.float32) * np.float32(1.25)).astype(BF_NP)

    def arr(a):
        return np.ascontiguousarray(np.asarray(a, np.float32))

    common = {}
    for l in range(NL):
        common[f"in_wT{l}"] = arr(np.asarray(L0_attn_in_proj_w)[l].T)
        common[f"owT{l}"] = arr(np.asarray(L0_attn_out_w)[l].T)
        xpw = arr(np.asarray(L0_attn_x_proj_w)[l].T)
        common[f"xpw_a{l}"] = np.ascontiguousarray(
            xpw.reshape(NG, 128, 96).transpose(1, 0, 2).reshape(128, NG * 96))
        common[f"dtwT{l}"] = arr(np.asarray(L0_attn_dt_w)[l].T)
        al = arr(np.asarray(L0_attn_A_log)[l])
        common[f"alog_a{l}"] = np.ascontiguousarray(
            al.reshape(NG, 128, NST).transpose(1, 0, 2).reshape(128, NG * NST))
        for nm, src in (("dtb_a", L0_attn_dt_b), ("cb_a", L0_attn_conv_b),
                        ("dp_a", L0_attn_Dp)):
            v = arr(np.asarray(src)[l])
            common[f"{nm}{l}"] = np.ascontiguousarray(v.reshape(NG, 128).T)
        cw = arr(np.asarray(L0_attn_conv_w)[l])
        common[f"cw_a{l}"] = np.ascontiguousarray(
            cw.reshape(NG, 128, 4).transpose(1, 0, 2).reshape(128, NG * 4))
        nw = arr(np.asarray(L0_attn_norm_w)[l])
        common[f"nw_a{l}"] = np.ascontiguousarray(nw.reshape(NDT, 128).T)
    common["W1T"] = arr(np.asarray(L0_ffn_W1).T)
    common["W2T"] = arr(np.asarray(L0_ffn_W2).T)
    common["b1r"] = arr(L0_ffn_b1).reshape(1, NF)
    common["b2r"] = arr(L0_ffn_b2).reshape(1, D)
    common["onesr"] = np.ones((1, TC), np.float32)
    common["ones_col"] = np.ones((128, 1), np.float32)
    common["ones_row"] = np.ones((1, 128), np.float32)
    common["sel32"] = np.eye(32, dtype=np.float32).astype(BF_NP)
    consts = np.zeros((128, 8), np.float32)
    consts[:, 0] = -1.0
    consts[:, 1] = 1.0
    consts[:, 2] = -0.5
    consts[:, 3] = 1e-5
    consts[:, 4] = BETA
    consts[:, 5] = 1.0 / D
    common["consts"] = consts
    common["zeros48"] = np.zeros((128, 48), BF_NP)

    in_maps = []
    for b in range(B):
        m = dict(common)
        m["x_t"] = np.ascontiguousarray(x[b].T)
        m["mask_t"] = np.ascontiguousarray(mask[b].T)
        in_maps.append(m)

    nc = _get_prog()
    res = run_bass_kernel_spmd(nc, in_maps, core_ids=list(range(B)))
    out = np.stack([np.asarray(r["out_t"]).T for r in res.results], axis=0)
    return out.astype(np.float32)


# version 20
# speedup vs baseline: 1.0396x; 1.0049x over previous
import numpy as np
import ml_dtypes
import concourse.bass as bass
import concourse.mybir as mybir
from concourse.bass_utils import run_bass_kernel_spmd

F32 = mybir.dt.float32
BF16 = mybir.dt.bfloat16
F8 = mybir.dt.float8e5
OP = mybir.AluOpType
AF = mybir.ActivationFunctionType

L, D, E, NST, RK = 2048, 1024, 2048, 16, 64
TC = 512
NCH = L // TC
NG = E // 128
NDT = D // 128
NF = 4096
NL = 2
BETA = 0.6
NGRP = 2
PIPE = True

BF_NP = ml_dtypes.bfloat16


class Sched:
    ENG = ("sync", "vector", "scalar", "tensor", "gpsimd")

    def __init__(self):
        self.ops = []
        self.bw = {}
        self.br = {}
        self.last_on = {}
        self.last_dma = {}

    def op(self, engine, fn, reads=(), writes=(), dma=None):
        opid = len(self.ops)
        deps = set()
        for b in reads:
            w = self.bw.get(b)
            if w is not None:
                deps.add(w)
        for b in writes:
            deps.update(self.br.get(b, ()))
            w = self.bw.get(b)
            if w is not None:
                deps.add(w)
        deps.discard(opid)
        self.ops.append({"e": engine, "fn": fn, "deps": deps, "dma": dma})
        for b in reads:
            self.br.setdefault(b, []).append(opid)
        for b in writes:
            self.bw[b] = opid
            self.br[b] = []
        self.last_on[engine] = opid
        if dma:
            self.last_dma[dma] = opid
        return opid

    def barrier(self):
        lasts = list(self.last_on.values()) + list(self.last_dma.values())
        for e in self.ENG:
            opid = len(self.ops)
            self.ops.append({"e": e, "fn": None,
                             "deps": set(d for d in lasts if d != opid),
                             "dma": None})
            self.last_on[e] = opid

    def emit(self, nc, block):
        marked = set()
        for i, rec in enumerate(self.ops):
            if rec["dma"]:
                marked.add(i)
            me = rec["e"]
            for d in rec["deps"]:
                if self.ops[d]["e"] != me or self.ops[d]["dma"]:
                    marked.add(d)

        def stream(rec):
            return rec["dma"] if rec["dma"] else rec["e"]

        R = 8
        counters, incno = {}, {}
        dma_slot = {}
        for i, rec in enumerate(self.ops):
            if i in marked:
                s = stream(rec)
                counters[s] = counters.get(s, 0) + 1
                incno[i] = counters[s]
                if rec["dma"]:
                    o = counters[s] - 1
                    dma_slot[i] = (s, o % R, o // R + 1)
        sems = {}
        for s in ("vector", "scalar", "tensor", "gpsimd", "sync"):
            sems[s] = nc.semaphore(f"sem_{s}").__enter__()
        qsems = {}
        for q in ("dms", "dmg"):
            qsems[q] = [nc.semaphore(f"sem_{q}{j}").__enter__() for j in range(R)]
        per_eng = {e: [] for e in self.ENG}
        for i, rec in enumerate(self.ops):
            per_eng[rec["e"]].append(i)

        def emit_engine(eng_obj, ename):
            water = {}
            for i in per_eng[ename]:
                rec = self.ops[i]
                byprod = {}
                for d in rec["deps"]:
                    drec = self.ops[d]
                    if drec["e"] == ename and drec["dma"] is None:
                        continue
                    if drec["dma"]:
                        q, j, u = dma_slot[d]
                        key = (q, j)
                        byprod[key] = max(byprod.get(key, 0), u)
                    else:
                        byprod[drec["e"]] = max(byprod.get(drec["e"], 0), incno[d])
                for ds, v in sorted(byprod.items(), key=str):
                    if water.get(ds, 0) >= v:
                        continue
                    sem = qsems[ds[0]][ds[1]] if isinstance(ds, tuple) else sems[ds]
                    mult = 16 if isinstance(ds, tuple) else 1
                    eng_obj.wait_ge(sem, v * mult)
                    water[ds] = v
                if rec["dma"]:
                    q, j, u = dma_slot[i]
                    if u > 1 and water.get((q, j), 0) < u - 1:
                        eng_obj.wait_ge(qsems[q][j], (u - 1) * 16)
                        water[(q, j)] = u - 1
                    ins = rec["fn"](eng_obj)
                    ins.then_inc(qsems[q][j], 16)
                else:
                    ins = eng_obj.nop() if rec["fn"] is None else rec["fn"](eng_obj)
                    if i in incno:
                        ins.then_inc(sems[rec["e"]], 1)

        @block.sync
        def _(sync):
            emit_engine(sync, "sync")
            for q in ("dms", "dmg"):
                tot = counters.get(q, 0)
                for j in range(R):
                    uses = tot // R + (1 if j < tot % R else 0)
                    if uses:
                        sync.wait_ge(qsems[q][j], uses * 16)

        @block.vector
        def _(v):
            emit_engine(v, "vector")

        @block.scalar
        def _(s):
            emit_engine(s, "scalar")

        @block.tensor
        def _(t):
            emit_engine(t, "tensor")

        @block.gpsimd
        def _(g):
            emit_engine(g, "gpsimd")


def build_program():
    nc = bass.Bass("TRN2", target_bir_lowering=False)
    dram = {}

    def din(name, shape, dt=F32):
        dram[name] = nc.dram_tensor(name, list(shape), dt, kind="ExternalInput").ap()
        return dram[name]

    def dscr(name, shape, dt):
        dram[name] = nc.dram_tensor(name, list(shape), dt, kind="Internal").ap()
        return dram[name]

    din("x_t", (D, L))
    din("mask_t", (D, L), BF16)
    for l in range(NL):
        din(f"in_wT{l}", (D, 2 * E))
        din(f"owT{l}", (E, D))
        din(f"xpw_a{l}", (128, NG * 96))
        din(f"dtwT{l}", (RK, E))
        din(f"alog_a{l}", (128, NG * NST))
        din(f"dtb_a{l}", (128, NG))
        din(f"cb_a{l}", (128, NG))
        din(f"dp_a{l}", (128, NG))
        din(f"cw_a{l}", (128, NG * 4))
        din(f"nw_a{l}", (128, NDT))
    din("W1T", (D, NF))
    din("W2T", (NF, D))
    din("b1r", (1, NF))
    din("b2r", (1, D))
    din("onesr", (1, TC))
    din("ones_col", (128, 1))
    din("ones_row", (1, 128))
    din("sel32", (32, 32), BF16)
    din("consts", (128, 8))
    din("zeros48", (128, 48), BF16)
    out_t = nc.dram_tensor("out_t", [D, L], F32, kind="ExternalOutput").ap()
    DBG = False
    if DBG:
        dbg_xn = nc.dram_tensor("dbg_xn", [D, L], F32, kind="ExternalOutput").ap()
        dbg_a = [nc.dram_tensor(f"dbg_a{l}", [D, L], F32, kind="ExternalOutput").ap()
                 for l in range(NL)]

    for l in range(NL):
        dscr(f"s_inw{l}", (D, 2 * E), BF16)
        dscr(f"s_owT{l}", (E, D), BF16)
    dscr("s_w1", (D, NF), F8)
    dscr("s_w2", (NF, D), F8)
    dscr("s_a", (D, L), F32)
    dscr("s_xn", (D, L), F32)

    sch = Sched()
    from contextlib import ExitStack
    stP = ExitStack()
    stM = ExitStack()

    def sbP(name, shape, dt=F32):
        return stP.enter_context(nc.sbuf_tensor(name, list(shape), dt))

    def sbM(name, shape, dt=F32):
        return stM.enter_context(nc.sbuf_tensor(name, list(shape), dt))

    c_ones = sbP("c_ones", (128, 1))
    c_onesr = sbP("c_onesr", (1, 128))
    c_const = sbP("c_const", (128, 8))
    nc.const_aps.register(F32, -1.0, c_const[:, 0:1])
    nc.const_aps.register(F32, 1.0, c_const[:, 1:2])
    nc.const_aps.register(F32, -0.5, c_const[:, 2:3])
    nc.const_aps.register(F32, 1e-5, c_const[:, 3:4])
    nc.const_aps.register(F32, 1.0 / D, c_const[:, 5:6])
    r_dtw = sbP("r_dtw", (RK, NL * E), BF16)
    r_xpw = sbP("r_xpw", (128, NL * NG * 96), BF16)
    r_wneg = sbP("r_wneg", (128, NL * NG * NST))
    r_dtb = sbP("r_dtb", (128, NL * NG))
    r_cb = sbP("r_cb", (128, NL * NG))
    r_dp = sbP("r_dp", (128, NL * NG))
    r_cw = sbP("r_cw", (128, NL * NG * 4))
    r_nw = sbP("r_nw", (128, NL * NDT))
    r_b1q = sbP("r_b1q", (1, NF), F8)
    r_b2q = sbP("r_b2q", (1, D), F8)
    r_ones8 = sbP("r_ones8", (1, TC), F8)
    a_st = sbP("a_st", (128, NDT * TC))
    x2_st = sbP("x2_st", (128, NDT * TC))
    wS = [sbP(f"wS{i}", (128, 1024), BF16) for i in range(4)]
    row_m = sbP("row_m", (1, TC))[:]
    row_e2 = sbP("row_e2", (1, TC))[:]
    row_rstd = sbP("row_rstd", (1, TC))[:]
    row_rs2 = row_e2
    rstd_rep = sbP("rstd_rep", (128, TC))
    rs2_rep = sbP("rs2_rep", (128, TC))
    pg_f8 = [sbP(f"pg_f8{i}", (128, 1024), F8) for i in range(2)]

    c_sel = sbM("c_sel", (32, 32), BF16)
    xbp = sbM("xbp", (128, NG * (TC + 3)), BF16)
    zg_bf = sbM("zg_bf", (128, NG * TC), BF16)
    rny2 = sbM("rny2", (128, NG * TC), BF16)
    rn_sep = sbM("rn_sep", (128, NDT * TC), BF16)
    halo = sbM("halo", (128, NG * 3), BF16)
    dbc_sb = sbM("dbc_sb", (96, TC))
    bc_bf = sbM("bc_bf", (32, TC), BF16)
    drbf = sbM("drbf", (RK, TC), BF16)
    brep = sbM("brep", (128, NST * TC), BF16)
    crep = sbM("crep", (128, NST * TC), BF16)
    t_delta = [sbM(f"t_delta{i}", (128, TC)) for i in range(2)]
    t_sp = sbM("t_sp", (128, TC))
    t_u = [sbM(f"t_u{i}", (128, TC), BF16) for i in range(2)]
    t_dA = [sbM(f"t_dA{i}", (128, NGRP * TC)) for i in range(2)]
    t_mi = [sbM(f"t_mi{i}", (128, NGRP * TC), BF16) for i in range(2)]
    t_v = [sbM(f"t_v{i}", (128, NGRP * TC), BF16) for i in range(2)]
    t_h = [sbM(f"t_h{i}", (128, NGRP * TC), BF16) for i in range(2)]
    t_yAB = sbM("t_yAB", (128, 2 * TC), BF16)
    t_hC2 = sbM("t_hC2", (128, 2 * TC), BF16)
    t_xc = [sbM(f"t_xc{i}", (128, TC)) for i in range(2)]
    st_v = sbM("st_v", (128, NG * NST))
    st_h = sbM("st_h", (128, NG * NST))

    pb = [stP.enter_context(nc.psum_tensor(f"pb{i}", [128, 512], F32)) for i in range(8)]

    beta_b = c_const[:, 4:5].broadcast_to((128, TC))
    rn_bf = rn_sep

    def dma(dst, src, q="dms", reads=(), writes=()):
        eng = "sync" if q == "dms" else "gpsimd"
        sch.op(eng, lambda e, d=dst, s=src: e.dma_start(out=d, in_=s),
               reads=reads, writes=writes, dma=q)

    dma(c_ones[:], dram["ones_col"], writes=["c_ones"])
    dma(c_onesr[:], dram["ones_row"], writes=["c_onesr"])
    dma(c_sel[:], dram["sel32"], writes=["c_sel"])
    dma(c_const[:], dram["consts"], writes=["c_const"])
    dma(x2_st[0:1, 0:NF], dram["b1r"], writes=["x2_st"])
    sch.op("vector", lambda e: e.tensor_copy(r_b1q[:], x2_st[0:1, 0:NF]),
           reads=["x2_st"], writes=["r_b1q"])
    dma(x2_st[0:1, 0:D], dram["b2r"], writes=["x2_st"])
    sch.op("vector", lambda e: e.tensor_copy(r_b2q[:], x2_st[0:1, 0:D]),
           reads=["x2_st"], writes=["r_b2q"])
    dma(x2_st[0:1, 0:TC], dram["onesr"], writes=["x2_st"])
    sch.op("vector", lambda e: e.tensor_copy(r_ones8[:], x2_st[0:1, 0:TC]),
           reads=["x2_st"], writes=["r_ones8"])
    for l in range(NL):
        dma(x2_st[0:RK, 0:E], dram[f"dtwT{l}"], writes=["x2_st"])
        sch.op("vector", lambda e, l=l: e.tensor_copy(
            r_dtw[:, l * E:(l + 1) * E], x2_st[0:RK, 0:E]),
            reads=["x2_st"], writes=["r_dtw"])
        dma(x2_st[:, 0:NG * 96], dram[f"xpw_a{l}"], writes=["x2_st"])
        sch.op("vector", lambda e, l=l: e.tensor_copy(
            r_xpw[:, l * NG * 96:(l + 1) * NG * 96], x2_st[:, 0:NG * 96]),
            reads=["x2_st"], writes=["r_xpw"])
        dma(x2_st[:, 0:NG * NST], dram[f"alog_a{l}"], writes=["x2_st"])
        sch.op("scalar", lambda e: e.activation(a_st[:, 0:NG * NST], x2_st[:, 0:NG * NST], AF.Exp),
               reads=["x2_st"], writes=["a_st"])
        sch.op("vector", lambda e, l=l: e.tensor_single_scalar(
            r_wneg[:, l * NG * NST:(l + 1) * NG * NST], a_st[:, 0:NG * NST], -1.0, OP.mult),
            reads=["a_st"], writes=["r_wneg"])
        for nm, buf in (("dtb_a", r_dtb), ("cb_a", r_cb), ("dp_a", r_dp)):
            dma(buf[:, l * NG:(l + 1) * NG], dram[f"{nm}{l}"], writes=[f"r_{nm}"])
        dma(r_cw[:, l * NG * 4:(l + 1) * NG * 4], dram[f"cw_a{l}"], writes=["r_cw"])
        dma(r_nw[:, l * NDT:(l + 1) * NDT], dram[f"nw_a{l}"], writes=["r_nw"])

    prep_i = [0]

    def prep_cast(src_ap, dst_ap, rows, cols, outdt, skey):
        for p in range(rows // 128):
            rs = p * 128
            for cs in range(0, cols, 1024):
                cw = min(1024, cols - cs)
                i = prep_i[0] % 4
                j = prep_i[0] % 4
                prep_i[0] += 1
                pin = x2_st[:, i * 1024:i * 1024 + cw]
                kin = f"x2h{i}"
                dma(pin, src_ap[rs:rs + 128, cs:cs + cw], writes=[kin])
                if outdt == BF16:
                    pout, kout = wS[j][:, 0:cw], f"wS{j}"
                else:
                    pout, kout = pg_f8[i % 2][:, 0:cw], f"pg_f8{i % 2}"
                sch.op("vector", lambda e, pout=pout, pin=pin: e.tensor_copy(pout, pin),
                       reads=[kin], writes=[kout])
                dma(dst_ap[rs:rs + 128, cs:cs + cw], pout, q="dmg",
                    reads=[kout], writes=[skey])

    prep_cast(dram["in_wT0"], dram["s_inw0"], D, 2 * E, BF16, "scr_in0")

    def chunk_view(t, c):
        return dram[t][:, c * TC:(c + 1) * TC].rearrange("(g p) t -> p g t", p=128)

    def stg3(buf):
        return buf[:].rearrange("p (g t) -> p g t", t=TC)

    abuf = [a_st, x2_st]

    def ln_chunk(c):
        par = 0
        A = abuf[par]
        ak = f"ab{par}"
        dma(stg3(A), chunk_view("x_t", c), writes=[ak])
        for dt in range(NDT):
            i = dt % 2
            sch.op("scalar", lambda e, dt=dt, A=A, i=i: e.square(
                t_xc[i][:], A[:, dt * TC:(dt + 1) * TC]),
                reads=[ak], writes=[f"xc{i}"])
            sch.op("tensor", lambda e, dt=dt, i=i: e.matmul(
                pb[0][0:1, :], c_ones[:], A[:, dt * TC:(dt + 1) * TC],
                start=(dt == 0), stop=(dt == NDT - 1)),
                reads=[ak, "c_ones"], writes=["pb0"])
            sch.op("tensor", lambda e, dt=dt, i=i: e.matmul(
                pb[1][0:1, :], c_ones[:], t_xc[i][:],
                start=(dt == 0), stop=(dt == NDT - 1)),
                reads=[f"xc{i}", "c_ones"], writes=["pb1"])
        sch.op("vector", lambda e: e.tensor_single_scalar(row_m, pb[0][0:1, :], 1.0 / D, OP.mult),
               reads=["pb0"], writes=["row_m"])
        sch.op("vector", lambda e: e.tensor_single_scalar(row_e2, pb[1][0:1, :], 1.0 / D, OP.mult),
               reads=["pb1"], writes=["row_e2"])
        sch.op("vector", lambda e: e.tensor_mul(row_rstd, row_m, row_m),
               reads=["row_m"], writes=["row_rstd"])
        sch.op("vector", lambda e: e.tensor_sub(row_e2, row_e2, row_rstd),
               reads=["row_rstd", "row_e2"], writes=["row_e2"])
        sch.op("scalar", lambda e: e.activation(row_rstd, row_e2, AF.Ln, bias=1e-5),
               reads=["row_e2"], writes=["row_rstd"])
        sch.op("scalar", lambda e: e.activation(row_rstd, row_rstd, AF.Exp, scale=-0.5),
               reads=["row_rstd"], writes=["row_rstd"])
        sch.op("vector", lambda e: e.tensor_mul(row_rs2, row_m, row_rstd),
               reads=["row_m", "row_rstd", "row_e2"], writes=["row_rs2"])
        sch.op("tensor", lambda e: e.matmul(pb[2][:], c_onesr[:], row_rstd,
                                            start=True, stop=True),
               reads=["row_rstd", "c_onesr"], writes=["pb2"])
        sch.op("tensor", lambda e: e.matmul(pb[3][:], c_onesr[:], row_rs2,
                                            start=True, stop=True),
               reads=["row_rs2", "c_onesr"], writes=["pb3"])
        sch.op("vector", lambda e: e.tensor_copy(rstd_rep[:], pb[2][:]),
               reads=["pb2"], writes=["rstd_rep"])
        sch.op("vector", lambda e: e.tensor_copy(rs2_rep[:], pb[3][:]),
               reads=["pb3"], writes=["rs2_rep"])
        rb = rstd_rep[:].unsqueeze(1).broadcast_to((128, NDT, TC))
        s2 = rs2_rep[:].unsqueeze(1).broadcast_to((128, NDT, TC))
        av = A[:].rearrange("p (g t) -> p g t", g=NDT)
        sch.op("vector", lambda e, av=av, rb=rb: e.tensor_tensor(av, av, rb, OP.mult),
               reads=["rstd_rep", ak], writes=[ak])
        sch.op("vector", lambda e, av=av, s2=s2: e.tensor_tensor(av, av, s2, OP.subtract),
               reads=["rs2_rep", ak], writes=[ak])
        dma(chunk_view("s_a", c), stg3(A), q="dmg", reads=[ak], writes=[f"s_a{c}"])
        dma(chunk_view("s_xn", c), stg3(A), q="dmg", reads=[ak], writes=["s_xn"])
        if DBG:
            dma(dbg_xn[:, c * TC:(c + 1) * TC].rearrange("(g p) t -> p g t", p=128),
                stg3(A), q="dmg", reads=[ak], writes=["dbg"])

    ln_chunk(0)
    prep_cast(dram["owT0"], dram["s_owT0"], E, D, BF16, "scr_ow0")
    ln_chunk(1)
    prep_cast(dram["in_wT1"], dram["s_inw1"], D, 2 * E, BF16, "scr_in1")
    ln_chunk(2)
    prep_cast(dram["owT1"], dram["s_owT1"], E, D, BF16, "scr_ow1")
    ln_chunk(3)
    prep_cast(dram["W1T"], dram["s_w1"], D, NF, F8, "scr_w1")
    prep_cast(dram["W2T"], dram["s_w2"], NF, D, F8, "scr_w2")

    def head_ops(l, c):
        par = c % 2
        A = abuf[par]
        ak = f"ab{par}"
        th = []

        def T(engine, fn, reads=(), writes=(), dma_q=None, ms=0):
            th.append((engine, fn, list(reads), list(writes), dma_q, ms))

        if c == 0:
            T("sync", lambda e: e.dma_start(out=halo[:], in_=dram["zeros48"]),
              [], ["halo"], "dms")
        T("sync", lambda e, A=A, c=c: e.dma_start(out=stg3(A), in_=chunk_view("s_a", c)),
          [f"s_a{c}", "x2h0", "x2h1", "x2h2", "x2h3"] if par == 1 else [f"s_a{c}"], [ak], "dms")
        for dt in range(NDT):
            i = dt % 2
            T("scalar", lambda e, dt=dt, A=A, i=i: e.square(
                t_xc[i][:], A[:, dt * TC:(dt + 1) * TC]), [ak], [f"xc{i}"])
            T("tensor", lambda e, dt=dt, i=i: e.matmul(
                pb[6][0:1, :], c_ones[:], t_xc[i][:],
                start=(dt == 0), stop=(dt == NDT - 1)),
                [f"xc{i}", "c_ones"], ["pb6"])
        T("scalar", lambda e: e.activation(row_rstd, pb[6][0:1, :], AF.Ln,
                                           scale=1.0 / D, bias=1e-5),
          ["pb6"], ["row_rstd"])
        T("scalar", lambda e: e.activation(row_rstd, row_rstd, AF.Exp, scale=-0.5),
          ["row_rstd"], ["row_rstd"])
        T("tensor", lambda e: e.matmul(pb[6][:], c_onesr[:], row_rstd,
                                       start=True, stop=True),
          ["row_rstd", "c_onesr"], ["pb6"])
        T("vector", lambda e: e.tensor_copy(rstd_rep[:], pb[6][:]),
          ["pb6"], ["rstd_rep"])
        for dt in range(NDT):
            T("vector", lambda e, dt=dt, l=l, A=A: e.scalar_tensor_tensor(
                rn_bf[:, dt * TC:(dt + 1) * TC], A[:, dt * TC:(dt + 1) * TC],
                r_nw[:, l * NDT + dt:l * NDT + dt + 1], rstd_rep[:], OP.mult, OP.mult),
                [ak, "rstd_rep", "r_nw"], ["rn_bf"])
        xbv3 = xbp[:].rearrange("p (g t) -> p g t", g=NG)
        T("vector", lambda e, xbv3=xbv3: e.tensor_copy(
            xbv3[:, :, 0:3], halo[:].rearrange("p (g t) -> p g t", g=NG)),
            ["halo"], ["xbp_pre"])
        for f0, fn_ in [(0, 6), (6, 6), (12, 4)]:
            for half in range(2):
                gate99 = (f0 == 12)
                def wsrc(d, half=half, f0=f0, fn_=fn_, l=l):
                    return dram[f"s_inw{l}"][d * 128:(d + 1) * 128,
                                             (half * 16 + f0) * 128:(half * 16 + f0 + fn_) * 128]
                for d in range(4):
                    wi = d % 4
                    T("sync", lambda e, d=d, wi=wi, wsrc=wsrc, fn_=fn_: e.dma_start(
                        out=wS[wi][:, 0:fn_ * 128], in_=wsrc(d)),
                        [f"scr_in{l}"], [f"wS{wi}"], "dms", ms=(99 if gate99 else 0))
                for d in range(NDT):
                    wi = d % 4
                    for fi in range(fn_):
                        T("tensor", lambda e, d=d, fi=fi, wi=wi: e.matmul(
                            pb[fi][:], wS[wi][:, fi * 128:(fi + 1) * 128],
                            rn_bf[:, d * TC:(d + 1) * TC],
                            start=(d == 0), stop=(d == NDT - 1)),
                            [f"wS{wi}", "rn_bf"], [f"pb{fi}"])
                    if d + 4 < NDT:
                        wi2 = (d + 4) % 4
                        T("sync", lambda e, d=d, wi2=wi2, wsrc=wsrc, fn_=fn_: e.dma_start(
                            out=wS[wi2][:, 0:fn_ * 128], in_=wsrc(d + 4)),
                            [f"scr_in{l}"], [f"wS{wi2}"], "dms")
                for fi in range(fn_):
                    g = f0 + fi
                    if half == 0:
                        T("scalar", lambda e, g=g, fi=fi: e.copy(
                            xbp[:, g * (TC + 3) + 3:(g + 1) * (TC + 3)], pb[fi][:]),
                            [f"pb{fi}"], [f"xbp{g}"], ms=g)
                    else:
                        T("scalar", lambda e, g=g, fi=fi: e.activation(
                            zg_bf[:, g * TC:(g + 1) * TC], pb[fi][:], AF.Silu),
                            [f"pb{fi}"], [f"zg{g}"], ms=g)
        T("vector", lambda e, xbv3=xbv3: e.tensor_copy(
            halo[:].rearrange("p (g t) -> p g t", g=NG), xbv3[:, :, TC:TC + 3]),
            [f"xbp{g}" for g in range(NG)] + ["xbp_pre"], ["halo"])
        for g in range(NG):
            xg = xbp[:, g * (TC + 3):(g + 1) * (TC + 3)]
            cwb = l * NG * 4 + g * 4
            T("vector", lambda e, xg=xg, cwb=cwb: e.tensor_scalar(
                t_xc[0][:], xg[:, 0:TC], r_cw[:, cwb:cwb + 1], None, OP.mult),
                [f"xbp{g}", "xbp_pre", "halo", "r_cw"], ["xc0"])
            for k in (1, 2, 3):
                si, so = (0, 1) if k % 2 == 1 else (1, 0)
                T("vector", lambda e, xg=xg, cwb=cwb, k=k, si=si, so=so:
                  e.scalar_tensor_tensor(
                      t_xc[so][:], xg[:, k:TC + k], r_cw[:, cwb + k:cwb + k + 1],
                      t_xc[si][:], OP.mult, OP.add),
                  [f"xc{si}", f"xbp{g}", "r_cw"], [f"xc{so}"])
            T("scalar", lambda e, xg=xg, g=g, l=l: e.activation(
                xg[:, 3:TC + 3], t_xc[1][:], AF.Silu,
                bias=r_cb[:, l * NG + g:l * NG + g + 1]),
                ["xc1", "r_cb"], [f"xbp{g}"], ms=g)
            T("tensor", lambda e, g=g, l=l: e.matmul(
                pb[6][0:96, :], r_xpw[:, (l * NG + g) * 96:(l * NG + g + 1) * 96],
                xbp[:, g * (TC + 3) + 3:(g + 1) * (TC + 3)],
                start=(g == 0), stop=(g == NG - 1)),
                [f"xbp{g}", "r_xpw"], ["pb6"], ms=g)
        T("vector", lambda e: e.tensor_copy(dbc_sb[:], pb[6][0:96, :]),
          ["pb6"], ["dbc_sb"])
        T("vector", lambda e: e.tensor_copy(drbf[:], dbc_sb[0:RK, :]),
          ["dbc_sb"], ["drbf"])
        T("gpsimd", lambda e: e.dma_start(out=bc_bf[:], in_=dbc_sb[RK:96, :]),
          ["dbc_sb"], ["bc_bf"], "dmg")
        for jj in range(32):
            nn, isC = jj // 2, jj % 2
            j = nn + 16 * isC
            bank = 6 + (jj % 2)
            T("tensor", lambda e, j=j, bank=bank: e.matmul(
                pb[bank][:], c_sel[:, j:j + 1].broadcast_to((32, 128)), bc_bf[:],
                start=True, stop=True),
                ["bc_bf", "c_sel"], [f"pb{bank}"])
            dstb, dkey = (brep, f"brep{nn}") if not isC else (crep, f"crep{nn}")
            T("scalar", lambda e, nn=nn, bank=bank, dstb=dstb: e.copy(
                dstb[:, nn * TC:(nn + 1) * TC], pb[bank][:]),
                [f"pb{bank}"], [dkey])
        return th

    def emit_thunks(th):
        for engine, fn, reads, writes, dma_q, ms in th:
            sch.op(engine, fn, reads=reads, writes=writes, dma=dma_q)

    def emit_gated(th, pos, g):
        while pos < len(th) and th[pos][5] <= g:
            engine, fn, reads, writes, dma_q, ms = th[pos]
            sch.op(engine, fn, reads=reads, writes=writes, dma=dma_q)
            pos += 1
        return pos

    seq = [(l, c) for l in range(NL) for c in range(NCH)]
    pend = head_ops(0, 0)
    emit_thunks(pend)

    for idx, (l, c) in enumerate(seq):
        par = c % 2
        A = abuf[par]
        ak = f"ab{par}"
        nxt = head_ops(*seq[idx + 1]) if idx + 1 < len(seq) else []
        if idx + 1 < len(seq) and seq[idx + 1][1] == 0:
            pass
        nxt_pos = 0
        def dt_chain(g):
            sl = g % 2
            gl = l * NG + g
            sch.op("tensor", lambda e, g=g, l=l: e.matmul(
                pb[7][:], r_dtw[:, l * E + g * 128:l * E + (g + 1) * 128], drbf[:],
                start=True, stop=True),
                reads=["drbf", "r_dtw"], writes=["pb7"])
            sch.op("scalar", lambda e, gl=gl: e.activation(
                t_sp[:], pb[7][:], AF.Abs, bias=r_dtb[:, gl:gl + 1]),
                reads=["pb7", "r_dtb"], writes=["tsp"])
            sch.op("scalar", lambda e: e.activation(t_sp[:], t_sp[:], AF.Exp, scale=-1.0),
                   reads=["tsp"], writes=["tsp"])
            sch.op("scalar", lambda e: e.activation(t_sp[:], t_sp[:], AF.Ln, bias=1.0),
                   reads=["tsp"], writes=["tsp"])
            sch.op("scalar", lambda e, gl=gl, sl=sl: e.activation(
                t_delta[sl][:], pb[7][:], AF.Relu, bias=r_dtb[:, gl:gl + 1]),
                reads=["pb7", "r_dtb"], writes=[f"tdelta{sl}"])
            sch.op("vector", lambda e, sl=sl: e.tensor_add(
                t_delta[sl][:], t_delta[sl][:], t_sp[:]),
                reads=["tsp", f"tdelta{sl}"], writes=[f"tdelta{sl}"])
            sch.op("gpsimd", lambda e, g=g, sl=sl: e.tensor_tensor(
                t_u[sl][:], t_delta[sl][:], xbp[:, g * (TC + 3) + 3:(g + 1) * (TC + 3)],
                OP.mult),
                reads=[f"tdelta{sl}", f"xbp{g}"], writes=[f"tu{sl}"])

        dt_chain(0)
        for g in range(NG):
            sl = g % 2
            gl = l * NG + g
            if g + 1 < NG:
                dt_chain(g + 1)
            for ng in range(NST // NGRP):
                bsl = ng % 2
                for k in range(NGRP):
                    n = ng * NGRP + k
                    sch.op("scalar", lambda e, n=n, k=k, gl=gl, sl=sl, bsl=bsl: e.activation(
                        t_dA[bsl][:, k * TC:(k + 1) * TC], t_delta[sl][:], AF.Exp,
                        scale=r_wneg[:, gl * NST + n:gl * NST + n + 1]),
                        reads=[f"tdelta{sl}", "r_wneg"], writes=[f"tdA{bsl}"])
                    sch.op("gpsimd", lambda e, n=n, k=k, bsl=bsl, sl=sl: e.tensor_tensor(
                        t_mi[bsl][:, k * TC:(k + 1) * TC], t_u[sl][:],
                        brep[:, n * TC:(n + 1) * TC], OP.mult),
                        reads=[f"tu{sl}", f"brep{n}"], writes=[f"tmi{bsl}"])
                for k in range(NGRP):
                    n = ng * NGRP + k
                    stc = g * NST + n
                    iv = 0.0 if c == 0 else st_v[:, stc:stc + 1]
                    ih = 0.0 if c == 0 else st_h[:, stc:stc + 1]
                    sch.op("vector", lambda e, k=k, bsl=bsl, iv=iv: e.tensor_tensor_scan(
                        t_v[bsl][:, k * TC:(k + 1) * TC], beta_b,
                        t_mi[bsl][:, k * TC:(k + 1) * TC], iv, OP.mult, OP.add),
                        reads=[f"tmi{bsl}", "st_v", "c_const"], writes=[f"tv{bsl}"])
                    sch.op("vector", lambda e, k=k, bsl=bsl, ih=ih: e.tensor_tensor_scan(
                        t_h[bsl][:, k * TC:(k + 1) * TC],
                        t_dA[bsl][:, k * TC:(k + 1) * TC],
                        t_v[bsl][:, k * TC:(k + 1) * TC], ih, OP.mult, OP.add),
                        reads=[f"tdA{bsl}", f"tv{bsl}", "st_h"], writes=[f"th{bsl}"])
                if c < NCH - 1:
                    st0 = g * NST + ng * NGRP
                    sch.op("vector", lambda e, bsl=bsl, st0=st0: e.tensor_copy(
                        st_v[:, st0:st0 + NGRP],
                        t_v[bsl][:, TC - 1:NGRP * TC:TC]),
                        reads=[f"tv{bsl}"], writes=["st_v"])
                    sch.op("vector", lambda e, bsl=bsl, st0=st0: e.tensor_copy(
                        st_h[:, st0:st0 + NGRP],
                        t_h[bsl][:, TC - 1:NGRP * TC:TC]),
                        reads=[f"th{bsl}"], writes=["st_h"])
                n0 = ng * NGRP
                if ng == 0:
                    sch.op("vector", lambda e, bsl=bsl, n0=n0: e.tensor_tensor(
                        t_yAB[:], t_h[bsl][:],
                        crep[:, n0 * TC:(n0 + 2) * TC], OP.mult),
                        reads=[f"th{bsl}", f"crep{n0}", f"crep{n0 + 1}"],
                        writes=["tyA", "tyB"])
                else:
                    sch.op("vector", lambda e, bsl=bsl, n0=n0: e.tensor_tensor(
                        t_hC2[:], t_h[bsl][:],
                        crep[:, n0 * TC:(n0 + 2) * TC], OP.mult),
                        reads=[f"th{bsl}", f"crep{n0}", f"crep{n0 + 1}"],
                        writes=["thC"])
                    sch.op("vector", lambda e: e.tensor_add(
                        t_yAB[:], t_yAB[:], t_hC2[:]),
                        reads=["thC", "tyA", "tyB"], writes=["tyA", "tyB"])
            sch.op("vector", lambda e, g=g, gl=gl: e.scalar_tensor_tensor(
                t_xc[0][:], xbp[:, g * (TC + 3) + 3:(g + 1) * (TC + 3)],
                r_dp[:, gl:gl + 1], t_yAB[:, 0:TC], OP.mult, OP.add),
                reads=[f"xbp{g}", "tyA", "r_dp"], writes=["xc0"])
            sch.op("vector", lambda e: e.tensor_add(t_xc[0][:], t_xc[0][:], t_yAB[:, TC:2 * TC]),
                   reads=["tyB", "xc0"], writes=["xc0"])
            sch.op("vector", lambda e, g=g: e.tensor_tensor(
                rny2[:, g * TC:(g + 1) * TC], t_xc[0][:],
                zg_bf[:, g * TC:(g + 1) * TC], OP.mult),
                reads=["xc0", f"zg{g}"], writes=[f"y2{g}"])
            if nxt and PIPE:
                nxt_pos = emit_gated(nxt, nxt_pos, g)
        for g in range(NG):
            wbuf, key = wS[g % 4], f"wS{g % 4}"
            dma(wbuf[:, 0:1024], dram[f"s_owT{l}"][g * 128:(g + 1) * 128, :],
                reads=[f"scr_ow{l}"], writes=[key])
            for d in range(NDT):
                sch.op("tensor", lambda e, g=g, d=d, wbuf=wbuf: e.matmul(
                    pb[d][:], wbuf[:, d * 128:(d + 1) * 128],
                    rny2[:, g * TC:(g + 1) * TC],
                    start=(g == 0), stop=(g == NG - 1)),
                    reads=[key, f"y2{g}"], writes=[f"pb{d}"])
        for d in range(NDT):
            sch.op("vector", lambda e, d=d, A=A: e.tensor_add(
                A[:, d * TC:(d + 1) * TC], A[:, d * TC:(d + 1) * TC], pb[d][:]),
                reads=[f"pb{d}"], writes=[ak])
        dma(chunk_view("s_a", c), stg3(A), q="dmg", reads=[ak], writes=[f"s_a{c}"])
        if DBG:
            dma(dbg_a[l][:, c * TC:(c + 1) * TC].rearrange("(g p) t -> p g t", p=128),
                stg3(A), q="dmg", reads=[ak], writes=["dbg"])
        while nxt_pos < len(nxt):
            e_, f_, r_, w_, d_, m_ = nxt[nxt_pos]
            sch.op(e_, f_, reads=r_, writes=w_, dma=d_)
            nxt_pos += 1

    sch.barrier()
    stM.close()
    stF = ExitStack()

    def sbF(name, shape, dt=F32):
        return stF.enter_context(nc.sbuf_tensor(name, list(shape), dt))

    f_w1 = sbF("f_w1", (128, 8 * 4096), F8)
    f_w2 = sbF("f_w2", (128, 32 * 1024), F8)
    f_h8_ = [sbF(f"f_h8_{i}", (128, 32 * TC), F8) for i in range(2)]
    f_hb_ = [sbF(f"f_hb{i}", (128, TC), BF16) for i in range(2)]
    f_fb = sbF("f_fb", (128, NDT * TC), BF16)
    f_msk = sbF("f_msk", (128, NDT * TC), BF16)
    f_q8_ = [sbF(f"f_q8_{i}", (128, NDT * TC), F8) for i in range(2)]

    for d in range(NDT):
        dma(f_w1[:, d * 4096:(d + 1) * 4096], dram["s_w1"][d * 128:(d + 1) * 128, :],
            reads=["scr_w1"], writes=["f_w1"])
    for f in range(32):
        dma(f_w2[:, f * 1024:(f + 1) * 1024], dram["s_w2"][f * 128:(f + 1) * 128, :],
            reads=["scr_w2"], writes=["f_w2"])
    for c in range(NCH):
        fpar = c % 2
        f_q8 = f_q8_[fpar]
        f_h8 = f_h8_[fpar]
        kq8 = f"f_q8{fpar}"
        kh8 = f"f_h8{fpar}"
        dma(stg3(f_msk), chunk_view("mask_t", c), writes=["f_msk"])
        dma(stg3(a_st), chunk_view("s_a", c), reads=[f"s_a{c}"], writes=["ab0"])
        dma(stg3(x2_st), chunk_view("s_xn", c), reads=["s_xn", "x2h0", "x2h1", "x2h2", "x2h3"], writes=["ab1"])
        sch.op("vector", lambda e, f_q8=f_q8: e.tensor_copy(f_q8[:], x2_st[:]),
               reads=["ab1"], writes=[kq8])
        dma(stg3(x2_st), chunk_view("x_t", c), writes=["ab1"])
        for fg in range(8):
            for f4i in range(4):
                f = fg * 4 + f4i
                bk = 4 + f4i
                for d in range(NDT):
                    sch.op("tensor", lambda e, f=f, d=d, bk=bk, f_q8=f_q8: e.matmul(
                        pb[bk][:], f_w1[:, d * 4096 + f * 128:d * 4096 + (f + 1) * 128],
                        f_q8[:, d * TC:(d + 1) * TC],
                        start=(d == 0), stop=False),
                        reads=["f_w1", kq8], writes=[f"pb{bk}"])
                sch.op("tensor", lambda e, f=f, bk=bk: e.matmul(
                    pb[bk][:], r_b1q[0:1, f * 128:(f + 1) * 128], r_ones8[:],
                    start=False, stop=True),
                    reads=["r_b1q", "r_ones8"], writes=[f"pb{bk}"])
            for f4i in range(4):
                f = fg * 4 + f4i
                bk = 4 + f4i
                hb = f_hb_[f4i % 2]
                hbk = f"f_hb{f4i % 2}"
                sch.op("scalar", lambda e, bk=bk, hb=hb: e.copy(hb[:], pb[bk][:]),
                       reads=[f"pb{bk}"], writes=[hbk])
                sch.op("vector", lambda e, hb=hb: e.tensor_relu(hb[:], hb[:]),
                       reads=[hbk], writes=[hbk])
                sch.op("vector", lambda e, f=f, f_h8=f_h8, hb=hb: e.tensor_copy(
                    f_h8[:, f * TC:(f + 1) * TC], hb[:]),
                    reads=[hbk], writes=[kh8])
        for d in range(NDT):
            bk = d % 4
            for f in range(32):
                sch.op("tensor", lambda e, d=d, f=f, bk=bk, f_h8=f_h8: e.matmul(
                    pb[bk][:], f_w2[:, f * 1024 + d * 128:f * 1024 + (d + 1) * 128],
                    f_h8[:, f * TC:(f + 1) * TC],
                    start=(f == 0), stop=False),
                    reads=["f_w2", kh8], writes=[f"pb{bk}"])
            sch.op("tensor", lambda e, d=d, bk=bk: e.matmul(
                pb[bk][:], r_b2q[0:1, d * 128:(d + 1) * 128], r_ones8[:],
                start=False, stop=True),
                reads=["r_b2q", "r_ones8"], writes=[f"pb{bk}"])
            sch.op("scalar", lambda e, d=d, bk=bk: e.copy(
                f_fb[:, d * TC:(d + 1) * TC], pb[bk][:]),
                reads=[f"pb{bk}"], writes=["f_fb"])
        sch.op("vector", lambda e: e.tensor_add(a_st[:], a_st[:], x2_st[:]),
               reads=["ab1", "ab0"], writes=["ab0"])
        sch.op("vector", lambda e: e.tensor_mul(x2_st[:], f_fb[:], f_msk[:]),
               reads=["f_fb", "f_msk", "ab1"], writes=["ab1"])
        sch.op("vector", lambda e: e.tensor_add(a_st[:], a_st[:], x2_st[:]),
               reads=["ab1", "ab0"], writes=["ab0"])
        dma(out_t[:, c * TC:(c + 1) * TC].rearrange("(g p) t -> p g t", p=128),
            stg3(a_st), reads=["ab0"], writes=["out"])

    with nc.Block() as block:
        sch.emit(nc, block)
    stF.close()
    stP.close()
    return nc


_PROG = None


def _get_prog():
    global _PROG
    if _PROG is None:
        _PROG = build_program()
    return _PROG


def kernel(L0_attn_x, L0_attn_norm_w, L0_attn_in_proj_w, L0_attn_conv_w,
           L0_attn_conv_b, L0_attn_x_proj_w, L0_attn_dt_w, L0_attn_dt_b,
           L0_attn_A_log, L0_attn_Dp, L0_attn_out_w, L0_ffn_W1, L0_ffn_b1,
           L0_ffn_W2, L0_ffn_b2):
    import jax
    x = np.ascontiguousarray(np.asarray(L0_attn_x, np.float32))
    B = x.shape[0]
    with jax.default_device(jax.devices("cpu")[0]):
        jax.config.update("jax_threefry_partitionable", True)
        keep = np.asarray(jax.random.bernoulli(
            jax.random.key(42, impl="threefry2x32"), 0.8, (8, 2048, 1024)))
    mask = (keep.astype(np.float32) * np.float32(1.25)).astype(BF_NP)

    def arr(a):
        return np.ascontiguousarray(np.asarray(a, np.float32))

    common = {}
    for l in range(NL):
        common[f"in_wT{l}"] = arr(np.asarray(L0_attn_in_proj_w)[l].T)
        common[f"owT{l}"] = arr(np.asarray(L0_attn_out_w)[l].T)
        xpw = arr(np.asarray(L0_attn_x_proj_w)[l].T)
        common[f"xpw_a{l}"] = np.ascontiguousarray(
            xpw.reshape(NG, 128, 96).transpose(1, 0, 2).reshape(128, NG * 96))
        common[f"dtwT{l}"] = arr(np.asarray(L0_attn_dt_w)[l].T)
        al = arr(np.asarray(L0_attn_A_log)[l])
        common[f"alog_a{l}"] = np.ascontiguousarray(
            al.reshape(NG, 128, NST).transpose(1, 0, 2).reshape(128, NG * NST))
        for nm, src in (("dtb_a", L0_attn_dt_b), ("cb_a", L0_attn_conv_b),
                        ("dp_a", L0_attn_Dp)):
            v = arr(np.asarray(src)[l])
            common[f"{nm}{l}"] = np.ascontiguousarray(v.reshape(NG, 128).T)
        cw = arr(np.asarray(L0_attn_conv_w)[l])
        common[f"cw_a{l}"] = np.ascontiguousarray(
            cw.reshape(NG, 128, 4).transpose(1, 0, 2).reshape(128, NG * 4))
        nw = arr(np.asarray(L0_attn_norm_w)[l])
        common[f"nw_a{l}"] = np.ascontiguousarray(nw.reshape(NDT, 128).T)
    common["W1T"] = arr(np.asarray(L0_ffn_W1).T)
    common["W2T"] = arr(np.asarray(L0_ffn_W2).T)
    common["b1r"] = arr(L0_ffn_b1).reshape(1, NF)
    common["b2r"] = arr(L0_ffn_b2).reshape(1, D)
    common["onesr"] = np.ones((1, TC), np.float32)
    common["ones_col"] = np.ones((128, 1), np.float32)
    common["ones_row"] = np.ones((1, 128), np.float32)
    common["sel32"] = np.eye(32, dtype=np.float32).astype(BF_NP)
    consts = np.zeros((128, 8), np.float32)
    consts[:, 0] = -1.0
    consts[:, 1] = 1.0
    consts[:, 2] = -0.5
    consts[:, 3] = 1e-5
    consts[:, 4] = BETA
    consts[:, 5] = 1.0 / D
    common["consts"] = consts
    common["zeros48"] = np.zeros((128, 48), BF_NP)

    in_maps = []
    for b in range(B):
        m = dict(common)
        m["x_t"] = np.ascontiguousarray(x[b].T)
        m["mask_t"] = np.ascontiguousarray(mask[b].T)
        in_maps.append(m)

    nc = _get_prog()
    res = run_bass_kernel_spmd(nc, in_maps, core_ids=list(range(B)))
    out = np.stack([np.asarray(r["out_t"]).T for r in res.results], axis=0)
    return out.astype(np.float32)


# version 22
# speedup vs baseline: 1.0408x; 1.0011x over previous
import numpy as np
import ml_dtypes
import concourse.bass as bass
import concourse.mybir as mybir
from concourse.bass_utils import run_bass_kernel_spmd

F32 = mybir.dt.float32
BF16 = mybir.dt.bfloat16
F8 = mybir.dt.float8e5
OP = mybir.AluOpType
AF = mybir.ActivationFunctionType

L, D, E, NST, RK = 2048, 1024, 2048, 16, 64
TC = 512
NCH = L // TC
NG = E // 128
NDT = D // 128
NF = 4096
NL = 2
BETA = 0.6
NGRP = 2
PIPE = True

BF_NP = ml_dtypes.bfloat16


class Sched:
    ENG = ("sync", "vector", "scalar", "tensor", "gpsimd")

    def __init__(self):
        self.ops = []
        self.bw = {}
        self.br = {}
        self.last_on = {}
        self.last_dma = {}

    def op(self, engine, fn, reads=(), writes=(), dma=None):
        opid = len(self.ops)
        deps = set()
        for b in reads:
            w = self.bw.get(b)
            if w is not None:
                deps.add(w)
        for b in writes:
            deps.update(self.br.get(b, ()))
            w = self.bw.get(b)
            if w is not None:
                deps.add(w)
        deps.discard(opid)
        self.ops.append({"e": engine, "fn": fn, "deps": deps, "dma": dma})
        for b in reads:
            self.br.setdefault(b, []).append(opid)
        for b in writes:
            self.bw[b] = opid
            self.br[b] = []
        self.last_on[engine] = opid
        if dma:
            self.last_dma[dma] = opid
        return opid

    def barrier(self):
        lasts = list(self.last_on.values()) + list(self.last_dma.values())
        for e in self.ENG:
            opid = len(self.ops)
            self.ops.append({"e": e, "fn": None,
                             "deps": set(d for d in lasts if d != opid),
                             "dma": None})
            self.last_on[e] = opid

    def emit(self, nc, block):
        marked = set()
        for i, rec in enumerate(self.ops):
            if rec["dma"]:
                marked.add(i)
            me = rec["e"]
            for d in rec["deps"]:
                if self.ops[d]["e"] != me or self.ops[d]["dma"]:
                    marked.add(d)

        def stream(rec):
            return rec["dma"] if rec["dma"] else rec["e"]

        R = 8
        counters, incno = {}, {}
        dma_slot = {}
        for i, rec in enumerate(self.ops):
            if i in marked:
                s = stream(rec)
                counters[s] = counters.get(s, 0) + 1
                incno[i] = counters[s]
                if rec["dma"]:
                    o = counters[s] - 1
                    dma_slot[i] = (s, o % R, o // R + 1)
        sems = {}
        for s in ("vector", "scalar", "tensor", "gpsimd", "sync"):
            sems[s] = nc.semaphore(f"sem_{s}").__enter__()
        qsems = {}
        for q in ("dms", "dmg"):
            qsems[q] = [nc.semaphore(f"sem_{q}{j}").__enter__() for j in range(R)]
        per_eng = {e: [] for e in self.ENG}
        for i, rec in enumerate(self.ops):
            per_eng[rec["e"]].append(i)

        def emit_engine(eng_obj, ename):
            water = {}
            for i in per_eng[ename]:
                rec = self.ops[i]
                byprod = {}
                for d in rec["deps"]:
                    drec = self.ops[d]
                    if drec["e"] == ename and drec["dma"] is None:
                        continue
                    if drec["dma"]:
                        q, j, u = dma_slot[d]
                        key = (q, j)
                        byprod[key] = max(byprod.get(key, 0), u)
                    else:
                        byprod[drec["e"]] = max(byprod.get(drec["e"], 0), incno[d])
                for ds, v in sorted(byprod.items(), key=str):
                    if water.get(ds, 0) >= v:
                        continue
                    sem = qsems[ds[0]][ds[1]] if isinstance(ds, tuple) else sems[ds]
                    mult = 16 if isinstance(ds, tuple) else 1
                    eng_obj.wait_ge(sem, v * mult)
                    water[ds] = v
                if rec["dma"]:
                    q, j, u = dma_slot[i]
                    if u > 1 and water.get((q, j), 0) < u - 1:
                        eng_obj.wait_ge(qsems[q][j], (u - 1) * 16)
                        water[(q, j)] = u - 1
                    ins = rec["fn"](eng_obj)
                    ins.then_inc(qsems[q][j], 16)
                else:
                    ins = eng_obj.nop() if rec["fn"] is None else rec["fn"](eng_obj)
                    if i in incno:
                        ins.then_inc(sems[rec["e"]], 1)

        @block.sync
        def _(sync):
            emit_engine(sync, "sync")
            for q in ("dms", "dmg"):
                tot = counters.get(q, 0)
                for j in range(R):
                    uses = tot // R + (1 if j < tot % R else 0)
                    if uses:
                        sync.wait_ge(qsems[q][j], uses * 16)

        @block.vector
        def _(v):
            emit_engine(v, "vector")

        @block.scalar
        def _(s):
            emit_engine(s, "scalar")

        @block.tensor
        def _(t):
            emit_engine(t, "tensor")

        @block.gpsimd
        def _(g):
            emit_engine(g, "gpsimd")


def build_program():
    nc = bass.Bass("TRN2", target_bir_lowering=False)
    dram = {}

    def din(name, shape, dt=F32):
        dram[name] = nc.dram_tensor(name, list(shape), dt, kind="ExternalInput").ap()
        return dram[name]

    def dscr(name, shape, dt):
        dram[name] = nc.dram_tensor(name, list(shape), dt, kind="Internal").ap()
        return dram[name]

    din("x_t", (D, L))
    din("mask_t", (D, L), BF16)
    for l in range(NL):
        din(f"in_wT{l}", (D, 2 * E))
        din(f"owT{l}", (E, D))
        din(f"xpw_a{l}", (128, NG * 96))
        din(f"dtwT{l}", (RK, E))
        din(f"alog_a{l}", (128, NG * NST))
        din(f"dtb_a{l}", (128, NG))
        din(f"cb_a{l}", (128, NG))
        din(f"dp_a{l}", (128, NG))
        din(f"cw_a{l}", (128, NG * 4))
        din(f"nw_a{l}", (128, NDT))
    din("W1T", (D, NF))
    din("W2T", (NF, D))
    din("b1r", (1, NF))
    din("b2r", (1, D))
    din("onesr", (1, TC))
    din("ones_col", (128, 1))
    din("ones_row", (1, 128))
    din("sel32", (32, 32), BF16)
    din("consts", (128, 8))
    din("zeros48", (128, 48), BF16)
    out_t = nc.dram_tensor("out_t", [D, L], F32, kind="ExternalOutput").ap()
    DBG = False
    if DBG:
        dbg_xn = nc.dram_tensor("dbg_xn", [D, L], F32, kind="ExternalOutput").ap()
        dbg_a = [nc.dram_tensor(f"dbg_a{l}", [D, L], F32, kind="ExternalOutput").ap()
                 for l in range(NL)]

    for l in range(NL):
        dscr(f"s_inw{l}", (D, 2 * E), BF16)
        dscr(f"s_owT{l}", (E, D), BF16)
    dscr("s_w1", (D, NF), F8)
    dscr("s_w2", (NF, D), F8)
    dscr("s_a", (D, L), F32)
    dscr("s_xn", (D, L), F32)

    sch = Sched()
    from contextlib import ExitStack
    stP = ExitStack()
    stM = ExitStack()

    def sbP(name, shape, dt=F32):
        return stP.enter_context(nc.sbuf_tensor(name, list(shape), dt))

    def sbM(name, shape, dt=F32):
        return stM.enter_context(nc.sbuf_tensor(name, list(shape), dt))

    c_ones = sbP("c_ones", (128, 1))
    c_onesr = sbP("c_onesr", (1, 128))
    c_const = sbP("c_const", (128, 8))
    nc.const_aps.register(F32, -1.0, c_const[:, 0:1])
    nc.const_aps.register(F32, 1.0, c_const[:, 1:2])
    nc.const_aps.register(F32, -0.5, c_const[:, 2:3])
    nc.const_aps.register(F32, 1e-5, c_const[:, 3:4])
    nc.const_aps.register(F32, 1.0 / D, c_const[:, 5:6])
    r_dtw = sbP("r_dtw", (RK, NL * E), BF16)
    r_xpw = sbP("r_xpw", (128, NL * NG * 96), BF16)
    r_wneg = sbP("r_wneg", (128, NL * NG * NST))
    r_dtb = sbP("r_dtb", (128, NL * NG))
    r_cb = sbP("r_cb", (128, NL * NG))
    r_dp = sbP("r_dp", (128, NL * NG))
    r_cw = sbP("r_cw", (128, NL * NG * 4))
    r_nw = sbP("r_nw", (128, NL * NDT))
    r_b1q = sbP("r_b1q", (1, NF), F8)
    r_b2q = sbP("r_b2q", (1, D), F8)
    r_ones8 = sbP("r_ones8", (1, TC), F8)
    a_st = sbP("a_st", (128, NDT * TC))
    x2_st = sbP("x2_st", (128, NDT * TC))
    wS = [sbP(f"wS{i}", (128, 1024), BF16) for i in range(4)]
    row_m = sbP("row_m", (1, TC))[:]
    row_e2 = sbP("row_e2", (1, TC))[:]
    row_rstd = sbP("row_rstd", (1, TC))[:]
    row_rs2 = row_e2
    rstd_rep = sbP("rstd_rep", (128, TC))
    rs2_rep = sbP("rs2_rep", (128, TC))
    pg_f8 = [sbP(f"pg_f8{i}", (128, 1024), F8) for i in range(2)]

    c_sel = sbM("c_sel", (32, 32), BF16)
    xbp = sbM("xbp", (128, NG * (TC + 3)), BF16)
    zg_bf = sbM("zg_bf", (128, NG * TC), BF16)
    rny2 = sbM("rny2", (128, NG * TC), BF16)
    rn_sep = sbM("rn_sep", (128, NDT * TC), BF16)
    halo = sbM("halo", (128, NG * 3), BF16)
    dbc_sb = sbM("dbc_sb", (96, TC))
    bc_bf = sbM("bc_bf", (32, TC), BF16)
    drbf = sbM("drbf", (RK, TC), BF16)
    brep = sbM("brep", (128, NST * TC), BF16)
    crep = sbM("crep", (128, NST * TC), BF16)
    t_delta = [sbM(f"t_delta{i}", (128, TC)) for i in range(2)]
    t_sp = sbM("t_sp", (128, TC))
    t_u = [sbM(f"t_u{i}", (128, TC), BF16) for i in range(2)]
    t_dA = [sbM(f"t_dA{i}", (128, NGRP * TC)) for i in range(2)]
    t_mi = [sbM(f"t_mi{i}", (128, NGRP * TC), BF16) for i in range(2)]
    t_v = [sbM(f"t_v{i}", (128, NGRP * TC), BF16) for i in range(2)]
    t_h = [sbM(f"t_h{i}", (128, NGRP * TC), BF16) for i in range(2)]
    t_yAB = sbM("t_yAB", (128, 2 * TC), BF16)
    t_hC2 = sbM("t_hC2", (128, 2 * TC), BF16)
    t_xc = [sbM(f"t_xc{i}", (128, TC)) for i in range(2)]
    st_v = sbM("st_v", (128, NG * NST))
    st_h = sbM("st_h", (128, NG * NST))

    pb = [stP.enter_context(nc.psum_tensor(f"pb{i}", [128, 512], F32)) for i in range(8)]

    beta_b = c_const[:, 4:5].broadcast_to((128, TC))
    rn_bf = rn_sep

    def dma(dst, src, q="dms", reads=(), writes=()):
        eng = "sync" if q == "dms" else "gpsimd"
        sch.op(eng, lambda e, d=dst, s=src: e.dma_start(out=d, in_=s),
               reads=reads, writes=writes, dma=q)

    dma(c_ones[:], dram["ones_col"], writes=["c_ones"])
    dma(c_onesr[:], dram["ones_row"], writes=["c_onesr"])
    dma(c_sel[:], dram["sel32"], writes=["c_sel"])
    dma(c_const[:], dram["consts"], writes=["c_const"])
    dma(x2_st[0:1, 0:NF], dram["b1r"], writes=["x2_st"])
    sch.op("vector", lambda e: e.tensor_copy(r_b1q[:], x2_st[0:1, 0:NF]),
           reads=["x2_st"], writes=["r_b1q"])
    dma(x2_st[0:1, 0:D], dram["b2r"], writes=["x2_st"])
    sch.op("vector", lambda e: e.tensor_copy(r_b2q[:], x2_st[0:1, 0:D]),
           reads=["x2_st"], writes=["r_b2q"])
    dma(x2_st[0:1, 0:TC], dram["onesr"], writes=["x2_st"])
    sch.op("vector", lambda e: e.tensor_copy(r_ones8[:], x2_st[0:1, 0:TC]),
           reads=["x2_st"], writes=["r_ones8"])
    for l in range(NL):
        dma(x2_st[0:RK, 0:E], dram[f"dtwT{l}"], writes=["x2_st"])
        sch.op("vector", lambda e, l=l: e.tensor_copy(
            r_dtw[:, l * E:(l + 1) * E], x2_st[0:RK, 0:E]),
            reads=["x2_st"], writes=["r_dtw"])
        dma(x2_st[:, 0:NG * 96], dram[f"xpw_a{l}"], writes=["x2_st"])
        sch.op("vector", lambda e, l=l: e.tensor_copy(
            r_xpw[:, l * NG * 96:(l + 1) * NG * 96], x2_st[:, 0:NG * 96]),
            reads=["x2_st"], writes=["r_xpw"])
        dma(x2_st[:, 0:NG * NST], dram[f"alog_a{l}"], writes=["x2_st"])
        sch.op("scalar", lambda e: e.activation(a_st[:, 0:NG * NST], x2_st[:, 0:NG * NST], AF.Exp),
               reads=["x2_st"], writes=["a_st"])
        sch.op("vector", lambda e, l=l: e.tensor_single_scalar(
            r_wneg[:, l * NG * NST:(l + 1) * NG * NST], a_st[:, 0:NG * NST], -1.0, OP.mult),
            reads=["a_st"], writes=["r_wneg"])
        for nm, buf in (("dtb_a", r_dtb), ("cb_a", r_cb), ("dp_a", r_dp)):
            dma(buf[:, l * NG:(l + 1) * NG], dram[f"{nm}{l}"], writes=[f"r_{nm}"])
        dma(r_cw[:, l * NG * 4:(l + 1) * NG * 4], dram[f"cw_a{l}"], writes=["r_cw"])
        dma(r_nw[:, l * NDT:(l + 1) * NDT], dram[f"nw_a{l}"], writes=["r_nw"])

    prep_i = [0]

    def prep_cast(src_ap, dst_ap, rows, cols, outdt, skey):
        for p in range(rows // 128):
            rs = p * 128
            for cs in range(0, cols, 1024):
                cw = min(1024, cols - cs)
                i = prep_i[0] % 4
                j = prep_i[0] % 4
                prep_i[0] += 1
                pin = x2_st[:, i * 1024:i * 1024 + cw]
                kin = f"x2h{i}"
                dma(pin, src_ap[rs:rs + 128, cs:cs + cw], writes=[kin])
                if outdt == BF16:
                    pout, kout = wS[j][:, 0:cw], f"wS{j}"
                else:
                    pout, kout = pg_f8[i % 2][:, 0:cw], f"pg_f8{i % 2}"
                sch.op("gpsimd", lambda e, pout=pout, pin=pin: e.tensor_copy(pout, pin),
                       reads=[kin], writes=[kout])
                dma(dst_ap[rs:rs + 128, cs:cs + cw], pout, q="dmg",
                    reads=[kout], writes=[skey])

    prep_cast(dram["in_wT0"], dram["s_inw0"], D, 2 * E, BF16, "scr_in0")

    def chunk_view(t, c):
        return dram[t][:, c * TC:(c + 1) * TC].rearrange("(g p) t -> p g t", p=128)

    def stg3(buf):
        return buf[:].rearrange("p (g t) -> p g t", t=TC)

    abuf = [a_st, x2_st]

    def ln_chunk(c):
        par = 0
        A = abuf[par]
        ak = f"ab{par}"
        dma(stg3(A), chunk_view("x_t", c), writes=[ak])
        for dt in range(NDT):
            i = dt % 2
            sch.op("scalar", lambda e, dt=dt, A=A, i=i: e.square(
                t_xc[i][:], A[:, dt * TC:(dt + 1) * TC]),
                reads=[ak], writes=[f"xc{i}"])
            sch.op("tensor", lambda e, dt=dt, i=i: e.matmul(
                pb[0][0:1, :], c_ones[:], A[:, dt * TC:(dt + 1) * TC],
                start=(dt == 0), stop=(dt == NDT - 1)),
                reads=[ak, "c_ones"], writes=["pb0"])
            sch.op("tensor", lambda e, dt=dt, i=i: e.matmul(
                pb[1][0:1, :], c_ones[:], t_xc[i][:],
                start=(dt == 0), stop=(dt == NDT - 1)),
                reads=[f"xc{i}", "c_ones"], writes=["pb1"])
        sch.op("vector", lambda e: e.tensor_single_scalar(row_m, pb[0][0:1, :], 1.0 / D, OP.mult),
               reads=["pb0"], writes=["row_m"])
        sch.op("vector", lambda e: e.tensor_single_scalar(row_e2, pb[1][0:1, :], 1.0 / D, OP.mult),
               reads=["pb1"], writes=["row_e2"])
        sch.op("vector", lambda e: e.tensor_mul(row_rstd, row_m, row_m),
               reads=["row_m"], writes=["row_rstd"])
        sch.op("vector", lambda e: e.tensor_sub(row_e2, row_e2, row_rstd),
               reads=["row_rstd", "row_e2"], writes=["row_e2"])
        sch.op("scalar", lambda e: e.activation(row_rstd, row_e2, AF.Ln, bias=1e-5),
               reads=["row_e2"], writes=["row_rstd"])
        sch.op("scalar", lambda e: e.activation(row_rstd, row_rstd, AF.Exp, scale=-0.5),
               reads=["row_rstd"], writes=["row_rstd"])
        sch.op("vector", lambda e: e.tensor_mul(row_rs2, row_m, row_rstd),
               reads=["row_m", "row_rstd", "row_e2"], writes=["row_rs2"])
        sch.op("tensor", lambda e: e.matmul(pb[2][:], c_onesr[:], row_rstd,
                                            start=True, stop=True),
               reads=["row_rstd", "c_onesr"], writes=["pb2"])
        sch.op("tensor", lambda e: e.matmul(pb[3][:], c_onesr[:], row_rs2,
                                            start=True, stop=True),
               reads=["row_rs2", "c_onesr"], writes=["pb3"])
        sch.op("vector", lambda e: e.tensor_copy(rstd_rep[:], pb[2][:]),
               reads=["pb2"], writes=["rstd_rep"])
        sch.op("vector", lambda e: e.tensor_copy(rs2_rep[:], pb[3][:]),
               reads=["pb3"], writes=["rs2_rep"])
        rb = rstd_rep[:].unsqueeze(1).broadcast_to((128, NDT, TC))
        s2 = rs2_rep[:].unsqueeze(1).broadcast_to((128, NDT, TC))
        av = A[:].rearrange("p (g t) -> p g t", g=NDT)
        sch.op("vector", lambda e, av=av, rb=rb: e.tensor_tensor(av, av, rb, OP.mult),
               reads=["rstd_rep", ak], writes=[ak])
        sch.op("vector", lambda e, av=av, s2=s2: e.tensor_tensor(av, av, s2, OP.subtract),
               reads=["rs2_rep", ak], writes=[ak])
        dma(chunk_view("s_a", c), stg3(A), q="dmg", reads=[ak], writes=[f"s_a{c}"])
        dma(chunk_view("s_xn", c), stg3(A), q="dmg", reads=[ak], writes=["s_xn"])
        if DBG:
            dma(dbg_xn[:, c * TC:(c + 1) * TC].rearrange("(g p) t -> p g t", p=128),
                stg3(A), q="dmg", reads=[ak], writes=["dbg"])

    ln_chunk(0)
    prep_cast(dram["owT0"], dram["s_owT0"], E, D, BF16, "scr_ow0")
    ln_chunk(1)
    prep_cast(dram["in_wT1"], dram["s_inw1"], D, 2 * E, BF16, "scr_in1")
    ln_chunk(2)
    prep_cast(dram["owT1"], dram["s_owT1"], E, D, BF16, "scr_ow1")
    ln_chunk(3)
    prep_cast(dram["W1T"], dram["s_w1"], D, NF, F8, "scr_w1")
    prep_cast(dram["W2T"], dram["s_w2"], NF, D, F8, "scr_w2")

    def head_ops(l, c):
        par = c % 2
        A = abuf[par]
        ak = f"ab{par}"
        th = []

        def T(engine, fn, reads=(), writes=(), dma_q=None, ms=0):
            th.append((engine, fn, list(reads), list(writes), dma_q, ms))

        if c == 0:
            T("sync", lambda e: e.dma_start(out=halo[:], in_=dram["zeros48"]),
              [], ["halo"], "dms")
        T("sync", lambda e, A=A, c=c: e.dma_start(out=stg3(A), in_=chunk_view("s_a", c)),
          [f"s_a{c}", "x2h0", "x2h1", "x2h2", "x2h3"] if par == 1 else [f"s_a{c}"], [ak], "dms")
        for dt in range(NDT):
            i = dt % 2
            T("scalar", lambda e, dt=dt, A=A, i=i: e.square(
                t_xc[i][:], A[:, dt * TC:(dt + 1) * TC]), [ak], [f"xc{i}"])
            T("tensor", lambda e, dt=dt, i=i: e.matmul(
                pb[6][0:1, :], c_ones[:], t_xc[i][:],
                start=(dt == 0), stop=(dt == NDT - 1)),
                [f"xc{i}", "c_ones"], ["pb6"])
        T("scalar", lambda e: e.activation(row_rstd, pb[6][0:1, :], AF.Ln,
                                           scale=1.0 / D, bias=1e-5),
          ["pb6"], ["row_rstd"])
        T("scalar", lambda e: e.activation(row_rstd, row_rstd, AF.Exp, scale=-0.5),
          ["row_rstd"], ["row_rstd"])
        T("tensor", lambda e: e.matmul(pb[6][:], c_onesr[:], row_rstd,
                                       start=True, stop=True),
          ["row_rstd", "c_onesr"], ["pb6"])
        T("vector", lambda e: e.tensor_copy(rstd_rep[:], pb[6][:]),
          ["pb6"], ["rstd_rep"])
        for dt in range(NDT):
            T("vector", lambda e, dt=dt, l=l, A=A: e.scalar_tensor_tensor(
                rn_bf[:, dt * TC:(dt + 1) * TC], A[:, dt * TC:(dt + 1) * TC],
                r_nw[:, l * NDT + dt:l * NDT + dt + 1], rstd_rep[:], OP.mult, OP.mult),
                [ak, "rstd_rep", "r_nw"], ["rn_bf"])
        xbv3 = xbp[:].rearrange("p (g t) -> p g t", g=NG)
        T("vector", lambda e, xbv3=xbv3: e.tensor_copy(
            xbv3[:, :, 0:3], halo[:].rearrange("p (g t) -> p g t", g=NG)),
            ["halo"], ["xbp_pre"])
        for f0, fn_ in [(0, 6), (6, 6), (12, 4)]:
            for half in range(2):
                gate99 = (f0 == 12)
                def wsrc(d, half=half, f0=f0, fn_=fn_, l=l):
                    return dram[f"s_inw{l}"][d * 128:(d + 1) * 128,
                                             (half * 16 + f0) * 128:(half * 16 + f0 + fn_) * 128]
                for d in range(4):
                    wi = d % 4
                    T("sync", lambda e, d=d, wi=wi, wsrc=wsrc, fn_=fn_: e.dma_start(
                        out=wS[wi][:, 0:fn_ * 128], in_=wsrc(d)),
                        [f"scr_in{l}"], [f"wS{wi}"], "dms", ms=(99 if gate99 else 0))
                for d in range(NDT):
                    wi = d % 4
                    for fi in range(fn_):
                        T("tensor", lambda e, d=d, fi=fi, wi=wi: e.matmul(
                            pb[fi][:], wS[wi][:, fi * 128:(fi + 1) * 128],
                            rn_bf[:, d * TC:(d + 1) * TC],
                            start=(d == 0), stop=(d == NDT - 1)),
                            [f"wS{wi}", "rn_bf"], [f"pb{fi}"])
                    if d + 4 < NDT:
                        wi2 = (d + 4) % 4
                        T("sync", lambda e, d=d, wi2=wi2, wsrc=wsrc, fn_=fn_: e.dma_start(
                            out=wS[wi2][:, 0:fn_ * 128], in_=wsrc(d + 4)),
                            [f"scr_in{l}"], [f"wS{wi2}"], "dms")
                for fi in range(fn_):
                    g = f0 + fi
                    if half == 0:
                        T("scalar", lambda e, g=g, fi=fi: e.copy(
                            xbp[:, g * (TC + 3) + 3:(g + 1) * (TC + 3)], pb[fi][:]),
                            [f"pb{fi}"], [f"xbp{g}"], ms=g)
                    else:
                        T("scalar", lambda e, g=g, fi=fi: e.activation(
                            zg_bf[:, g * TC:(g + 1) * TC], pb[fi][:], AF.Silu),
                            [f"pb{fi}"], [f"zg{g}"], ms=g)
        T("vector", lambda e, xbv3=xbv3: e.tensor_copy(
            halo[:].rearrange("p (g t) -> p g t", g=NG), xbv3[:, :, TC:TC + 3]),
            [f"xbp{g}" for g in range(NG)] + ["xbp_pre"], ["halo"])
        for g in range(NG):
            xg = xbp[:, g * (TC + 3):(g + 1) * (TC + 3)]
            cwb = l * NG * 4 + g * 4
            T("vector", lambda e, xg=xg, cwb=cwb: e.tensor_scalar(
                t_xc[0][:], xg[:, 0:TC], r_cw[:, cwb:cwb + 1], None, OP.mult),
                [f"xbp{g}", "xbp_pre", "halo", "r_cw"], ["xc0"])
            for k in (1, 2, 3):
                si, so = (0, 1) if k % 2 == 1 else (1, 0)
                T("vector", lambda e, xg=xg, cwb=cwb, k=k, si=si, so=so:
                  e.scalar_tensor_tensor(
                      t_xc[so][:], xg[:, k:TC + k], r_cw[:, cwb + k:cwb + k + 1],
                      t_xc[si][:], OP.mult, OP.add),
                  [f"xc{si}", f"xbp{g}", "r_cw"], [f"xc{so}"])
            T("scalar", lambda e, xg=xg, g=g, l=l: e.activation(
                xg[:, 3:TC + 3], t_xc[1][:], AF.Silu,
                bias=r_cb[:, l * NG + g:l * NG + g + 1]),
                ["xc1", "r_cb"], [f"xbp{g}"], ms=g)
            T("tensor", lambda e, g=g, l=l: e.matmul(
                pb[6][0:96, :], r_xpw[:, (l * NG + g) * 96:(l * NG + g + 1) * 96],
                xbp[:, g * (TC + 3) + 3:(g + 1) * (TC + 3)],
                start=(g == 0), stop=(g == NG - 1)),
                [f"xbp{g}", "r_xpw"], ["pb6"], ms=g)
        T("vector", lambda e: e.tensor_copy(dbc_sb[:], pb[6][0:96, :]),
          ["pb6"], ["dbc_sb"])
        T("vector", lambda e: e.tensor_copy(drbf[:], dbc_sb[0:RK, :]),
          ["dbc_sb"], ["drbf"])
        T("gpsimd", lambda e: e.dma_start(out=bc_bf[:], in_=dbc_sb[RK:96, :]),
          ["dbc_sb"], ["bc_bf"], "dmg")
        for jj in range(32):
            nn, isC = jj // 2, jj % 2
            j = nn + 16 * isC
            bank = 6 + (jj % 2)
            T("tensor", lambda e, j=j, bank=bank: e.matmul(
                pb[bank][:], c_sel[:, j:j + 1].broadcast_to((32, 128)), bc_bf[:],
                start=True, stop=True),
                ["bc_bf", "c_sel"], [f"pb{bank}"])
            dstb, dkey = (brep, f"brep{nn}") if not isC else (crep, f"crep{nn}")
            T("scalar", lambda e, nn=nn, bank=bank, dstb=dstb: e.copy(
                dstb[:, nn * TC:(nn + 1) * TC], pb[bank][:]),
                [f"pb{bank}"], [dkey])
        return th

    def emit_thunks(th):
        for engine, fn, reads, writes, dma_q, ms in th:
            sch.op(engine, fn, reads=reads, writes=writes, dma=dma_q)

    def emit_gated(th, pos, g):
        while pos < len(th) and th[pos][5] <= g:
            engine, fn, reads, writes, dma_q, ms = th[pos]
            sch.op(engine, fn, reads=reads, writes=writes, dma=dma_q)
            pos += 1
        return pos

    seq = [(l, c) for l in range(NL) for c in range(NCH)]
    pend = head_ops(0, 0)
    emit_thunks(pend)

    for idx, (l, c) in enumerate(seq):
        par = c % 2
        A = abuf[par]
        ak = f"ab{par}"
        nxt = head_ops(*seq[idx + 1]) if idx + 1 < len(seq) else []
        if idx + 1 < len(seq) and seq[idx + 1][1] == 0:
            pass
        nxt_pos = 0
        def dt_chain(g):
            sl = g % 2
            gl = l * NG + g
            sch.op("tensor", lambda e, g=g, l=l: e.matmul(
                pb[7][:], r_dtw[:, l * E + g * 128:l * E + (g + 1) * 128], drbf[:],
                start=True, stop=True),
                reads=["drbf", "r_dtw"], writes=["pb7"])
            sch.op("scalar", lambda e, gl=gl: e.activation(
                t_sp[:], pb[7][:], AF.Abs, bias=r_dtb[:, gl:gl + 1]),
                reads=["pb7", "r_dtb"], writes=["tsp"])
            sch.op("scalar", lambda e: e.activation(t_sp[:], t_sp[:], AF.Exp, scale=-1.0),
                   reads=["tsp"], writes=["tsp"])
            sch.op("scalar", lambda e: e.activation(t_sp[:], t_sp[:], AF.Ln, bias=1.0),
                   reads=["tsp"], writes=["tsp"])
            sch.op("scalar", lambda e, gl=gl, sl=sl: e.activation(
                t_delta[sl][:], pb[7][:], AF.Relu, bias=r_dtb[:, gl:gl + 1]),
                reads=["pb7", "r_dtb"], writes=[f"tdelta{sl}"])
            sch.op("vector", lambda e, sl=sl: e.tensor_add(
                t_delta[sl][:], t_delta[sl][:], t_sp[:]),
                reads=["tsp", f"tdelta{sl}"], writes=[f"tdelta{sl}"])
            sch.op("gpsimd", lambda e, g=g, sl=sl: e.tensor_tensor(
                t_u[sl][:], t_delta[sl][:], xbp[:, g * (TC + 3) + 3:(g + 1) * (TC + 3)],
                OP.mult),
                reads=[f"tdelta{sl}", f"xbp{g}"], writes=[f"tu{sl}"])

        dt_chain(0)
        for g in range(NG):
            sl = g % 2
            gl = l * NG + g
            if g + 1 < NG:
                dt_chain(g + 1)
            for ng in range(NST // NGRP):
                bsl = ng % 2
                n0 = ng * NGRP
                for k in range(NGRP):
                    n = ng * NGRP + k
                    sch.op("scalar", lambda e, n=n, k=k, gl=gl, sl=sl, bsl=bsl: e.activation(
                        t_dA[bsl][:, k * TC:(k + 1) * TC], t_delta[sl][:], AF.Exp,
                        scale=r_wneg[:, gl * NST + n:gl * NST + n + 1]),
                        reads=[f"tdelta{sl}", "r_wneg"], writes=[f"tdA{bsl}"])
                sch.op("gpsimd", lambda e, bsl=bsl, sl=sl, n0=n0: e.tensor_tensor(
                    t_mi[bsl][:].rearrange("p (k t) -> p k t", k=NGRP),
                    t_u[sl][:].unsqueeze(1).broadcast_to((128, NGRP, TC)),
                    crep[:, n0 * TC:(n0 + 2) * TC].rearrange("p (k t) -> p k t", k=NGRP)
                    if False else
                    brep[:, n0 * TC:(n0 + 2) * TC].rearrange("p (k t) -> p k t", k=NGRP),
                    OP.mult),
                    reads=[f"tu{sl}", f"brep{n0}", f"brep{n0 + 1}"], writes=[f"tmi{bsl}"])
                for k in range(NGRP):
                    n = ng * NGRP + k
                    stc = g * NST + n
                    iv = 0.0 if c == 0 else st_v[:, stc:stc + 1]
                    ih = 0.0 if c == 0 else st_h[:, stc:stc + 1]
                    sch.op("vector", lambda e, k=k, bsl=bsl, iv=iv: e.tensor_tensor_scan(
                        t_v[bsl][:, k * TC:(k + 1) * TC], beta_b,
                        t_mi[bsl][:, k * TC:(k + 1) * TC], iv, OP.mult, OP.add),
                        reads=[f"tmi{bsl}", "st_v", "c_const"], writes=[f"tv{bsl}"])
                    sch.op("vector", lambda e, k=k, bsl=bsl, ih=ih: e.tensor_tensor_scan(
                        t_h[bsl][:, k * TC:(k + 1) * TC],
                        t_dA[bsl][:, k * TC:(k + 1) * TC],
                        t_v[bsl][:, k * TC:(k + 1) * TC], ih, OP.mult, OP.add),
                        reads=[f"tdA{bsl}", f"tv{bsl}", "st_h"], writes=[f"th{bsl}"])
                if c < NCH - 1:
                    st0 = g * NST + ng * NGRP
                    sch.op("vector", lambda e, bsl=bsl, st0=st0: e.tensor_copy(
                        st_v[:, st0:st0 + NGRP],
                        t_v[bsl][:, TC - 1:NGRP * TC:TC]),
                        reads=[f"tv{bsl}"], writes=["st_v"])
                    sch.op("vector", lambda e, bsl=bsl, st0=st0: e.tensor_copy(
                        st_h[:, st0:st0 + NGRP],
                        t_h[bsl][:, TC - 1:NGRP * TC:TC]),
                        reads=[f"th{bsl}"], writes=["st_h"])
                n0 = ng * NGRP
                if ng == 0:
                    sch.op("vector", lambda e, bsl=bsl, n0=n0: e.tensor_tensor(
                        t_yAB[:], t_h[bsl][:],
                        crep[:, n0 * TC:(n0 + 2) * TC], OP.mult),
                        reads=[f"th{bsl}", f"crep{n0}", f"crep{n0 + 1}"],
                        writes=["tyA", "tyB"])
                else:
                    sch.op("vector", lambda e, bsl=bsl, n0=n0: e.tensor_tensor(
                        t_hC2[:], t_h[bsl][:],
                        crep[:, n0 * TC:(n0 + 2) * TC], OP.mult),
                        reads=[f"th{bsl}", f"crep{n0}", f"crep{n0 + 1}"],
                        writes=["thC"])
                    sch.op("vector", lambda e: e.tensor_add(
                        t_yAB[:], t_yAB[:], t_hC2[:]),
                        reads=["thC", "tyA", "tyB"], writes=["tyA", "tyB"])
            sch.op("vector", lambda e, g=g, gl=gl: e.scalar_tensor_tensor(
                t_xc[0][:], xbp[:, g * (TC + 3) + 3:(g + 1) * (TC + 3)],
                r_dp[:, gl:gl + 1], t_yAB[:, 0:TC], OP.mult, OP.add),
                reads=[f"xbp{g}", "tyA", "r_dp"], writes=["xc0"])
            sch.op("vector", lambda e: e.tensor_add(t_xc[0][:], t_xc[0][:], t_yAB[:, TC:2 * TC]),
                   reads=["tyB", "xc0"], writes=["xc0"])
            sch.op("vector", lambda e, g=g: e.tensor_tensor(
                rny2[:, g * TC:(g + 1) * TC], t_xc[0][:],
                zg_bf[:, g * TC:(g + 1) * TC], OP.mult),
                reads=["xc0", f"zg{g}"], writes=[f"y2{g}"])
            if nxt and PIPE:
                nxt_pos = emit_gated(nxt, nxt_pos, g)
        for g in range(NG):
            wbuf, key = wS[g % 4], f"wS{g % 4}"
            dma(wbuf[:, 0:1024], dram[f"s_owT{l}"][g * 128:(g + 1) * 128, :],
                reads=[f"scr_ow{l}"], writes=[key])
            for d in range(NDT):
                sch.op("tensor", lambda e, g=g, d=d, wbuf=wbuf: e.matmul(
                    pb[d][:], wbuf[:, d * 128:(d + 1) * 128],
                    rny2[:, g * TC:(g + 1) * TC],
                    start=(g == 0), stop=(g == NG - 1)),
                    reads=[key, f"y2{g}"], writes=[f"pb{d}"])
        for d in range(NDT):
            sch.op("vector", lambda e, d=d, A=A: e.tensor_add(
                A[:, d * TC:(d + 1) * TC], A[:, d * TC:(d + 1) * TC], pb[d][:]),
                reads=[f"pb{d}"], writes=[ak])
        dma(chunk_view("s_a", c), stg3(A), q="dmg", reads=[ak], writes=[f"s_a{c}"])
        if DBG:
            dma(dbg_a[l][:, c * TC:(c + 1) * TC].rearrange("(g p) t -> p g t", p=128),
                stg3(A), q="dmg", reads=[ak], writes=["dbg"])
        while nxt_pos < len(nxt):
            e_, f_, r_, w_, d_, m_ = nxt[nxt_pos]
            sch.op(e_, f_, reads=r_, writes=w_, dma=d_)
            nxt_pos += 1

    sch.barrier()
    stM.close()
    stF = ExitStack()

    def sbF(name, shape, dt=F32):
        return stF.enter_context(nc.sbuf_tensor(name, list(shape), dt))

    f_w1 = sbF("f_w1", (128, 8 * 4096), F8)
    f_w2 = sbF("f_w2", (128, 32 * 1024), F8)
    f_h8_ = [sbF(f"f_h8_{i}", (128, 32 * TC), F8) for i in range(2)]
    f_hb_ = [sbF(f"f_hb{i}", (128, TC), BF16) for i in range(2)]
    f_fb = sbF("f_fb", (128, NDT * TC), BF16)
    f_msk = sbF("f_msk", (128, NDT * TC), BF16)
    f_q8full = sbF("f_q8full", (128, NDT * L), F8)

    for d in range(NDT):
        dma(f_w1[:, d * 4096:(d + 1) * 4096], dram["s_w1"][d * 128:(d + 1) * 128, :],
            reads=["scr_w1"], writes=["f_w1"])
    for f in range(32):
        dma(f_w2[:, f * 1024:(f + 1) * 1024], dram["s_w2"][f * 128:(f + 1) * 128, :],
            reads=["scr_w2"], writes=["f_w2"])
    for c in range(NCH):
        dma(stg3(x2_st), chunk_view("s_xn", c),
            reads=["s_xn", "x2h0", "x2h1", "x2h2", "x2h3"] if c == 0 else ["s_xn"],
            writes=["ab1"])
        sch.op("vector", lambda e, c=c: e.tensor_copy(
            f_q8full[:].rearrange("p (g t) -> p g t", t=L)[:, :, c * TC:(c + 1) * TC],
            stg3(x2_st)),
            reads=["ab1"], writes=[f"fq8c{c}"])
    for c in range(NCH):
        fpar = c % 2
        f_h8 = f_h8_[fpar]
        kh8 = f"f_h8{fpar}"
        dma(stg3(f_msk), chunk_view("mask_t", c), writes=["f_msk"])
        dma(stg3(a_st), chunk_view("s_a", c), reads=[f"s_a{c}"], writes=["ab0"])
        dma(stg3(x2_st), chunk_view("x_t", c), writes=["ab1"])
        for fg in range(8):
            for f4i in range(4):
                f = fg * 4 + f4i
                bk = 4 + f4i
                for d in range(NDT):
                    sch.op("tensor", lambda e, f=f, d=d, bk=bk, c=c: e.matmul(
                        pb[bk][:], f_w1[:, d * 4096 + f * 128:d * 4096 + (f + 1) * 128],
                        f_q8full[:, d * L + c * TC:d * L + (c + 1) * TC],
                        start=(d == 0), stop=False),
                        reads=["f_w1", f"fq8c{c}"], writes=[f"pb{bk}"])
                sch.op("tensor", lambda e, f=f, bk=bk: e.matmul(
                    pb[bk][:], r_b1q[0:1, f * 128:(f + 1) * 128], r_ones8[:],
                    start=False, stop=True),
                    reads=["r_b1q", "r_ones8"], writes=[f"pb{bk}"])
            for f4i in range(4):
                f = fg * 4 + f4i
                bk = 4 + f4i
                hb = f_hb_[f4i % 2]
                hbk = f"f_hb{f4i % 2}"
                sch.op("scalar", lambda e, bk=bk, hb=hb: e.copy(hb[:], pb[bk][:]),
                       reads=[f"pb{bk}"], writes=[hbk])
                sch.op("vector", lambda e, hb=hb: e.tensor_relu(hb[:], hb[:]),
                       reads=[hbk], writes=[hbk])
                sch.op("vector", lambda e, f=f, f_h8=f_h8, hb=hb: e.tensor_copy(
                    f_h8[:, f * TC:(f + 1) * TC], hb[:]),
                    reads=[hbk], writes=[kh8])
        for d in range(NDT):
            bk = d % 4
            for f in range(32):
                sch.op("tensor", lambda e, d=d, f=f, bk=bk, f_h8=f_h8: e.matmul(
                    pb[bk][:], f_w2[:, f * 1024 + d * 128:f * 1024 + (d + 1) * 128],
                    f_h8[:, f * TC:(f + 1) * TC],
                    start=(f == 0), stop=False),
                    reads=["f_w2", kh8], writes=[f"pb{bk}"])
            sch.op("tensor", lambda e, d=d, bk=bk: e.matmul(
                pb[bk][:], r_b2q[0:1, d * 128:(d + 1) * 128], r_ones8[:],
                start=False, stop=True),
                reads=["r_b2q", "r_ones8"], writes=[f"pb{bk}"])
            sch.op("scalar", lambda e, d=d, bk=bk: e.copy(
                f_fb[:, d * TC:(d + 1) * TC], pb[bk][:]),
                reads=[f"pb{bk}"], writes=["f_fb"])
        sch.op("vector", lambda e: e.tensor_add(a_st[:], a_st[:], x2_st[:]),
               reads=["ab1", "ab0"], writes=["ab0"])
        sch.op("vector", lambda e: e.tensor_mul(x2_st[:], f_fb[:], f_msk[:]),
               reads=["f_fb", "f_msk", "ab1"], writes=["ab1"])
        sch.op("vector", lambda e: e.tensor_add(a_st[:], a_st[:], x2_st[:]),
               reads=["ab1", "ab0"], writes=["ab0"])
        dma(out_t[:, c * TC:(c + 1) * TC].rearrange("(g p) t -> p g t", p=128),
            stg3(a_st), reads=["ab0"], writes=["out"])

    with nc.Block() as block:
        sch.emit(nc, block)
    stF.close()
    stP.close()
    return nc


_PROG = None


def _get_prog():
    global _PROG
    if _PROG is None:
        _PROG = build_program()
    return _PROG


def kernel(L0_attn_x, L0_attn_norm_w, L0_attn_in_proj_w, L0_attn_conv_w,
           L0_attn_conv_b, L0_attn_x_proj_w, L0_attn_dt_w, L0_attn_dt_b,
           L0_attn_A_log, L0_attn_Dp, L0_attn_out_w, L0_ffn_W1, L0_ffn_b1,
           L0_ffn_W2, L0_ffn_b2):
    import jax
    x = np.ascontiguousarray(np.asarray(L0_attn_x, np.float32))
    B = x.shape[0]
    with jax.default_device(jax.devices("cpu")[0]):
        jax.config.update("jax_threefry_partitionable", True)
        keep = np.asarray(jax.random.bernoulli(
            jax.random.key(42, impl="threefry2x32"), 0.8, (8, 2048, 1024)))
    mask = (keep.astype(np.float32) * np.float32(1.25)).astype(BF_NP)

    def arr(a):
        return np.ascontiguousarray(np.asarray(a, np.float32))

    common = {}
    for l in range(NL):
        common[f"in_wT{l}"] = arr(np.asarray(L0_attn_in_proj_w)[l].T)
        common[f"owT{l}"] = arr(np.asarray(L0_attn_out_w)[l].T)
        xpw = arr(np.asarray(L0_attn_x_proj_w)[l].T)
        common[f"xpw_a{l}"] = np.ascontiguousarray(
            xpw.reshape(NG, 128, 96).transpose(1, 0, 2).reshape(128, NG * 96))
        common[f"dtwT{l}"] = arr(np.asarray(L0_attn_dt_w)[l].T)
        al = arr(np.asarray(L0_attn_A_log)[l])
        common[f"alog_a{l}"] = np.ascontiguousarray(
            al.reshape(NG, 128, NST).transpose(1, 0, 2).reshape(128, NG * NST))
        for nm, src in (("dtb_a", L0_attn_dt_b), ("cb_a", L0_attn_conv_b),
                        ("dp_a", L0_attn_Dp)):
            v = arr(np.asarray(src)[l])
            common[f"{nm}{l}"] = np.ascontiguousarray(v.reshape(NG, 128).T)
        cw = arr(np.asarray(L0_attn_conv_w)[l])
        common[f"cw_a{l}"] = np.ascontiguousarray(
            cw.reshape(NG, 128, 4).transpose(1, 0, 2).reshape(128, NG * 4))
        nw = arr(np.asarray(L0_attn_norm_w)[l])
        common[f"nw_a{l}"] = np.ascontiguousarray(nw.reshape(NDT, 128).T)
    common["W1T"] = arr(np.asarray(L0_ffn_W1).T)
    common["W2T"] = arr(np.asarray(L0_ffn_W2).T)
    common["b1r"] = arr(L0_ffn_b1).reshape(1, NF)
    common["b2r"] = arr(L0_ffn_b2).reshape(1, D)
    common["onesr"] = np.ones((1, TC), np.float32)
    common["ones_col"] = np.ones((128, 1), np.float32)
    common["ones_row"] = np.ones((1, 128), np.float32)
    common["sel32"] = np.eye(32, dtype=np.float32).astype(BF_NP)
    consts = np.zeros((128, 8), np.float32)
    consts[:, 0] = -1.0
    consts[:, 1] = 1.0
    consts[:, 2] = -0.5
    consts[:, 3] = 1e-5
    consts[:, 4] = BETA
    consts[:, 5] = 1.0 / D
    common["consts"] = consts
    common["zeros48"] = np.zeros((128, 48), BF_NP)

    in_maps = []
    for b in range(B):
        m = dict(common)
        m["x_t"] = np.ascontiguousarray(x[b].T)
        m["mask_t"] = np.ascontiguousarray(mask[b].T)
        in_maps.append(m)

    nc = _get_prog()
    res = run_bass_kernel_spmd(nc, in_maps, core_ids=list(range(B)))
    out = np.stack([np.asarray(r["out_t"]).T for r in res.results], axis=0)
    return out.astype(np.float32)


# version 23
# speedup vs baseline: 1.0688x; 1.0269x over previous
import numpy as np
import ml_dtypes
import concourse.bass as bass
import concourse.mybir as mybir
from concourse.bass_utils import run_bass_kernel_spmd

F32 = mybir.dt.float32
BF16 = mybir.dt.bfloat16
F8 = mybir.dt.float8e5
OP = mybir.AluOpType
AF = mybir.ActivationFunctionType

L, D, E, NST, RK = 2048, 1024, 2048, 16, 64
TC = 512
NCH = L // TC
NG = E // 128
NDT = D // 128
NF = 4096
NL = 2
BETA = 0.6
NGRP = 2
PIPE = True

BF_NP = ml_dtypes.bfloat16


class Sched:
    ENG = ("sync", "vector", "scalar", "tensor", "gpsimd")

    def __init__(self):
        self.ops = []
        self.bw = {}
        self.br = {}
        self.last_on = {}
        self.last_dma = {}

    def op(self, engine, fn, reads=(), writes=(), dma=None):
        opid = len(self.ops)
        deps = set()
        for b in reads:
            w = self.bw.get(b)
            if w is not None:
                deps.add(w)
        for b in writes:
            deps.update(self.br.get(b, ()))
            w = self.bw.get(b)
            if w is not None:
                deps.add(w)
        deps.discard(opid)
        self.ops.append({"e": engine, "fn": fn, "deps": deps, "dma": dma})
        for b in reads:
            self.br.setdefault(b, []).append(opid)
        for b in writes:
            self.bw[b] = opid
            self.br[b] = []
        self.last_on[engine] = opid
        if dma:
            self.last_dma[dma] = opid
        return opid

    def barrier(self):
        lasts = list(self.last_on.values()) + list(self.last_dma.values())
        for e in self.ENG:
            opid = len(self.ops)
            self.ops.append({"e": e, "fn": None,
                             "deps": set(d for d in lasts if d != opid),
                             "dma": None})
            self.last_on[e] = opid

    def emit(self, nc, block):
        marked = set()
        for i, rec in enumerate(self.ops):
            if rec["dma"]:
                marked.add(i)
            me = rec["e"]
            for d in rec["deps"]:
                if self.ops[d]["e"] != me or self.ops[d]["dma"]:
                    marked.add(d)

        def stream(rec):
            return rec["dma"] if rec["dma"] else rec["e"]

        R = 8
        counters, incno = {}, {}
        dma_slot = {}
        for i, rec in enumerate(self.ops):
            if i in marked:
                s = stream(rec)
                counters[s] = counters.get(s, 0) + 1
                incno[i] = counters[s]
                if rec["dma"]:
                    o = counters[s] - 1
                    dma_slot[i] = (s, o % R, o // R + 1)
        sems = {}
        for s in ("vector", "scalar", "tensor", "gpsimd", "sync"):
            sems[s] = nc.semaphore(f"sem_{s}").__enter__()
        qsems = {}
        for q in ("dms", "dmg"):
            qsems[q] = [nc.semaphore(f"sem_{q}{j}").__enter__() for j in range(R)]
        per_eng = {e: [] for e in self.ENG}
        for i, rec in enumerate(self.ops):
            per_eng[rec["e"]].append(i)

        def emit_engine(eng_obj, ename):
            water = {}
            for i in per_eng[ename]:
                rec = self.ops[i]
                byprod = {}
                for d in rec["deps"]:
                    drec = self.ops[d]
                    if drec["e"] == ename and drec["dma"] is None:
                        continue
                    if drec["dma"]:
                        q, j, u = dma_slot[d]
                        key = (q, j)
                        byprod[key] = max(byprod.get(key, 0), u)
                    else:
                        byprod[drec["e"]] = max(byprod.get(drec["e"], 0), incno[d])
                for ds, v in sorted(byprod.items(), key=str):
                    if water.get(ds, 0) >= v:
                        continue
                    sem = qsems[ds[0]][ds[1]] if isinstance(ds, tuple) else sems[ds]
                    mult = 16 if isinstance(ds, tuple) else 1
                    eng_obj.wait_ge(sem, v * mult)
                    water[ds] = v
                if rec["dma"]:
                    q, j, u = dma_slot[i]
                    if u > 1 and water.get((q, j), 0) < u - 1:
                        eng_obj.wait_ge(qsems[q][j], (u - 1) * 16)
                        water[(q, j)] = u - 1
                    ins = rec["fn"](eng_obj)
                    ins.then_inc(qsems[q][j], 16)
                else:
                    ins = eng_obj.nop() if rec["fn"] is None else rec["fn"](eng_obj)
                    if i in incno:
                        ins.then_inc(sems[rec["e"]], 1)

        @block.sync
        def _(sync):
            emit_engine(sync, "sync")
            for q in ("dms", "dmg"):
                tot = counters.get(q, 0)
                for j in range(R):
                    uses = tot // R + (1 if j < tot % R else 0)
                    if uses:
                        sync.wait_ge(qsems[q][j], uses * 16)

        @block.vector
        def _(v):
            emit_engine(v, "vector")

        @block.scalar
        def _(s):
            emit_engine(s, "scalar")

        @block.tensor
        def _(t):
            emit_engine(t, "tensor")

        @block.gpsimd
        def _(g):
            emit_engine(g, "gpsimd")


def build_program():
    nc = bass.Bass("TRN2", target_bir_lowering=False)
    dram = {}

    def din(name, shape, dt=F32):
        dram[name] = nc.dram_tensor(name, list(shape), dt, kind="ExternalInput").ap()
        return dram[name]

    def dscr(name, shape, dt):
        dram[name] = nc.dram_tensor(name, list(shape), dt, kind="Internal").ap()
        return dram[name]

    din("x_t", (D, L))
    din("mask_t", (D, L), BF16)
    for l in range(NL):
        din(f"in_wT{l}", (D, 2 * E))
        din(f"owT{l}", (E, D))
        din(f"xpw_a{l}", (128, NG * 96))
        din(f"dtwT{l}", (RK, E))
        din(f"alog_a{l}", (128, NG * NST))
        din(f"dtb_a{l}", (128, NG))
        din(f"cb_a{l}", (128, NG))
        din(f"dp_a{l}", (128, NG))
        din(f"cw_a{l}", (128, NG * 4))
        din(f"nw_a{l}", (128, NDT))
    din("W1T", (D, NF))
    din("W2T", (NF, D))
    din("b1r", (1, NF))
    din("b2r", (1, D))
    din("onesr", (1, TC))
    din("ones_col", (128, 1))
    din("ones_row", (1, 128))
    din("sel32", (32, 32), BF16)
    din("consts", (128, 8))
    din("zeros48", (128, 48), BF16)
    out_t = nc.dram_tensor("out_t", [D, L], F32, kind="ExternalOutput").ap()
    DBG = False
    if DBG:
        dbg_xn = nc.dram_tensor("dbg_xn", [D, L], F32, kind="ExternalOutput").ap()
        dbg_a = [nc.dram_tensor(f"dbg_a{l}", [D, L], F32, kind="ExternalOutput").ap()
                 for l in range(NL)]

    dscr("s_a", (D, L), F32)
    dscr("s_xn", (D, L), F32)

    sch = Sched()
    from contextlib import ExitStack
    stP = ExitStack()
    stM = ExitStack()

    def sbP(name, shape, dt=F32):
        return stP.enter_context(nc.sbuf_tensor(name, list(shape), dt))

    def sbM(name, shape, dt=F32):
        return stM.enter_context(nc.sbuf_tensor(name, list(shape), dt))

    c_ones = sbP("c_ones", (128, 1))
    c_onesr = sbP("c_onesr", (1, 128))
    c_const = sbP("c_const", (128, 8))
    nc.const_aps.register(F32, -1.0, c_const[:, 0:1])
    nc.const_aps.register(F32, 1.0, c_const[:, 1:2])
    nc.const_aps.register(F32, -0.5, c_const[:, 2:3])
    nc.const_aps.register(F32, 1e-5, c_const[:, 3:4])
    nc.const_aps.register(F32, 1.0 / D, c_const[:, 5:6])
    r_dtw = sbP("r_dtw", (RK, NL * E), BF16)
    r_xpw = sbP("r_xpw", (128, NL * NG * 96), BF16)
    r_wneg = sbP("r_wneg", (128, NL * NG * NST))
    r_dtb = sbP("r_dtb", (128, NL * NG))
    r_cb = sbP("r_cb", (128, NL * NG))
    r_dp = sbP("r_dp", (128, NL * NG))
    r_cw = sbP("r_cw", (128, NL * NG * 4))
    r_nw = sbP("r_nw", (128, NL * NDT))
    r_b1q = sbP("r_b1q", (1, NF), F8)
    r_b2q = sbP("r_b2q", (1, D), F8)
    r_ones8 = sbP("r_ones8", (1, TC), F8)
    a_st = sbP("a_st", (128, NDT * TC))
    x2_st = sbP("x2_st", (128, NDT * TC))
    wS = [sbP(f"wS{i}", (128, 1024), BF16) for i in range(4)]
    row_m = sbP("row_m", (1, TC))[:]
    row_e2 = sbP("row_e2", (1, TC))[:]
    row_rstd = sbP("row_rstd", (1, TC))[:]
    row_rs2 = row_e2
    rstd_rep = sbP("rstd_rep", (128, TC))
    rs2_rep = sbP("rs2_rep", (128, TC))
    pg_f8 = [sbP(f"pg_f8{i}", (128, 1024), F8) for i in range(2)]

    c_sel = sbM("c_sel", (32, 32), BF16)
    xbp = sbM("xbp", (128, NG * (TC + 3)), BF16)
    zg_bf = sbM("zg_bf", (128, NG * TC), BF16)
    rny2 = sbM("rny2", (128, NG * TC), BF16)
    rn_sep = sbM("rn_sep", (128, NDT * TC), BF16)
    halo = sbM("halo", (128, NG * 3), BF16)
    dbc_sb = sbM("dbc_sb", (96, TC))
    bc_bf = sbM("bc_bf", (32, TC), BF16)
    drbf = sbM("drbf", (RK, TC), BF16)
    brep = sbM("brep", (128, NST * TC), BF16)
    crep = sbM("crep", (128, NST * TC), BF16)
    t_delta = [sbM(f"t_delta{i}", (128, TC)) for i in range(2)]
    t_sp = sbM("t_sp", (128, TC))
    t_u = [sbM(f"t_u{i}", (128, TC), BF16) for i in range(2)]
    t_dA = [sbM(f"t_dA{i}", (128, NGRP * TC)) for i in range(2)]
    t_mi = [sbM(f"t_mi{i}", (128, NGRP * TC), BF16) for i in range(2)]
    t_v = [sbM(f"t_v{i}", (128, NGRP * TC), BF16) for i in range(2)]
    t_h = [sbM(f"t_h{i}", (128, NGRP * TC), BF16) for i in range(2)]
    t_yAB = sbM("t_yAB", (128, 2 * TC), BF16)
    t_hC2 = sbM("t_hC2", (128, 2 * TC), BF16)
    t_xc = [sbM(f"t_xc{i}", (128, TC)) for i in range(2)]
    st_v = sbM("st_v", (128, NG * NST))
    st_h = sbM("st_h", (128, NG * NST))

    pb = [stP.enter_context(nc.psum_tensor(f"pb{i}", [128, 512], F32)) for i in range(8)]

    beta_b = c_const[:, 4:5].broadcast_to((128, TC))
    rn_bf = rn_sep

    def dma(dst, src, q="dms", reads=(), writes=()):
        eng = "sync" if q == "dms" else "gpsimd"
        sch.op(eng, lambda e, d=dst, s=src: e.dma_start(out=d, in_=s),
               reads=reads, writes=writes, dma=q)

    dma(c_ones[:], dram["ones_col"], writes=["c_ones"])
    dma(c_onesr[:], dram["ones_row"], writes=["c_onesr"])
    dma(c_sel[:], dram["sel32"], writes=["c_sel"])
    dma(c_const[:], dram["consts"], writes=["c_const"])
    dma(x2_st[0:1, 0:NF], dram["b1r"], writes=["x2_st"])
    sch.op("vector", lambda e: e.tensor_copy(r_b1q[:], x2_st[0:1, 0:NF]),
           reads=["x2_st"], writes=["r_b1q"])
    dma(x2_st[0:1, 0:D], dram["b2r"], writes=["x2_st"])
    sch.op("vector", lambda e: e.tensor_copy(r_b2q[:], x2_st[0:1, 0:D]),
           reads=["x2_st"], writes=["r_b2q"])
    dma(x2_st[0:1, 0:TC], dram["onesr"], writes=["x2_st"])
    sch.op("vector", lambda e: e.tensor_copy(r_ones8[:], x2_st[0:1, 0:TC]),
           reads=["x2_st"], writes=["r_ones8"])
    for l in range(NL):
        dma(x2_st[0:RK, 0:E], dram[f"dtwT{l}"], writes=["x2_st"])
        sch.op("vector", lambda e, l=l: e.tensor_copy(
            r_dtw[:, l * E:(l + 1) * E], x2_st[0:RK, 0:E]),
            reads=["x2_st"], writes=["r_dtw"])
        dma(x2_st[:, 0:NG * 96], dram[f"xpw_a{l}"], writes=["x2_st"])
        sch.op("vector", lambda e, l=l: e.tensor_copy(
            r_xpw[:, l * NG * 96:(l + 1) * NG * 96], x2_st[:, 0:NG * 96]),
            reads=["x2_st"], writes=["r_xpw"])
        dma(x2_st[:, 0:NG * NST], dram[f"alog_a{l}"], writes=["x2_st"])
        sch.op("scalar", lambda e: e.activation(a_st[:, 0:NG * NST], x2_st[:, 0:NG * NST], AF.Exp),
               reads=["x2_st"], writes=["a_st"])
        sch.op("vector", lambda e, l=l: e.tensor_single_scalar(
            r_wneg[:, l * NG * NST:(l + 1) * NG * NST], a_st[:, 0:NG * NST], -1.0, OP.mult),
            reads=["a_st"], writes=["r_wneg"])
        for nm, buf in (("dtb_a", r_dtb), ("cb_a", r_cb), ("dp_a", r_dp)):
            dma(buf[:, l * NG:(l + 1) * NG], dram[f"{nm}{l}"], writes=[f"r_{nm}"])
        dma(r_cw[:, l * NG * 4:(l + 1) * NG * 4], dram[f"cw_a{l}"], writes=["r_cw"])
        dma(r_nw[:, l * NDT:(l + 1) * NDT], dram[f"nw_a{l}"], writes=["r_nw"])

    prep_i = [0]

    def prep_cast(src_ap, dst_ap, rows, cols, outdt, skey):
        for p in range(rows // 128):
            rs = p * 128
            for cs in range(0, cols, 1024):
                cw = min(1024, cols - cs)
                i = prep_i[0] % 4
                j = prep_i[0] % 4
                prep_i[0] += 1
                pin = x2_st[:, i * 1024:i * 1024 + cw]
                kin = f"x2h{i}"
                dma(pin, src_ap[rs:rs + 128, cs:cs + cw], writes=[kin])
                if outdt == BF16:
                    pout, kout = wS[j][:, 0:cw], f"wS{j}"
                else:
                    pout, kout = pg_f8[i % 2][:, 0:cw], f"pg_f8{i % 2}"
                sch.op("gpsimd", lambda e, pout=pout, pin=pin: e.tensor_copy(pout, pin),
                       reads=[kin], writes=[kout])
                dma(dst_ap[rs:rs + 128, cs:cs + cw], pout, q="dmg",
                    reads=[kout], writes=[skey])


    def chunk_view(t, c):
        return dram[t][:, c * TC:(c + 1) * TC].rearrange("(g p) t -> p g t", p=128)

    def stg3(buf):
        return buf[:].rearrange("p (g t) -> p g t", t=TC)

    abuf = [a_st, x2_st]

    def ln_chunk(c):
        par = 0
        A = abuf[par]
        ak = f"ab{par}"
        dma(stg3(A), chunk_view("x_t", c), writes=[ak])
        for dt in range(NDT):
            i = dt % 2
            sch.op("scalar", lambda e, dt=dt, A=A, i=i: e.square(
                t_xc[i][:], A[:, dt * TC:(dt + 1) * TC]),
                reads=[ak], writes=[f"xc{i}"])
            sch.op("tensor", lambda e, dt=dt, i=i: e.matmul(
                pb[0][0:1, :], c_ones[:], A[:, dt * TC:(dt + 1) * TC],
                start=(dt == 0), stop=(dt == NDT - 1)),
                reads=[ak, "c_ones"], writes=["pb0"])
            sch.op("tensor", lambda e, dt=dt, i=i: e.matmul(
                pb[1][0:1, :], c_ones[:], t_xc[i][:],
                start=(dt == 0), stop=(dt == NDT - 1)),
                reads=[f"xc{i}", "c_ones"], writes=["pb1"])
        sch.op("vector", lambda e: e.tensor_single_scalar(row_m, pb[0][0:1, :], 1.0 / D, OP.mult),
               reads=["pb0"], writes=["row_m"])
        sch.op("vector", lambda e: e.tensor_single_scalar(row_e2, pb[1][0:1, :], 1.0 / D, OP.mult),
               reads=["pb1"], writes=["row_e2"])
        sch.op("vector", lambda e: e.tensor_mul(row_rstd, row_m, row_m),
               reads=["row_m"], writes=["row_rstd"])
        sch.op("vector", lambda e: e.tensor_sub(row_e2, row_e2, row_rstd),
               reads=["row_rstd", "row_e2"], writes=["row_e2"])
        sch.op("scalar", lambda e: e.activation(row_rstd, row_e2, AF.Ln, bias=1e-5),
               reads=["row_e2"], writes=["row_rstd"])
        sch.op("scalar", lambda e: e.activation(row_rstd, row_rstd, AF.Exp, scale=-0.5),
               reads=["row_rstd"], writes=["row_rstd"])
        sch.op("vector", lambda e: e.tensor_mul(row_rs2, row_m, row_rstd),
               reads=["row_m", "row_rstd", "row_e2"], writes=["row_rs2"])
        sch.op("tensor", lambda e: e.matmul(pb[2][:], c_onesr[:], row_rstd,
                                            start=True, stop=True),
               reads=["row_rstd", "c_onesr"], writes=["pb2"])
        sch.op("tensor", lambda e: e.matmul(pb[3][:], c_onesr[:], row_rs2,
                                            start=True, stop=True),
               reads=["row_rs2", "c_onesr"], writes=["pb3"])
        sch.op("vector", lambda e: e.tensor_copy(rstd_rep[:], pb[2][:]),
               reads=["pb2"], writes=["rstd_rep"])
        sch.op("vector", lambda e: e.tensor_copy(rs2_rep[:], pb[3][:]),
               reads=["pb3"], writes=["rs2_rep"])
        rb = rstd_rep[:].unsqueeze(1).broadcast_to((128, NDT, TC))
        s2 = rs2_rep[:].unsqueeze(1).broadcast_to((128, NDT, TC))
        av = A[:].rearrange("p (g t) -> p g t", g=NDT)
        sch.op("vector", lambda e, av=av, rb=rb: e.tensor_tensor(av, av, rb, OP.mult),
               reads=["rstd_rep", ak], writes=[ak])
        sch.op("vector", lambda e, av=av, s2=s2: e.tensor_tensor(av, av, s2, OP.subtract),
               reads=["rs2_rep", ak], writes=[ak])
        dma(chunk_view("s_a", c), stg3(A), q="dmg", reads=[ak], writes=[f"s_a{c}"])
        dma(chunk_view("s_xn", c), stg3(A), q="dmg", reads=[ak], writes=["s_xn"])
        if DBG:
            dma(dbg_xn[:, c * TC:(c + 1) * TC].rearrange("(g p) t -> p g t", p=128),
                stg3(A), q="dmg", reads=[ak], writes=["dbg"])

    ln_chunk(0)
    ln_chunk(1)
    ln_chunk(2)
    ln_chunk(3)

    def head_ops(l, c):
        par = c % 2
        A = abuf[par]
        ak = f"ab{par}"
        th = []

        def T(engine, fn, reads=(), writes=(), dma_q=None, ms=0):
            th.append((engine, fn, list(reads), list(writes), dma_q, ms))

        if c == 0:
            T("sync", lambda e: e.dma_start(out=halo[:], in_=dram["zeros48"]),
              [], ["halo"], "dms")
        T("sync", lambda e, A=A, c=c: e.dma_start(out=stg3(A), in_=chunk_view("s_a", c)),
          [f"s_a{c}", "x2h0", "x2h1", "x2h2", "x2h3"] if par == 1 else [f"s_a{c}"], [ak], "dms")
        for dt in range(NDT):
            i = dt % 2
            T("scalar", lambda e, dt=dt, A=A, i=i: e.square(
                t_xc[i][:], A[:, dt * TC:(dt + 1) * TC]), [ak], [f"xc{i}"])
            T("tensor", lambda e, dt=dt, i=i: e.matmul(
                pb[6][0:1, :], c_ones[:], t_xc[i][:],
                start=(dt == 0), stop=(dt == NDT - 1)),
                [f"xc{i}", "c_ones"], ["pb6"])
        T("scalar", lambda e: e.activation(row_rstd, pb[6][0:1, :], AF.Ln,
                                           scale=1.0 / D, bias=1e-5),
          ["pb6"], ["row_rstd"])
        T("scalar", lambda e: e.activation(row_rstd, row_rstd, AF.Exp, scale=-0.5),
          ["row_rstd"], ["row_rstd"])
        T("tensor", lambda e: e.matmul(pb[6][:], c_onesr[:], row_rstd,
                                       start=True, stop=True),
          ["row_rstd", "c_onesr"], ["pb6"])
        T("vector", lambda e: e.tensor_copy(rstd_rep[:], pb[6][:]),
          ["pb6"], ["rstd_rep"])
        for dt in range(NDT):
            T("vector", lambda e, dt=dt, l=l, A=A: e.scalar_tensor_tensor(
                rn_bf[:, dt * TC:(dt + 1) * TC], A[:, dt * TC:(dt + 1) * TC],
                r_nw[:, l * NDT + dt:l * NDT + dt + 1], rstd_rep[:], OP.mult, OP.mult),
                [ak, "rstd_rep", "r_nw"], ["rn_bf"])
        xbv3 = xbp[:].rearrange("p (g t) -> p g t", g=NG)
        T("vector", lambda e, xbv3=xbv3: e.tensor_copy(
            xbv3[:, :, 0:3], halo[:].rearrange("p (g t) -> p g t", g=NG)),
            ["halo"], ["xbp_pre"])
        for f0, fn_ in [(0, 6), (6, 6), (12, 4)]:
            for half in range(2):
                gate99 = (f0 == 12)
                def wsrc(d, half=half, f0=f0, fn_=fn_, l=l):
                    return dram[f"in_wT{l}"][d * 128:(d + 1) * 128,
                                             (half * 16 + f0) * 128:(half * 16 + f0 + fn_) * 128]
                for d in range(4):
                    wi = d % 4
                    T("gpsimd", lambda e, d=d, wi=wi, wsrc=wsrc, fn_=fn_: e.dma_start(
                        out=wS[wi][:, 0:fn_ * 128], in_=wsrc(d)),
                        [], [f"wS{wi}"], "dmg", ms=(99 if gate99 else 0))
                for d in range(NDT):
                    wi = d % 4
                    for fi in range(fn_):
                        T("tensor", lambda e, d=d, fi=fi, wi=wi: e.matmul(
                            pb[fi][:], wS[wi][:, fi * 128:(fi + 1) * 128],
                            rn_bf[:, d * TC:(d + 1) * TC],
                            start=(d == 0), stop=(d == NDT - 1)),
                            [f"wS{wi}", "rn_bf"], [f"pb{fi}"])
                    if d + 4 < NDT:
                        wi2 = (d + 4) % 4
                        T("gpsimd", lambda e, d=d, wi2=wi2, wsrc=wsrc, fn_=fn_: e.dma_start(
                            out=wS[wi2][:, 0:fn_ * 128], in_=wsrc(d + 4)),
                            [], [f"wS{wi2}"], "dmg")
                for fi in range(fn_):
                    g = f0 + fi
                    if half == 0:
                        T("scalar", lambda e, g=g, fi=fi: e.copy(
                            xbp[:, g * (TC + 3) + 3:(g + 1) * (TC + 3)], pb[fi][:]),
                            [f"pb{fi}"], [f"xbp{g}"], ms=g)
                    else:
                        T("scalar", lambda e, g=g, fi=fi: e.activation(
                            zg_bf[:, g * TC:(g + 1) * TC], pb[fi][:], AF.Silu),
                            [f"pb{fi}"], [f"zg{g}"], ms=g)
        T("vector", lambda e, xbv3=xbv3: e.tensor_copy(
            halo[:].rearrange("p (g t) -> p g t", g=NG), xbv3[:, :, TC:TC + 3]),
            [f"xbp{g}" for g in range(NG)] + ["xbp_pre"], ["halo"])
        for g in range(NG):
            xg = xbp[:, g * (TC + 3):(g + 1) * (TC + 3)]
            cwb = l * NG * 4 + g * 4
            T("vector", lambda e, xg=xg, cwb=cwb: e.tensor_scalar(
                t_xc[0][:], xg[:, 0:TC], r_cw[:, cwb:cwb + 1], None, OP.mult),
                [f"xbp{g}", "xbp_pre", "halo", "r_cw"], ["xc0"])
            for k in (1, 2, 3):
                si, so = (0, 1) if k % 2 == 1 else (1, 0)
                T("vector", lambda e, xg=xg, cwb=cwb, k=k, si=si, so=so:
                  e.scalar_tensor_tensor(
                      t_xc[so][:], xg[:, k:TC + k], r_cw[:, cwb + k:cwb + k + 1],
                      t_xc[si][:], OP.mult, OP.add),
                  [f"xc{si}", f"xbp{g}", "r_cw"], [f"xc{so}"])
            T("scalar", lambda e, xg=xg, g=g, l=l: e.activation(
                xg[:, 3:TC + 3], t_xc[1][:], AF.Silu,
                bias=r_cb[:, l * NG + g:l * NG + g + 1]),
                ["xc1", "r_cb"], [f"xbp{g}"], ms=g)
            T("tensor", lambda e, g=g, l=l: e.matmul(
                pb[6][0:96, :], r_xpw[:, (l * NG + g) * 96:(l * NG + g + 1) * 96],
                xbp[:, g * (TC + 3) + 3:(g + 1) * (TC + 3)],
                start=(g == 0), stop=(g == NG - 1)),
                [f"xbp{g}", "r_xpw"], ["pb6"], ms=g)
        T("vector", lambda e: e.tensor_copy(dbc_sb[:], pb[6][0:96, :]),
          ["pb6"], ["dbc_sb"])
        T("vector", lambda e: e.tensor_copy(drbf[:], dbc_sb[0:RK, :]),
          ["dbc_sb"], ["drbf"])
        T("gpsimd", lambda e: e.dma_start(out=bc_bf[:], in_=dbc_sb[RK:96, :]),
          ["dbc_sb"], ["bc_bf"], "dmg")
        for jj in range(32):
            nn, isC = jj // 2, jj % 2
            j = nn + 16 * isC
            bank = 6 + (jj % 2)
            T("tensor", lambda e, j=j, bank=bank: e.matmul(
                pb[bank][:], c_sel[:, j:j + 1].broadcast_to((32, 128)), bc_bf[:],
                start=True, stop=True),
                ["bc_bf", "c_sel"], [f"pb{bank}"])
            dstb, dkey = (brep, f"brep{nn}") if not isC else (crep, f"crep{nn}")
            T("scalar", lambda e, nn=nn, bank=bank, dstb=dstb: e.copy(
                dstb[:, nn * TC:(nn + 1) * TC], pb[bank][:]),
                [f"pb{bank}"], [dkey])
        return th

    def emit_thunks(th):
        for engine, fn, reads, writes, dma_q, ms in th:
            sch.op(engine, fn, reads=reads, writes=writes, dma=dma_q)

    def emit_gated(th, pos, g):
        while pos < len(th) and th[pos][5] <= g:
            engine, fn, reads, writes, dma_q, ms = th[pos]
            sch.op(engine, fn, reads=reads, writes=writes, dma=dma_q)
            pos += 1
        return pos

    seq = [(l, c) for l in range(NL) for c in range(NCH)]
    pend = head_ops(0, 0)
    emit_thunks(pend)

    for idx, (l, c) in enumerate(seq):
        par = c % 2
        A = abuf[par]
        ak = f"ab{par}"
        nxt = head_ops(*seq[idx + 1]) if idx + 1 < len(seq) else []
        if idx + 1 < len(seq) and seq[idx + 1][1] == 0:
            pass
        nxt_pos = 0
        def dt_chain(g):
            sl = g % 2
            gl = l * NG + g
            sch.op("tensor", lambda e, g=g, l=l: e.matmul(
                pb[7][:], r_dtw[:, l * E + g * 128:l * E + (g + 1) * 128], drbf[:],
                start=True, stop=True),
                reads=["drbf", "r_dtw"], writes=["pb7"])
            sch.op("scalar", lambda e, gl=gl: e.activation(
                t_sp[:], pb[7][:], AF.Abs, bias=r_dtb[:, gl:gl + 1]),
                reads=["pb7", "r_dtb"], writes=["tsp"])
            sch.op("scalar", lambda e: e.activation(t_sp[:], t_sp[:], AF.Exp, scale=-1.0),
                   reads=["tsp"], writes=["tsp"])
            sch.op("scalar", lambda e: e.activation(t_sp[:], t_sp[:], AF.Ln, bias=1.0),
                   reads=["tsp"], writes=["tsp"])
            sch.op("scalar", lambda e, gl=gl, sl=sl: e.activation(
                t_delta[sl][:], pb[7][:], AF.Relu, bias=r_dtb[:, gl:gl + 1]),
                reads=["pb7", "r_dtb"], writes=[f"tdelta{sl}"])
            sch.op("vector", lambda e, sl=sl: e.tensor_add(
                t_delta[sl][:], t_delta[sl][:], t_sp[:]),
                reads=["tsp", f"tdelta{sl}"], writes=[f"tdelta{sl}"])
            sch.op("gpsimd", lambda e, g=g, sl=sl: e.tensor_tensor(
                t_u[sl][:], t_delta[sl][:], xbp[:, g * (TC + 3) + 3:(g + 1) * (TC + 3)],
                OP.mult),
                reads=[f"tdelta{sl}", f"xbp{g}"], writes=[f"tu{sl}"])

        dt_chain(0)
        for g in range(NG):
            sl = g % 2
            gl = l * NG + g
            if g + 1 < NG:
                dt_chain(g + 1)
            for ng in range(NST // NGRP):
                bsl = ng % 2
                n0 = ng * NGRP
                for k in range(NGRP):
                    n = ng * NGRP + k
                    sch.op("scalar", lambda e, n=n, k=k, gl=gl, sl=sl, bsl=bsl: e.activation(
                        t_dA[bsl][:, k * TC:(k + 1) * TC], t_delta[sl][:], AF.Exp,
                        scale=r_wneg[:, gl * NST + n:gl * NST + n + 1]),
                        reads=[f"tdelta{sl}", "r_wneg"], writes=[f"tdA{bsl}"])
                sch.op("gpsimd", lambda e, bsl=bsl, sl=sl, n0=n0: e.tensor_tensor(
                    t_mi[bsl][:].rearrange("p (k t) -> p k t", k=NGRP),
                    t_u[sl][:].unsqueeze(1).broadcast_to((128, NGRP, TC)),
                    crep[:, n0 * TC:(n0 + 2) * TC].rearrange("p (k t) -> p k t", k=NGRP)
                    if False else
                    brep[:, n0 * TC:(n0 + 2) * TC].rearrange("p (k t) -> p k t", k=NGRP),
                    OP.mult),
                    reads=[f"tu{sl}", f"brep{n0}", f"brep{n0 + 1}"], writes=[f"tmi{bsl}"])
                for k in range(NGRP):
                    n = ng * NGRP + k
                    stc = g * NST + n
                    iv = 0.0 if c == 0 else st_v[:, stc:stc + 1]
                    ih = 0.0 if c == 0 else st_h[:, stc:stc + 1]
                    sch.op("vector", lambda e, k=k, bsl=bsl, iv=iv: e.tensor_tensor_scan(
                        t_v[bsl][:, k * TC:(k + 1) * TC], beta_b,
                        t_mi[bsl][:, k * TC:(k + 1) * TC], iv, OP.mult, OP.add),
                        reads=[f"tmi{bsl}", "st_v", "c_const"], writes=[f"tv{bsl}"])
                    sch.op("vector", lambda e, k=k, bsl=bsl, ih=ih: e.tensor_tensor_scan(
                        t_h[bsl][:, k * TC:(k + 1) * TC],
                        t_dA[bsl][:, k * TC:(k + 1) * TC],
                        t_v[bsl][:, k * TC:(k + 1) * TC], ih, OP.mult, OP.add),
                        reads=[f"tdA{bsl}", f"tv{bsl}", "st_h"], writes=[f"th{bsl}"])
                if c < NCH - 1:
                    st0 = g * NST + ng * NGRP
                    sch.op("vector", lambda e, bsl=bsl, st0=st0: e.tensor_copy(
                        st_v[:, st0:st0 + NGRP],
                        t_v[bsl][:, TC - 1:NGRP * TC:TC]),
                        reads=[f"tv{bsl}"], writes=["st_v"])
                    sch.op("vector", lambda e, bsl=bsl, st0=st0: e.tensor_copy(
                        st_h[:, st0:st0 + NGRP],
                        t_h[bsl][:, TC - 1:NGRP * TC:TC]),
                        reads=[f"th{bsl}"], writes=["st_h"])
                n0 = ng * NGRP
                if ng == 0:
                    sch.op("vector", lambda e, bsl=bsl, n0=n0: e.tensor_tensor(
                        t_yAB[:], t_h[bsl][:],
                        crep[:, n0 * TC:(n0 + 2) * TC], OP.mult),
                        reads=[f"th{bsl}", f"crep{n0}", f"crep{n0 + 1}"],
                        writes=["tyA", "tyB"])
                else:
                    sch.op("vector", lambda e, bsl=bsl, n0=n0: e.tensor_tensor(
                        t_hC2[:], t_h[bsl][:],
                        crep[:, n0 * TC:(n0 + 2) * TC], OP.mult),
                        reads=[f"th{bsl}", f"crep{n0}", f"crep{n0 + 1}"],
                        writes=["thC"])
                    sch.op("vector", lambda e: e.tensor_add(
                        t_yAB[:], t_yAB[:], t_hC2[:]),
                        reads=["thC", "tyA", "tyB"], writes=["tyA", "tyB"])
            sch.op("vector", lambda e, g=g, gl=gl: e.scalar_tensor_tensor(
                t_xc[0][:], xbp[:, g * (TC + 3) + 3:(g + 1) * (TC + 3)],
                r_dp[:, gl:gl + 1], t_yAB[:, 0:TC], OP.mult, OP.add),
                reads=[f"xbp{g}", "tyA", "r_dp"], writes=["xc0"])
            sch.op("vector", lambda e: e.tensor_add(t_xc[0][:], t_xc[0][:], t_yAB[:, TC:2 * TC]),
                   reads=["tyB", "xc0"], writes=["xc0"])
            sch.op("vector", lambda e, g=g: e.tensor_tensor(
                rny2[:, g * TC:(g + 1) * TC], t_xc[0][:],
                zg_bf[:, g * TC:(g + 1) * TC], OP.mult),
                reads=["xc0", f"zg{g}"], writes=[f"y2{g}"])
            if nxt and PIPE:
                nxt_pos = emit_gated(nxt, nxt_pos, g)
        for g in range(NG):
            wbuf, key = wS[g % 4], f"wS{g % 4}"
            dma(wbuf[:, 0:1024], dram[f"owT{l}"][g * 128:(g + 1) * 128, :],
                q="dmg", reads=[], writes=[key])
            for d in range(NDT):
                sch.op("tensor", lambda e, g=g, d=d, wbuf=wbuf: e.matmul(
                    pb[d][:], wbuf[:, d * 128:(d + 1) * 128],
                    rny2[:, g * TC:(g + 1) * TC],
                    start=(g == 0), stop=(g == NG - 1)),
                    reads=[key, f"y2{g}"], writes=[f"pb{d}"])
        for d in range(NDT):
            sch.op("vector", lambda e, d=d, A=A: e.tensor_add(
                A[:, d * TC:(d + 1) * TC], A[:, d * TC:(d + 1) * TC], pb[d][:]),
                reads=[f"pb{d}"], writes=[ak])
        dma(chunk_view("s_a", c), stg3(A), q="dmg", reads=[ak], writes=[f"s_a{c}"])
        if DBG:
            dma(dbg_a[l][:, c * TC:(c + 1) * TC].rearrange("(g p) t -> p g t", p=128),
                stg3(A), q="dmg", reads=[ak], writes=["dbg"])
        while nxt_pos < len(nxt):
            e_, f_, r_, w_, d_, m_ = nxt[nxt_pos]
            sch.op(e_, f_, reads=r_, writes=w_, dma=d_)
            nxt_pos += 1

    sch.barrier()
    stM.close()
    stF = ExitStack()

    def sbF(name, shape, dt=F32):
        return stF.enter_context(nc.sbuf_tensor(name, list(shape), dt))

    f_w1 = sbF("f_w1", (128, 8 * 4096), F8)
    f_w2 = sbF("f_w2", (128, 32 * 1024), F8)
    f_h8_ = [sbF(f"f_h8_{i}", (128, 32 * TC), F8) for i in range(2)]
    f_hb_ = [sbF(f"f_hb{i}", (128, TC), BF16) for i in range(2)]
    f_fb = sbF("f_fb", (128, NDT * TC), BF16)
    f_msk = sbF("f_msk", (128, NDT * TC), BF16)
    f_q8full = sbF("f_q8full", (128, NDT * L), F8)

    for d in range(NDT):
        dma(f_w1[:, d * 4096:(d + 1) * 4096], dram["W1T"][d * 128:(d + 1) * 128, :],
            q="dmg", reads=[], writes=["f_w1"])
    for f in range(32):
        dma(f_w2[:, f * 1024:(f + 1) * 1024], dram["W2T"][f * 128:(f + 1) * 128, :],
            q="dmg", reads=[], writes=["f_w2"])
    for c in range(NCH):
        dma(stg3(x2_st), chunk_view("s_xn", c),
            reads=["s_xn", "x2h0", "x2h1", "x2h2", "x2h3"] if c == 0 else ["s_xn"],
            writes=["ab1"])
        sch.op("vector", lambda e, c=c: e.tensor_copy(
            f_q8full[:].rearrange("p (g t) -> p g t", t=L)[:, :, c * TC:(c + 1) * TC],
            stg3(x2_st)),
            reads=["ab1"], writes=[f"fq8c{c}"])
    for c in range(NCH):
        fpar = c % 2
        f_h8 = f_h8_[fpar]
        kh8 = f"f_h8{fpar}"
        dma(stg3(f_msk), chunk_view("mask_t", c), writes=["f_msk"])
        dma(stg3(a_st), chunk_view("s_a", c), reads=[f"s_a{c}"], writes=["ab0"])
        dma(stg3(x2_st), chunk_view("x_t", c), writes=["ab1"])
        for fg in range(8):
            for f4i in range(4):
                f = fg * 4 + f4i
                bk = 4 + f4i
                for d in range(NDT):
                    sch.op("tensor", lambda e, f=f, d=d, bk=bk, c=c: e.matmul(
                        pb[bk][:], f_w1[:, d * 4096 + f * 128:d * 4096 + (f + 1) * 128],
                        f_q8full[:, d * L + c * TC:d * L + (c + 1) * TC],
                        start=(d == 0), stop=False),
                        reads=["f_w1", f"fq8c{c}"], writes=[f"pb{bk}"])
                sch.op("tensor", lambda e, f=f, bk=bk: e.matmul(
                    pb[bk][:], r_b1q[0:1, f * 128:(f + 1) * 128], r_ones8[:],
                    start=False, stop=True),
                    reads=["r_b1q", "r_ones8"], writes=[f"pb{bk}"])
            for f4i in range(4):
                f = fg * 4 + f4i
                bk = 4 + f4i
                hb = f_hb_[f4i % 2]
                hbk = f"f_hb{f4i % 2}"
                sch.op("scalar", lambda e, bk=bk, hb=hb: e.copy(hb[:], pb[bk][:]),
                       reads=[f"pb{bk}"], writes=[hbk])
                sch.op("vector", lambda e, hb=hb: e.tensor_relu(hb[:], hb[:]),
                       reads=[hbk], writes=[hbk])
                sch.op("vector", lambda e, f=f, f_h8=f_h8, hb=hb: e.tensor_copy(
                    f_h8[:, f * TC:(f + 1) * TC], hb[:]),
                    reads=[hbk], writes=[kh8])
        for d in range(NDT):
            bk = d % 4
            for f in range(32):
                sch.op("tensor", lambda e, d=d, f=f, bk=bk, f_h8=f_h8: e.matmul(
                    pb[bk][:], f_w2[:, f * 1024 + d * 128:f * 1024 + (d + 1) * 128],
                    f_h8[:, f * TC:(f + 1) * TC],
                    start=(f == 0), stop=False),
                    reads=["f_w2", kh8], writes=[f"pb{bk}"])
            sch.op("tensor", lambda e, d=d, bk=bk: e.matmul(
                pb[bk][:], r_b2q[0:1, d * 128:(d + 1) * 128], r_ones8[:],
                start=False, stop=True),
                reads=["r_b2q", "r_ones8"], writes=[f"pb{bk}"])
            sch.op("scalar", lambda e, d=d, bk=bk: e.copy(
                f_fb[:, d * TC:(d + 1) * TC], pb[bk][:]),
                reads=[f"pb{bk}"], writes=["f_fb"])
        sch.op("vector", lambda e: e.tensor_add(a_st[:], a_st[:], x2_st[:]),
               reads=["ab1", "ab0"], writes=["ab0"])
        sch.op("vector", lambda e: e.tensor_mul(x2_st[:], f_fb[:], f_msk[:]),
               reads=["f_fb", "f_msk", "ab1"], writes=["ab1"])
        sch.op("vector", lambda e: e.tensor_add(a_st[:], a_st[:], x2_st[:]),
               reads=["ab1", "ab0"], writes=["ab0"])
        dma(out_t[:, c * TC:(c + 1) * TC].rearrange("(g p) t -> p g t", p=128),
            stg3(a_st), reads=["ab0"], writes=["out"])

    with nc.Block() as block:
        sch.emit(nc, block)
    stF.close()
    stP.close()
    return nc


_PROG = None


def _get_prog():
    global _PROG
    if _PROG is None:
        _PROG = build_program()
    return _PROG


def kernel(L0_attn_x, L0_attn_norm_w, L0_attn_in_proj_w, L0_attn_conv_w,
           L0_attn_conv_b, L0_attn_x_proj_w, L0_attn_dt_w, L0_attn_dt_b,
           L0_attn_A_log, L0_attn_Dp, L0_attn_out_w, L0_ffn_W1, L0_ffn_b1,
           L0_ffn_W2, L0_ffn_b2):
    import jax
    x = np.ascontiguousarray(np.asarray(L0_attn_x, np.float32))
    B = x.shape[0]
    with jax.default_device(jax.devices("cpu")[0]):
        jax.config.update("jax_threefry_partitionable", True)
        keep = np.asarray(jax.random.bernoulli(
            jax.random.key(42, impl="threefry2x32"), 0.8, (8, 2048, 1024)))
    mask = (keep.astype(np.float32) * np.float32(1.25)).astype(BF_NP)

    def arr(a):
        return np.ascontiguousarray(np.asarray(a, np.float32))

    common = {}
    for l in range(NL):
        common[f"in_wT{l}"] = arr(np.asarray(L0_attn_in_proj_w)[l].T)
        common[f"owT{l}"] = arr(np.asarray(L0_attn_out_w)[l].T)
        xpw = arr(np.asarray(L0_attn_x_proj_w)[l].T)
        common[f"xpw_a{l}"] = np.ascontiguousarray(
            xpw.reshape(NG, 128, 96).transpose(1, 0, 2).reshape(128, NG * 96))
        common[f"dtwT{l}"] = arr(np.asarray(L0_attn_dt_w)[l].T)
        al = arr(np.asarray(L0_attn_A_log)[l])
        common[f"alog_a{l}"] = np.ascontiguousarray(
            al.reshape(NG, 128, NST).transpose(1, 0, 2).reshape(128, NG * NST))
        for nm, src in (("dtb_a", L0_attn_dt_b), ("cb_a", L0_attn_conv_b),
                        ("dp_a", L0_attn_Dp)):
            v = arr(np.asarray(src)[l])
            common[f"{nm}{l}"] = np.ascontiguousarray(v.reshape(NG, 128).T)
        cw = arr(np.asarray(L0_attn_conv_w)[l])
        common[f"cw_a{l}"] = np.ascontiguousarray(
            cw.reshape(NG, 128, 4).transpose(1, 0, 2).reshape(128, NG * 4))
        nw = arr(np.asarray(L0_attn_norm_w)[l])
        common[f"nw_a{l}"] = np.ascontiguousarray(nw.reshape(NDT, 128).T)
    common["W1T"] = arr(np.asarray(L0_ffn_W1).T)
    common["W2T"] = arr(np.asarray(L0_ffn_W2).T)
    common["b1r"] = arr(L0_ffn_b1).reshape(1, NF)
    common["b2r"] = arr(L0_ffn_b2).reshape(1, D)
    common["onesr"] = np.ones((1, TC), np.float32)
    common["ones_col"] = np.ones((128, 1), np.float32)
    common["ones_row"] = np.ones((1, 128), np.float32)
    common["sel32"] = np.eye(32, dtype=np.float32).astype(BF_NP)
    consts = np.zeros((128, 8), np.float32)
    consts[:, 0] = -1.0
    consts[:, 1] = 1.0
    consts[:, 2] = -0.5
    consts[:, 3] = 1e-5
    consts[:, 4] = BETA
    consts[:, 5] = 1.0 / D
    common["consts"] = consts
    common["zeros48"] = np.zeros((128, 48), BF_NP)

    in_maps = []
    for b in range(B):
        m = dict(common)
        m["x_t"] = np.ascontiguousarray(x[b].T)
        m["mask_t"] = np.ascontiguousarray(mask[b].T)
        in_maps.append(m)

    nc = _get_prog()
    res = run_bass_kernel_spmd(nc, in_maps, core_ids=list(range(B)))
    out = np.stack([np.asarray(r["out_t"]).T for r in res.results], axis=0)
    return out.astype(np.float32)


# version 24
# speedup vs baseline: 1.0742x; 1.0051x over previous
import numpy as np
import ml_dtypes
import concourse.bass as bass
import concourse.mybir as mybir
from concourse.bass_utils import run_bass_kernel_spmd

F32 = mybir.dt.float32
BF16 = mybir.dt.bfloat16
F8 = mybir.dt.float8e5
OP = mybir.AluOpType
AF = mybir.ActivationFunctionType

L, D, E, NST, RK = 2048, 1024, 2048, 16, 64
TC = 512
NCH = L // TC
NG = E // 128
NDT = D // 128
NF = 4096
NL = 2
BETA = 0.6
NGRP = 2
PIPE = True

BF_NP = ml_dtypes.bfloat16


class Sched:
    ENG = ("sync", "vector", "scalar", "tensor", "gpsimd")

    def __init__(self):
        self.ops = []
        self.bw = {}
        self.br = {}
        self.last_on = {}
        self.last_dma = {}

    def op(self, engine, fn, reads=(), writes=(), dma=None):
        opid = len(self.ops)
        deps = set()
        for b in reads:
            w = self.bw.get(b)
            if w is not None:
                deps.add(w)
        for b in writes:
            deps.update(self.br.get(b, ()))
            w = self.bw.get(b)
            if w is not None:
                deps.add(w)
        deps.discard(opid)
        self.ops.append({"e": engine, "fn": fn, "deps": deps, "dma": dma})
        for b in reads:
            self.br.setdefault(b, []).append(opid)
        for b in writes:
            self.bw[b] = opid
            self.br[b] = []
        self.last_on[engine] = opid
        if dma:
            self.last_dma[dma] = opid
        return opid

    def barrier(self):
        lasts = list(self.last_on.values()) + list(self.last_dma.values())
        for e in self.ENG:
            opid = len(self.ops)
            self.ops.append({"e": e, "fn": None,
                             "deps": set(d for d in lasts if d != opid),
                             "dma": None})
            self.last_on[e] = opid

    def emit(self, nc, block):
        marked = set()
        for i, rec in enumerate(self.ops):
            if rec["dma"]:
                marked.add(i)
            me = rec["e"]
            for d in rec["deps"]:
                if self.ops[d]["e"] != me or self.ops[d]["dma"]:
                    marked.add(d)

        def stream(rec):
            return rec["dma"] if rec["dma"] else rec["e"]

        R = 8
        counters, incno = {}, {}
        dma_slot = {}
        for i, rec in enumerate(self.ops):
            if i in marked:
                s = stream(rec)
                counters[s] = counters.get(s, 0) + 1
                incno[i] = counters[s]
                if rec["dma"]:
                    o = counters[s] - 1
                    dma_slot[i] = (s, o % R, o // R + 1)
        sems = {}
        for s in ("vector", "scalar", "tensor", "gpsimd", "sync"):
            sems[s] = nc.semaphore(f"sem_{s}").__enter__()
        qsems = {}
        for q in ("dms", "dmg"):
            qsems[q] = [nc.semaphore(f"sem_{q}{j}").__enter__() for j in range(R)]
        per_eng = {e: [] for e in self.ENG}
        for i, rec in enumerate(self.ops):
            per_eng[rec["e"]].append(i)

        def emit_engine(eng_obj, ename):
            water = {}
            for i in per_eng[ename]:
                rec = self.ops[i]
                byprod = {}
                for d in rec["deps"]:
                    drec = self.ops[d]
                    if drec["e"] == ename and drec["dma"] is None:
                        continue
                    if drec["dma"]:
                        q, j, u = dma_slot[d]
                        key = (q, j)
                        byprod[key] = max(byprod.get(key, 0), u)
                    else:
                        byprod[drec["e"]] = max(byprod.get(drec["e"], 0), incno[d])
                for ds, v in sorted(byprod.items(), key=str):
                    if water.get(ds, 0) >= v:
                        continue
                    sem = qsems[ds[0]][ds[1]] if isinstance(ds, tuple) else sems[ds]
                    mult = 16 if isinstance(ds, tuple) else 1
                    eng_obj.wait_ge(sem, v * mult)
                    water[ds] = v
                if rec["dma"]:
                    q, j, u = dma_slot[i]
                    if u > 1 and water.get((q, j), 0) < u - 1:
                        eng_obj.wait_ge(qsems[q][j], (u - 1) * 16)
                        water[(q, j)] = u - 1
                    ins = rec["fn"](eng_obj)
                    ins.then_inc(qsems[q][j], 16)
                else:
                    ins = eng_obj.nop() if rec["fn"] is None else rec["fn"](eng_obj)
                    if i in incno:
                        ins.then_inc(sems[rec["e"]], 1)

        @block.sync
        def _(sync):
            emit_engine(sync, "sync")
            for q in ("dms", "dmg"):
                tot = counters.get(q, 0)
                for j in range(R):
                    uses = tot // R + (1 if j < tot % R else 0)
                    if uses:
                        sync.wait_ge(qsems[q][j], uses * 16)

        @block.vector
        def _(v):
            emit_engine(v, "vector")

        @block.scalar
        def _(s):
            emit_engine(s, "scalar")

        @block.tensor
        def _(t):
            emit_engine(t, "tensor")

        @block.gpsimd
        def _(g):
            emit_engine(g, "gpsimd")


def build_program():
    nc = bass.Bass("TRN2", target_bir_lowering=False)
    dram = {}

    def din(name, shape, dt=F32):
        dram[name] = nc.dram_tensor(name, list(shape), dt, kind="ExternalInput").ap()
        return dram[name]

    def dscr(name, shape, dt):
        dram[name] = nc.dram_tensor(name, list(shape), dt, kind="Internal").ap()
        return dram[name]

    din("x_t", (D, L))
    din("mask_t", (D, L), BF16)
    for l in range(NL):
        din(f"in_wT{l}", (D, 2 * E))
        din(f"owT{l}", (E, D))
        din(f"xpw_a{l}", (128, NG * 96))
        din(f"dtwT{l}", (RK, E))
        din(f"alog_a{l}", (128, NG * NST))
        din(f"dtb_a{l}", (128, NG))
        din(f"cb_a{l}", (128, NG))
        din(f"dp_a{l}", (128, NG))
        din(f"cw_a{l}", (128, NG * 4))
        din(f"nw_a{l}", (128, NDT))
    din("W1T", (D, NF))
    din("W2T", (NF, D))
    din("b1r", (1, NF))
    din("b2r", (1, D))
    din("onesr", (1, TC))
    din("ones_col", (128, 1))
    din("ones_row", (1, 128))
    din("sel32", (32, 32), BF16)
    din("consts", (128, 8))
    din("zeros48", (128, 48), BF16)
    out_t = nc.dram_tensor("out_t", [D, L], F32, kind="ExternalOutput").ap()
    DBG = False
    if DBG:
        dbg_xn = nc.dram_tensor("dbg_xn", [D, L], F32, kind="ExternalOutput").ap()
        dbg_a = [nc.dram_tensor(f"dbg_a{l}", [D, L], F32, kind="ExternalOutput").ap()
                 for l in range(NL)]

    for l in range(NL):
        dscr(f"s_owT{l}", (E, D), BF16)
    dscr("s_a", (D, L), F32)
    dscr("s_xn", (D, L), F32)

    sch = Sched()
    from contextlib import ExitStack
    stP = ExitStack()
    stM = ExitStack()

    def sbP(name, shape, dt=F32):
        return stP.enter_context(nc.sbuf_tensor(name, list(shape), dt))

    def sbM(name, shape, dt=F32):
        return stM.enter_context(nc.sbuf_tensor(name, list(shape), dt))

    c_ones = sbP("c_ones", (128, 1))
    c_onesr = sbP("c_onesr", (1, 128))
    c_const = sbP("c_const", (128, 8))
    nc.const_aps.register(F32, -1.0, c_const[:, 0:1])
    nc.const_aps.register(F32, 1.0, c_const[:, 1:2])
    nc.const_aps.register(F32, -0.5, c_const[:, 2:3])
    nc.const_aps.register(F32, 1e-5, c_const[:, 3:4])
    nc.const_aps.register(F32, 1.0 / D, c_const[:, 5:6])
    r_dtw = sbP("r_dtw", (RK, NL * E), BF16)
    r_xpw = sbP("r_xpw", (128, NL * NG * 96), BF16)
    r_wneg = sbP("r_wneg", (128, NL * NG * NST))
    r_dtb = sbP("r_dtb", (128, NL * NG))
    r_cb = sbP("r_cb", (128, NL * NG))
    r_dp = sbP("r_dp", (128, NL * NG))
    r_cw = sbP("r_cw", (128, NL * NG * 4))
    r_nw = sbP("r_nw", (128, NL * NDT))
    r_b1q = sbP("r_b1q", (1, NF), F8)
    r_b2q = sbP("r_b2q", (1, D), F8)
    r_ones8 = sbP("r_ones8", (1, TC), F8)
    a_st = sbP("a_st", (128, NDT * TC))
    x2_st = sbP("x2_st", (128, NDT * TC))
    wS = [sbP(f"wS{i}", (128, 1024), BF16) for i in range(4)]
    row_m = sbP("row_m", (1, TC))[:]
    row_e2 = sbP("row_e2", (1, TC))[:]
    row_rstd = sbP("row_rstd", (1, TC))[:]
    row_rs2 = row_e2
    rstd_rep = sbP("rstd_rep", (128, TC))
    rs2_rep = sbP("rs2_rep", (128, TC))
    pg_f8 = [sbP(f"pg_f8{i}", (128, 1024), F8) for i in range(2)]

    c_sel = sbM("c_sel", (32, 32), BF16)
    xbp = sbM("xbp", (128, NG * (TC + 3)), BF16)
    zg_bf = sbM("zg_bf", (128, NG * TC), BF16)
    rny2 = sbM("rny2", (128, NG * TC), BF16)
    rn_sep = sbM("rn_sep", (128, NDT * TC), BF16)
    halo = sbM("halo", (128, NG * 3), BF16)
    dbc_sb = sbM("dbc_sb", (96, TC))
    bc_bf = sbM("bc_bf", (32, TC), BF16)
    drbf = sbM("drbf", (RK, TC), BF16)
    brep = sbM("brep", (128, NST * TC), BF16)
    crep = sbM("crep", (128, NST * TC), BF16)
    t_delta = [sbM(f"t_delta{i}", (128, TC)) for i in range(2)]
    t_sp = sbM("t_sp", (128, TC))
    t_u = [sbM(f"t_u{i}", (128, TC), BF16) for i in range(2)]
    t_dA = [sbM(f"t_dA{i}", (128, NGRP * TC)) for i in range(2)]
    t_mi = [sbM(f"t_mi{i}", (128, NGRP * TC), BF16) for i in range(2)]
    t_v = [sbM(f"t_v{i}", (128, NGRP * TC), BF16) for i in range(2)]
    t_h = [sbM(f"t_h{i}", (128, NGRP * TC), BF16) for i in range(2)]
    t_yAB = sbM("t_yAB", (128, 2 * TC), BF16)
    t_hC2 = sbM("t_hC2", (128, 2 * TC), BF16)
    t_xc = [sbM(f"t_xc{i}", (128, TC)) for i in range(2)]
    st_v = sbM("st_v", (128, NG * NST))
    st_h = sbM("st_h", (128, NG * NST))

    pb = [stP.enter_context(nc.psum_tensor(f"pb{i}", [128, 512], F32)) for i in range(8)]

    beta_b = c_const[:, 4:5].broadcast_to((128, TC))
    rn_bf = rn_sep

    def dma(dst, src, q="dms", reads=(), writes=()):
        eng = "sync" if q == "dms" else "gpsimd"
        sch.op(eng, lambda e, d=dst, s=src: e.dma_start(out=d, in_=s),
               reads=reads, writes=writes, dma=q)

    dma(c_ones[:], dram["ones_col"], writes=["c_ones"])
    dma(c_onesr[:], dram["ones_row"], writes=["c_onesr"])
    dma(c_sel[:], dram["sel32"], writes=["c_sel"])
    dma(c_const[:], dram["consts"], writes=["c_const"])
    dma(x2_st[0:1, 0:NF], dram["b1r"], writes=["x2_st"])
    sch.op("vector", lambda e: e.tensor_copy(r_b1q[:], x2_st[0:1, 0:NF]),
           reads=["x2_st"], writes=["r_b1q"])
    dma(x2_st[0:1, 0:D], dram["b2r"], writes=["x2_st"])
    sch.op("vector", lambda e: e.tensor_copy(r_b2q[:], x2_st[0:1, 0:D]),
           reads=["x2_st"], writes=["r_b2q"])
    dma(x2_st[0:1, 0:TC], dram["onesr"], writes=["x2_st"])
    sch.op("vector", lambda e: e.tensor_copy(r_ones8[:], x2_st[0:1, 0:TC]),
           reads=["x2_st"], writes=["r_ones8"])
    for l in range(NL):
        dma(x2_st[0:RK, 0:E], dram[f"dtwT{l}"], writes=["x2_st"])
        sch.op("vector", lambda e, l=l: e.tensor_copy(
            r_dtw[:, l * E:(l + 1) * E], x2_st[0:RK, 0:E]),
            reads=["x2_st"], writes=["r_dtw"])
        dma(x2_st[:, 0:NG * 96], dram[f"xpw_a{l}"], writes=["x2_st"])
        sch.op("vector", lambda e, l=l: e.tensor_copy(
            r_xpw[:, l * NG * 96:(l + 1) * NG * 96], x2_st[:, 0:NG * 96]),
            reads=["x2_st"], writes=["r_xpw"])
        dma(x2_st[:, 0:NG * NST], dram[f"alog_a{l}"], writes=["x2_st"])
        sch.op("scalar", lambda e: e.activation(a_st[:, 0:NG * NST], x2_st[:, 0:NG * NST], AF.Exp),
               reads=["x2_st"], writes=["a_st"])
        sch.op("vector", lambda e, l=l: e.tensor_single_scalar(
            r_wneg[:, l * NG * NST:(l + 1) * NG * NST], a_st[:, 0:NG * NST], -1.0, OP.mult),
            reads=["a_st"], writes=["r_wneg"])
        for nm, buf in (("dtb_a", r_dtb), ("cb_a", r_cb), ("dp_a", r_dp)):
            dma(buf[:, l * NG:(l + 1) * NG], dram[f"{nm}{l}"], writes=[f"r_{nm}"])
        dma(r_cw[:, l * NG * 4:(l + 1) * NG * 4], dram[f"cw_a{l}"], writes=["r_cw"])
        dma(r_nw[:, l * NDT:(l + 1) * NDT], dram[f"nw_a{l}"], writes=["r_nw"])

    prep_i = [0]

    def prep_cast(src_ap, dst_ap, rows, cols, outdt, skey):
        for p in range(rows // 128):
            rs = p * 128
            for cs in range(0, cols, 1024):
                cw = min(1024, cols - cs)
                i = prep_i[0] % 4
                j = prep_i[0] % 4
                prep_i[0] += 1
                pin = x2_st[:, i * 1024:i * 1024 + cw]
                kin = f"x2h{i}"
                dma(pin, src_ap[rs:rs + 128, cs:cs + cw], writes=[kin])
                if outdt == BF16:
                    pout, kout = wS[j][:, 0:cw], f"wS{j}"
                else:
                    pout, kout = pg_f8[i % 2][:, 0:cw], f"pg_f8{i % 2}"
                sch.op("gpsimd", lambda e, pout=pout, pin=pin: e.tensor_copy(pout, pin),
                       reads=[kin], writes=[kout])
                dma(dst_ap[rs:rs + 128, cs:cs + cw], pout, q="dmg",
                    reads=[kout], writes=[skey])


    def chunk_view(t, c):
        return dram[t][:, c * TC:(c + 1) * TC].rearrange("(g p) t -> p g t", p=128)

    def stg3(buf):
        return buf[:].rearrange("p (g t) -> p g t", t=TC)

    abuf = [a_st, x2_st]

    def ln_chunk(c):
        par = 0
        A = abuf[par]
        ak = f"ab{par}"
        dma(stg3(A), chunk_view("x_t", c), writes=[ak])
        for dt in range(NDT):
            i = dt % 2
            sch.op("scalar", lambda e, dt=dt, A=A, i=i: e.square(
                t_xc[i][:], A[:, dt * TC:(dt + 1) * TC]),
                reads=[ak], writes=[f"xc{i}"])
            sch.op("tensor", lambda e, dt=dt, i=i: e.matmul(
                pb[0][0:1, :], c_ones[:], A[:, dt * TC:(dt + 1) * TC],
                start=(dt == 0), stop=(dt == NDT - 1)),
                reads=[ak, "c_ones"], writes=["pb0"])
            sch.op("tensor", lambda e, dt=dt, i=i: e.matmul(
                pb[1][0:1, :], c_ones[:], t_xc[i][:],
                start=(dt == 0), stop=(dt == NDT - 1)),
                reads=[f"xc{i}", "c_ones"], writes=["pb1"])
        sch.op("vector", lambda e: e.tensor_single_scalar(row_m, pb[0][0:1, :], 1.0 / D, OP.mult),
               reads=["pb0"], writes=["row_m"])
        sch.op("vector", lambda e: e.tensor_single_scalar(row_e2, pb[1][0:1, :], 1.0 / D, OP.mult),
               reads=["pb1"], writes=["row_e2"])
        sch.op("vector", lambda e: e.tensor_mul(row_rstd, row_m, row_m),
               reads=["row_m"], writes=["row_rstd"])
        sch.op("vector", lambda e: e.tensor_sub(row_e2, row_e2, row_rstd),
               reads=["row_rstd", "row_e2"], writes=["row_e2"])
        sch.op("scalar", lambda e: e.activation(row_rstd, row_e2, AF.Ln, bias=1e-5),
               reads=["row_e2"], writes=["row_rstd"])
        sch.op("scalar", lambda e: e.activation(row_rstd, row_rstd, AF.Exp, scale=-0.5),
               reads=["row_rstd"], writes=["row_rstd"])
        sch.op("vector", lambda e: e.tensor_mul(row_rs2, row_m, row_rstd),
               reads=["row_m", "row_rstd", "row_e2"], writes=["row_rs2"])
        sch.op("tensor", lambda e: e.matmul(pb[2][:], c_onesr[:], row_rstd,
                                            start=True, stop=True),
               reads=["row_rstd", "c_onesr"], writes=["pb2"])
        sch.op("tensor", lambda e: e.matmul(pb[3][:], c_onesr[:], row_rs2,
                                            start=True, stop=True),
               reads=["row_rs2", "c_onesr"], writes=["pb3"])
        sch.op("vector", lambda e: e.tensor_copy(rstd_rep[:], pb[2][:]),
               reads=["pb2"], writes=["rstd_rep"])
        sch.op("vector", lambda e: e.tensor_copy(rs2_rep[:], pb[3][:]),
               reads=["pb3"], writes=["rs2_rep"])
        rb = rstd_rep[:].unsqueeze(1).broadcast_to((128, NDT, TC))
        s2 = rs2_rep[:].unsqueeze(1).broadcast_to((128, NDT, TC))
        av = A[:].rearrange("p (g t) -> p g t", g=NDT)
        sch.op("vector", lambda e, av=av, rb=rb: e.tensor_tensor(av, av, rb, OP.mult),
               reads=["rstd_rep", ak], writes=[ak])
        sch.op("vector", lambda e, av=av, s2=s2: e.tensor_tensor(av, av, s2, OP.subtract),
               reads=["rs2_rep", ak], writes=[ak])
        dma(chunk_view("s_a", c), stg3(A), q="dmg", reads=[ak], writes=[f"s_a{c}"])
        dma(chunk_view("s_xn", c), stg3(A), q="dmg", reads=[ak], writes=["s_xn"])
        if DBG:
            dma(dbg_xn[:, c * TC:(c + 1) * TC].rearrange("(g p) t -> p g t", p=128),
                stg3(A), q="dmg", reads=[ak], writes=["dbg"])

    ln_chunk(0)
    prep_cast(dram["owT0"], dram["s_owT0"], E, D, BF16, "scr_ow0")
    ln_chunk(1)
    ln_chunk(2)
    prep_cast(dram["owT1"], dram["s_owT1"], E, D, BF16, "scr_ow1")
    ln_chunk(3)

    def head_ops(l, c):
        par = c % 2
        A = abuf[par]
        ak = f"ab{par}"
        th = []

        def T(engine, fn, reads=(), writes=(), dma_q=None, ms=0):
            th.append((engine, fn, list(reads), list(writes), dma_q, ms))

        if c == 0:
            T("sync", lambda e: e.dma_start(out=halo[:], in_=dram["zeros48"]),
              [], ["halo"], "dms")
        T("sync", lambda e, A=A, c=c: e.dma_start(out=stg3(A), in_=chunk_view("s_a", c)),
          [f"s_a{c}", "x2h0", "x2h1", "x2h2", "x2h3"] if par == 1 else [f"s_a{c}"], [ak], "dms")
        for dt in range(NDT):
            i = dt % 2
            T("scalar", lambda e, dt=dt, A=A, i=i: e.square(
                t_xc[i][:], A[:, dt * TC:(dt + 1) * TC]), [ak], [f"xc{i}"])
            T("tensor", lambda e, dt=dt, i=i: e.matmul(
                pb[6][0:1, :], c_ones[:], t_xc[i][:],
                start=(dt == 0), stop=(dt == NDT - 1)),
                [f"xc{i}", "c_ones"], ["pb6"])
        T("scalar", lambda e: e.activation(row_rstd, pb[6][0:1, :], AF.Ln,
                                           scale=1.0 / D, bias=1e-5),
          ["pb6"], ["row_rstd"])
        T("scalar", lambda e: e.activation(row_rstd, row_rstd, AF.Exp, scale=-0.5),
          ["row_rstd"], ["row_rstd"])
        T("tensor", lambda e: e.matmul(pb[6][:], c_onesr[:], row_rstd,
                                       start=True, stop=True),
          ["row_rstd", "c_onesr"], ["pb6"])
        T("vector", lambda e: e.tensor_copy(rstd_rep[:], pb[6][:]),
          ["pb6"], ["rstd_rep"])
        for dt in range(NDT):
            T("vector", lambda e, dt=dt, l=l, A=A: e.scalar_tensor_tensor(
                rn_bf[:, dt * TC:(dt + 1) * TC], A[:, dt * TC:(dt + 1) * TC],
                r_nw[:, l * NDT + dt:l * NDT + dt + 1], rstd_rep[:], OP.mult, OP.mult),
                [ak, "rstd_rep", "r_nw"], ["rn_bf"])
        xbv3 = xbp[:].rearrange("p (g t) -> p g t", g=NG)
        T("vector", lambda e, xbv3=xbv3: e.tensor_copy(
            xbv3[:, :, 0:3], halo[:].rearrange("p (g t) -> p g t", g=NG)),
            ["halo"], ["xbp_pre"])
        for f0, fn_ in [(0, 6), (6, 6), (12, 4)]:
            for half in range(2):
                gate99 = (f0 == 12)
                def wsrc(d, half=half, f0=f0, fn_=fn_, l=l):
                    return dram[f"in_wT{l}"][d * 128:(d + 1) * 128,
                                             (half * 16 + f0) * 128:(half * 16 + f0 + fn_) * 128]
                for d in range(4):
                    wi = d % 4
                    T("gpsimd", lambda e, d=d, wi=wi, wsrc=wsrc, fn_=fn_: e.dma_start(
                        out=wS[wi][:, 0:fn_ * 128], in_=wsrc(d)),
                        [], [f"wS{wi}"], "dmg", ms=(99 if gate99 else 0))
                for d in range(NDT):
                    wi = d % 4
                    for fi in range(fn_):
                        T("tensor", lambda e, d=d, fi=fi, wi=wi: e.matmul(
                            pb[fi][:], wS[wi][:, fi * 128:(fi + 1) * 128],
                            rn_bf[:, d * TC:(d + 1) * TC],
                            start=(d == 0), stop=(d == NDT - 1)),
                            [f"wS{wi}", "rn_bf"], [f"pb{fi}"])
                    if d + 4 < NDT:
                        wi2 = (d + 4) % 4
                        T("gpsimd", lambda e, d=d, wi2=wi2, wsrc=wsrc, fn_=fn_: e.dma_start(
                            out=wS[wi2][:, 0:fn_ * 128], in_=wsrc(d + 4)),
                            [], [f"wS{wi2}"], "dmg")
                for fi in range(fn_):
                    g = f0 + fi
                    if half == 0:
                        T("scalar", lambda e, g=g, fi=fi: e.copy(
                            xbp[:, g * (TC + 3) + 3:(g + 1) * (TC + 3)], pb[fi][:]),
                            [f"pb{fi}"], [f"xbp{g}"], ms=g)
                    else:
                        T("scalar", lambda e, g=g, fi=fi: e.activation(
                            zg_bf[:, g * TC:(g + 1) * TC], pb[fi][:], AF.Silu),
                            [f"pb{fi}"], [f"zg{g}"], ms=g)
        T("vector", lambda e, xbv3=xbv3: e.tensor_copy(
            halo[:].rearrange("p (g t) -> p g t", g=NG), xbv3[:, :, TC:TC + 3]),
            [f"xbp{g}" for g in range(NG)] + ["xbp_pre"], ["halo"])
        for g in range(NG):
            xg = xbp[:, g * (TC + 3):(g + 1) * (TC + 3)]
            cwb = l * NG * 4 + g * 4
            T("vector", lambda e, xg=xg, cwb=cwb: e.tensor_scalar(
                t_xc[0][:], xg[:, 0:TC], r_cw[:, cwb:cwb + 1], None, OP.mult),
                [f"xbp{g}", "xbp_pre", "halo", "r_cw"], ["xc0"])
            for k in (1, 2, 3):
                si, so = (0, 1) if k % 2 == 1 else (1, 0)
                T("vector", lambda e, xg=xg, cwb=cwb, k=k, si=si, so=so:
                  e.scalar_tensor_tensor(
                      t_xc[so][:], xg[:, k:TC + k], r_cw[:, cwb + k:cwb + k + 1],
                      t_xc[si][:], OP.mult, OP.add),
                  [f"xc{si}", f"xbp{g}", "r_cw"], [f"xc{so}"])
            T("scalar", lambda e, xg=xg, g=g, l=l: e.activation(
                xg[:, 3:TC + 3], t_xc[1][:], AF.Silu,
                bias=r_cb[:, l * NG + g:l * NG + g + 1]),
                ["xc1", "r_cb"], [f"xbp{g}"], ms=g)
            T("tensor", lambda e, g=g, l=l: e.matmul(
                pb[6][0:96, :], r_xpw[:, (l * NG + g) * 96:(l * NG + g + 1) * 96],
                xbp[:, g * (TC + 3) + 3:(g + 1) * (TC + 3)],
                start=(g == 0), stop=(g == NG - 1)),
                [f"xbp{g}", "r_xpw"], ["pb6"], ms=g)
        T("vector", lambda e: e.tensor_copy(dbc_sb[:], pb[6][0:96, :]),
          ["pb6"], ["dbc_sb"])
        T("vector", lambda e: e.tensor_copy(drbf[:], dbc_sb[0:RK, :]),
          ["dbc_sb"], ["drbf"])
        T("gpsimd", lambda e: e.dma_start(out=bc_bf[:], in_=dbc_sb[RK:96, :]),
          ["dbc_sb"], ["bc_bf"], "dmg")
        for jj in range(32):
            nn, isC = jj // 2, jj % 2
            j = nn + 16 * isC
            bank = 6 + (jj % 2)
            T("tensor", lambda e, j=j, bank=bank: e.matmul(
                pb[bank][:], c_sel[:, j:j + 1].broadcast_to((32, 128)), bc_bf[:],
                start=True, stop=True),
                ["bc_bf", "c_sel"], [f"pb{bank}"])
            dstb, dkey = (brep, f"brep{nn}") if not isC else (crep, f"crep{nn}")
            T("scalar", lambda e, nn=nn, bank=bank, dstb=dstb: e.copy(
                dstb[:, nn * TC:(nn + 1) * TC], pb[bank][:]),
                [f"pb{bank}"], [dkey])
        return th

    def emit_thunks(th):
        for engine, fn, reads, writes, dma_q, ms in th:
            sch.op(engine, fn, reads=reads, writes=writes, dma=dma_q)

    def emit_gated(th, pos, g):
        while pos < len(th) and th[pos][5] <= g:
            engine, fn, reads, writes, dma_q, ms = th[pos]
            sch.op(engine, fn, reads=reads, writes=writes, dma=dma_q)
            pos += 1
        return pos

    seq = [(l, c) for l in range(NL) for c in range(NCH)]
    pend = head_ops(0, 0)
    emit_thunks(pend)

    for idx, (l, c) in enumerate(seq):
        par = c % 2
        A = abuf[par]
        ak = f"ab{par}"
        nxt = head_ops(*seq[idx + 1]) if idx + 1 < len(seq) else []
        if idx + 1 < len(seq) and seq[idx + 1][1] == 0:
            pass
        nxt_pos = 0
        def dt_chain(g):
            sl = g % 2
            gl = l * NG + g
            sch.op("tensor", lambda e, g=g, l=l: e.matmul(
                pb[7][:], r_dtw[:, l * E + g * 128:l * E + (g + 1) * 128], drbf[:],
                start=True, stop=True),
                reads=["drbf", "r_dtw"], writes=["pb7"])
            sch.op("scalar", lambda e, gl=gl: e.activation(
                t_sp[:], pb[7][:], AF.Abs, bias=r_dtb[:, gl:gl + 1]),
                reads=["pb7", "r_dtb"], writes=["tsp"])
            sch.op("scalar", lambda e: e.activation(t_sp[:], t_sp[:], AF.Exp, scale=-1.0),
                   reads=["tsp"], writes=["tsp"])
            sch.op("scalar", lambda e: e.activation(t_sp[:], t_sp[:], AF.Ln, bias=1.0),
                   reads=["tsp"], writes=["tsp"])
            sch.op("scalar", lambda e, gl=gl, sl=sl: e.activation(
                t_delta[sl][:], pb[7][:], AF.Relu, bias=r_dtb[:, gl:gl + 1]),
                reads=["pb7", "r_dtb"], writes=[f"tdelta{sl}"])
            sch.op("vector", lambda e, sl=sl: e.tensor_add(
                t_delta[sl][:], t_delta[sl][:], t_sp[:]),
                reads=["tsp", f"tdelta{sl}"], writes=[f"tdelta{sl}"])
            sch.op("gpsimd", lambda e, g=g, sl=sl: e.tensor_tensor(
                t_u[sl][:], t_delta[sl][:], xbp[:, g * (TC + 3) + 3:(g + 1) * (TC + 3)],
                OP.mult),
                reads=[f"tdelta{sl}", f"xbp{g}"], writes=[f"tu{sl}"])

        dt_chain(0)
        for g in range(NG):
            sl = g % 2
            gl = l * NG + g
            if g + 1 < NG:
                dt_chain(g + 1)
            for ng in range(NST // NGRP):
                bsl = ng % 2
                n0 = ng * NGRP
                for k in range(NGRP):
                    n = ng * NGRP + k
                    sch.op("scalar", lambda e, n=n, k=k, gl=gl, sl=sl, bsl=bsl: e.activation(
                        t_dA[bsl][:, k * TC:(k + 1) * TC], t_delta[sl][:], AF.Exp,
                        scale=r_wneg[:, gl * NST + n:gl * NST + n + 1]),
                        reads=[f"tdelta{sl}", "r_wneg"], writes=[f"tdA{bsl}"])
                sch.op("gpsimd", lambda e, bsl=bsl, sl=sl, n0=n0: e.tensor_tensor(
                    t_mi[bsl][:].rearrange("p (k t) -> p k t", k=NGRP),
                    t_u[sl][:].unsqueeze(1).broadcast_to((128, NGRP, TC)),
                    crep[:, n0 * TC:(n0 + 2) * TC].rearrange("p (k t) -> p k t", k=NGRP)
                    if False else
                    brep[:, n0 * TC:(n0 + 2) * TC].rearrange("p (k t) -> p k t", k=NGRP),
                    OP.mult),
                    reads=[f"tu{sl}", f"brep{n0}", f"brep{n0 + 1}"], writes=[f"tmi{bsl}"])
                for k in range(NGRP):
                    n = ng * NGRP + k
                    stc = g * NST + n
                    iv = 0.0 if c == 0 else st_v[:, stc:stc + 1]
                    ih = 0.0 if c == 0 else st_h[:, stc:stc + 1]
                    sch.op("vector", lambda e, k=k, bsl=bsl, iv=iv: e.tensor_tensor_scan(
                        t_v[bsl][:, k * TC:(k + 1) * TC], beta_b,
                        t_mi[bsl][:, k * TC:(k + 1) * TC], iv, OP.mult, OP.add),
                        reads=[f"tmi{bsl}", "st_v", "c_const"], writes=[f"tv{bsl}"])
                    sch.op("vector", lambda e, k=k, bsl=bsl, ih=ih: e.tensor_tensor_scan(
                        t_h[bsl][:, k * TC:(k + 1) * TC],
                        t_dA[bsl][:, k * TC:(k + 1) * TC],
                        t_v[bsl][:, k * TC:(k + 1) * TC], ih, OP.mult, OP.add),
                        reads=[f"tdA{bsl}", f"tv{bsl}", "st_h"], writes=[f"th{bsl}"])
                if c < NCH - 1:
                    st0 = g * NST + ng * NGRP
                    sch.op("vector", lambda e, bsl=bsl, st0=st0: e.tensor_copy(
                        st_v[:, st0:st0 + NGRP],
                        t_v[bsl][:, TC - 1:NGRP * TC:TC]),
                        reads=[f"tv{bsl}"], writes=["st_v"])
                    sch.op("vector", lambda e, bsl=bsl, st0=st0: e.tensor_copy(
                        st_h[:, st0:st0 + NGRP],
                        t_h[bsl][:, TC - 1:NGRP * TC:TC]),
                        reads=[f"th{bsl}"], writes=["st_h"])
                n0 = ng * NGRP
                if ng == 0:
                    sch.op("vector", lambda e, bsl=bsl, n0=n0: e.tensor_tensor(
                        t_yAB[:], t_h[bsl][:],
                        crep[:, n0 * TC:(n0 + 2) * TC], OP.mult),
                        reads=[f"th{bsl}", f"crep{n0}", f"crep{n0 + 1}"],
                        writes=["tyA", "tyB"])
                else:
                    sch.op("vector", lambda e, bsl=bsl, n0=n0: e.tensor_tensor(
                        t_hC2[:], t_h[bsl][:],
                        crep[:, n0 * TC:(n0 + 2) * TC], OP.mult),
                        reads=[f"th{bsl}", f"crep{n0}", f"crep{n0 + 1}"],
                        writes=["thC"])
                    sch.op("vector", lambda e: e.tensor_add(
                        t_yAB[:], t_yAB[:], t_hC2[:]),
                        reads=["thC", "tyA", "tyB"], writes=["tyA", "tyB"])
            sch.op("vector", lambda e, g=g, gl=gl: e.scalar_tensor_tensor(
                t_xc[0][:], xbp[:, g * (TC + 3) + 3:(g + 1) * (TC + 3)],
                r_dp[:, gl:gl + 1], t_yAB[:, 0:TC], OP.mult, OP.add),
                reads=[f"xbp{g}", "tyA", "r_dp"], writes=["xc0"])
            sch.op("vector", lambda e: e.tensor_add(t_xc[0][:], t_xc[0][:], t_yAB[:, TC:2 * TC]),
                   reads=["tyB", "xc0"], writes=["xc0"])
            sch.op("vector", lambda e, g=g: e.tensor_tensor(
                rny2[:, g * TC:(g + 1) * TC], t_xc[0][:],
                zg_bf[:, g * TC:(g + 1) * TC], OP.mult),
                reads=["xc0", f"zg{g}"], writes=[f"y2{g}"])
            if nxt and PIPE:
                nxt_pos = emit_gated(nxt, nxt_pos, g)
        for g in range(NG):
            wbuf, key = wS[g % 4], f"wS{g % 4}"
            dma(wbuf[:, 0:1024], dram[f"s_owT{l}"][g * 128:(g + 1) * 128, :],
                reads=[f"scr_ow{l}"], writes=[key])
            for d in range(NDT):
                sch.op("tensor", lambda e, g=g, d=d, wbuf=wbuf: e.matmul(
                    pb[d][:], wbuf[:, d * 128:(d + 1) * 128],
                    rny2[:, g * TC:(g + 1) * TC],
                    start=(g == 0), stop=(g == NG - 1)),
                    reads=[key, f"y2{g}"], writes=[f"pb{d}"])
        for d in range(NDT):
            sch.op("vector", lambda e, d=d, A=A: e.tensor_add(
                A[:, d * TC:(d + 1) * TC], A[:, d * TC:(d + 1) * TC], pb[d][:]),
                reads=[f"pb{d}"], writes=[ak])
        dma(chunk_view("s_a", c), stg3(A), q="dmg", reads=[ak], writes=[f"s_a{c}"])
        if DBG:
            dma(dbg_a[l][:, c * TC:(c + 1) * TC].rearrange("(g p) t -> p g t", p=128),
                stg3(A), q="dmg", reads=[ak], writes=["dbg"])
        while nxt_pos < len(nxt):
            e_, f_, r_, w_, d_, m_ = nxt[nxt_pos]
            sch.op(e_, f_, reads=r_, writes=w_, dma=d_)
            nxt_pos += 1

    sch.barrier()
    stM.close()
    stF = ExitStack()

    def sbF(name, shape, dt=F32):
        return stF.enter_context(nc.sbuf_tensor(name, list(shape), dt))

    f_w1 = sbF("f_w1", (128, 8 * 4096), F8)
    f_w2 = sbF("f_w2", (128, 32 * 1024), F8)
    f_h8_ = [sbF(f"f_h8_{i}", (128, 32 * TC), F8) for i in range(2)]
    f_hb_ = [sbF(f"f_hb{i}", (128, TC), BF16) for i in range(2)]
    f_fb = sbF("f_fb", (128, NDT * TC), BF16)
    f_msk = sbF("f_msk", (128, NDT * TC), BF16)
    f_q8full = sbF("f_q8full", (128, NDT * L), F8)

    for d in range(NDT):
        dma(f_w1[:, d * 4096:(d + 1) * 4096], dram["W1T"][d * 128:(d + 1) * 128, :],
            q="dmg", reads=[], writes=["f_w1"])
    for f in range(32):
        dma(f_w2[:, f * 1024:(f + 1) * 1024], dram["W2T"][f * 128:(f + 1) * 128, :],
            q="dmg", reads=[], writes=["f_w2"])
    for c in range(NCH):
        dma(stg3(x2_st), chunk_view("s_xn", c),
            reads=["s_xn", "x2h0", "x2h1", "x2h2", "x2h3"] if c == 0 else ["s_xn"],
            writes=["ab1"])
        sch.op("vector", lambda e, c=c: e.tensor_copy(
            f_q8full[:].rearrange("p (g t) -> p g t", t=L)[:, :, c * TC:(c + 1) * TC],
            stg3(x2_st)),
            reads=["ab1"], writes=[f"fq8c{c}"])
    for c in range(NCH):
        fpar = c % 2
        f_h8 = f_h8_[fpar]
        kh8 = f"f_h8{fpar}"
        dma(stg3(f_msk), chunk_view("mask_t", c), writes=["f_msk"])
        dma(stg3(a_st), chunk_view("s_a", c), reads=[f"s_a{c}"], writes=["ab0"])
        dma(stg3(x2_st), chunk_view("x_t", c), writes=["ab1"])
        for fg in range(8):
            for f4i in range(4):
                f = fg * 4 + f4i
                bk = 4 + f4i
                for d in range(NDT):
                    sch.op("tensor", lambda e, f=f, d=d, bk=bk, c=c: e.matmul(
                        pb[bk][:], f_w1[:, d * 4096 + f * 128:d * 4096 + (f + 1) * 128],
                        f_q8full[:, d * L + c * TC:d * L + (c + 1) * TC],
                        start=(d == 0), stop=False),
                        reads=["f_w1", f"fq8c{c}"], writes=[f"pb{bk}"])
                sch.op("tensor", lambda e, f=f, bk=bk: e.matmul(
                    pb[bk][:], r_b1q[0:1, f * 128:(f + 1) * 128], r_ones8[:],
                    start=False, stop=True),
                    reads=["r_b1q", "r_ones8"], writes=[f"pb{bk}"])
            for f4i in range(4):
                f = fg * 4 + f4i
                bk = 4 + f4i
                hb = f_hb_[f4i % 2]
                hbk = f"f_hb{f4i % 2}"
                sch.op("scalar", lambda e, bk=bk, hb=hb: e.copy(hb[:], pb[bk][:]),
                       reads=[f"pb{bk}"], writes=[hbk])
                sch.op("vector", lambda e, hb=hb: e.tensor_relu(hb[:], hb[:]),
                       reads=[hbk], writes=[hbk])
                sch.op("vector", lambda e, f=f, f_h8=f_h8, hb=hb: e.tensor_copy(
                    f_h8[:, f * TC:(f + 1) * TC], hb[:]),
                    reads=[hbk], writes=[kh8])
        for d in range(NDT):
            bk = d % 4
            for f in range(32):
                sch.op("tensor", lambda e, d=d, f=f, bk=bk, f_h8=f_h8: e.matmul(
                    pb[bk][:], f_w2[:, f * 1024 + d * 128:f * 1024 + (d + 1) * 128],
                    f_h8[:, f * TC:(f + 1) * TC],
                    start=(f == 0), stop=False),
                    reads=["f_w2", kh8], writes=[f"pb{bk}"])
            sch.op("tensor", lambda e, d=d, bk=bk: e.matmul(
                pb[bk][:], r_b2q[0:1, d * 128:(d + 1) * 128], r_ones8[:],
                start=False, stop=True),
                reads=["r_b2q", "r_ones8"], writes=[f"pb{bk}"])
            sch.op("scalar", lambda e, d=d, bk=bk: e.copy(
                f_fb[:, d * TC:(d + 1) * TC], pb[bk][:]),
                reads=[f"pb{bk}"], writes=["f_fb"])
        sch.op("vector", lambda e: e.tensor_add(a_st[:], a_st[:], x2_st[:]),
               reads=["ab1", "ab0"], writes=["ab0"])
        sch.op("vector", lambda e: e.tensor_mul(x2_st[:], f_fb[:], f_msk[:]),
               reads=["f_fb", "f_msk", "ab1"], writes=["ab1"])
        sch.op("vector", lambda e: e.tensor_add(a_st[:], a_st[:], x2_st[:]),
               reads=["ab1", "ab0"], writes=["ab0"])
        dma(out_t[:, c * TC:(c + 1) * TC].rearrange("(g p) t -> p g t", p=128),
            stg3(a_st), reads=["ab0"], writes=["out"])

    with nc.Block() as block:
        sch.emit(nc, block)
    stF.close()
    stP.close()
    return nc


_PROG = None


def _get_prog():
    global _PROG
    if _PROG is None:
        _PROG = build_program()
    return _PROG


def kernel(L0_attn_x, L0_attn_norm_w, L0_attn_in_proj_w, L0_attn_conv_w,
           L0_attn_conv_b, L0_attn_x_proj_w, L0_attn_dt_w, L0_attn_dt_b,
           L0_attn_A_log, L0_attn_Dp, L0_attn_out_w, L0_ffn_W1, L0_ffn_b1,
           L0_ffn_W2, L0_ffn_b2):
    import jax
    x = np.ascontiguousarray(np.asarray(L0_attn_x, np.float32))
    B = x.shape[0]
    with jax.default_device(jax.devices("cpu")[0]):
        jax.config.update("jax_threefry_partitionable", True)
        keep = np.asarray(jax.random.bernoulli(
            jax.random.key(42, impl="threefry2x32"), 0.8, (8, 2048, 1024)))
    mask = (keep.astype(np.float32) * np.float32(1.25)).astype(BF_NP)

    def arr(a):
        return np.ascontiguousarray(np.asarray(a, np.float32))

    common = {}
    for l in range(NL):
        common[f"in_wT{l}"] = arr(np.asarray(L0_attn_in_proj_w)[l].T)
        common[f"owT{l}"] = arr(np.asarray(L0_attn_out_w)[l].T)
        xpw = arr(np.asarray(L0_attn_x_proj_w)[l].T)
        common[f"xpw_a{l}"] = np.ascontiguousarray(
            xpw.reshape(NG, 128, 96).transpose(1, 0, 2).reshape(128, NG * 96))
        common[f"dtwT{l}"] = arr(np.asarray(L0_attn_dt_w)[l].T)
        al = arr(np.asarray(L0_attn_A_log)[l])
        common[f"alog_a{l}"] = np.ascontiguousarray(
            al.reshape(NG, 128, NST).transpose(1, 0, 2).reshape(128, NG * NST))
        for nm, src in (("dtb_a", L0_attn_dt_b), ("cb_a", L0_attn_conv_b),
                        ("dp_a", L0_attn_Dp)):
            v = arr(np.asarray(src)[l])
            common[f"{nm}{l}"] = np.ascontiguousarray(v.reshape(NG, 128).T)
        cw = arr(np.asarray(L0_attn_conv_w)[l])
        common[f"cw_a{l}"] = np.ascontiguousarray(
            cw.reshape(NG, 128, 4).transpose(1, 0, 2).reshape(128, NG * 4))
        nw = arr(np.asarray(L0_attn_norm_w)[l])
        common[f"nw_a{l}"] = np.ascontiguousarray(nw.reshape(NDT, 128).T)
    common["W1T"] = arr(np.asarray(L0_ffn_W1).T)
    common["W2T"] = arr(np.asarray(L0_ffn_W2).T)
    common["b1r"] = arr(L0_ffn_b1).reshape(1, NF)
    common["b2r"] = arr(L0_ffn_b2).reshape(1, D)
    common["onesr"] = np.ones((1, TC), np.float32)
    common["ones_col"] = np.ones((128, 1), np.float32)
    common["ones_row"] = np.ones((1, 128), np.float32)
    common["sel32"] = np.eye(32, dtype=np.float32).astype(BF_NP)
    consts = np.zeros((128, 8), np.float32)
    consts[:, 0] = -1.0
    consts[:, 1] = 1.0
    consts[:, 2] = -0.5
    consts[:, 3] = 1e-5
    consts[:, 4] = BETA
    consts[:, 5] = 1.0 / D
    common["consts"] = consts
    common["zeros48"] = np.zeros((128, 48), BF_NP)

    in_maps = []
    for b in range(B):
        m = dict(common)
        m["x_t"] = np.ascontiguousarray(x[b].T)
        m["mask_t"] = np.ascontiguousarray(mask[b].T)
        in_maps.append(m)

    nc = _get_prog()
    res = run_bass_kernel_spmd(nc, in_maps, core_ids=list(range(B)))
    out = np.stack([np.asarray(r["out_t"]).T for r in res.results], axis=0)
    return out.astype(np.float32)


# version 25
# speedup vs baseline: 1.0970x; 1.0212x over previous
import numpy as np
import ml_dtypes
import concourse.bass as bass
import concourse.mybir as mybir
from concourse.bass_utils import run_bass_kernel_spmd

F32 = mybir.dt.float32
BF16 = mybir.dt.bfloat16
F8 = mybir.dt.float8e5
OP = mybir.AluOpType
AF = mybir.ActivationFunctionType

L, D, E, NST, RK = 2048, 1024, 2048, 16, 64
TC = 512
NCH = L // TC
NG = E // 128
NDT = D // 128
NF = 4096
NL = 2
BETA = 0.6
NGRP = 2
PIPE = True

BF_NP = ml_dtypes.bfloat16


class Sched:
    ENG = ("sync", "vector", "scalar", "tensor", "gpsimd")

    def __init__(self):
        self.ops = []
        self.bw = {}
        self.br = {}
        self.last_on = {}
        self.last_dma = {}

    def op(self, engine, fn, reads=(), writes=(), dma=None):
        opid = len(self.ops)
        deps = set()
        for b in reads:
            w = self.bw.get(b)
            if w is not None:
                deps.add(w)
        for b in writes:
            deps.update(self.br.get(b, ()))
            w = self.bw.get(b)
            if w is not None:
                deps.add(w)
        deps.discard(opid)
        self.ops.append({"e": engine, "fn": fn, "deps": deps, "dma": dma})
        for b in reads:
            self.br.setdefault(b, []).append(opid)
        for b in writes:
            self.bw[b] = opid
            self.br[b] = []
        self.last_on[engine] = opid
        if dma:
            self.last_dma[dma] = opid
        return opid

    def barrier(self):
        lasts = list(self.last_on.values()) + list(self.last_dma.values())
        for e in self.ENG:
            opid = len(self.ops)
            self.ops.append({"e": e, "fn": None,
                             "deps": set(d for d in lasts if d != opid),
                             "dma": None})
            self.last_on[e] = opid

    def emit(self, nc, block):
        marked = set()
        for i, rec in enumerate(self.ops):
            if rec["dma"]:
                marked.add(i)
            me = rec["e"]
            for d in rec["deps"]:
                if self.ops[d]["e"] != me or self.ops[d]["dma"]:
                    marked.add(d)

        def stream(rec):
            return rec["dma"] if rec["dma"] else rec["e"]

        R = 8
        counters, incno = {}, {}
        dma_slot = {}
        for i, rec in enumerate(self.ops):
            if i in marked:
                s = stream(rec)
                counters[s] = counters.get(s, 0) + 1
                incno[i] = counters[s]
                if rec["dma"]:
                    o = counters[s] - 1
                    dma_slot[i] = (s, o % R, o // R + 1)
        sems = {}
        for s in ("vector", "scalar", "tensor", "gpsimd", "sync"):
            sems[s] = nc.semaphore(f"sem_{s}").__enter__()
        qsems = {}
        for q in ("dms", "dmg"):
            qsems[q] = [nc.semaphore(f"sem_{q}{j}").__enter__() for j in range(R)]
        per_eng = {e: [] for e in self.ENG}
        for i, rec in enumerate(self.ops):
            per_eng[rec["e"]].append(i)

        def emit_engine(eng_obj, ename):
            water = {}
            for i in per_eng[ename]:
                rec = self.ops[i]
                byprod = {}
                for d in rec["deps"]:
                    drec = self.ops[d]
                    if drec["e"] == ename and drec["dma"] is None:
                        continue
                    if drec["dma"]:
                        q, j, u = dma_slot[d]
                        key = (q, j)
                        byprod[key] = max(byprod.get(key, 0), u)
                    else:
                        byprod[drec["e"]] = max(byprod.get(drec["e"], 0), incno[d])
                for ds, v in sorted(byprod.items(), key=str):
                    if water.get(ds, 0) >= v:
                        continue
                    sem = qsems[ds[0]][ds[1]] if isinstance(ds, tuple) else sems[ds]
                    mult = 16 if isinstance(ds, tuple) else 1
                    eng_obj.wait_ge(sem, v * mult)
                    water[ds] = v
                if rec["dma"]:
                    q, j, u = dma_slot[i]
                    if u > 1 and water.get((q, j), 0) < u - 1:
                        eng_obj.wait_ge(qsems[q][j], (u - 1) * 16)
                        water[(q, j)] = u - 1
                    ins = rec["fn"](eng_obj)
                    ins.then_inc(qsems[q][j], 16)
                else:
                    ins = eng_obj.nop() if rec["fn"] is None else rec["fn"](eng_obj)
                    if i in incno:
                        ins.then_inc(sems[rec["e"]], 1)

        @block.sync
        def _(sync):
            emit_engine(sync, "sync")
            for q in ("dms", "dmg"):
                tot = counters.get(q, 0)
                for j in range(R):
                    uses = tot // R + (1 if j < tot % R else 0)
                    if uses:
                        sync.wait_ge(qsems[q][j], uses * 16)

        @block.vector
        def _(v):
            emit_engine(v, "vector")

        @block.scalar
        def _(s):
            emit_engine(s, "scalar")

        @block.tensor
        def _(t):
            emit_engine(t, "tensor")

        @block.gpsimd
        def _(g):
            emit_engine(g, "gpsimd")


def build_program():
    nc = bass.Bass("TRN2", target_bir_lowering=False)
    dram = {}

    def din(name, shape, dt=F32):
        dram[name] = nc.dram_tensor(name, list(shape), dt, kind="ExternalInput").ap()
        return dram[name]

    def dscr(name, shape, dt):
        dram[name] = nc.dram_tensor(name, list(shape), dt, kind="Internal").ap()
        return dram[name]

    din("x_t", (D, L))
    din("mask_t", (D, L), BF16)
    for l in range(NL):
        din(f"in_wT{l}", (D, 2 * E))
        din(f"owT{l}", (E, D))
        din(f"xpw_a{l}", (128, NG * 96))
        din(f"dtwT{l}", (RK, E))
        din(f"alog_a{l}", (128, NG * NST))
        din(f"dtb_a{l}", (128, NG))
        din(f"cb_a{l}", (128, NG))
        din(f"dp_a{l}", (128, NG))
        din(f"cw_a{l}", (128, NG * 4))
        din(f"nw_a{l}", (128, NDT))
    din("W1T", (D, NF))
    din("W2T", (NF, D))
    din("b1r", (1, NF))
    din("b2r", (1, D))
    din("onesr", (1, TC))
    din("ones_col", (128, 1))
    din("ones_row", (1, 128))
    din("sel32", (32, 32), BF16)
    din("consts", (128, 8))
    din("zeros48", (128, 48), BF16)
    out_t = nc.dram_tensor("out_t", [D, L], F32, kind="ExternalOutput").ap()
    DBG = False
    if DBG:
        dbg_xn = nc.dram_tensor("dbg_xn", [D, L], F32, kind="ExternalOutput").ap()
        dbg_a = [nc.dram_tensor(f"dbg_a{l}", [D, L], F32, kind="ExternalOutput").ap()
                 for l in range(NL)]

    for l in range(NL):
        dscr(f"s_owT{l}", (E, D), BF16)
    dscr("s_a", (D, L), F32)
    dscr("s_xn", (D, L), F32)

    sch = Sched()
    from contextlib import ExitStack
    stP = ExitStack()
    stM = ExitStack()

    def sbP(name, shape, dt=F32):
        return stP.enter_context(nc.sbuf_tensor(name, list(shape), dt))

    def sbM(name, shape, dt=F32):
        return stM.enter_context(nc.sbuf_tensor(name, list(shape), dt))

    c_ones = sbP("c_ones", (128, 1))
    c_onesr = sbP("c_onesr", (1, 128))
    c_const = sbP("c_const", (128, 8))
    nc.const_aps.register(F32, -1.0, c_const[:, 0:1])
    nc.const_aps.register(F32, 1.0, c_const[:, 1:2])
    nc.const_aps.register(F32, -0.5, c_const[:, 2:3])
    nc.const_aps.register(F32, 1e-5, c_const[:, 3:4])
    nc.const_aps.register(F32, 1.0 / D, c_const[:, 5:6])
    r_dtw = sbP("r_dtw", (RK, NL * E), BF16)
    r_xpw = sbP("r_xpw", (128, NL * NG * 96), BF16)
    r_wneg = sbP("r_wneg", (128, NL * NG * NST))
    r_dtb = sbP("r_dtb", (128, NL * NG))
    r_cb = sbP("r_cb", (128, NL * NG))
    r_dp = sbP("r_dp", (128, NL * NG))
    r_cw = sbP("r_cw", (128, NL * NG * 4))
    r_nw = sbP("r_nw", (128, NL * NDT))
    r_b1q = sbP("r_b1q", (1, NF), F8)
    r_b2q = sbP("r_b2q", (1, D), F8)
    r_ones8 = sbP("r_ones8", (1, TC), F8)
    a_st = sbP("a_st", (128, NDT * TC))
    x2_st = sbP("x2_st", (128, NDT * TC))
    wS = [sbP(f"wS{i}", (128, 1024), BF16) for i in range(4)]
    row_m = sbP("row_m", (1, TC))[:]
    row_e2 = sbP("row_e2", (1, TC))[:]
    row_rstd = sbP("row_rstd", (1, TC))[:]
    row_rs2 = row_e2
    rstd_rep = sbP("rstd_rep", (128, TC))
    rs2_rep = sbP("rs2_rep", (128, TC))
    pg_f8 = [sbP(f"pg_f8{i}", (128, 1024), F8) for i in range(2)]

    c_sel = sbM("c_sel", (32, 32), BF16)
    xbp = sbM("xbp", (128, NG * (TC + 3)), BF16)
    zg_bf = sbM("zg_bf", (128, NG * TC), BF16)
    rny2 = sbM("rny2", (128, NG * TC), BF16)
    rn_sep = sbM("rn_sep", (128, NDT * TC), BF16)
    halo = sbM("halo", (128, NG * 3), BF16)
    dbc_sb = sbM("dbc_sb", (96, TC))
    bc_bf = sbM("bc_bf", (32, TC), BF16)
    drbf = sbM("drbf", (RK, TC), BF16)
    brep = sbM("brep", (128, NST * TC), BF16)
    crep = sbM("crep", (128, NST * TC), BF16)
    t_delta = [sbM(f"t_delta{i}", (128, TC)) for i in range(2)]
    t_sp = sbM("t_sp", (128, TC))
    t_u = [sbM(f"t_u{i}", (128, TC), BF16) for i in range(2)]
    t_dA = [sbM(f"t_dA{i}", (128, NGRP * TC)) for i in range(2)]
    t_mi = [sbM(f"t_mi{i}", (128, NGRP * TC), BF16) for i in range(2)]
    t_v = [sbM(f"t_v{i}", (128, NGRP * TC), BF16) for i in range(2)]
    t_h = [sbM(f"t_h{i}", (128, NGRP * TC), BF16) for i in range(2)]
    t_yAB = sbM("t_yAB", (128, 2 * TC), BF16)
    t_hC2 = sbM("t_hC2", (128, 2 * TC), BF16)
    t_xc = [sbM(f"t_xc{i}", (128, TC)) for i in range(2)]
    st_v = sbM("st_v", (128, NG * NST))
    st_h = sbM("st_h", (128, NG * NST))

    pb = [stP.enter_context(nc.psum_tensor(f"pb{i}", [128, 512], F32)) for i in range(8)]

    beta_b = c_const[:, 4:5].broadcast_to((128, TC))
    rn_bf = rn_sep

    def dma(dst, src, q="dms", reads=(), writes=()):
        eng = "sync" if q == "dms" else "gpsimd"
        sch.op(eng, lambda e, d=dst, s=src: e.dma_start(out=d, in_=s),
               reads=reads, writes=writes, dma=q)

    dma(c_ones[:], dram["ones_col"], writes=["c_ones"])
    dma(c_onesr[:], dram["ones_row"], writes=["c_onesr"])
    dma(c_sel[:], dram["sel32"], writes=["c_sel"])
    dma(c_const[:], dram["consts"], writes=["c_const"])
    dma(x2_st[0:1, 0:NF], dram["b1r"], writes=["x2_st"])
    sch.op("vector", lambda e: e.tensor_copy(r_b1q[:], x2_st[0:1, 0:NF]),
           reads=["x2_st"], writes=["r_b1q"])
    dma(x2_st[0:1, 0:D], dram["b2r"], writes=["x2_st"])
    sch.op("vector", lambda e: e.tensor_copy(r_b2q[:], x2_st[0:1, 0:D]),
           reads=["x2_st"], writes=["r_b2q"])
    dma(x2_st[0:1, 0:TC], dram["onesr"], writes=["x2_st"])
    sch.op("vector", lambda e: e.tensor_copy(r_ones8[:], x2_st[0:1, 0:TC]),
           reads=["x2_st"], writes=["r_ones8"])
    for l in range(NL):
        dma(x2_st[0:RK, 0:E], dram[f"dtwT{l}"], writes=["x2_st"])
        sch.op("vector", lambda e, l=l: e.tensor_copy(
            r_dtw[:, l * E:(l + 1) * E], x2_st[0:RK, 0:E]),
            reads=["x2_st"], writes=["r_dtw"])
        dma(x2_st[:, 0:NG * 96], dram[f"xpw_a{l}"], writes=["x2_st"])
        sch.op("vector", lambda e, l=l: e.tensor_copy(
            r_xpw[:, l * NG * 96:(l + 1) * NG * 96], x2_st[:, 0:NG * 96]),
            reads=["x2_st"], writes=["r_xpw"])
        dma(x2_st[:, 0:NG * NST], dram[f"alog_a{l}"], writes=["x2_st"])
        sch.op("scalar", lambda e: e.activation(a_st[:, 0:NG * NST], x2_st[:, 0:NG * NST], AF.Exp),
               reads=["x2_st"], writes=["a_st"])
        sch.op("vector", lambda e, l=l: e.tensor_single_scalar(
            r_wneg[:, l * NG * NST:(l + 1) * NG * NST], a_st[:, 0:NG * NST], -1.0, OP.mult),
            reads=["a_st"], writes=["r_wneg"])
        for nm, buf in (("dtb_a", r_dtb), ("cb_a", r_cb), ("dp_a", r_dp)):
            dma(buf[:, l * NG:(l + 1) * NG], dram[f"{nm}{l}"], writes=[f"r_{nm}"])
        dma(r_cw[:, l * NG * 4:(l + 1) * NG * 4], dram[f"cw_a{l}"], writes=["r_cw"])
        dma(r_nw[:, l * NDT:(l + 1) * NDT], dram[f"nw_a{l}"], writes=["r_nw"])

    prep_i = [0]

    def prep_cast(src_ap, dst_ap, rows, cols, outdt, skey):
        for p in range(rows // 128):
            rs = p * 128
            for cs in range(0, cols, 1024):
                cw = min(1024, cols - cs)
                i = prep_i[0] % 4
                j = prep_i[0] % 4
                prep_i[0] += 1
                pin = x2_st[:, i * 1024:i * 1024 + cw]
                kin = f"x2h{i}"
                dma(pin, src_ap[rs:rs + 128, cs:cs + cw], writes=[kin])
                if outdt == BF16:
                    pout, kout = wS[j][:, 0:cw], f"wS{j}"
                else:
                    pout, kout = pg_f8[i % 2][:, 0:cw], f"pg_f8{i % 2}"
                sch.op("gpsimd", lambda e, pout=pout, pin=pin: e.tensor_copy(pout, pin),
                       reads=[kin], writes=[kout])
                dma(dst_ap[rs:rs + 128, cs:cs + cw], pout, q="dmg",
                    reads=[kout], writes=[skey])


    def chunk_view(t, c):
        return dram[t][:, c * TC:(c + 1) * TC].rearrange("(g p) t -> p g t", p=128)

    def stg3(buf):
        return buf[:].rearrange("p (g t) -> p g t", t=TC)

    abuf = [a_st, x2_st]

    def ln_chunk(c):
        par = 0
        A = abuf[par]
        ak = f"ab{par}"
        dma(stg3(A), chunk_view("x_t", c), writes=[ak])
        for dt in range(NDT):
            i = dt % 2
            sch.op("scalar", lambda e, dt=dt, A=A, i=i: e.square(
                t_xc[i][:], A[:, dt * TC:(dt + 1) * TC]),
                reads=[ak], writes=[f"xc{i}"])
            sch.op("tensor", lambda e, dt=dt, i=i: e.matmul(
                pb[0][0:1, :], c_ones[:], A[:, dt * TC:(dt + 1) * TC],
                start=(dt == 0), stop=(dt == NDT - 1)),
                reads=[ak, "c_ones"], writes=["pb0"])
            sch.op("tensor", lambda e, dt=dt, i=i: e.matmul(
                pb[1][0:1, :], c_ones[:], t_xc[i][:],
                start=(dt == 0), stop=(dt == NDT - 1)),
                reads=[f"xc{i}", "c_ones"], writes=["pb1"])
        sch.op("vector", lambda e: e.tensor_single_scalar(row_m, pb[0][0:1, :], 1.0 / D, OP.mult),
               reads=["pb0"], writes=["row_m"])
        sch.op("vector", lambda e: e.tensor_single_scalar(row_e2, pb[1][0:1, :], 1.0 / D, OP.mult),
               reads=["pb1"], writes=["row_e2"])
        sch.op("vector", lambda e: e.tensor_mul(row_rstd, row_m, row_m),
               reads=["row_m"], writes=["row_rstd"])
        sch.op("vector", lambda e: e.tensor_sub(row_e2, row_e2, row_rstd),
               reads=["row_rstd", "row_e2"], writes=["row_e2"])
        sch.op("scalar", lambda e: e.activation(row_rstd, row_e2, AF.Ln, bias=1e-5),
               reads=["row_e2"], writes=["row_rstd"])
        sch.op("scalar", lambda e: e.activation(row_rstd, row_rstd, AF.Exp, scale=-0.5),
               reads=["row_rstd"], writes=["row_rstd"])
        sch.op("vector", lambda e: e.tensor_mul(row_rs2, row_m, row_rstd),
               reads=["row_m", "row_rstd", "row_e2"], writes=["row_rs2"])
        sch.op("tensor", lambda e: e.matmul(pb[2][:], c_onesr[:], row_rstd,
                                            start=True, stop=True),
               reads=["row_rstd", "c_onesr"], writes=["pb2"])
        sch.op("tensor", lambda e: e.matmul(pb[3][:], c_onesr[:], row_rs2,
                                            start=True, stop=True),
               reads=["row_rs2", "c_onesr"], writes=["pb3"])
        sch.op("vector", lambda e: e.tensor_copy(rstd_rep[:], pb[2][:]),
               reads=["pb2"], writes=["rstd_rep"])
        sch.op("vector", lambda e: e.tensor_copy(rs2_rep[:], pb[3][:]),
               reads=["pb3"], writes=["rs2_rep"])
        rb = rstd_rep[:].unsqueeze(1).broadcast_to((128, NDT, TC))
        s2 = rs2_rep[:].unsqueeze(1).broadcast_to((128, NDT, TC))
        av = A[:].rearrange("p (g t) -> p g t", g=NDT)
        sch.op("vector", lambda e, av=av, rb=rb: e.tensor_tensor(av, av, rb, OP.mult),
               reads=["rstd_rep", ak], writes=[ak])
        sch.op("vector", lambda e, av=av, s2=s2: e.tensor_tensor(av, av, s2, OP.subtract),
               reads=["rs2_rep", ak], writes=[ak])
        dma(chunk_view("s_a", c), stg3(A), q="dmg", reads=[ak], writes=[f"s_a{c}"])
        dma(chunk_view("s_xn", c), stg3(A), q="dmg", reads=[ak], writes=["s_xn"])
        if DBG:
            dma(dbg_xn[:, c * TC:(c + 1) * TC].rearrange("(g p) t -> p g t", p=128),
                stg3(A), q="dmg", reads=[ak], writes=["dbg"])

    ln_chunk(0)
    prep_cast(dram["owT0"], dram["s_owT0"], E, D, BF16, "scr_ow0")
    ln_chunk(1)
    ln_chunk(2)
    prep_cast(dram["owT1"], dram["s_owT1"], E, D, BF16, "scr_ow1")
    ln_chunk(3)

    def head_ops(l, c):
        par = c % 2
        A = abuf[par]
        ak = f"ab{par}"
        th = []

        def T(engine, fn, reads=(), writes=(), dma_q=None, ms=0):
            th.append((engine, fn, list(reads), list(writes), dma_q, ms))

        if c == 0:
            T("sync", lambda e: e.dma_start(out=halo[:], in_=dram["zeros48"]),
              [], ["halo"], "dms")
        T("sync", lambda e, A=A, c=c: e.dma_start(out=stg3(A), in_=chunk_view("s_a", c)),
          [f"s_a{c}", "x2h0", "x2h1", "x2h2", "x2h3"] if par == 1 else [f"s_a{c}"], [ak], "dms")
        for dt in range(NDT):
            i = dt % 2
            T("scalar", lambda e, dt=dt, A=A, i=i: e.square(
                t_xc[i][:], A[:, dt * TC:(dt + 1) * TC]), [ak], [f"xc{i}"])
            T("tensor", lambda e, dt=dt, i=i: e.matmul(
                pb[6][0:1, :], c_ones[:], t_xc[i][:],
                start=(dt == 0), stop=(dt == NDT - 1)),
                [f"xc{i}", "c_ones"], ["pb6"])
        T("scalar", lambda e: e.activation(row_rstd, pb[6][0:1, :], AF.Ln,
                                           scale=1.0 / D, bias=1e-5),
          ["pb6"], ["row_rstd"])
        T("scalar", lambda e: e.activation(row_rstd, row_rstd, AF.Exp, scale=-0.5),
          ["row_rstd"], ["row_rstd"])
        T("tensor", lambda e: e.matmul(pb[6][:], c_onesr[:], row_rstd,
                                       start=True, stop=True),
          ["row_rstd", "c_onesr"], ["pb6"])
        T("vector", lambda e: e.tensor_copy(rstd_rep[:], pb[6][:]),
          ["pb6"], ["rstd_rep"])
        for dt in range(NDT):
            T("vector", lambda e, dt=dt, l=l, A=A: e.scalar_tensor_tensor(
                rn_bf[:, dt * TC:(dt + 1) * TC], A[:, dt * TC:(dt + 1) * TC],
                r_nw[:, l * NDT + dt:l * NDT + dt + 1], rstd_rep[:], OP.mult, OP.mult),
                [ak, "rstd_rep", "r_nw"], ["rn_bf"])
        xbv3 = xbp[:].rearrange("p (g t) -> p g t", g=NG)
        T("vector", lambda e, xbv3=xbv3: e.tensor_copy(
            xbv3[:, :, 0:3], halo[:].rearrange("p (g t) -> p g t", g=NG)),
            ["halo"], ["xbp_pre"])
        for f0, fn_ in [(0, 6), (6, 6), (12, 4)]:
            for half in range(2):
                gate99 = (f0 == 12)
                def wsrc(d, half=half, f0=f0, fn_=fn_, l=l):
                    return dram[f"in_wT{l}"][d * 128:(d + 1) * 128,
                                             (half * 16 + f0) * 128:(half * 16 + f0 + fn_) * 128]
                for d in range(4):
                    wi = d % 4
                    T("gpsimd", lambda e, d=d, wi=wi, wsrc=wsrc, fn_=fn_: e.dma_start(
                        out=wS[wi][:, 0:fn_ * 128], in_=wsrc(d)),
                        [], [f"wS{wi}"], "dmg", ms=(99 if gate99 else 0))
                for d in range(NDT):
                    wi = d % 4
                    for fi in range(fn_):
                        T("tensor", lambda e, d=d, fi=fi, wi=wi: e.matmul(
                            pb[fi][:], wS[wi][:, fi * 128:(fi + 1) * 128],
                            rn_bf[:, d * TC:(d + 1) * TC],
                            start=(d == 0), stop=(d == NDT - 1)),
                            [f"wS{wi}", "rn_bf"], [f"pb{fi}"])
                    if d + 4 < NDT:
                        wi2 = (d + 4) % 4
                        T("gpsimd", lambda e, d=d, wi2=wi2, wsrc=wsrc, fn_=fn_: e.dma_start(
                            out=wS[wi2][:, 0:fn_ * 128], in_=wsrc(d + 4)),
                            [], [f"wS{wi2}"], "dmg")
                for fi in range(fn_):
                    g = f0 + fi
                    if half == 0:
                        T("scalar", lambda e, g=g, fi=fi: e.copy(
                            xbp[:, g * (TC + 3) + 3:(g + 1) * (TC + 3)], pb[fi][:]),
                            [f"pb{fi}"], [f"xbp{g}"], ms=g)
                    else:
                        T("scalar", lambda e, g=g, fi=fi: e.activation(
                            zg_bf[:, g * TC:(g + 1) * TC], pb[fi][:], AF.Silu),
                            [f"pb{fi}"], [f"zg{g}"], ms=g)
        T("vector", lambda e, xbv3=xbv3: e.tensor_copy(
            halo[:].rearrange("p (g t) -> p g t", g=NG), xbv3[:, :, TC:TC + 3]),
            [f"xbp{g}" for g in range(NG)] + ["xbp_pre"], ["halo"])
        for g in range(NG):
            xg = xbp[:, g * (TC + 3):(g + 1) * (TC + 3)]
            cwb = l * NG * 4 + g * 4
            T("vector", lambda e, xg=xg, cwb=cwb: e.tensor_scalar(
                t_xc[0][:], xg[:, 0:TC], r_cw[:, cwb:cwb + 1], None, OP.mult),
                [f"xbp{g}", "xbp_pre", "halo", "r_cw"], ["xc0"])
            for k in (1, 2, 3):
                si, so = (0, 1) if k % 2 == 1 else (1, 0)
                T("vector", lambda e, xg=xg, cwb=cwb, k=k, si=si, so=so:
                  e.scalar_tensor_tensor(
                      t_xc[so][:], xg[:, k:TC + k], r_cw[:, cwb + k:cwb + k + 1],
                      t_xc[si][:], OP.mult, OP.add),
                  [f"xc{si}", f"xbp{g}", "r_cw"], [f"xc{so}"])
            T("scalar", lambda e, xg=xg, g=g, l=l: e.activation(
                xg[:, 3:TC + 3], t_xc[1][:], AF.Silu,
                bias=r_cb[:, l * NG + g:l * NG + g + 1]),
                ["xc1", "r_cb"], [f"xbp{g}"], ms=g)
            T("tensor", lambda e, g=g, l=l: e.matmul(
                pb[6][0:96, :], r_xpw[:, (l * NG + g) * 96:(l * NG + g + 1) * 96],
                xbp[:, g * (TC + 3) + 3:(g + 1) * (TC + 3)],
                start=(g == 0), stop=(g == NG - 1)),
                [f"xbp{g}", "r_xpw"], ["pb6"], ms=g)
        T("vector", lambda e: e.tensor_copy(dbc_sb[:], pb[6][0:96, :]),
          ["pb6"], ["dbc_sb"])
        T("vector", lambda e: e.tensor_copy(drbf[:], dbc_sb[0:RK, :]),
          ["dbc_sb"], ["drbf"])
        T("gpsimd", lambda e: e.dma_start(out=bc_bf[:], in_=dbc_sb[RK:96, :]),
          ["dbc_sb"], ["bc_bf"], "dmg")
        for jj in range(32):
            nn, isC = jj // 2, jj % 2
            j = nn + 16 * isC
            bank = 6 + (jj % 2)
            T("tensor", lambda e, j=j, bank=bank: e.matmul(
                pb[bank][:], c_sel[:, j:j + 1].broadcast_to((32, 128)), bc_bf[:],
                start=True, stop=True),
                ["bc_bf", "c_sel"], [f"pb{bank}"])
            dstb, dkey = (brep, f"brep{nn}") if not isC else (crep, f"crep{nn}")
            T("scalar", lambda e, nn=nn, bank=bank, dstb=dstb: e.copy(
                dstb[:, nn * TC:(nn + 1) * TC], pb[bank][:]),
                [f"pb{bank}"], [dkey])
        return th

    def emit_thunks(th):
        for engine, fn, reads, writes, dma_q, ms in th:
            sch.op(engine, fn, reads=reads, writes=writes, dma=dma_q)

    def emit_gated(th, pos, g):
        while pos < len(th) and th[pos][5] <= g:
            engine, fn, reads, writes, dma_q, ms = th[pos]
            sch.op(engine, fn, reads=reads, writes=writes, dma=dma_q)
            pos += 1
        return pos

    seq = [(l, c) for l in range(NL) for c in range(NCH)]
    pend = head_ops(0, 0)
    emit_thunks(pend)

    for idx, (l, c) in enumerate(seq):
        par = c % 2
        A = abuf[par]
        ak = f"ab{par}"
        nxt = head_ops(*seq[idx + 1]) if idx + 1 < len(seq) else []
        if idx + 1 < len(seq) and seq[idx + 1][1] == 0:
            pass
        nxt_pos = 0
        def dt_chain(g):
            sl = g % 2
            gl = l * NG + g
            sch.op("tensor", lambda e, g=g, l=l: e.matmul(
                pb[7][:], r_dtw[:, l * E + g * 128:l * E + (g + 1) * 128], drbf[:],
                start=True, stop=True),
                reads=["drbf", "r_dtw"], writes=["pb7"])
            sch.op("scalar", lambda e, gl=gl: e.activation(
                t_sp[:], pb[7][:], AF.Abs, bias=r_dtb[:, gl:gl + 1]),
                reads=["pb7", "r_dtb"], writes=["tsp"])
            sch.op("scalar", lambda e: e.activation(t_sp[:], t_sp[:], AF.Exp, scale=-1.0),
                   reads=["tsp"], writes=["tsp"])
            sch.op("scalar", lambda e: e.activation(t_sp[:], t_sp[:], AF.Ln, bias=1.0),
                   reads=["tsp"], writes=["tsp"])
            sch.op("scalar", lambda e, gl=gl, sl=sl: e.activation(
                t_delta[sl][:], pb[7][:], AF.Relu, bias=r_dtb[:, gl:gl + 1]),
                reads=["pb7", "r_dtb"], writes=[f"tdelta{sl}"])
            sch.op("vector", lambda e, sl=sl: e.tensor_add(
                t_delta[sl][:], t_delta[sl][:], t_sp[:]),
                reads=["tsp", f"tdelta{sl}"], writes=[f"tdelta{sl}"])
            sch.op("vector", lambda e, g=g, sl=sl: e.tensor_tensor(
                t_u[sl][:], t_delta[sl][:], xbp[:, g * (TC + 3) + 3:(g + 1) * (TC + 3)],
                OP.mult),
                reads=[f"tdelta{sl}", f"xbp{g}"], writes=[f"tu{sl}"])

        dt_chain(0)
        for g in range(NG):
            sl = g % 2
            gl = l * NG + g
            if g + 1 < NG:
                dt_chain(g + 1)
            for ng in range(NST // NGRP):
                bsl = ng % 2
                n0 = ng * NGRP
                for k in range(NGRP):
                    n = ng * NGRP + k
                    sch.op("scalar", lambda e, n=n, k=k, gl=gl, sl=sl, bsl=bsl: e.activation(
                        t_dA[bsl][:, k * TC:(k + 1) * TC], t_delta[sl][:], AF.Exp,
                        scale=r_wneg[:, gl * NST + n:gl * NST + n + 1]),
                        reads=[f"tdelta{sl}", "r_wneg"], writes=[f"tdA{bsl}"])
                sch.op("gpsimd", lambda e, bsl=bsl, sl=sl, n0=n0: e.tensor_tensor(
                    t_mi[bsl][:].rearrange("p (k t) -> p k t", k=NGRP),
                    t_u[sl][:].unsqueeze(1).broadcast_to((128, NGRP, TC)),
                    crep[:, n0 * TC:(n0 + 2) * TC].rearrange("p (k t) -> p k t", k=NGRP)
                    if False else
                    brep[:, n0 * TC:(n0 + 2) * TC].rearrange("p (k t) -> p k t", k=NGRP),
                    OP.mult),
                    reads=[f"tu{sl}", f"brep{n0}", f"brep{n0 + 1}"], writes=[f"tmi{bsl}"])
                for k in range(NGRP):
                    n = ng * NGRP + k
                    stc = g * NST + n
                    iv = 0.0 if c == 0 else st_v[:, stc:stc + 1]
                    ih = 0.0 if c == 0 else st_h[:, stc:stc + 1]
                    sch.op("vector", lambda e, k=k, bsl=bsl, iv=iv: e.tensor_tensor_scan(
                        t_v[bsl][:, k * TC:(k + 1) * TC], beta_b,
                        t_mi[bsl][:, k * TC:(k + 1) * TC], iv, OP.mult, OP.add),
                        reads=[f"tmi{bsl}", "st_v", "c_const"], writes=[f"tv{bsl}"])
                    sch.op("vector", lambda e, k=k, bsl=bsl, ih=ih: e.tensor_tensor_scan(
                        t_h[bsl][:, k * TC:(k + 1) * TC],
                        t_dA[bsl][:, k * TC:(k + 1) * TC],
                        t_v[bsl][:, k * TC:(k + 1) * TC], ih, OP.mult, OP.add),
                        reads=[f"tdA{bsl}", f"tv{bsl}", "st_h"], writes=[f"th{bsl}"])
                if c < NCH - 1:
                    st0 = g * NST + ng * NGRP
                    sch.op("vector", lambda e, bsl=bsl, st0=st0: e.tensor_copy(
                        st_v[:, st0:st0 + NGRP],
                        t_v[bsl][:, TC - 1:NGRP * TC:TC]),
                        reads=[f"tv{bsl}"], writes=["st_v"])
                    sch.op("vector", lambda e, bsl=bsl, st0=st0: e.tensor_copy(
                        st_h[:, st0:st0 + NGRP],
                        t_h[bsl][:, TC - 1:NGRP * TC:TC]),
                        reads=[f"th{bsl}"], writes=["st_h"])
                n0 = ng * NGRP
                if ng == 0:
                    sch.op("vector", lambda e, bsl=bsl, n0=n0: e.tensor_tensor(
                        t_yAB[:], t_h[bsl][:],
                        crep[:, n0 * TC:(n0 + 2) * TC], OP.mult),
                        reads=[f"th{bsl}", f"crep{n0}", f"crep{n0 + 1}"],
                        writes=["tyA", "tyB"])
                else:
                    sch.op("vector", lambda e, bsl=bsl, n0=n0: e.tensor_tensor(
                        t_hC2[:], t_h[bsl][:],
                        crep[:, n0 * TC:(n0 + 2) * TC], OP.mult),
                        reads=[f"th{bsl}", f"crep{n0}", f"crep{n0 + 1}"],
                        writes=["thC"])
                    sch.op("vector", lambda e: e.tensor_add(
                        t_yAB[:], t_yAB[:], t_hC2[:]),
                        reads=["thC", "tyA", "tyB"], writes=["tyA", "tyB"])
            sch.op("vector", lambda e, g=g, gl=gl: e.scalar_tensor_tensor(
                t_xc[0][:], xbp[:, g * (TC + 3) + 3:(g + 1) * (TC + 3)],
                r_dp[:, gl:gl + 1], t_yAB[:, 0:TC], OP.mult, OP.add),
                reads=[f"xbp{g}", "tyA", "r_dp"], writes=["xc0"])
            sch.op("vector", lambda e: e.tensor_add(t_xc[0][:], t_xc[0][:], t_yAB[:, TC:2 * TC]),
                   reads=["tyB", "xc0"], writes=["xc0"])
            sch.op("vector", lambda e, g=g: e.tensor_tensor(
                rny2[:, g * TC:(g + 1) * TC], t_xc[0][:],
                zg_bf[:, g * TC:(g + 1) * TC], OP.mult),
                reads=["xc0", f"zg{g}"], writes=[f"y2{g}"])
            if nxt and PIPE:
                nxt_pos = emit_gated(nxt, nxt_pos, g)
        for g in range(NG):
            wbuf, key = wS[g % 4], f"wS{g % 4}"
            dma(wbuf[:, 0:1024], dram[f"s_owT{l}"][g * 128:(g + 1) * 128, :],
                reads=[f"scr_ow{l}"], writes=[key])
            for d in range(NDT):
                sch.op("tensor", lambda e, g=g, d=d, wbuf=wbuf: e.matmul(
                    pb[d][:], wbuf[:, d * 128:(d + 1) * 128],
                    rny2[:, g * TC:(g + 1) * TC],
                    start=(g == 0), stop=(g == NG - 1)),
                    reads=[key, f"y2{g}"], writes=[f"pb{d}"])
        for d in range(NDT):
            sch.op("vector", lambda e, d=d, A=A: e.tensor_add(
                A[:, d * TC:(d + 1) * TC], A[:, d * TC:(d + 1) * TC], pb[d][:]),
                reads=[f"pb{d}"], writes=[ak])
        dma(chunk_view("s_a", c), stg3(A), q="dmg", reads=[ak], writes=[f"s_a{c}"])
        if DBG:
            dma(dbg_a[l][:, c * TC:(c + 1) * TC].rearrange("(g p) t -> p g t", p=128),
                stg3(A), q="dmg", reads=[ak], writes=["dbg"])
        while nxt_pos < len(nxt):
            e_, f_, r_, w_, d_, m_ = nxt[nxt_pos]
            sch.op(e_, f_, reads=r_, writes=w_, dma=d_)
            nxt_pos += 1

    sch.barrier()
    stM.close()
    stF = ExitStack()

    def sbF(name, shape, dt=F32):
        return stF.enter_context(nc.sbuf_tensor(name, list(shape), dt))

    f_w1 = sbF("f_w1", (128, 8 * 4096), F8)
    f_w2 = sbF("f_w2", (128, 32 * 1024), F8)
    f_h8_ = [sbF(f"f_h8_{i}", (128, 32 * TC), F8) for i in range(2)]
    f_hb_ = [sbF(f"f_hb{i}", (128, TC), BF16) for i in range(2)]
    f_fb = sbF("f_fb", (128, NDT * TC), BF16)
    f_msk = sbF("f_msk", (128, NDT * TC), BF16)
    f_q8full = sbF("f_q8full", (128, NDT * L), F8)

    for d in range(NDT):
        dma(f_w1[:, d * 4096:(d + 1) * 4096], dram["W1T"][d * 128:(d + 1) * 128, :],
            q="dmg", reads=[], writes=["f_w1"])
    for f in range(32):
        dma(f_w2[:, f * 1024:(f + 1) * 1024], dram["W2T"][f * 128:(f + 1) * 128, :],
            q="dmg", reads=[], writes=["f_w2"])
    for c in range(NCH):
        dma(stg3(x2_st), chunk_view("s_xn", c),
            reads=["s_xn", "x2h0", "x2h1", "x2h2", "x2h3"] if c == 0 else ["s_xn"],
            writes=["ab1"])
        sch.op("vector", lambda e, c=c: e.tensor_copy(
            f_q8full[:].rearrange("p (g t) -> p g t", t=L)[:, :, c * TC:(c + 1) * TC],
            stg3(x2_st)),
            reads=["ab1"], writes=[f"fq8c{c}"])
    for c in range(NCH):
        fpar = c % 2
        f_h8 = f_h8_[fpar]
        kh8 = f"f_h8{fpar}"
        dma(stg3(f_msk), chunk_view("mask_t", c), writes=["f_msk"])
        dma(stg3(a_st), chunk_view("s_a", c), reads=[f"s_a{c}"], writes=["ab0"])
        dma(stg3(x2_st), chunk_view("x_t", c), writes=["ab1"])
        for fg in range(8):
            for f4i in range(4):
                f = fg * 4 + f4i
                bk = 4 + f4i
                for d in range(NDT):
                    sch.op("tensor", lambda e, f=f, d=d, bk=bk, c=c: e.matmul(
                        pb[bk][:], f_w1[:, d * 4096 + f * 128:d * 4096 + (f + 1) * 128],
                        f_q8full[:, d * L + c * TC:d * L + (c + 1) * TC],
                        start=(d == 0), stop=False),
                        reads=["f_w1", f"fq8c{c}"], writes=[f"pb{bk}"])
                sch.op("tensor", lambda e, f=f, bk=bk: e.matmul(
                    pb[bk][:], r_b1q[0:1, f * 128:(f + 1) * 128], r_ones8[:],
                    start=False, stop=True),
                    reads=["r_b1q", "r_ones8"], writes=[f"pb{bk}"])
            for f4i in range(4):
                f = fg * 4 + f4i
                bk = 4 + f4i
                hb = f_hb_[f4i % 2]
                hbk = f"f_hb{f4i % 2}"
                sch.op("scalar", lambda e, bk=bk, hb=hb: e.copy(hb[:], pb[bk][:]),
                       reads=[f"pb{bk}"], writes=[hbk])
                sch.op("vector", lambda e, hb=hb: e.tensor_relu(hb[:], hb[:]),
                       reads=[hbk], writes=[hbk])
                sch.op("vector", lambda e, f=f, f_h8=f_h8, hb=hb: e.tensor_copy(
                    f_h8[:, f * TC:(f + 1) * TC], hb[:]),
                    reads=[hbk], writes=[kh8])
        for d in range(NDT):
            bk = d % 4
            for f in range(32):
                sch.op("tensor", lambda e, d=d, f=f, bk=bk, f_h8=f_h8: e.matmul(
                    pb[bk][:], f_w2[:, f * 1024 + d * 128:f * 1024 + (d + 1) * 128],
                    f_h8[:, f * TC:(f + 1) * TC],
                    start=(f == 0), stop=False),
                    reads=["f_w2", kh8], writes=[f"pb{bk}"])
            sch.op("tensor", lambda e, d=d, bk=bk: e.matmul(
                pb[bk][:], r_b2q[0:1, d * 128:(d + 1) * 128], r_ones8[:],
                start=False, stop=True),
                reads=["r_b2q", "r_ones8"], writes=[f"pb{bk}"])
            sch.op("scalar", lambda e, d=d, bk=bk: e.copy(
                f_fb[:, d * TC:(d + 1) * TC], pb[bk][:]),
                reads=[f"pb{bk}"], writes=["f_fb"])
        sch.op("vector", lambda e: e.tensor_add(a_st[:], a_st[:], x2_st[:]),
               reads=["ab1", "ab0"], writes=["ab0"])
        sch.op("vector", lambda e: e.tensor_mul(x2_st[:], f_fb[:], f_msk[:]),
               reads=["f_fb", "f_msk", "ab1"], writes=["ab1"])
        sch.op("vector", lambda e: e.tensor_add(a_st[:], a_st[:], x2_st[:]),
               reads=["ab1", "ab0"], writes=["ab0"])
        dma(out_t[:, c * TC:(c + 1) * TC].rearrange("(g p) t -> p g t", p=128),
            stg3(a_st), reads=["ab0"], writes=["out"])

    with nc.Block() as block:
        sch.emit(nc, block)
    stF.close()
    stP.close()
    return nc


_PROG = None


def _get_prog():
    global _PROG
    if _PROG is None:
        _PROG = build_program()
    return _PROG


def kernel(L0_attn_x, L0_attn_norm_w, L0_attn_in_proj_w, L0_attn_conv_w,
           L0_attn_conv_b, L0_attn_x_proj_w, L0_attn_dt_w, L0_attn_dt_b,
           L0_attn_A_log, L0_attn_Dp, L0_attn_out_w, L0_ffn_W1, L0_ffn_b1,
           L0_ffn_W2, L0_ffn_b2):
    import jax
    x = np.ascontiguousarray(np.asarray(L0_attn_x, np.float32))
    B = x.shape[0]
    with jax.default_device(jax.devices("cpu")[0]):
        jax.config.update("jax_threefry_partitionable", True)
        keep = np.asarray(jax.random.bernoulli(
            jax.random.key(42, impl="threefry2x32"), 0.8, (8, 2048, 1024)))
    mask = (keep.astype(np.float32) * np.float32(1.25)).astype(BF_NP)

    def arr(a):
        return np.ascontiguousarray(np.asarray(a, np.float32))

    common = {}
    for l in range(NL):
        common[f"in_wT{l}"] = arr(np.asarray(L0_attn_in_proj_w)[l].T)
        common[f"owT{l}"] = arr(np.asarray(L0_attn_out_w)[l].T)
        xpw = arr(np.asarray(L0_attn_x_proj_w)[l].T)
        common[f"xpw_a{l}"] = np.ascontiguousarray(
            xpw.reshape(NG, 128, 96).transpose(1, 0, 2).reshape(128, NG * 96))
        common[f"dtwT{l}"] = arr(np.asarray(L0_attn_dt_w)[l].T)
        al = arr(np.asarray(L0_attn_A_log)[l])
        common[f"alog_a{l}"] = np.ascontiguousarray(
            al.reshape(NG, 128, NST).transpose(1, 0, 2).reshape(128, NG * NST))
        for nm, src in (("dtb_a", L0_attn_dt_b), ("cb_a", L0_attn_conv_b),
                        ("dp_a", L0_attn_Dp)):
            v = arr(np.asarray(src)[l])
            common[f"{nm}{l}"] = np.ascontiguousarray(v.reshape(NG, 128).T)
        cw = arr(np.asarray(L0_attn_conv_w)[l])
        common[f"cw_a{l}"] = np.ascontiguousarray(
            cw.reshape(NG, 128, 4).transpose(1, 0, 2).reshape(128, NG * 4))
        nw = arr(np.asarray(L0_attn_norm_w)[l])
        common[f"nw_a{l}"] = np.ascontiguousarray(nw.reshape(NDT, 128).T)
    common["W1T"] = arr(np.asarray(L0_ffn_W1).T)
    common["W2T"] = arr(np.asarray(L0_ffn_W2).T)
    common["b1r"] = arr(L0_ffn_b1).reshape(1, NF)
    common["b2r"] = arr(L0_ffn_b2).reshape(1, D)
    common["onesr"] = np.ones((1, TC), np.float32)
    common["ones_col"] = np.ones((128, 1), np.float32)
    common["ones_row"] = np.ones((1, 128), np.float32)
    common["sel32"] = np.eye(32, dtype=np.float32).astype(BF_NP)
    consts = np.zeros((128, 8), np.float32)
    consts[:, 0] = -1.0
    consts[:, 1] = 1.0
    consts[:, 2] = -0.5
    consts[:, 3] = 1e-5
    consts[:, 4] = BETA
    consts[:, 5] = 1.0 / D
    common["consts"] = consts
    common["zeros48"] = np.zeros((128, 48), BF_NP)

    in_maps = []
    for b in range(B):
        m = dict(common)
        m["x_t"] = np.ascontiguousarray(x[b].T)
        m["mask_t"] = np.ascontiguousarray(mask[b].T)
        in_maps.append(m)

    nc = _get_prog()
    res = run_bass_kernel_spmd(nc, in_maps, core_ids=list(range(B)))
    out = np.stack([np.asarray(r["out_t"]).T for r in res.results], axis=0)
    return out.astype(np.float32)


# version 26
# speedup vs baseline: 1.1013x; 1.0039x over previous
import numpy as np
import ml_dtypes
import concourse.bass as bass
import concourse.mybir as mybir
from concourse.bass_utils import run_bass_kernel_spmd

F32 = mybir.dt.float32
BF16 = mybir.dt.bfloat16
F8 = mybir.dt.float8e5
OP = mybir.AluOpType
AF = mybir.ActivationFunctionType

L, D, E, NST, RK = 2048, 1024, 2048, 16, 64
TC = 512
NCH = L // TC
NG = E // 128
NDT = D // 128
NF = 4096
NL = 2
BETA = 0.6
NGRP = 2
PIPE = True

BF_NP = ml_dtypes.bfloat16


class Sched:
    ENG = ("sync", "vector", "scalar", "tensor", "gpsimd")

    def __init__(self):
        self.ops = []
        self.bw = {}
        self.br = {}
        self.last_on = {}
        self.last_dma = {}

    def op(self, engine, fn, reads=(), writes=(), dma=None):
        opid = len(self.ops)
        deps = set()
        for b in reads:
            w = self.bw.get(b)
            if w is not None:
                deps.add(w)
        for b in writes:
            deps.update(self.br.get(b, ()))
            w = self.bw.get(b)
            if w is not None:
                deps.add(w)
        deps.discard(opid)
        self.ops.append({"e": engine, "fn": fn, "deps": deps, "dma": dma})
        for b in reads:
            self.br.setdefault(b, []).append(opid)
        for b in writes:
            self.bw[b] = opid
            self.br[b] = []
        self.last_on[engine] = opid
        if dma:
            self.last_dma[dma] = opid
        return opid

    def barrier(self):
        lasts = list(self.last_on.values()) + list(self.last_dma.values())
        for e in self.ENG:
            opid = len(self.ops)
            self.ops.append({"e": e, "fn": None,
                             "deps": set(d for d in lasts if d != opid),
                             "dma": None})
            self.last_on[e] = opid

    def emit(self, nc, block):
        marked = set()
        for i, rec in enumerate(self.ops):
            if rec["dma"]:
                marked.add(i)
            me = rec["e"]
            for d in rec["deps"]:
                if self.ops[d]["e"] != me or self.ops[d]["dma"]:
                    marked.add(d)

        def stream(rec):
            return rec["dma"] if rec["dma"] else rec["e"]

        R = 8
        counters, incno = {}, {}
        dma_slot = {}
        for i, rec in enumerate(self.ops):
            if i in marked:
                s = stream(rec)
                counters[s] = counters.get(s, 0) + 1
                incno[i] = counters[s]
                if rec["dma"]:
                    o = counters[s] - 1
                    dma_slot[i] = (s, o % R, o // R + 1)
        sems = {}
        for s in ("vector", "scalar", "tensor", "gpsimd", "sync"):
            sems[s] = nc.semaphore(f"sem_{s}").__enter__()
        qsems = {}
        for q in ("dms", "dmg"):
            qsems[q] = [nc.semaphore(f"sem_{q}{j}").__enter__() for j in range(R)]
        per_eng = {e: [] for e in self.ENG}
        for i, rec in enumerate(self.ops):
            per_eng[rec["e"]].append(i)

        def emit_engine(eng_obj, ename):
            water = {}
            for i in per_eng[ename]:
                rec = self.ops[i]
                byprod = {}
                for d in rec["deps"]:
                    drec = self.ops[d]
                    if drec["e"] == ename and drec["dma"] is None:
                        continue
                    if drec["dma"]:
                        q, j, u = dma_slot[d]
                        key = (q, j)
                        byprod[key] = max(byprod.get(key, 0), u)
                    else:
                        byprod[drec["e"]] = max(byprod.get(drec["e"], 0), incno[d])
                for ds, v in sorted(byprod.items(), key=str):
                    if water.get(ds, 0) >= v:
                        continue
                    sem = qsems[ds[0]][ds[1]] if isinstance(ds, tuple) else sems[ds]
                    mult = 16 if isinstance(ds, tuple) else 1
                    eng_obj.wait_ge(sem, v * mult)
                    water[ds] = v
                if rec["dma"]:
                    q, j, u = dma_slot[i]
                    if u > 1 and water.get((q, j), 0) < u - 1:
                        eng_obj.wait_ge(qsems[q][j], (u - 1) * 16)
                        water[(q, j)] = u - 1
                    ins = rec["fn"](eng_obj)
                    ins.then_inc(qsems[q][j], 16)
                else:
                    ins = eng_obj.nop() if rec["fn"] is None else rec["fn"](eng_obj)
                    if i in incno:
                        ins.then_inc(sems[rec["e"]], 1)

        @block.sync
        def _(sync):
            emit_engine(sync, "sync")
            for q in ("dms", "dmg"):
                tot = counters.get(q, 0)
                for j in range(R):
                    uses = tot // R + (1 if j < tot % R else 0)
                    if uses:
                        sync.wait_ge(qsems[q][j], uses * 16)

        @block.vector
        def _(v):
            emit_engine(v, "vector")

        @block.scalar
        def _(s):
            emit_engine(s, "scalar")

        @block.tensor
        def _(t):
            emit_engine(t, "tensor")

        @block.gpsimd
        def _(g):
            emit_engine(g, "gpsimd")


def build_program():
    nc = bass.Bass("TRN2", target_bir_lowering=False)
    dram = {}

    def din(name, shape, dt=F32):
        dram[name] = nc.dram_tensor(name, list(shape), dt, kind="ExternalInput").ap()
        return dram[name]

    def dscr(name, shape, dt):
        dram[name] = nc.dram_tensor(name, list(shape), dt, kind="Internal").ap()
        return dram[name]

    din("x_t", (D, L))
    din("mask_t", (D, L), BF16)
    for l in range(NL):
        din(f"in_wT{l}", (D, 2 * E))
        din(f"owT{l}", (E, D))
        din(f"xpw_a{l}", (128, NG * 96))
        din(f"dtwT{l}", (RK, E))
        din(f"alog_a{l}", (128, NG * NST))
        din(f"dtb_a{l}", (128, NG))
        din(f"cb_a{l}", (128, NG))
        din(f"dp_a{l}", (128, NG))
        din(f"cw_a{l}", (128, NG * 4))
        din(f"nw_a{l}", (128, NDT))
    din("W1T", (D, NF))
    din("W2T", (NF, D))
    din("b1r", (1, NF))
    din("b2r", (1, D))
    din("onesr", (1, TC))
    din("ones_col", (128, 1))
    din("ones_row", (1, 128))
    din("sel32", (32, 32), BF16)
    din("consts", (128, 8))
    din("zeros48", (128, 48), BF16)
    out_t = nc.dram_tensor("out_t", [D, L], F32, kind="ExternalOutput").ap()
    DBG = False
    if DBG:
        dbg_xn = nc.dram_tensor("dbg_xn", [D, L], F32, kind="ExternalOutput").ap()
        dbg_a = [nc.dram_tensor(f"dbg_a{l}", [D, L], F32, kind="ExternalOutput").ap()
                 for l in range(NL)]

    for l in range(NL):
        dscr(f"s_owT{l}", (E, D), BF16)
    dscr("s_a", (D, L), F32)
    dscr("s_xn", (D, L), F32)

    sch = Sched()
    from contextlib import ExitStack
    stP = ExitStack()
    stM = ExitStack()

    def sbP(name, shape, dt=F32):
        return stP.enter_context(nc.sbuf_tensor(name, list(shape), dt))

    def sbM(name, shape, dt=F32):
        return stM.enter_context(nc.sbuf_tensor(name, list(shape), dt))

    c_ones = sbP("c_ones", (128, 1))
    c_onesr = sbP("c_onesr", (1, 128))
    c_const = sbP("c_const", (128, 8))
    nc.const_aps.register(F32, -1.0, c_const[:, 0:1])
    nc.const_aps.register(F32, 1.0, c_const[:, 1:2])
    nc.const_aps.register(F32, -0.5, c_const[:, 2:3])
    nc.const_aps.register(F32, 1e-5, c_const[:, 3:4])
    nc.const_aps.register(F32, 1.0 / D, c_const[:, 5:6])
    r_dtw = sbP("r_dtw", (RK, NL * E), BF16)
    r_xpw = sbP("r_xpw", (128, NL * NG * 96), BF16)
    r_wneg = sbP("r_wneg", (128, NL * NG * NST))
    r_dtb = sbP("r_dtb", (128, NL * NG))
    r_cb = sbP("r_cb", (128, NL * NG))
    r_dp = sbP("r_dp", (128, NL * NG))
    r_cw = sbP("r_cw", (128, NL * NG * 4))
    r_nw = sbP("r_nw", (128, NL * NDT))
    r_b1q = sbP("r_b1q", (1, NF), F8)
    r_b2q = sbP("r_b2q", (1, D), F8)
    r_ones8 = sbP("r_ones8", (1, TC), F8)
    a_st = sbP("a_st", (128, NDT * TC))
    x2_st = sbP("x2_st", (128, NDT * TC))
    wS = [sbP(f"wS{i}", (128, 1024), BF16) for i in range(4)]
    row_m = sbP("row_m", (1, TC))[:]
    row_e2 = sbP("row_e2", (1, TC))[:]
    row_rstd = sbP("row_rstd", (1, TC))[:]
    row_rs2 = row_e2
    rstd_rep = sbP("rstd_rep", (128, TC))
    rs2_rep = sbP("rs2_rep", (128, TC))
    pg_f8 = [sbP(f"pg_f8{i}", (128, 1024), F8) for i in range(2)]

    c_sel = sbM("c_sel", (32, 32), BF16)
    xbp = sbM("xbp", (128, NG * (TC + 3)), BF16)
    zg_bf = sbM("zg_bf", (128, NG * TC), BF16)
    rny2 = sbM("rny2", (128, NG * TC), BF16)
    rn_sep = sbM("rn_sep", (128, NDT * TC), BF16)
    halo = sbM("halo", (128, NG * 3), BF16)
    dbc_sb = sbM("dbc_sb", (96, TC))
    bc_bf = sbM("bc_bf", (32, TC), BF16)
    drbf = sbM("drbf", (RK, TC), BF16)
    brep = sbM("brep", (128, NST * TC), BF16)
    crep = sbM("crep", (128, NST * TC), BF16)
    t_delta = [sbM(f"t_delta{i}", (128, TC)) for i in range(2)]
    t_sp = sbM("t_sp", (128, TC))
    t_u = [sbM(f"t_u{i}", (128, TC), BF16) for i in range(2)]
    t_dA = [sbM(f"t_dA{i}", (128, NGRP * TC)) for i in range(2)]
    t_mi = [sbM(f"t_mi{i}", (128, NGRP * TC), BF16) for i in range(2)]
    t_v = [sbM(f"t_v{i}", (128, NGRP * TC), BF16) for i in range(2)]
    t_h = [sbM(f"t_h{i}", (128, NGRP * TC), BF16) for i in range(2)]
    t_yAB = sbM("t_yAB", (128, 2 * TC), BF16)
    t_hC2 = sbM("t_hC2", (128, 2 * TC), BF16)
    t_xc = [sbM(f"t_xc{i}", (128, TC)) for i in range(2)]
    st_v = sbM("st_v", (128, NG * NST))
    st_h = sbM("st_h", (128, NG * NST))

    pb = [stP.enter_context(nc.psum_tensor(f"pb{i}", [128, 512], F32)) for i in range(8)]

    beta_b = c_const[:, 4:5].broadcast_to((128, TC))
    rn_bf = rn_sep

    def dma(dst, src, q="dms", reads=(), writes=()):
        eng = "sync" if q == "dms" else "gpsimd"
        sch.op(eng, lambda e, d=dst, s=src: e.dma_start(out=d, in_=s),
               reads=reads, writes=writes, dma=q)

    dma(c_ones[:], dram["ones_col"], writes=["c_ones"])
    dma(c_onesr[:], dram["ones_row"], writes=["c_onesr"])
    dma(c_sel[:], dram["sel32"], writes=["c_sel"])
    dma(c_const[:], dram["consts"], writes=["c_const"])
    dma(x2_st[0:1, 0:NF], dram["b1r"], writes=["ab1"])
    sch.op("vector", lambda e: e.tensor_copy(r_b1q[:], x2_st[0:1, 0:NF]),
           reads=["ab1"], writes=["r_b1q"])
    dma(x2_st[0:1, 0:D], dram["b2r"], writes=["ab1"])
    sch.op("vector", lambda e: e.tensor_copy(r_b2q[:], x2_st[0:1, 0:D]),
           reads=["ab1"], writes=["r_b2q"])
    dma(x2_st[0:1, 0:TC], dram["onesr"], writes=["ab1"])
    sch.op("vector", lambda e: e.tensor_copy(r_ones8[:], x2_st[0:1, 0:TC]),
           reads=["ab1"], writes=["r_ones8"])
    for l in range(NL):
        dma(x2_st[0:RK, 0:E], dram[f"dtwT{l}"], writes=["ab1"])
        sch.op("vector", lambda e, l=l: e.tensor_copy(
            r_dtw[:, l * E:(l + 1) * E], x2_st[0:RK, 0:E]),
            reads=["ab1"], writes=["r_dtw"])
        dma(x2_st[:, 0:NG * 96], dram[f"xpw_a{l}"], writes=["ab1"])
        sch.op("vector", lambda e, l=l: e.tensor_copy(
            r_xpw[:, l * NG * 96:(l + 1) * NG * 96], x2_st[:, 0:NG * 96]),
            reads=["ab1"], writes=["r_xpw"])
        dma(x2_st[:, 0:NG * NST], dram[f"alog_a{l}"], writes=["ab1"])
        sch.op("scalar", lambda e: e.activation(a_st[:, 0:NG * NST], x2_st[:, 0:NG * NST], AF.Exp),
               reads=["ab1"], writes=["ab0"])
        sch.op("vector", lambda e, l=l: e.tensor_single_scalar(
            r_wneg[:, l * NG * NST:(l + 1) * NG * NST], a_st[:, 0:NG * NST], -1.0, OP.mult),
            reads=["ab0"], writes=["r_wneg"])
        for nm, buf in (("dtb_a", r_dtb), ("cb_a", r_cb), ("dp_a", r_dp)):
            dma(buf[:, l * NG:(l + 1) * NG], dram[f"{nm}{l}"], writes=[f"r_{nm}"])
        dma(r_cw[:, l * NG * 4:(l + 1) * NG * 4], dram[f"cw_a{l}"], writes=["r_cw"])
        dma(r_nw[:, l * NDT:(l + 1) * NDT], dram[f"nw_a{l}"], writes=["r_nw"])

    prep_i = [0]

    def prep_cast(src_ap, dst_ap, rows, cols, outdt, skey):
        assert outdt == BF16
        for p in range(rows // 128):
            rs = p * 128
            for cs in range(0, cols, 1024):
                cw = min(1024, cols - cs)
                j = prep_i[0] % 4
                prep_i[0] += 1
                pout, kout = wS[j][:, 0:cw], f"wS{j}"
                dma(pout, src_ap[rs:rs + 128, cs:cs + cw], q="dmg", writes=[kout])
                dma(dst_ap[rs:rs + 128, cs:cs + cw], pout, q="dmg",
                    reads=[kout], writes=[skey])


    def chunk_view(t, c):
        return dram[t][:, c * TC:(c + 1) * TC].rearrange("(g p) t -> p g t", p=128)

    def stg3(buf):
        return buf[:].rearrange("p (g t) -> p g t", t=TC)

    abuf = [a_st, x2_st]

    def ln_chunk(c):
        par = c % 2
        A = abuf[par]
        ak = f"ab{par}"
        dma(stg3(A), chunk_view("x_t", c), writes=[ak])
        for dt in range(NDT):
            i = dt % 2
            sch.op("scalar", lambda e, dt=dt, A=A, i=i: e.square(
                t_xc[i][:], A[:, dt * TC:(dt + 1) * TC]),
                reads=[ak], writes=[f"xc{i}"])
            sch.op("tensor", lambda e, dt=dt, i=i: e.matmul(
                pb[0][0:1, :], c_ones[:], A[:, dt * TC:(dt + 1) * TC],
                start=(dt == 0), stop=(dt == NDT - 1)),
                reads=[ak, "c_ones"], writes=["pb0"])
            sch.op("tensor", lambda e, dt=dt, i=i: e.matmul(
                pb[1][0:1, :], c_ones[:], t_xc[i][:],
                start=(dt == 0), stop=(dt == NDT - 1)),
                reads=[f"xc{i}", "c_ones"], writes=["pb1"])
        sch.op("vector", lambda e: e.tensor_single_scalar(row_m, pb[0][0:1, :], 1.0 / D, OP.mult),
               reads=["pb0"], writes=["row_m"])
        sch.op("vector", lambda e: e.tensor_single_scalar(row_e2, pb[1][0:1, :], 1.0 / D, OP.mult),
               reads=["pb1"], writes=["row_e2"])
        sch.op("vector", lambda e: e.tensor_mul(row_rstd, row_m, row_m),
               reads=["row_m"], writes=["row_rstd"])
        sch.op("vector", lambda e: e.tensor_sub(row_e2, row_e2, row_rstd),
               reads=["row_rstd", "row_e2"], writes=["row_e2"])
        sch.op("scalar", lambda e: e.activation(row_rstd, row_e2, AF.Ln, bias=1e-5),
               reads=["row_e2"], writes=["row_rstd"])
        sch.op("scalar", lambda e: e.activation(row_rstd, row_rstd, AF.Exp, scale=-0.5),
               reads=["row_rstd"], writes=["row_rstd"])
        sch.op("vector", lambda e: e.tensor_mul(row_rs2, row_m, row_rstd),
               reads=["row_m", "row_rstd", "row_e2"], writes=["row_rs2"])
        sch.op("tensor", lambda e: e.matmul(pb[2][:], c_onesr[:], row_rstd,
                                            start=True, stop=True),
               reads=["row_rstd", "c_onesr"], writes=["pb2"])
        sch.op("tensor", lambda e: e.matmul(pb[3][:], c_onesr[:], row_rs2,
                                            start=True, stop=True),
               reads=["row_rs2", "c_onesr"], writes=["pb3"])
        sch.op("vector", lambda e: e.tensor_copy(rstd_rep[:], pb[2][:]),
               reads=["pb2"], writes=["rstd_rep"])
        sch.op("vector", lambda e: e.tensor_copy(rs2_rep[:], pb[3][:]),
               reads=["pb3"], writes=["rs2_rep"])
        rb = rstd_rep[:].unsqueeze(1).broadcast_to((128, NDT, TC))
        s2 = rs2_rep[:].unsqueeze(1).broadcast_to((128, NDT, TC))
        av = A[:].rearrange("p (g t) -> p g t", g=NDT)
        sch.op("vector", lambda e, av=av, rb=rb: e.tensor_tensor(av, av, rb, OP.mult),
               reads=["rstd_rep", ak], writes=[ak])
        sch.op("vector", lambda e, av=av, s2=s2: e.tensor_tensor(av, av, s2, OP.subtract),
               reads=["rs2_rep", ak], writes=[ak])
        dma(chunk_view("s_a", c), stg3(A), q="dmg", reads=[ak], writes=[f"s_a{c}"])
        dma(chunk_view("s_xn", c), stg3(A), q="dmg", reads=[ak], writes=["s_xn"])
        if DBG:
            dma(dbg_xn[:, c * TC:(c + 1) * TC].rearrange("(g p) t -> p g t", p=128),
                stg3(A), q="dmg", reads=[ak], writes=["dbg"])

    ln_chunk(0)
    prep_cast(dram["owT0"], dram["s_owT0"], E, D, BF16, "scr_ow0")
    ln_chunk(1)
    ln_chunk(2)
    prep_cast(dram["owT1"], dram["s_owT1"], E, D, BF16, "scr_ow1")
    ln_chunk(3)

    def head_ops(l, c):
        par = c % 2
        A = abuf[par]
        ak = f"ab{par}"
        th = []

        def T(engine, fn, reads=(), writes=(), dma_q=None, ms=0):
            th.append((engine, fn, list(reads), list(writes), dma_q, ms))

        if c == 0:
            T("sync", lambda e: e.dma_start(out=halo[:], in_=dram["zeros48"]),
              [], ["halo"], "dms")
        T("sync", lambda e, A=A, c=c: e.dma_start(out=stg3(A), in_=chunk_view("s_a", c)),
          [f"s_a{c}", "x2h0", "x2h1", "x2h2", "x2h3"] if par == 1 else [f"s_a{c}"], [ak], "dms")
        for dt in range(NDT):
            i = dt % 2
            T("scalar", lambda e, dt=dt, A=A, i=i: e.square(
                t_xc[i][:], A[:, dt * TC:(dt + 1) * TC]), [ak], [f"xc{i}"])
            T("tensor", lambda e, dt=dt, i=i: e.matmul(
                pb[6][0:1, :], c_ones[:], t_xc[i][:],
                start=(dt == 0), stop=(dt == NDT - 1)),
                [f"xc{i}", "c_ones"], ["pb6"])
        T("scalar", lambda e: e.activation(row_rstd, pb[6][0:1, :], AF.Ln,
                                           scale=1.0 / D, bias=1e-5),
          ["pb6"], ["row_rstd"])
        T("scalar", lambda e: e.activation(row_rstd, row_rstd, AF.Exp, scale=-0.5),
          ["row_rstd"], ["row_rstd"])
        T("tensor", lambda e: e.matmul(pb[6][:], c_onesr[:], row_rstd,
                                       start=True, stop=True),
          ["row_rstd", "c_onesr"], ["pb6"])
        T("vector", lambda e: e.tensor_copy(rstd_rep[:], pb[6][:]),
          ["pb6"], ["rstd_rep"])
        for dt in range(NDT):
            T("vector", lambda e, dt=dt, l=l, A=A: e.scalar_tensor_tensor(
                rn_bf[:, dt * TC:(dt + 1) * TC], A[:, dt * TC:(dt + 1) * TC],
                r_nw[:, l * NDT + dt:l * NDT + dt + 1], rstd_rep[:], OP.mult, OP.mult),
                [ak, "rstd_rep", "r_nw"], ["rn_bf"])
        xbv3 = xbp[:].rearrange("p (g t) -> p g t", g=NG)
        T("vector", lambda e, xbv3=xbv3: e.tensor_copy(
            xbv3[:, :, 0:3], halo[:].rearrange("p (g t) -> p g t", g=NG)),
            ["halo"], ["xbp_pre"])
        for f0, fn_ in [(0, 6), (6, 6), (12, 4)]:
            for half in range(2):
                gate99 = (f0 == 12)
                def wsrc(d, half=half, f0=f0, fn_=fn_, l=l):
                    return dram[f"in_wT{l}"][d * 128:(d + 1) * 128,
                                             (half * 16 + f0) * 128:(half * 16 + f0 + fn_) * 128]
                for d in range(4):
                    wi = d % 4
                    T("gpsimd", lambda e, d=d, wi=wi, wsrc=wsrc, fn_=fn_: e.dma_start(
                        out=wS[wi][:, 0:fn_ * 128], in_=wsrc(d)),
                        [], [f"wS{wi}"], "dmg", ms=(99 if gate99 else 0))
                for d in range(NDT):
                    wi = d % 4
                    for fi in range(fn_):
                        T("tensor", lambda e, d=d, fi=fi, wi=wi: e.matmul(
                            pb[fi][:], wS[wi][:, fi * 128:(fi + 1) * 128],
                            rn_bf[:, d * TC:(d + 1) * TC],
                            start=(d == 0), stop=(d == NDT - 1)),
                            [f"wS{wi}", "rn_bf"], [f"pb{fi}"])
                    if d + 4 < NDT:
                        wi2 = (d + 4) % 4
                        T("gpsimd", lambda e, d=d, wi2=wi2, wsrc=wsrc, fn_=fn_: e.dma_start(
                            out=wS[wi2][:, 0:fn_ * 128], in_=wsrc(d + 4)),
                            [], [f"wS{wi2}"], "dmg")
                for fi in range(fn_):
                    g = f0 + fi
                    if half == 0:
                        T("scalar", lambda e, g=g, fi=fi: e.copy(
                            xbp[:, g * (TC + 3) + 3:(g + 1) * (TC + 3)], pb[fi][:]),
                            [f"pb{fi}"], [f"xbp{g}"], ms=g)
                    else:
                        T("scalar", lambda e, g=g, fi=fi: e.activation(
                            zg_bf[:, g * TC:(g + 1) * TC], pb[fi][:], AF.Silu),
                            [f"pb{fi}"], [f"zg{g}"], ms=g)
        T("vector", lambda e, xbv3=xbv3: e.tensor_copy(
            halo[:].rearrange("p (g t) -> p g t", g=NG), xbv3[:, :, TC:TC + 3]),
            [f"xbp{g}" for g in range(NG)] + ["xbp_pre"], ["halo"])
        for g in range(NG):
            xg = xbp[:, g * (TC + 3):(g + 1) * (TC + 3)]
            cwb = l * NG * 4 + g * 4
            T("vector", lambda e, xg=xg, cwb=cwb: e.tensor_scalar(
                t_xc[0][:], xg[:, 0:TC], r_cw[:, cwb:cwb + 1], None, OP.mult),
                [f"xbp{g}", "xbp_pre", "halo", "r_cw"], ["xc0"])
            for k in (1, 2, 3):
                si, so = (0, 1) if k % 2 == 1 else (1, 0)
                T("vector", lambda e, xg=xg, cwb=cwb, k=k, si=si, so=so:
                  e.scalar_tensor_tensor(
                      t_xc[so][:], xg[:, k:TC + k], r_cw[:, cwb + k:cwb + k + 1],
                      t_xc[si][:], OP.mult, OP.add),
                  [f"xc{si}", f"xbp{g}", "r_cw"], [f"xc{so}"])
            T("scalar", lambda e, xg=xg, g=g, l=l: e.activation(
                xg[:, 3:TC + 3], t_xc[1][:], AF.Silu,
                bias=r_cb[:, l * NG + g:l * NG + g + 1]),
                ["xc1", "r_cb"], [f"xbp{g}"], ms=g)
            T("tensor", lambda e, g=g, l=l: e.matmul(
                pb[6][0:96, :], r_xpw[:, (l * NG + g) * 96:(l * NG + g + 1) * 96],
                xbp[:, g * (TC + 3) + 3:(g + 1) * (TC + 3)],
                start=(g == 0), stop=(g == NG - 1)),
                [f"xbp{g}", "r_xpw"], ["pb6"], ms=g)
        T("vector", lambda e: e.tensor_copy(dbc_sb[:], pb[6][0:96, :]),
          ["pb6"], ["dbc_sb"])
        T("vector", lambda e: e.tensor_copy(drbf[:], dbc_sb[0:RK, :]),
          ["dbc_sb"], ["drbf"])
        T("gpsimd", lambda e: e.dma_start(out=bc_bf[:], in_=dbc_sb[RK:96, :]),
          ["dbc_sb"], ["bc_bf"], "dmg")
        for jj in range(32):
            nn, isC = jj // 2, jj % 2
            j = nn + 16 * isC
            bank = 6 + (jj % 2)
            T("tensor", lambda e, j=j, bank=bank: e.matmul(
                pb[bank][:], c_sel[:, j:j + 1].broadcast_to((32, 128)), bc_bf[:],
                start=True, stop=True),
                ["bc_bf", "c_sel"], [f"pb{bank}"])
            dstb, dkey = (brep, f"brep{nn}") if not isC else (crep, f"crep{nn}")
            T("scalar", lambda e, nn=nn, bank=bank, dstb=dstb: e.copy(
                dstb[:, nn * TC:(nn + 1) * TC], pb[bank][:]),
                [f"pb{bank}"], [dkey])
        return th

    def emit_thunks(th):
        for engine, fn, reads, writes, dma_q, ms in th:
            sch.op(engine, fn, reads=reads, writes=writes, dma=dma_q)

    def emit_gated(th, pos, g):
        while pos < len(th) and th[pos][5] <= g:
            engine, fn, reads, writes, dma_q, ms = th[pos]
            sch.op(engine, fn, reads=reads, writes=writes, dma=dma_q)
            pos += 1
        return pos

    seq = [(l, c) for l in range(NL) for c in range(NCH)]
    pend = head_ops(0, 0)
    emit_thunks(pend)

    for idx, (l, c) in enumerate(seq):
        par = c % 2
        A = abuf[par]
        ak = f"ab{par}"
        nxt = head_ops(*seq[idx + 1]) if idx + 1 < len(seq) else []
        if idx + 1 < len(seq) and seq[idx + 1][1] == 0:
            pass
        nxt_pos = 0
        def dt_chain(g):
            sl = g % 2
            gl = l * NG + g
            sch.op("tensor", lambda e, g=g, l=l: e.matmul(
                pb[7][:], r_dtw[:, l * E + g * 128:l * E + (g + 1) * 128], drbf[:],
                start=True, stop=True),
                reads=["drbf", "r_dtw"], writes=["pb7"])
            sch.op("scalar", lambda e, gl=gl: e.activation(
                t_sp[:], pb[7][:], AF.Abs, bias=r_dtb[:, gl:gl + 1]),
                reads=["pb7", "r_dtb"], writes=["tsp"])
            sch.op("scalar", lambda e: e.activation(t_sp[:], t_sp[:], AF.Exp, scale=-1.0),
                   reads=["tsp"], writes=["tsp"])
            sch.op("scalar", lambda e: e.activation(t_sp[:], t_sp[:], AF.Ln, bias=1.0),
                   reads=["tsp"], writes=["tsp"])
            sch.op("scalar", lambda e, gl=gl, sl=sl: e.activation(
                t_delta[sl][:], pb[7][:], AF.Relu, bias=r_dtb[:, gl:gl + 1]),
                reads=["pb7", "r_dtb"], writes=[f"tdelta{sl}"])
            sch.op("vector", lambda e, sl=sl: e.tensor_add(
                t_delta[sl][:], t_delta[sl][:], t_sp[:]),
                reads=["tsp", f"tdelta{sl}"], writes=[f"tdelta{sl}"])
            sch.op("vector", lambda e, g=g, sl=sl: e.tensor_tensor(
                t_u[sl][:], t_delta[sl][:], xbp[:, g * (TC + 3) + 3:(g + 1) * (TC + 3)],
                OP.mult),
                reads=[f"tdelta{sl}", f"xbp{g}"], writes=[f"tu{sl}"])

        dt_chain(0)
        for g in range(NG):
            sl = g % 2
            gl = l * NG + g
            if g + 1 < NG:
                dt_chain(g + 1)
            for ng in range(NST // NGRP):
                bsl = ng % 2
                n0 = ng * NGRP
                for k in range(NGRP):
                    n = ng * NGRP + k
                    sch.op("scalar", lambda e, n=n, k=k, gl=gl, sl=sl, bsl=bsl: e.activation(
                        t_dA[bsl][:, k * TC:(k + 1) * TC], t_delta[sl][:], AF.Exp,
                        scale=r_wneg[:, gl * NST + n:gl * NST + n + 1]),
                        reads=[f"tdelta{sl}", "r_wneg"], writes=[f"tdA{bsl}"])
                sch.op("gpsimd", lambda e, bsl=bsl, sl=sl, n0=n0: e.tensor_tensor(
                    t_mi[bsl][:].rearrange("p (k t) -> p k t", k=NGRP),
                    t_u[sl][:].unsqueeze(1).broadcast_to((128, NGRP, TC)),
                    crep[:, n0 * TC:(n0 + 2) * TC].rearrange("p (k t) -> p k t", k=NGRP)
                    if False else
                    brep[:, n0 * TC:(n0 + 2) * TC].rearrange("p (k t) -> p k t", k=NGRP),
                    OP.mult),
                    reads=[f"tu{sl}", f"brep{n0}", f"brep{n0 + 1}"], writes=[f"tmi{bsl}"])
                for k in range(NGRP):
                    n = ng * NGRP + k
                    stc = g * NST + n
                    iv = 0.0 if c == 0 else st_v[:, stc:stc + 1]
                    ih = 0.0 if c == 0 else st_h[:, stc:stc + 1]
                    sch.op("vector", lambda e, k=k, bsl=bsl, iv=iv: e.tensor_tensor_scan(
                        t_v[bsl][:, k * TC:(k + 1) * TC], beta_b,
                        t_mi[bsl][:, k * TC:(k + 1) * TC], iv, OP.mult, OP.add),
                        reads=[f"tmi{bsl}", "st_v", "c_const"], writes=[f"tv{bsl}"])
                    sch.op("vector", lambda e, k=k, bsl=bsl, ih=ih: e.tensor_tensor_scan(
                        t_h[bsl][:, k * TC:(k + 1) * TC],
                        t_dA[bsl][:, k * TC:(k + 1) * TC],
                        t_v[bsl][:, k * TC:(k + 1) * TC], ih, OP.mult, OP.add),
                        reads=[f"tdA{bsl}", f"tv{bsl}", "st_h"], writes=[f"th{bsl}"])
                if c < NCH - 1:
                    st0 = g * NST + ng * NGRP
                    sch.op("vector", lambda e, bsl=bsl, st0=st0: e.tensor_copy(
                        st_v[:, st0:st0 + NGRP],
                        t_v[bsl][:, TC - 1:NGRP * TC:TC]),
                        reads=[f"tv{bsl}"], writes=["st_v"])
                    sch.op("vector", lambda e, bsl=bsl, st0=st0: e.tensor_copy(
                        st_h[:, st0:st0 + NGRP],
                        t_h[bsl][:, TC - 1:NGRP * TC:TC]),
                        reads=[f"th{bsl}"], writes=["st_h"])
                n0 = ng * NGRP
                if ng == 0:
                    sch.op("vector", lambda e, bsl=bsl, n0=n0: e.tensor_tensor(
                        t_yAB[:], t_h[bsl][:],
                        crep[:, n0 * TC:(n0 + 2) * TC], OP.mult),
                        reads=[f"th{bsl}", f"crep{n0}", f"crep{n0 + 1}"],
                        writes=["tyA", "tyB"])
                else:
                    sch.op("vector", lambda e, bsl=bsl, n0=n0: e.tensor_tensor(
                        t_hC2[:], t_h[bsl][:],
                        crep[:, n0 * TC:(n0 + 2) * TC], OP.mult),
                        reads=[f"th{bsl}", f"crep{n0}", f"crep{n0 + 1}"],
                        writes=["thC"])
                    sch.op("vector", lambda e: e.tensor_add(
                        t_yAB[:], t_yAB[:], t_hC2[:]),
                        reads=["thC", "tyA", "tyB"], writes=["tyA", "tyB"])
            sch.op("vector", lambda e, g=g, gl=gl: e.scalar_tensor_tensor(
                t_xc[0][:], xbp[:, g * (TC + 3) + 3:(g + 1) * (TC + 3)],
                r_dp[:, gl:gl + 1], t_yAB[:, 0:TC], OP.mult, OP.add),
                reads=[f"xbp{g}", "tyA", "r_dp"], writes=["xc0"])
            sch.op("vector", lambda e: e.tensor_add(t_xc[0][:], t_xc[0][:], t_yAB[:, TC:2 * TC]),
                   reads=["tyB", "xc0"], writes=["xc0"])
            sch.op("vector", lambda e, g=g: e.tensor_tensor(
                rny2[:, g * TC:(g + 1) * TC], t_xc[0][:],
                zg_bf[:, g * TC:(g + 1) * TC], OP.mult),
                reads=["xc0", f"zg{g}"], writes=[f"y2{g}"])
            if nxt and PIPE:
                nxt_pos = emit_gated(nxt, nxt_pos, g)
        for g in range(NG):
            wbuf, key = wS[g % 4], f"wS{g % 4}"
            dma(wbuf[:, 0:1024], dram[f"s_owT{l}"][g * 128:(g + 1) * 128, :],
                reads=[f"scr_ow{l}"], writes=[key])
            for d in range(NDT):
                sch.op("tensor", lambda e, g=g, d=d, wbuf=wbuf: e.matmul(
                    pb[d][:], wbuf[:, d * 128:(d + 1) * 128],
                    rny2[:, g * TC:(g + 1) * TC],
                    start=(g == 0), stop=(g == NG - 1)),
                    reads=[key, f"y2{g}"], writes=[f"pb{d}"])
        for d in range(NDT):
            sch.op("vector", lambda e, d=d, A=A: e.tensor_add(
                A[:, d * TC:(d + 1) * TC], A[:, d * TC:(d + 1) * TC], pb[d][:]),
                reads=[f"pb{d}"], writes=[ak])
        dma(chunk_view("s_a", c), stg3(A), q="dmg", reads=[ak], writes=[f"s_a{c}"])
        if DBG:
            dma(dbg_a[l][:, c * TC:(c + 1) * TC].rearrange("(g p) t -> p g t", p=128),
                stg3(A), q="dmg", reads=[ak], writes=["dbg"])
        while nxt_pos < len(nxt):
            e_, f_, r_, w_, d_, m_ = nxt[nxt_pos]
            sch.op(e_, f_, reads=r_, writes=w_, dma=d_)
            nxt_pos += 1

    sch.barrier()
    stM.close()
    stF = ExitStack()

    def sbF(name, shape, dt=F32):
        return stF.enter_context(nc.sbuf_tensor(name, list(shape), dt))

    f_w1 = sbF("f_w1", (128, 8 * 4096), F8)
    f_w2 = sbF("f_w2", (128, 32 * 1024), F8)
    f_h8_ = [sbF(f"f_h8_{i}", (128, 32 * TC), F8) for i in range(2)]
    f_hb_ = [sbF(f"f_hb{i}", (128, TC), BF16) for i in range(2)]
    f_fb = sbF("f_fb", (128, NDT * TC), BF16)
    f_msk = sbF("f_msk", (128, NDT * TC), BF16)
    f_q8full = sbF("f_q8full", (128, NDT * L), F8)

    for d in range(NDT):
        dma(f_w1[:, d * 4096:(d + 1) * 4096], dram["W1T"][d * 128:(d + 1) * 128, :],
            q="dmg", reads=[], writes=["f_w1"])
    for f in range(32):
        dma(f_w2[:, f * 1024:(f + 1) * 1024], dram["W2T"][f * 128:(f + 1) * 128, :],
            q="dmg", reads=[], writes=["f_w2"])
    for c in range(NCH):
        dma(stg3(x2_st), chunk_view("s_xn", c),
            reads=["s_xn", "x2h0", "x2h1", "x2h2", "x2h3"] if c == 0 else ["s_xn"],
            writes=["ab1"])
        sch.op("vector", lambda e, c=c: e.tensor_copy(
            f_q8full[:].rearrange("p (g t) -> p g t", t=L)[:, :, c * TC:(c + 1) * TC],
            stg3(x2_st)),
            reads=["ab1"], writes=[f"fq8c{c}"])
    for c in range(NCH):
        fpar = c % 2
        f_h8 = f_h8_[fpar]
        kh8 = f"f_h8{fpar}"
        dma(stg3(f_msk), chunk_view("mask_t", c), writes=["f_msk"])
        dma(stg3(a_st), chunk_view("s_a", c), reads=[f"s_a{c}"], writes=["ab0"])
        dma(stg3(x2_st), chunk_view("x_t", c), writes=["ab1"])
        for fg in range(8):
            for f4i in range(4):
                f = fg * 4 + f4i
                bk = 4 + f4i
                for d in range(NDT):
                    sch.op("tensor", lambda e, f=f, d=d, bk=bk, c=c: e.matmul(
                        pb[bk][:], f_w1[:, d * 4096 + f * 128:d * 4096 + (f + 1) * 128],
                        f_q8full[:, d * L + c * TC:d * L + (c + 1) * TC],
                        start=(d == 0), stop=False),
                        reads=["f_w1", f"fq8c{c}"], writes=[f"pb{bk}"])
                sch.op("tensor", lambda e, f=f, bk=bk: e.matmul(
                    pb[bk][:], r_b1q[0:1, f * 128:(f + 1) * 128], r_ones8[:],
                    start=False, stop=True),
                    reads=["r_b1q", "r_ones8"], writes=[f"pb{bk}"])
            for f4i in range(4):
                f = fg * 4 + f4i
                bk = 4 + f4i
                hb = f_hb_[f4i % 2]
                hbk = f"f_hb{f4i % 2}"
                sch.op("scalar", lambda e, bk=bk, hb=hb: e.copy(hb[:], pb[bk][:]),
                       reads=[f"pb{bk}"], writes=[hbk])
                sch.op("vector", lambda e, hb=hb: e.tensor_relu(hb[:], hb[:]),
                       reads=[hbk], writes=[hbk])
                sch.op("vector", lambda e, f=f, f_h8=f_h8, hb=hb: e.tensor_copy(
                    f_h8[:, f * TC:(f + 1) * TC], hb[:]),
                    reads=[hbk], writes=[kh8])
        for d in range(NDT):
            bk = d % 4
            for f in range(32):
                sch.op("tensor", lambda e, d=d, f=f, bk=bk, f_h8=f_h8: e.matmul(
                    pb[bk][:], f_w2[:, f * 1024 + d * 128:f * 1024 + (d + 1) * 128],
                    f_h8[:, f * TC:(f + 1) * TC],
                    start=(f == 0), stop=False),
                    reads=["f_w2", kh8], writes=[f"pb{bk}"])
            sch.op("tensor", lambda e, d=d, bk=bk: e.matmul(
                pb[bk][:], r_b2q[0:1, d * 128:(d + 1) * 128], r_ones8[:],
                start=False, stop=True),
                reads=["r_b2q", "r_ones8"], writes=[f"pb{bk}"])
            sch.op("scalar", lambda e, d=d, bk=bk: e.copy(
                f_fb[:, d * TC:(d + 1) * TC], pb[bk][:]),
                reads=[f"pb{bk}"], writes=["f_fb"])
        sch.op("vector", lambda e: e.tensor_add(a_st[:], a_st[:], x2_st[:]),
               reads=["ab1", "ab0"], writes=["ab0"])
        sch.op("vector", lambda e: e.tensor_mul(x2_st[:], f_fb[:], f_msk[:]),
               reads=["f_fb", "f_msk", "ab1"], writes=["ab1"])
        sch.op("vector", lambda e: e.tensor_add(a_st[:], a_st[:], x2_st[:]),
               reads=["ab1", "ab0"], writes=["ab0"])
        dma(out_t[:, c * TC:(c + 1) * TC].rearrange("(g p) t -> p g t", p=128),
            stg3(a_st), reads=["ab0"], writes=["out"])

    with nc.Block() as block:
        sch.emit(nc, block)
    stF.close()
    stP.close()
    return nc


_PROG = None


def _get_prog():
    global _PROG
    if _PROG is None:
        _PROG = build_program()
    return _PROG


def kernel(L0_attn_x, L0_attn_norm_w, L0_attn_in_proj_w, L0_attn_conv_w,
           L0_attn_conv_b, L0_attn_x_proj_w, L0_attn_dt_w, L0_attn_dt_b,
           L0_attn_A_log, L0_attn_Dp, L0_attn_out_w, L0_ffn_W1, L0_ffn_b1,
           L0_ffn_W2, L0_ffn_b2):
    import jax
    x = np.ascontiguousarray(np.asarray(L0_attn_x, np.float32))
    B = x.shape[0]
    with jax.default_device(jax.devices("cpu")[0]):
        jax.config.update("jax_threefry_partitionable", True)
        keep = np.asarray(jax.random.bernoulli(
            jax.random.key(42, impl="threefry2x32"), 0.8, (8, 2048, 1024)))
    mask = (keep.astype(np.float32) * np.float32(1.25)).astype(BF_NP)

    def arr(a):
        return np.ascontiguousarray(np.asarray(a, np.float32))

    common = {}
    for l in range(NL):
        common[f"in_wT{l}"] = arr(np.asarray(L0_attn_in_proj_w)[l].T)
        common[f"owT{l}"] = arr(np.asarray(L0_attn_out_w)[l].T)
        xpw = arr(np.asarray(L0_attn_x_proj_w)[l].T)
        common[f"xpw_a{l}"] = np.ascontiguousarray(
            xpw.reshape(NG, 128, 96).transpose(1, 0, 2).reshape(128, NG * 96))
        common[f"dtwT{l}"] = arr(np.asarray(L0_attn_dt_w)[l].T)
        al = arr(np.asarray(L0_attn_A_log)[l])
        common[f"alog_a{l}"] = np.ascontiguousarray(
            al.reshape(NG, 128, NST).transpose(1, 0, 2).reshape(128, NG * NST))
        for nm, src in (("dtb_a", L0_attn_dt_b), ("cb_a", L0_attn_conv_b),
                        ("dp_a", L0_attn_Dp)):
            v = arr(np.asarray(src)[l])
            common[f"{nm}{l}"] = np.ascontiguousarray(v.reshape(NG, 128).T)
        cw = arr(np.asarray(L0_attn_conv_w)[l])
        common[f"cw_a{l}"] = np.ascontiguousarray(
            cw.reshape(NG, 128, 4).transpose(1, 0, 2).reshape(128, NG * 4))
        nw = arr(np.asarray(L0_attn_norm_w)[l])
        common[f"nw_a{l}"] = np.ascontiguousarray(nw.reshape(NDT, 128).T)
    common["W1T"] = arr(np.asarray(L0_ffn_W1).T)
    common["W2T"] = arr(np.asarray(L0_ffn_W2).T)
    common["b1r"] = arr(L0_ffn_b1).reshape(1, NF)
    common["b2r"] = arr(L0_ffn_b2).reshape(1, D)
    common["onesr"] = np.ones((1, TC), np.float32)
    common["ones_col"] = np.ones((128, 1), np.float32)
    common["ones_row"] = np.ones((1, 128), np.float32)
    common["sel32"] = np.eye(32, dtype=np.float32).astype(BF_NP)
    consts = np.zeros((128, 8), np.float32)
    consts[:, 0] = -1.0
    consts[:, 1] = 1.0
    consts[:, 2] = -0.5
    consts[:, 3] = 1e-5
    consts[:, 4] = BETA
    consts[:, 5] = 1.0 / D
    common["consts"] = consts
    common["zeros48"] = np.zeros((128, 48), BF_NP)

    in_maps = []
    for b in range(B):
        m = dict(common)
        m["x_t"] = np.ascontiguousarray(x[b].T)
        m["mask_t"] = np.ascontiguousarray(mask[b].T)
        in_maps.append(m)

    nc = _get_prog()
    res = run_bass_kernel_spmd(nc, in_maps, core_ids=list(range(B)))
    out = np.stack([np.asarray(r["out_t"]).T for r in res.results], axis=0)
    return out.astype(np.float32)
